# version 07
# baseline (speedup 1.0000x reference)
import math
from collections import defaultdict
from contextlib import ExitStack

import numpy as np
import ml_dtypes

import concourse.bass as bass
import concourse.mybir as mybir
from concourse.bass_utils import run_bass_kernel_spmd

F32 = mybir.dt.float32
BF16 = mybir.dt.bfloat16
I32 = mybir.dt.int32
AF = mybir.ActivationFunctionType
ALU = mybir.AluOpType
AXX = mybir.AxisListType.X

BF = ml_dtypes.bfloat16

D = 2048
T = 4096
B = 2
NH = 16
HD = 128
HP = 8
DFF = 8192
NCORE = 8
NTOK = 1024
CH = 512
LAMBDA_INIT = 0.8 - 0.6 * float(np.exp(-0.3 * 6))
SC_HI = np.float32(np.float64(HD) ** -0.5)
SC_LO = np.float32(np.float64(HD) ** -0.5 - np.float64(SC_HI))
PI_LIM = np.float32(3.1415925)

FIXUP_ROUND = False


def core_chunks(c):
    p = c % 4
    return (p, 7 - p)


def core_token_slices(c):
    a, b_ = core_chunks(c)
    return (slice(a * CH, (a + 1) * CH), slice(b_ * CH, (b_ + 1) * CH))


class Prog:
    ENGINES = ("sync", "scalar", "vector", "tensor", "gpsimd")

    def __init__(self, name="prog"):
        self.nc = bass.Bass("TRN2", target_bir_lowering=False)
        self._stack = ExitStack()
        self._stack.__enter__()
        self.ops = {e: [] for e in self.ENGINES}
        self._sems = {}
        self._semnames = set()
        self.cnt = defaultdict(int)
        self.name = name
        self.tag = ""
        self._stage_stack = None

    def begin_stage(self, tag):
        if self._stage_stack is not None:
            snap = [(s, v) for s, v in self.cnt.items() if v > 0]
            for eng in self.ENGINES:
                self.wait(eng, *snap)
            self._stage_stack.close()
        self._stage_stack = ExitStack()
        self.tag = tag + "_"

    def sbuf(self, name, shape, dt=F32):
        st = self._stage_stack if self._stage_stack is not None else self._stack
        return st.enter_context(self.nc.sbuf_tensor(self.tag + name, shape, dt))

    def psum(self, name, shape, dt=F32):
        st = self._stage_stack if self._stage_stack is not None else self._stack
        return st.enter_context(self.nc.psum_tensor(self.tag + name, shape, dt))

    def dram(self, name, shape, dt=F32, kind="Internal"):
        return self.nc.dram_tensor(self.tag + name, shape, dt, kind=kind)

    def dram_io(self, io, name, shape, dt=F32, kind="Internal"):
        if io is not None and name in io:
            return io[name]
        return self.dram(name, shape, dt, kind=kind)

    def semh(self, name):
        if name not in self._sems:
            self._sems[name] = self._stack.enter_context(self.nc.semaphore(name))
        return self._sems[name]

    def raw(self, eng, fn):
        self.ops[eng].append(fn)

    AUTO = {"vector": "dv", "scalar": "ac", "gpsimd": "pl"}

    def op(self, eng, method, *args, sem=None, inc=1, waits=(), **kw):
        waits = tuple(waits)
        for (ws, _) in waits:
            self._semnames.add(ws)
        auto = self.AUTO.get(eng)
        if auto is not None and method != "dma_start":
            if sem is None:
                sem = auto
            if sem == auto and self.cnt[auto] > 0:
                waits = waits + ((auto, self.cnt[auto]),)
        if sem is not None:
            self._semnames.add(sem)
            self.cnt[sem] += inc
            target = self.cnt[sem]
        else:
            target = None
        semname = sem
        myinc = inc

        _opidx = len(self.ops[eng])

        def fn(engine, _m=method, _a=args, _k=kw, _w=waits, _s=semname, _i=myinc,
               _oi=_opidx):
            for (ws, wv) in _w:
                if wv > 0:
                    engine.wait_ge(self.semh(ws), wv)
            aa = [a() if callable(a) else a for a in _a]
            kk = {k2: (v() if callable(v) else v) for k2, v in _k.items()}
            try:
                instr = getattr(engine, _m)(*aa, **kk)
            except Exception as e:
                import traceback as _tb
                det = [str(getattr(a, "shape", a))[:60] for a in aa]
                raise RuntimeError(f"op#{_oi} {_m} on {engine} failed: {e} args={det}\n"
                                   f"INNER:\n{_tb.format_exc()}") from e
            if _s is not None:
                instr.then_inc(self.semh(_s), _i)
            return instr

        self.ops[eng].append(fn)
        return target

    def wait(self, eng, *waits):
        for (ws, _) in waits:
            self._semnames.add(ws)
        def fn(engine, _w=tuple(waits)):
            for (ws, wv) in _w:
                if wv > 0:
                    engine.wait_ge(self.semh(ws), wv)
        self.ops[eng].append(fn)

    def dma(self, queue, out, in_, sem=None, waits=(), **kw):
        return self.op(queue, "dma_start", sem=sem, inc=16, waits=waits,
                       out=out, in_=in_, **kw)

    def allgather(self, src_d, dst_d, groups, sem, waits=()):
        self._semnames.add(sem)
        for (ws, _) in waits:
            self._semnames.add(ws)
        self.cnt[sem] += 1
        target = self.cnt[sem]

        def fn(engine, _w=tuple(waits), _t=target):
            for (ws, wv) in _w:
                if wv > 0:
                    engine.wait_ge(self.semh(ws), wv)
            engine.collective_compute(
                "AllGather", ALU.bypass, replica_groups=groups,
                ins=[src_d.ap().opt()], outs=[dst_d.ap().opt()],
            ).then_inc(self.semh(sem))
        self.ops["gpsimd"].append(fn)
        return target

    def finalize(self):
        for n in sorted(self._semnames):
            self.semh(n)
        with self.nc.Block() as block:
            for eng in self.ENGINES:
                lst = self.ops[eng]
                if not lst:
                    continue
                deco = getattr(block, eng)

                def make(lst):
                    def body(engine):
                        for fn in lst:
                            fn(engine)
                    return body

                deco(make(lst))
        return self.nc


NF = D // 128
DBG_STRIP = ()
DBG_UNITS = 16
HALF = NTOK // 512


def bcast_row(p, row_ap, out_psum, ones1, sem_pe, waits):
    return p.op("tensor", "matmul", out_psum, ones1, row_ap,
                sem=sem_pe, waits=waits, start=True, stop=True)


def emit_round(p, src_fp32, dst_bf16, int_t, scr, eng="vector", waits=(), sem=None,
               fixup=None):
    if fixup is None:
        fixup = FIXUP_ROUND
    if sem is None:
        sem = Prog.AUTO.get(eng)
    c = p.op(eng, "tensor_copy", int_t, src_fp32, sem=sem, waits=waits)
    if not fixup:
        return p.op(eng, "tensor_copy", dst_bf16, int_t, sem=sem,
                    waits=((sem, c),))
    c = p.op(eng, "tensor_copy", scr, int_t, sem=sem, waits=((sem, c),))
    n = src_fp32.shape[-1]
    fu = p._fix_u[:, 0:n]
    fv = p._fix_v[:, 0:n]
    cu = p.op(eng, "scalar_tensor_tensor", fu, scr, 0.5, src_fp32,
              ALU.add, ALU.is_le, sem=sem, waits=((sem, c),))
    cv = p.op(eng, "scalar_tensor_tensor", fv, scr, -0.5, src_fp32,
              ALU.add, ALU.is_ge, sem=sem, waits=((sem, cu),))
    cw = p.op(eng, "tensor_add", scr, scr, fu, sem=sem, waits=((sem, cv),))
    return p.op(eng, "tensor_sub", dst_bf16, scr, fv, sem=sem,
                waits=((sem, cw),))


def alloc_fix(p, shape):
    p._fix_u = p.sbuf("fix_u", shape, F32)
    p._fix_v = p.sbuf("fix_v", shape, F32)


def shard_tokens(x_btd, c):
    b = c // 4
    s1, s2 = core_token_slices(c)
    return np.concatenate([x_btd[b, s1], x_btd[b, s2]], axis=0)


def unshard_tokens(shards):
    out = np.empty((B, T, shards[0].shape[1]), dtype=shards[0].dtype)
    for c in range(NCORE):
        b = c // 4
        s1, s2 = core_token_slices(c)
        out[b, s1] = shards[c][:CH]
        out[b, s2] = shards[c][CH:]
    return out


def fm(x_nd):
    return np.ascontiguousarray(x_nd.T)


def rope_host_tables(c):
    s1, s2 = core_token_slices(c)
    tvals = np.concatenate([np.arange(s1.start, s1.stop), np.arange(s2.start, s2.stop)])
    ln1e4 = np.log(np.float64(10000.0))
    hi = np.float32(ln1e4); lo = np.float32(ln1e4 - np.float64(hi))
    i = np.arange(64, dtype=np.float64)
    invf = np.exp(-(i / 64) * np.float64(hi)) * np.exp(-(i / 64) * np.float64(lo))
    ang = np.outer(invf, np.float64(tvals))
    twopi = 2 * np.pi
    k_s = np.rint(ang / twopi).astype(np.float32)
    k_c = np.rint((ang + np.pi / 2) / twopi).astype(np.float32)
    return (tvals.astype(np.float32)[None, :], k_s, k_c)


def build_prep(p=None, io=None):
    if p is None:
        p = Prog("prep")
    io = io or {}
    nc = p.nc
    wg_d = p.dram_io(io, "wg", [DFF, D], F32, kind="ExternalInput")
    wu_d = p.dram_io(io, "wu", [DFF, D], F32, kind="ExternalInput")
    wd_d = p.dram_io(io, "wd", [D, DFF], F32, kind="ExternalInput")
    fac_d = p.dram_io(io, "fac", [8, D, D], F32, kind="ExternalInput")
    ones_d = p.dram_io(io, "onesc", [1, 128], F32, kind="ExternalInput")
    onescol_d = p.dram_io(io, "onescol", [128, 1], F32, kind="ExternalInput")
    slice_d = p.dram_io(io, "slc", [1, 1], F32, kind="ExternalInput")

    tg_o = p.dram_io(io, "tg_T", [D, 1024], BF16, kind="ExternalOutput")
    tu_o = p.dram_io(io, "tu_T", [D, 1024], BF16, kind="ExternalOutput")
    td_o = p.dram_io(io, "td_T", [1024, D], BF16, kind="ExternalOutput")
    fs_o = p.dram_io(io, "fsum", [D, 256], BF16, kind="ExternalOutput")
    m_o = p.dram_io(io, "mvals", [1, 4], F32, kind="ExternalOutput")

    wgs_d = p.dram_io(io, "wgs", [1024, D], F32, kind="ExternalInput")
    wus_d = p.dram_io(io, "wus", [1024, D], F32, kind="ExternalInput")
    wds_d = p.dram_io(io, "wds", [D, 1024], F32, kind="ExternalInput")
    facs_d = p.dram_io(io, "facs", [8, D, 256], F32, kind="ExternalInput")

    ones1 = p.sbuf("ones1", [1, 128], F32)
    onescol = p.sbuf("onescol_t", [128, 1], F32)
    idn = p.sbuf("idn", [128, 128], BF16)
    asum = p.sbuf("asum", [128, 3], F32)
    arow = p.sbuf("arow", [1, 3], F32)
    mrow = p.sbuf("mrow", [1, 4], F32)
    minv = p.sbuf("minv", [128, 3], F32)
    wt = [p.sbuf(f"wt{i}", [128, D], F32) for i in range(2)]
    tt_ = [p.sbuf(f"tt{i}", [128, D], BF16) for i in range(2)]
    tvt = p.sbuf("tvt", [128, 2048], BF16)
    red = p.psum("red", [1, 512], F32)
    trp = p.psum("trp", [128, 128], BF16)
    redsc = p.sbuf("redsc", [128, 4], F32)
    fsl = [p.sbuf(f"fsl{i}", [128, 256], F32) for i in range(2)]
    facc = p.sbuf("facc", [128, 256], F32)
    fbf = p.sbuf("fbf", [128, 256], BF16)

    def absmean(which, w_dram, nrows, col):
        ntile = nrows // 128
        for i in range(ntile):
            buf = wt[i % 2]
            ld = p.dma("sync", buf[:], w_dram.ap()[i * 128:(i + 1) * 128, :],
                       sem="ldw", waits=(("dv", p.cnt["dv"]),))
            c1 = p.op("vector", "reduce_sum", redsc[:, 0:1], buf[:], AXX,
                      apply_absolute_value=True, sem="dv", waits=(("ldw", ld),))
            p.op("vector", "tensor_add", asum[:, col:col + 1], asum[:, col:col + 1],
                 redsc[:, 0:1], sem="dv", waits=(("dv", c1),))

    c0 = p.op("vector", "memset", asum[:], 0.0, sem="dv")
    p.dma("sync", ones1[:], ones_d.ap(), sem="ldc")
    p.dma("sync", onescol[:], onescol_d.ap(), sem="ldc")
    absmean(0, wg_d, DFF, 0)
    absmean(1, wu_d, DFF, 1)
    ntile = D // 128
    for i in range(ntile):
        buf = wt[i % 2]
        ld = p.dma("sync", buf[:], wd_d.ap()[i * 128:(i + 1) * 128, 0:2048],
                   sem="ldw", waits=(("dv", p.cnt["dv"]),))
        cc = ld
        for j in range(4):
            if j > 0:
                cc = p.dma("sync", buf[:], wd_d.ap()[i * 128:(i + 1) * 128, j * 2048:(j + 1) * 2048],
                           sem="ldw", waits=(("dv", p.cnt["dv"]),))
            c1 = p.op("vector", "reduce_sum", redsc[:, 0:1], buf[:], AXX,
                      apply_absolute_value=True, sem="dv", waits=(("ldw", cc),))
            p.op("vector", "tensor_add", asum[:, 2:3], asum[:, 2:3],
                 redsc[:, 0:1], sem="dv", waits=(("dv", c1),))
    cpe = p.op("tensor", "matmul", red[0:1, 0:3], onescol[:], asum[:, 0:3],
               start=True, stop=True, sem="pe",
               waits=(("dv", p.cnt["dv"]), ("ldc", 32)))
    c1 = p.op("vector", "tensor_scalar", arow[:], red[0:1, 0:3], 1.0 / (1 << 24), 1e-5,
              ALU.mult, ALU.max, sem="dv", waits=(("pe", cpe),))
    c1 = p.op("vector", "tensor_copy", mrow[0:1, 0:3], arow[:], sem="dv", waits=(("dv", c1),))
    c1 = p.op("vector", "memset", mrow[0:1, 3:4], 0.0, sem="dv", waits=(("dv", c1),))
    crec = p.op("vector", "reciprocal", arow[:], arow[:], sem="dv", waits=(("dv", c1),))
    bc3 = p.psum("bc3", [128, 4], F32)
    cb = p.op("tensor", "matmul", bc3[0:128, 0:3], ones1[:],
              arow[:], start=True, stop=True, sem="pe", waits=(("dv", crec),))
    cmi = p.op("vector", "tensor_copy", minv[:], bc3[0:128, 0:3], sem="dv",
               waits=(("pe", cb),))
    p.dma("gpsimd", m_o.ap(), mrow[:], sem="sto", waits=(("dv", p.cnt["dv"]),))

    ci = p.op("vector", "memset", idn[:], 1.0, sem="dv")
    cid = p.op("gpsimd", "affine_select", idn[:], idn[:], pattern=[[-1, 128]],
               compare_op=ALU.is_equal, fill=0.0, channel_multiplier=1,
               sem="pl", waits=(("dv", ci),))

    def ternarize_slice(w_slice_d, out_d, nrows, col, transpose_out):
        ntile = nrows // 128
        ncols = w_slice_d.shape[1] if hasattr(w_slice_d, "shape") else None
        for i in range(ntile):
            sto_before = p.cnt["sto"]
            buf = wt[i % 2]
            tb = tt_[i % 2]
            ld = p.dma("scalar", buf[:], w_slice_d.ap()[i * 128:(i + 1) * 128, :],
                       sem="lds", waits=(("dv", p.cnt["dv"]),))
            c1 = p.op("vector", "tensor_scalar", p._t1[:], buf[:], minv[:, col:col + 1],
                      0.5, ALU.mult, ALU.is_ge, sem="dv",
                      waits=(("lds", ld), ("pe2", p.cnt["pe2"])))
            c2 = p.op("vector", "tensor_scalar", p._t2[:], buf[:], minv[:, col:col + 1],
                      -0.5, ALU.mult, ALU.is_le, sem="dv", waits=(("dv", c1),))
            c3 = p.op("vector", "tensor_sub", tb[:], p._t1[:], p._t2[:], sem="dv",
                      waits=(("dv", c2),))
            nb = tb.shape[1] // 128
            for j in range(nb):
                cpe2 = p.op("tensor", "transpose", trp[:], tb[:, j * 128:(j + 1) * 128],
                            idn[:], sem="pe2",
                            waits=(("dv", c3), ("pl", cid), ("ac", p.cnt["ac"])))
                p.op("scalar", "copy", tvt[:, j * 128:(j + 1) * 128], trp[:],
                     sem="ac", waits=(("pe2", cpe2), ("sto", sto_before)))
            for j in range(nb):
                p.dma("gpsimd", out_d.ap()[j * 128:(j + 1) * 128, i * 128:(i + 1) * 128],
                      tvt[:, j * 128:(j + 1) * 128], sem="sto",
                      waits=(("ac", p.cnt["ac"]),))

    p._t1 = p.sbuf("t1s", [128, D], F32)
    p._t2 = p.sbuf("t2s", [128, D], F32)
    ternarize_slice(wgs_d, tg_o, 1024, 0, True)
    ternarize_slice(wus_d, tu_o, 1024, 1, True)

    ntile = D // 128
    for i in range(ntile):
        sto_before = p.cnt["sto"]
        buf = wt[i % 2]
        tb = tt_[i % 2]
        ld = p.dma("scalar", buf[:, 0:1024], wds_d.ap()[i * 128:(i + 1) * 128, :],
                   sem="lds", waits=(("dv", p.cnt["dv"]),))
        c1 = p.op("vector", "tensor_scalar", p._t1[:, 0:1024], buf[:, 0:1024],
                  minv[:, 2:3], 0.5, ALU.mult, ALU.is_ge, sem="dv",
                  waits=(("lds", ld), ("pe2", p.cnt["pe2"])))
        c2 = p.op("vector", "tensor_scalar", p._t2[:, 0:1024], buf[:, 0:1024],
                  minv[:, 2:3], -0.5, ALU.mult, ALU.is_le, sem="dv", waits=(("dv", c1),))
        c3 = p.op("vector", "tensor_sub", tb[:, 0:1024], p._t1[:, 0:1024],
                  p._t2[:, 0:1024], sem="dv", waits=(("dv", c2),))
        for j in range(8):
            cpe2 = p.op("tensor", "transpose", trp[:], tb[:, j * 128:(j + 1) * 128],
                        idn[:], sem="pe2",
                        waits=(("dv", c3), ("pl", cid), ("ac", p.cnt["ac"])))
            p.op("scalar", "copy", tvt[:, j * 128:(j + 1) * 128], trp[:],
                 sem="ac", waits=(("pe2", cpe2), ("sto", sto_before)))
        for j in range(8):
            p.dma("gpsimd", td_o.ap()[j * 128:(j + 1) * 128, i * 128:(i + 1) * 128],
                  tvt[:, j * 128:(j + 1) * 128], sem="sto",
                  waits=(("ac", p.cnt["ac"]),))

    prev_cb = 0
    for i in range(D // 128):
        cz = p.op("vector", "memset", facc[:], 0.0, sem="dv",
                  waits=((("sto", p.cnt["sto"]),) if i == 0 else (("dv", prev_cb),)))
        for r in range(8):
            buf = fsl[r % 2]
            ld = p.dma("sync", buf[:], facs_d.ap()[r, i * 128:(i + 1) * 128, :],
                       sem="ldf", waits=(("dv", p.cnt["dv"]),))
            cz = p.op("vector", "tensor_add", facc[:], facc[:], buf[:], sem="dv",
                      waits=(("ldf", ld), ("dv", cz)))
        cb = p.op("vector", "tensor_copy", fbf[:], facc[:], sem="dv",
                  waits=(("dv", cz), ("sto", p.cnt["sto"])))
        prev_cb = cb
        p.dma("gpsimd", fs_o.ap()[i * 128:(i + 1) * 128, :], fbf[:], sem="sto",
              waits=(("dv", cb),))

    p.wait("gpsimd", ("sto", p.cnt["sto"]))
    return p


def host_prep_inputs(ins, c):
    wg = ins["L0_ffn_w_gate"]; wu = ins["L0_ffn_w_up"]; wd = ins["L0_ffn_w_down"]
    fac = ins["L1_ffn_factors"]
    sl = slice(c * 1024, (c + 1) * 1024)
    se = slice(c * 256, (c + 1) * 256)
    return {
        "wg": np.ascontiguousarray(wg), "wu": np.ascontiguousarray(wu),
        "wd": np.ascontiguousarray(wd), "fac": np.ascontiguousarray(fac),
        "onesc": np.ones((1, 128), dtype=np.float32),
        "onescol": np.ones((128, 1), dtype=np.float32),
        "slc": np.zeros((1, 1), dtype=np.float32),
        "wgs": np.ascontiguousarray(wg[sl]), "wus": np.ascontiguousarray(wu[sl]),
        "wds": np.ascontiguousarray(wd[:, sl]),
        "facs": np.ascontiguousarray(fac[:, :, se]),
    }


def build_l0_ffn(p=None, io=None):
    if p is None:
        p = Prog("l0ffn")
    io = io or {}
    xin_d = p.dram_io(io, "xin", [D, NTOK], F32, kind="ExternalInput")
    tg_d = p.dram_io(io, "tgT", [D, DFF], BF16, kind="ExternalInput")
    tu_d = p.dram_io(io, "tuT", [D, DFF], BF16, kind="ExternalInput")
    td_d = p.dram_io(io, "tdT", [DFF, D], BF16, kind="ExternalInput")
    lnw_d = p.dram_io(io, "lnw", [128, 64], F32, kind="ExternalInput")
    mv_d = p.dram_io(io, "mv", [1, 4], F32, kind="ExternalInput")
    ones_d = p.dram_io(io, "onesc", [1, 128], F32, kind="ExternalInput")
    onescolb_d = p.dram_io(io, "onescolb", [128, 1], BF16, kind="ExternalInput")
    xout_d = p.dram_io(io, "xout", [D, NTOK], F32, kind="ExternalOutput")
    u2_d = p.dram_io(io, "u2scr", [DFF, NTOK], BF16, kind="Internal")
    STO0 = p.cnt["sto"]
    STU0 = p.cnt["stu"]

    R1 = p.sbuf("R1", [128, NF * NTOK], BF16)
    wg = [p.sbuf(f"wgb{i}", [128, NF * 128], BF16) for i in range(2)]
    wu = [p.sbuf(f"wub{i}", [128, NF * 128], BF16) for i in range(2)]
    macc = p.sbuf("macc", [128, NTOK], BF16)
    mincc = p.sbuf("mincc", [128, NTOK], BF16)
    xb = p.sbuf("xb", [128, NTOK], BF16)
    sq = p.sbuf("sq", [128, NTOK], BF16)
    u2st = [p.sbuf(f"u2st{i}", [128, NTOK], BF16) for i in range(2)]
    pad0 = p.sbuf("pad0", [128, NTOK], F32)
    intt = p.sbuf("intt", [128, NTOK], I32)
    r1_addr = p.nc.lookup_mloc(R1).addr
    R2 = p.nc.alloc_sbuf_tensor_at(p.tag + "R2", [128, 64 * 512], BF16, offset=r1_addr)

    xt = [p.sbuf(f"xt{i}", [128, NTOK], F32) for i in range(2)]
    scrA = p.sbuf("scrA", [128, NTOK], F32)
    scrB = p.sbuf("scrB", [128, NTOK], F32)
    scrC = p.sbuf("scrC", [128, NTOK], F32)
    scrD = p.sbuf("scrD", [128, NTOK], F32)
    wd = [p.sbuf(f"wdb{i}", [128, 64 * 128], BF16) for i in range(2)]
    u2ld = [p.sbuf(f"u2ld{i}", [128, 512], BF16) for i in range(2)]
    ones1 = p.sbuf("ones1", [1, 128], F32)
    onescolb = p.sbuf("onescolbt", [128, 1], BF16)
    lnw = p.sbuf("lnwt", [128, 64], F32)
    mv = p.sbuf("mvt", [1, 4], F32)
    rows = p.sbuf("rows", [1, 9 * NTOK], F32)
    sc = p.sbuf("sc", [1, 8], F32)
    tmcol = p.sbuf("tmcol", [128, 8], F32)
    bcA = p.sbuf("bcA", [128, NTOK], F32)
    bcB = p.sbuf("bcB", [128, NTOK], F32)
    bcCG = p.sbuf("bcCG", [128, NTOK], F32)
    bcS2 = p.sbuf("bcS2", [128, NTOK], F32)
    bcFS = p.sbuf("bcFS", [128, NTOK], F32)
    idn = p.sbuf("idnf", [128, 128], F32)
    alloc_fix(p, [128, NTOK])

    psG = p.psum("psG", [128, 512], F32)
    psU = p.psum("psU", [128, 512], F32)
    psG2 = p.psum("psG2", [128, 512], F32)
    psU2 = p.psum("psU2", [128, 512], F32)
    msH = [p.psum(f"msH{h}", [1, 512], F32) for h in range(2)]
    psBC = p.psum("psBC", [128, 512], F32)
    psD = p.psum("psD", [128, 512], F32)

    def row(i):
        return rows[0:1, i * NTOK:(i + 1) * NTOK]

    def rowh(i, h):
        return rows[0:1, i * NTOK + h * 512:i * NTOK + (h + 1) * 512]

    p.dma("sync", ones1[:], ones_d.ap(), sem="ldc")
    p.dma("sync", onescolb[:], onescolb_d.ap(), sem="ldc")
    p.dma("sync", lnw[:], lnw_d.ap(), sem="ldc")
    p.dma("sync", mv[:], mv_d.ap(), sem="ldc")
    LDC = p.cnt["ldc"]
    cmi = p.op("vector", "memset", idn[:], 1.0, waits=(("ldc", LDC),))
    p.op("gpsimd", "affine_select", idn[:], idn[:], pattern=[[-1, 128]],
         compare_op=ALU.is_equal, fill=0.0, channel_multiplier=1,
         waits=(("dv", cmi),))
    PL_ID = p.cnt["pl"]

    def bc_to(dst, row_ap, n=NTOK):
        last = 0
        for h in range(n // 512):
            cpe = p.op("tensor", "matmul", psBC[:, 0:512], ones1[:],
                       row_ap[0:1, h * 512:(h + 1) * 512], start=True, stop=True,
                       sem="pe", waits=(("ac", p.cnt["ac"]), ("dv", p.cnt["dv"]),
                                        ("ldc", LDC)))
            last = p.op("scalar", "copy", dst[:, h * 512:(h + 1) * 512],
                        psBC[:, 0:512], waits=(("pe", cpe),))
        return last

    def part_max_row(row_out):
        lastac = 0
        for j in range(NTOK // 128):
            cpe = p.op("tensor", "transpose", psD[:, 0:128],
                       scrA[:, j * 128:(j + 1) * 128], idn[:], sem="pe",
                       waits=(("dv", p.cnt["dv"]), ("pl", PL_ID),
                              ("dv2", p.cnt["dv2"]), ("ac", p.cnt["ac"])))
            p.op("vector", "reduce_max", tmcol[:, j:j + 1], psD[:, 0:128], AXX,
                 sem="dv2", waits=(("pe", cpe), ("dv", p.cnt["dv"])))
        for j in range(NTOK // 128):
            cpe = p.op("tensor", "transpose", psD[0:1, 128:256], tmcol[:, j:j + 1],
                       idn[:], sem="pe",
                       waits=(("dv2", p.cnt["dv2"]), ("ac", p.cnt["ac"])))
            lastac = p.op("scalar", "copy", row_out[0:1, j * 128:(j + 1) * 128],
                          psD[0:1, 128:256], waits=(("pe", cpe),))
        return lastac

    for i in range(NF):
        buf = xt[i % 2]
        ld = p.dma("sync", buf[:], xin_d.ap()[i * 128:(i + 1) * 128, :], sem="ldx",
                   waits=(("ac", p.cnt["ac"]), ("dv", p.cnt["dv"])))
        cb = p.op("vector", "tensor_copy", xb[:], buf[:],
                  waits=(("ldx", ld), ("pe", p.cnt["pe"])))
        cs = p.op("scalar", "square", sq[:], buf[:],
                  waits=(("ldx", ld), ("pe", p.cnt["pe"])))
        for h in range(2):
            ps_s = psG if h == 0 else psG2
            ps_q = psU if h == 0 else psU2
            p.op("tensor", "matmul", ps_s[0:1, 0:512], onescolb[:],
                 xb[:, h * 512:(h + 1) * 512], start=(i == 0), stop=(i == NF - 1),
                 sem="pe", waits=(("dv", cb), ("ldc", LDC)))
            p.op("tensor", "matmul", ps_q[0:1, 0:512], onescolb[:],
                 sq[:, h * 512:(h + 1) * 512], start=(i == 0), stop=(i == NF - 1),
                 sem="pe", waits=(("ac", cs),))
    PE_STATS = p.cnt["pe"]
    for h in range(2):
        ps_s = psG if h == 0 else psG2
        ps_q = psU if h == 0 else psU2
        p.op("vector", "tensor_scalar", rowh(0, h), ps_s[0:1, 0:512],
             1.0 / D, None, ALU.mult, waits=(("pe", PE_STATS),))
        p.op("vector", "tensor_scalar", rowh(1, h), ps_q[0:1, 0:512],
             1.0 / D, None, ALU.mult, waits=(("pe", PE_STATS),))
    p.op("vector", "tensor_mul", row(2), row(0), row(0))
    p.op("vector", "tensor_sub", row(1), row(1), row(2))
    p.op("vector", "tensor_scalar", row(1), row(1), 1e-5, None, ALU.add)
    cac = p.op("scalar", "sqrt", row(1), row(1), waits=(("dv", p.cnt["dv"]),))
    p.op("vector", "reciprocal", row(2), row(1), waits=(("ac", cac),))
    p.op("vector", "tensor_mul", row(3), row(0), row(2))
    bc_to(bcA, row(2))
    bc_to(bcB, row(3))

    BC_DONE = p.cnt["ac"]
    for i in range(NF):
        buf = xt[i % 2]
        ld = p.dma("sync", buf[:], xin_d.ap()[i * 128:(i + 1) * 128, :], sem="ldx",
                   waits=(("dv", p.cnt["dv"]),))
        c1 = p.op("vector", "tensor_mul", scrA[:], buf[:], bcA[:],
                  waits=(("ldx", ld), ("ac", BC_DONE), ("pl", p.cnt["pl"])))
        c2 = p.op("vector", "tensor_sub", scrA[:], scrA[:], bcB[:])
        if i == 0:
            p.op("vector", "tensor_copy", scrB[:], scrA[:])
            p.op("vector", "tensor_copy", scrC[:], scrA[:])
        else:
            p.op("vector", "tensor_max", scrB[:], scrB[:], scrA[:])
            p.op("vector", "tensor_tensor", scrC[:], scrC[:], scrA[:], ALU.min)
    p.op("vector", "tensor_scalar", scrD[:], scrC[:], -1.0, None, ALU.mult,
         waits=(("pl", p.cnt["pl"]),))
    p.op("vector", "tensor_max", scrA[:], scrB[:], scrD[:])
    part_max_row(row(4))
    p.op("vector", "tensor_scalar", row(4), row(4), 1e-5, None, ALU.max,
         waits=(("ac", p.cnt["ac"]),))
    p.op("vector", "reciprocal", row(5), row(4))
    p.op("vector", "tensor_scalar", row(5), row(5), 127.0, None, ALU.mult)
    p.op("vector", "tensor_mul", row(6), row(2), row(5))
    p.op("vector", "tensor_mul", row(7), row(3), row(5))
    bc_to(bcA, row(6))
    bc_to(bcB, row(7))
    p.op("vector", "tensor_scalar", sc[0:1, 0:1], mv[0:1, 0:1], 1.0 / 127, None,
         ALU.mult, waits=(("ldc", LDC),))
    p.op("vector", "tensor_scalar", sc[0:1, 1:2], mv[0:1, 1:2], 1.0 / 127, None,
         ALU.mult)
    p.op("vector", "tensor_scalar", sc[0:1, 2:3], mv[0:1, 2:3], 0.5 / 127, None,
         ALU.mult)
    p.op("vector", "tensor_scalar", row(7), row(4), sc[0:1, 0:1], None, ALU.mult,
         waits=(("pe", p.cnt["pe"]),))
    p.op("vector", "tensor_scalar", row(8), row(4), sc[0:1, 1:2], None, ALU.mult)
    bc_to(bcCG, row(7))

    BC2 = p.cnt["ac"]
    for i in range(NF):
        buf = xt[i % 2]
        ld = p.dma("sync", buf[:], xin_d.ap()[i * 128:(i + 1) * 128, :], sem="ldx",
                   waits=(("dv", p.cnt["dv"]),))
        p.op("vector", "tensor_mul", scrA[:], buf[:], bcA[:],
             waits=(("ldx", ld), ("ac", BC2), ("pl", p.cnt["pl"])))
        p.op("vector", "tensor_sub", scrA[:], scrA[:], bcB[:])
        for h in range(2):
            sl = slice(h * 512, (h + 1) * 512)
            emit_round(p, scrA[:, sl], R1[:, i * NTOK + h * 512:i * NTOK + (h + 1) * 512],
                       intt[:, 0:512], scrD[:, 0:512])
    R1_DONE = p.cnt["dv"]

    gathered_w = io.get("gather_w", False)

    def load_w(dst, src_d, j, pewait):
        if gathered_w:
            co = j // 8
            src = (src_d.ap().rearrange("(A p) o -> p A o", p=128)
                   [:, co * 16:(co + 1) * 16, (j % 8) * 128:(j % 8 + 1) * 128])
        else:
            src = (src_d.ap().rearrange("(a p) o -> p a o", p=128)
                   [:, :, j * 128:(j + 1) * 128])
        return p.dma("scalar", dst.ap().rearrange("p (a o) -> p a o", o=128),
                     src, sem=f"ldw{j % 2}", waits=(("pe", pewait),))

    lam_ev = {}
    pe_end = {}
    for j in range(64):
        lw1 = load_w(wg[j % 2], tg_d, j, pe_end.get(j - 2, 0))
        lw2 = load_w(wu[j % 2], tu_d, j, pe_end.get(j - 2, 0))
        lwsem = f"ldw{j % 2}"
        for h in range(2):
            g = j * 2 + h
            ps_g = psG if g % 2 == 0 else psG2
            ps_u = psU if g % 2 == 0 else psU2
            mm = 0
            for i in range(NF):
                mm = p.op("tensor", "matmul", ps_g[:],
                          wg[j % 2][:, i * 128:(i + 1) * 128],
                          R1[:, i * NTOK + h * 512:i * NTOK + h * 512 + 512],
                          start=(i == 0), stop=(i == NF - 1), sem="pe",
                          waits=(((lwsem, lw2), ("dv", R1_DONE),
                                  ("dv", lam_ev.get(g - 2, 0))) if i == 0 else ()))
            for i in range(NF):
                mm = p.op("tensor", "matmul", ps_u[:],
                          wu[j % 2][:, i * 128:(i + 1) * 128],
                          R1[:, i * NTOK + h * 512:i * NTOK + h * 512 + 512],
                          start=(i == 0), stop=(i == NF - 1), sem="pe")
            sl = slice(h * 512, (h + 1) * 512)
            c1 = p.op("vector", "tensor_mul", scrA[:, sl], ps_g[:], bcCG[:, sl],
                      waits=(("pe", mm),))
            c2 = p.op("scalar", "activation", scrB[:, sl], scrA[:, sl], AF.Silu,
                      waits=(("dv", c1),))
            c3 = p.op("vector", "tensor_mul", scrC[:, sl], scrB[:, sl], ps_u[:],
                      waits=(("ac", c2),))
            lam_ev[g] = c3
            c4 = p.op("scalar", "square", sq[:, sl], scrC[:, sl],
                      waits=(("dv", c3),))
            p.op("tensor", "matmul", msH[h][0:1, 0:512], onescolb[:], sq[:, sl],
                 start=(j == 0), stop=(j == 63), sem="pe", waits=(("ac", c4),))
            p.op("vector", "tensor_scalar", u2st[j % 2][:, sl], scrC[:, sl],
                 lnw[:, j:j + 1], None, ALU.mult,
                 waits=(("stu", STU0 + 16 * j),))
            if j == 0:
                p.op("vector", "tensor_copy", macc[:, sl], u2st[j % 2][:, sl])
                p.op("vector", "tensor_copy", mincc[:, sl], u2st[j % 2][:, sl])
            else:
                p.op("vector", "tensor_max", macc[:, sl], macc[:, sl],
                     u2st[j % 2][:, sl])
                p.op("vector", "tensor_tensor", mincc[:, sl], mincc[:, sl],
                     u2st[j % 2][:, sl], ALU.min)
        p.dma("gpsimd", u2_d.ap()[j * 128:(j + 1) * 128, :], u2st[j % 2][:],
              sem="stu", waits=(("dv", p.cnt["dv"]), ("pl", p.cnt["pl"])))
        pe_end[j] = p.cnt["pe"]
    GU_PE = p.cnt["pe"]

    p.op("vector", "tensor_copy", scrA[:], macc[:],
         waits=(("pe", GU_PE), ("ac", p.cnt["ac"]), ("stu", STU0 + 16 * 64)))
    p.op("vector", "tensor_scalar", scrB[:], mincc[:], -1.0, None, ALU.mult,
         waits=(("pl", p.cnt["pl"]),))
    p.op("vector", "tensor_max", scrA[:], scrA[:], scrB[:])
    part_max_row(row(4))
    for h in range(2):
        p.op("vector", "tensor_scalar", rowh(0, h), msH[h][0:1, 0:512],
             1.0 / DFF, None, ALU.mult, waits=(("pe", GU_PE), ("ac", p.cnt["ac"])))
    p.op("vector", "tensor_mul", row(1), row(8), row(8))
    p.op("vector", "tensor_mul", row(1), row(1), row(0))
    p.op("vector", "tensor_scalar", row(1), row(1), 1e-6, None, ALU.add)
    cac = p.op("scalar", "sqrt", row(1), row(1), waits=(("dv", p.cnt["dv"]),))
    p.op("vector", "reciprocal", row(2), row(1), waits=(("ac", cac),))
    p.op("vector", "tensor_mul", row(2), row(8), row(2))
    p.op("vector", "reciprocal", row(6), row(2))
    p.op("vector", "tensor_scalar", row(6), row(6), 1e-5, None, ALU.mult)
    p.op("vector", "tensor_max", row(3), row(4), row(6))
    p.op("vector", "reciprocal", row(5), row(3))
    p.op("vector", "tensor_scalar", row(5), row(5), 127.0, None, ALU.mult)
    p.op("vector", "tensor_mul", row(6), row(2), row(3))
    p.op("vector", "tensor_scalar", row(6), row(6), sc[0:1, 2:3], None, ALU.mult)
    bc_to(bcS2, row(5))
    bc_to(bcFS, row(6))
    BC3 = p.cnt["ac"]

    pe_end2 = {}
    for h in range(2):
        sl = slice(h * 512, (h + 1) * 512)
        r2last = 0
        for j in range(64):
            lb = u2ld[j % 2]
            ld = p.dma("sync", lb[:], u2_d.ap()[j * 128:(j + 1) * 128, sl],
                       sem="ldu", waits=(("dv", p.cnt["dv"]), ("stu", STU0 + 16 * 64)))
            p.op("vector", "tensor_mul", scrA[:, 0:512], lb[:], bcS2[:, sl],
                 waits=(("ldu", ld), ("ac", BC3), ("pe", p.cnt["pe"]),
                        ("pl", p.cnt["pl"])))
            r2last = emit_round(p, scrA[:, 0:512], R2[:, j * 512:(j + 1) * 512],
                                intt[:, 0:512], scrD[:, 0:512])
        prev_ev = 0
        for d in range(NF):
            lw = p.dma("scalar", wd[d % 2].ap().rearrange("p (a o) -> p a o", o=128),
                       td_d.ap().rearrange("(a p) d -> p a d", p=128)[:, :, d * 128:(d + 1) * 128],
                       sem=f"ldd{d % 2}", waits=(("pe", pe_end2.get((h, d - 2), pe_end2.get((h - 1, NF - 2 + d), 0))),))
            mm = 0
            for j in range(64):
                mm = p.op("tensor", "matmul", psD[:],
                          wd[d % 2][:, j * 128:(j + 1) * 128],
                          R2[:, j * 512:(j + 1) * 512],
                          start=(j == 0), stop=(j == 63), sem="pe",
                          waits=(((f"ldd{d % 2}", lw), ("dv", r2last),
                                  ("dv", prev_ev), ("ac", p.cnt["ac"])) if j == 0 else ()))
            pe_end2[(h, d)] = p.cnt["pe"]
            n = h * NF + d
            ldx2 = p.dma("sync", xt[d % 2][:, 0:512],
                         xin_d.ap()[d * 128:(d + 1) * 128, sl], sem="ldx2",
                         waits=(("sto", STO0 + 16 * n), ("dv", R1_DONE)))
            ev = scrC[:, (n % 2) * 512:(n % 2) * 512 + 512]
            c1 = p.op("vector", "tensor_mul", scrB[:, 0:512], psD[:], bcFS[:, sl],
                      waits=(("pe", mm),))
            prev_ev = c1
            c2 = p.op("vector", "tensor_add", ev, scrB[:, 0:512],
                      xt[d % 2][:, 0:512],
                      waits=(("ldx2", ldx2), ("sto", STO0 + 16 * n)))
            p.dma("gpsimd", xout_d.ap()[d * 128:(d + 1) * 128, sl],
                  ev, sem="sto", waits=(("dv", c2),))
    p.wait("gpsimd", ("sto", p.cnt["sto"]))
    return p


def host_ffn_inputs(x_fm_c, tgT, tuT, tdT, lnw_dev, mval):
    return {
        "xin": x_fm_c, "tgT": tgT, "tuT": tuT, "tdT": tdT,
        "lnw": lnw_dev, "mv": mval,
        "onesc": np.ones((1, 128), dtype=np.float32),
        "onescolb": np.ones((128, 1), dtype=BF),
    }


def build_l0_attn_a(p=None, io=None):
    if p is None:
        p = Prog("l0attna")
    io = io or {}
    xin_d = p.dram_io(io, "xin", [D, NTOK], F32, kind="ExternalInput")
    wqkv_d = p.dram_io(io, "wqkv", [D, 3 * D], F32, kind="ExternalInput")
    wgm_d = p.dram_io(io, "wgm", [D, D], F32, kind="ExternalInput")
    trow_d = p.dram_io(io, "trow", [1, NTOK], F32, kind="ExternalInput")
    ks_d = p.dram_io(io, "ks", [64, NTOK], F32, kind="ExternalInput")
    kc_d = p.dram_io(io, "kc", [64, NTOK], F32, kind="ExternalInput")
    ones_d = p.dram_io(io, "onesc", [1, 128], F32, kind="ExternalInput")
    onescolb_d = p.dram_io(io, "onescolb", [128, 1], BF16, kind="ExternalInput")
    qs_o = p.dram_io(io, "qs", [D, NTOK], BF16, kind="ExternalOutput")
    k_o = p.dram_io(io, "ko", [D, NTOK], BF16, kind="ExternalOutput")
    vt_o = p.dram_io(io, "vt", [NTOK, D], BF16, kind="ExternalOutput")
    g_o = p.dram_io(io, "go", [D, NTOK], BF16, kind="ExternalOutput")

    xt = [p.sbuf(f"xt{i}", [128, NTOK], F32) for i in range(2)]
    xb = p.sbuf("xb", [128, NTOK], BF16)
    sq = p.sbuf("sq", [128, NTOK], BF16)
    scrA = p.sbuf("scrA", [128, NTOK], F32)
    scrB = p.sbuf("scrB", [128, NTOK], F32)
    scrC = p.sbuf("scrC", [128, NTOK], F32)
    scrD = p.sbuf("scrD", [128, NTOK], F32)
    XLN = p.sbuf("XLN", [128, NF * NTOK], BF16)
    wb = [p.sbuf(f"wb{i}", [128, NF * 128], BF16) for i in range(2)]
    qbh = p.sbuf("qbh", [128, NTOK], BF16)
    obh = p.sbuf("obh", [128, NTOK], BF16)
    vstage = p.sbuf("vstage", [128, NTOK], BF16)
    vtr = p.sbuf("vtr", [128, NTOK], BF16)
    ones1 = p.sbuf("ones1", [1, 128], F32)
    onescolb = p.sbuf("onescolbt", [128, 1], BF16)
    trow = p.sbuf("trowt", [1, NTOK], F32)
    kst = p.sbuf("kst", [64, NTOK], F32)
    kct = p.sbuf("kct", [64, NTOK], F32)
    invf = p.sbuf("invf", [1, 64], F32)
    invf2 = p.sbuf("invf2", [1, 64], F32)
    SS2 = p.sbuf("SS2", [128, NTOK], F32)
    CC2 = p.sbuf("CC2", [128, NTOK], F32)
    rows = p.sbuf("rows", [1, 4 * NTOK], F32)
    bcA = p.sbuf("bcA", [128, NTOK], F32)
    bcB = p.sbuf("bcB", [128, NTOK], F32)
    bcR = p.sbuf("bcR", [128, NTOK], F32)
    idn = p.sbuf("idnf", [128, 128], BF16)
    ones128 = p.sbuf("ones128", [128, 128], BF16)

    psQ = [p.psum(f"psQ{i}", [128, 512], F32) for i in range(4)]
    psM = [p.psum(f"psM{h}", [1, 512], F32) for h in range(2)]
    psBC = p.psum("psBC", [128, 512], F32)
    psTR = p.psum("psTR", [128, 128], BF16)

    def row(i, h=None):
        if h is None:
            return rows[0:1, i * NTOK:(i + 1) * NTOK]
        return rows[0:1, i * NTOK + h * 512:i * NTOK + (h + 1) * 512]

    p.dma("sync", ones1[:], ones_d.ap(), sem="ldc")
    p.dma("sync", onescolb[:], onescolb_d.ap(), sem="ldc")
    p.dma("sync", trow[:], trow_d.ap(), sem="ldc")
    p.dma("sync", kst[:], ks_d.ap(), sem="ldc")
    p.dma("sync", kct[:], kc_d.ap(), sem="ldc")
    LDC = p.cnt["ldc"]
    cmo = p.op("vector", "memset", ones128[:], 1.0, waits=(("ldc", LDC),))
    p.op("gpsimd", "affine_select", idn[:], ones128[:], pattern=[[-1, 128]],
         compare_op=ALU.is_equal, fill=0.0, channel_multiplier=1,
         waits=(("dv", cmo),))
    PL_ID = p.cnt["pl"]

    def bc_to(dst, row_ap, n=NTOK):
        last = 0
        for h in range(n // 512):
            cpe = p.op("tensor", "matmul", psBC[:, 0:512], ones1[:],
                       row_ap[0:1, h * 512:(h + 1) * 512], start=True, stop=True,
                       sem="pe", waits=(("ac", p.cnt["ac"]), ("dv", p.cnt["dv"]),
                                        ("ldc", LDC)))
            last = p.op("scalar", "copy", dst[:, h * 512:(h + 1) * 512],
                        psBC[:, 0:512], waits=(("pe", cpe),))
        return last

    _ln4 = np.log(np.float64(10000.0))
    HI_LN = float(np.float32(_ln4))
    LO_LN = float(np.float32(_ln4 - np.float64(np.float32(_ln4))))
    TWO_PI_HI = float(np.float32(2 * np.pi))
    TWO_PI_LO = float(np.float32(2 * np.pi - np.float64(np.float32(2 * np.pi))))
    PIH = float(np.float32(np.pi / 2))
    ci = p.op("gpsimd", "iota", invf[:], pattern=[[1, 64]], base=0,
              channel_multiplier=0, allow_small_or_imprecise_dtypes=True)
    c1 = p.op("scalar", "activation", invf2[:], invf[:], AF.Exp, bias=0.0,
              scale=-HI_LN / 64, waits=(("pl", ci),))
    c2 = p.op("scalar", "activation", invf[:], invf[:], AF.Exp, bias=0.0,
              scale=-LO_LN / 64, waits=(("ac", c1),))
    c3 = p.op("vector", "tensor_mul", invf[:], invf[:], invf2[:],
              waits=(("ac", c2),))
    for h in range(2):
        p.op("tensor", "matmul", psQ[h][0:64, 0:512],
             invf[0:1, :], trow[0:1, h * 512:(h + 1) * 512],
             start=True, stop=True, sem="pe",
             waits=(("dv", c3), ("ldc", LDC)))
    PE_ANG = p.cnt["pe"]
    for h in range(2):
        sl = slice(h * 512, (h + 1) * 512)
        p.op("vector", "scalar_tensor_tensor", scrA[0:64, sl], kst[:, sl], -TWO_PI_HI,
             psQ[h][0:64, 0:512], ALU.mult, ALU.add, waits=(("pe", PE_ANG),))
        p.op("vector", "scalar_tensor_tensor", scrA[0:64, sl], kst[:, sl], -TWO_PI_LO,
             scrA[0:64, sl], ALU.mult, ALU.add)
    p.op("vector", "tensor_scalar", scrA[0:64, :], scrA[0:64, :], float(PI_LIM),
         -float(PI_LIM), ALU.min, ALU.max)
    cs4 = p.op("scalar", "activation", SS2[0:64, :], scrA[0:64, :], AF.Sin,
               waits=(("dv", p.cnt["dv"]),))
    for h in range(2):
        sl = slice(h * 512, (h + 1) * 512)
        p.op("vector", "tensor_scalar", scrB[0:64, sl], psQ[h][0:64, 0:512], PIH, None,
             ALU.add, waits=(("pe", PE_ANG),))
    p.op("vector", "scalar_tensor_tensor", scrB[0:64, :], kct[:], -TWO_PI_HI,
         scrB[0:64, :], ALU.mult, ALU.add)
    p.op("vector", "scalar_tensor_tensor", scrB[0:64, :], kct[:], -TWO_PI_LO,
         scrB[0:64, :], ALU.mult, ALU.add)
    p.op("vector", "tensor_scalar", scrB[0:64, :], scrB[0:64, :], float(PI_LIM),
         -float(PI_LIM), ALU.min, ALU.max)
    cc5 = p.op("scalar", "activation", CC2[0:64, :], scrB[0:64, :], AF.Sin,
               waits=(("dv", p.cnt["dv"]),))
    p.op("vector", "tensor_scalar", SS2[64:128, :], SS2[0:64, :], -1.0, None,
         ALU.mult, waits=(("ac", cs4),))
    cdup = p.op("scalar", "copy", CC2[64:128, :], CC2[0:64, :], waits=(("ac", cc5),))
    ROPE_DV = p.cnt["dv"]
    ROPE_AC = p.cnt["ac"]

    for i in range(NF):
        buf = xt[i % 2]
        ld = p.dma("sync", buf[:], xin_d.ap()[i * 128:(i + 1) * 128, :], sem="ldx",
                   waits=(("ac", p.cnt["ac"]), ("dv", p.cnt["dv"])))
        cb = p.op("vector", "tensor_copy", xb[:], buf[:],
                  waits=(("ldx", ld), ("pe", p.cnt["pe"])))
        cs = p.op("scalar", "square", sq[:], buf[:],
                  waits=(("ldx", ld), ("pe", p.cnt["pe"])))
        for h in range(2):
            ps_s = psQ[0] if h == 0 else psQ[1]
            ps_q = psQ[2] if h == 0 else psQ[3]
            p.op("tensor", "matmul", ps_s[0:1, 0:512], onescolb[:],
                 xb[:, h * 512:(h + 1) * 512], start=(i == 0), stop=(i == NF - 1),
                 sem="pe", waits=(("dv", cb), ("ldc", LDC)))
            p.op("tensor", "matmul", ps_q[0:1, 0:512], onescolb[:],
                 sq[:, h * 512:(h + 1) * 512], start=(i == 0), stop=(i == NF - 1),
                 sem="pe", waits=(("ac", cs),))
    PE_STATS = p.cnt["pe"]
    for h in range(2):
        p.op("vector", "tensor_scalar", row(0, h), (psQ[0] if h == 0 else psQ[1])[0:1, 0:512],
             1.0 / D, None, ALU.mult, waits=(("pe", PE_STATS),))
        p.op("vector", "tensor_scalar", row(1, h), (psQ[2] if h == 0 else psQ[3])[0:1, 0:512],
             1.0 / D, None, ALU.mult, waits=(("pe", PE_STATS),))
    p.op("vector", "tensor_mul", row(2), row(0), row(0))
    p.op("vector", "tensor_sub", row(1), row(1), row(2))
    p.op("vector", "tensor_scalar", row(1), row(1), 1e-5, None, ALU.add)
    cac = p.op("scalar", "sqrt", row(1), row(1), waits=(("dv", p.cnt["dv"]),))
    p.op("vector", "reciprocal", row(2), row(1), waits=(("ac", cac),))
    p.op("vector", "tensor_mul", row(3), row(0), row(2))
    bc_to(bcA, row(2))
    bc_to(bcB, row(3))
    BC_DONE = p.cnt["ac"]
    for i in range(NF):
        buf = xt[i % 2]
        ld = p.dma("sync", buf[:], xin_d.ap()[i * 128:(i + 1) * 128, :], sem="ldx",
                   waits=(("dv", p.cnt["dv"]),))
        p.op("vector", "tensor_mul", scrA[:], buf[:], bcA[:],
             waits=(("ldx", ld), ("ac", BC_DONE)))
        p.op("vector", "tensor_sub", scrA[:], scrA[:], bcB[:])
        p.op("vector", "tensor_copy", XLN[:, i * NTOK:(i + 1) * NTOK], scrA[:])
    XLN_DONE = p.cnt["dv"]

    def load_wchunk(dst, src_d, c, pewait):
        return p.dma("gpsimd",
                     dst.ap().rearrange("p (a o) -> p a o", o=128),
                     src_d.ap().rearrange("(a p) o -> p a o", p=128)[:, :, c * 128:(c + 1) * 128],
                     sem=f"ldw{c % 2}", waits=(("pe", pewait),))

    pe_end = {}
    pe_ev_ac = 0
    pe_ev_dv = 0
    for c in range(48 + 16):
        is_g = c >= 48
        lw = load_wchunk(wb[c % 2], wgm_d if is_g else wqkv_d,
                         c - 48 if is_g else c, pe_end.get(c - 2, 0))
        head = c // 3
        kind = 3 if is_g else (c % 3)
        mm = {}
        for h in range(2):
            ps = psQ[2 * (c % 2) + h]
            for i in range(NF):
                mm[h] = p.op("tensor", "matmul", ps[:],
                             wb[c % 2][:, i * 128:(i + 1) * 128],
                             XLN[:, i * NTOK + h * 512:i * NTOK + h * 512 + 512],
                             start=(i == 0), stop=(i == NF - 1), sem="pe",
                             waits=(((f"ldw{c % 2}", lw), ("dv", XLN_DONE),
                                     ("ac", pe_ev_ac), ("dv", pe_ev_dv))
                                    if (i == 0 and h == 0) else ()))
        if kind == 3:
            gi = c - 48
            for h in range(2):
                ps = psQ[2 * (c % 2) + h]
                p.op("scalar", "activation", obh[:, h * 512:(h + 1) * 512],
                     ps[:], AF.Silu, waits=(("pe", mm[h]), ("sto", p.cnt["sto"])))
            pe_ev_ac, pe_ev_dv = p.cnt["ac"], p.cnt["dv"]
            p.dma("gpsimd", g_o.ap()[gi * 128:(gi + 1) * 128, :],
                  obh[:], sem="sto", waits=(("ac", p.cnt["ac"]),))
        elif kind == 2:
            for h in range(2):
                ps = psQ[2 * (c % 2) + h]
                p.op("scalar", "copy", vstage[:, h * 512:(h + 1) * 512], ps[:],
                     waits=(("pe", mm[h]), ("pe", p.cnt["pe"])))
            for j in range(8):
                cpe = p.op("tensor", "transpose", psTR[:],
                           vstage[:, j * 128:(j + 1) * 128], idn[:], sem="pe",
                           waits=(("ac", p.cnt["ac"]), ("pl", PL_ID)))
                p.op("scalar", "copy", vtr[:, j * 128:(j + 1) * 128], psTR[:],
                     waits=(("pe", cpe), ("sto", p.cnt["sto"])))
            pe_ev_ac, pe_ev_dv = p.cnt["ac"], p.cnt["dv"]
            for j in range(8):
                p.dma("gpsimd", vt_o.ap()[j * 128:(j + 1) * 128,
                                          head * 128:(head + 1) * 128],
                      vtr[:, j * 128:(j + 1) * 128], sem="sto",
                      waits=(("ac", p.cnt["ac"]),))
        else:
            for h in range(2):
                ps = psQ[2 * (c % 2) + h]
                cqe = p.op("scalar", "copy", qbh[:, h * 512:(h + 1) * 512], ps[:],
                           waits=(("pe", mm[h]), ("sto", p.cnt["sto"]),
                                  ("dv", p.cnt["dv"])))
                cqs = p.op("scalar", "square", sq[:, h * 512:(h + 1) * 512],
                           qbh[:, h * 512:(h + 1) * 512], waits=(("ac", cqe),))
                p.op("tensor", "matmul", psM[h][0:1, 0:512], onescolb[:],
                     sq[:, h * 512:(h + 1) * 512], start=True, stop=True,
                     sem="pe", waits=(("ac", cqs), ("dv", p.cnt["dv"])))
            PE_MS = p.cnt["pe"]
            p.op("vector", "tensor_scalar", row(0, 0), psM[0][0:1, 0:512],
                 1.0 / 128, 1e-6, ALU.mult, ALU.add, waits=(("pe", PE_MS),))
            p.op("vector", "tensor_scalar", row(0, 1), psM[1][0:1, 0:512],
                 1.0 / 128, 1e-6, ALU.mult, ALU.add)
            cac = p.op("scalar", "sqrt", row(0), row(0), waits=(("dv", p.cnt["dv"]),))
            p.op("vector", "reciprocal", row(0), row(0), waits=(("ac", cac),))
            bc_to(bcR, row(0))
            c5 = p.op("vector", "tensor_mul", scrA[:], qbh[:], bcR[:],
                      waits=(("ac", p.cnt["ac"]),))
            c6 = p.op("scalar", "copy", scrB[0:64, :], scrA[64:128, :],
                      waits=(("dv", c5),))
            c7 = p.op("scalar", "copy", scrB[64:128, :], scrA[0:64, :])
            p.op("vector", "tensor_mul", scrC[:], scrA[:], CC2[:],
                 waits=(("ac", ROPE_AC), ("dv", ROPE_DV)))
            p.op("vector", "tensor_mul", scrD[:], scrB[:], SS2[:],
                 waits=(("ac", c7),))
            p.op("vector", "tensor_add", scrC[:], scrC[:], scrD[:])
            if kind == 0:
                p.op("vector", "tensor_scalar", scrD[:], scrC[:],
                     float(SC_HI), None, ALU.mult)
                p.op("vector", "scalar_tensor_tensor", obh[:], scrC[:],
                     float(SC_LO), scrD[:], ALU.mult, ALU.add,
                     waits=(("sto", p.cnt["sto"]),))
                pe_ev_ac, pe_ev_dv = p.cnt["ac"], p.cnt["dv"]
                p.dma("gpsimd", qs_o.ap()[head * 128:(head + 1) * 128, :],
                      obh[:], sem="sto", waits=(("dv", p.cnt["dv"]),))
            else:
                p.op("vector", "tensor_copy", obh[:], scrC[:],
                     waits=(("sto", p.cnt["sto"]),))
                pe_ev_ac, pe_ev_dv = p.cnt["ac"], p.cnt["dv"]
                p.dma("gpsimd", k_o.ap()[head * 128:(head + 1) * 128, :],
                      obh[:], sem="sto", waits=(("dv", p.cnt["dv"]),))
        pe_end[c] = p.cnt["pe"]
    p.wait("gpsimd", ("sto", p.cnt["sto"]))
    return p


def host_attn_a_inputs(x_fm_c, ins, c):
    trow, ks, kc = rope_host_tables(c)
    return {
        "xin": x_fm_c,
        "wqkv": np.ascontiguousarray(np.float32(ins["L0_attn_W_qkv"])),
        "wgm": np.ascontiguousarray(np.float32(ins["L0_attn_W_g"])),
        "trow": trow, "ks": np.ascontiguousarray(ks), "kc": np.ascontiguousarray(kc),
        "onesc": np.ones((1, 128), dtype=np.float32),
        "onescolb": np.ones((128, 1), dtype=BF),
    }


def build_l0_attn_b(cpos):
    chunkA, chunkB = cpos, 7 - cpos
    p = Prog(f"l0attnb{cpos}")
    io = {}
    xin_d = p.dram_io(io, "xin", [D, NTOK], F32, kind="ExternalInput")
    qs_d = p.dram_io(io, "qs", [D, NTOK], BF16, kind="ExternalInput")
    kf_d = p.dram_io(io, "kf", [D, T], BF16, kind="ExternalInput")
    vf_d = p.dram_io(io, "vf", [T, D], BF16, kind="ExternalInput")
    g_d = p.dram_io(io, "go", [D, NTOK], BF16, kind="ExternalInput")
    lq1_d = p.dram_io(io, "lq1", [128, 8], F32, kind="ExternalInput")
    lk1_d = p.dram_io(io, "lk1", [128, 8], F32, kind="ExternalInput")
    lq2_d = p.dram_io(io, "lq2", [128, 8], F32, kind="ExternalInput")
    lk2_d = p.dram_io(io, "lk2", [128, 8], F32, kind="ExternalInput")
    ones_d = p.dram_io(io, "onesc", [1, 128], F32, kind="ExternalInput")
    onescol_d = p.dram_io(io, "onescol", [128, 1], F32, kind="ExternalInput")
    onescolb_d = p.dram_io(io, "onescolb", [128, 1], BF16, kind="ExternalInput")
    xout_d = p.dram_io(io, "xout", [D, NTOK], F32, kind="ExternalOutput")

    xst = [p.sbuf(f"xst{i}", [128, NTOK], F32) for i in range(2)]
    QS = p.sbuf("QS", [128, NF * NTOK], BF16)
    G = p.sbuf("G", [128, NF * NTOK], BF16)
    KT = [p.sbuf(f"KT{i}", [128, T], BF16) for i in range(2)]
    VT = [p.sbuf(f"VT{i}", [128, 32 * 256], BF16) for i in range(2)]
    PT = [p.sbuf(f"PT{i}", [128, 512], BF16) for i in range(2)]
    dacc = p.sbuf("dacc", [128, 512], F32)
    scrA = p.sbuf("scrA", [128, NTOK], F32)
    scrB = p.sbuf("scrB", [128, NTOK], F32)
    yv = [p.sbuf(f"yv{v}", [128, 4 * 512], F32) for v in range(2)]
    sqt = p.sbuf("sqt", [128, NTOK], BF16)
    MSK = p.sbuf("MSK", [128, 4 * 512], F32)
    lamt = p.sbuf("lamt", [128, 8], F32)
    lam2t = p.sbuf("lam2t", [128, 8], F32)
    lamr = p.sbuf("lamr", [1, 16], F32)
    lamcols = p.sbuf("lamcols", [128, 8], F32)
    onescol = None
    ones1 = p.sbuf("ones1", [1, 128], F32)
    onescol = p.sbuf("onescolt", [128, 1], F32)
    onescolb = p.sbuf("onescolbt", [128, 1], BF16)
    rows = p.sbuf("rows", [1, 2 * NTOK], F32)
    bcR = p.sbuf("bcR", [128, NTOK], F32)

    psS = [p.psum(f"psS{i}", [128, 512], F32) for i in range(2)]
    psY = [p.psum(f"psY{i}", [128, 512], F32) for i in range(2)]
    psBC = p.psum("psBC", [128, 512], F32)
    psR = [p.psum(f"psR{i}", [1, 512], F32) for i in range(2)]
    psL = p.psum("psL", [128, 8], F32)

    def row(i, h=None):
        if h is None:
            return rows[0:1, i * NTOK:(i + 1) * NTOK]
        return rows[0:1, i * NTOK + h * 512:i * NTOK + (h + 1) * 512]

    p.dma("sync", ones1[:], ones_d.ap(), sem="ldc")
    p.dma("sync", onescol[:], onescol_d.ap(), sem="ldc")
    p.dma("sync", onescolb[:], onescolb_d.ap(), sem="ldc")
    LDC = p.cnt["ldc"]

    def bc_to(row_ap, n=NTOK):
        last = 0
        for h in range(n // 512):
            cpe = p.op("tensor", "matmul", psBC[:, 0:512], ones1[:],
                       row_ap[0:1, h * 512:(h + 1) * 512], start=True, stop=True,
                       sem="pe", waits=(("ac", p.cnt["ac"]), ("dv", p.cnt["dv"]),
                                        ("ldc", LDC)))
            last = p.op("scalar", "copy", bcR[:, h * 512:(h + 1) * 512],
                        psBC[:, 0:512], waits=(("pe", cpe),))
        return last

    for i in range(NF):
        p.dma("scalar", QS[:, i * NTOK:(i + 1) * NTOK],
              qs_d.ap()[i * 128:(i + 1) * 128, :], sem="ldq")
        p.dma("scalar", G[:, i * NTOK:(i + 1) * NTOK],
              g_d.ap()[i * 128:(i + 1) * 128, :], sem="ldq")
    LDQ = p.cnt["ldq"]

    p.dma("sync", lamt[:], lq1_d.ap(), sem="ldl")
    p.dma("sync", lam2t[:], lk1_d.ap(), sem="ldl")
    cm = p.op("vector", "tensor_mul", lamt[:], lamt[:], lam2t[:],
              waits=(("ldl", p.cnt["ldl"]),))
    c1m = p.op("tensor", "matmul", psL[0:1, 0:8], onescol[:], lamt[:],
               start=True, stop=True, sem="pe",
               waits=(("dv", cm), ("ldc", LDC)))
    ce1 = p.op("scalar", "activation", lamr[0:1, 0:8], psL[0:1, 0:8], AF.Exp,
               waits=(("pe", c1m),))
    p.dma("sync", lamt[:], lq2_d.ap(), sem="ldl", waits=(("dv", p.cnt["dv"]), ("pe", c1m)))
    p.dma("sync", lam2t[:], lk2_d.ap(), sem="ldl")
    cm2 = p.op("vector", "tensor_mul", lamt[:], lamt[:], lam2t[:],
               waits=(("ldl", p.cnt["ldl"]),))
    c2m = p.op("tensor", "matmul", psL[0:1, 0:8], onescol[:], lamt[:],
               start=True, stop=True, sem="pe",
               waits=(("dv", cm2), ("ac", ce1)))
    ce2 = p.op("scalar", "activation", lamr[0:1, 8:16], psL[0:1, 0:8], AF.Exp,
               waits=(("pe", c2m),))
    p.op("vector", "tensor_sub", lamr[0:1, 0:8], lamr[0:1, 0:8], lamr[0:1, 8:16],
         waits=(("ac", ce2),))
    p.op("vector", "tensor_scalar", lamr[0:1, 0:8], lamr[0:1, 0:8],
         float(LAMBDA_INIT), None, ALU.add)
    cbl = p.op("tensor", "matmul", psL[:, 0:8], ones1[:], lamr[0:1, 0:8],
               start=True, stop=True, sem="pe", waits=(("dv", p.cnt["dv"]),))
    LAM_AC = p.op("scalar", "copy", lamcols[:], psL[:, 0:8], waits=(("pe", cbl),))

    cz = p.op("vector", "memset", scrB[:, 0:512], 0.0)
    cmsk = 0
    for j in range(4):
        cmsk = p.op("gpsimd", "affine_select", MSK[:, j * 512:(j + 1) * 512], scrB[:, 0:512],
                    pattern=[[1, 512]], compare_op=ALU.is_ge, fill=-1e30,
                    channel_multiplier=-1, base=-128 * j, waits=(("dv", cz),))
    MSK_PL = cmsk

    unit_pe_end = {}
    for g in range(16):
        hp, v = g // 2, g % 2
        h = 2 * hp + v
        kbuf = KT[g % 2]
        lk = p.dma("sync", kbuf[:], kf_d.ap()[h * 128:(h + 1) * 128, :],
                   sem=f"ldk{g % 2}", waits=(("pe", unit_pe_end.get(g - 2, 0)),))
        if v == 0:
            vbuf = VT[hp % 2]
            lv = p.dma("scalar", vbuf.ap().rearrange("p (a o) -> p a o", o=256),
                       vf_d.ap().rearrange("(a p) d -> p a d", p=128)[:, :, hp * 256:(hp + 1) * 256],
                       sem=f"ldv{hp % 2}",
                       waits=(("pe", unit_pe_end.get(g - 4, 0)), ("ldq", LDQ)))
            lv_cur = lv
        qrow = QS[:, h * NTOK:(h + 1) * NTOK]
        for qc, (gchunk, nkt) in enumerate(((chunkA, 4 * (chunkA + 1)),
                                            (chunkB, 4 * (chunkB + 1)))):
            qsl = qrow[:, qc * 512:(qc + 1) * 512]
            mmy = {}
            exs = {}
            das = {}
            for kt in range(nkt):
                first = (kt == 0)
                pss = psS[kt % 2]
                pt = PT[kt % 2]
                diag = (kt >= 4 * gchunk)
                cpe = p.op("tensor", "matmul", pss[:],
                           kbuf[:, kt * 128:(kt + 1) * 128], qsl,
                           start=True, stop=True, sem="pe",
                           waits=(((f"ldk{g % 2}", lk), ("ldq", LDQ),
                                   ("ac", p.cnt["ac"]), ("dv", p.cnt["dv"]))
                                  if first else
                                  (("ac", exs.get(kt - 2, 0)),)))
                if diag:
                    j = kt - 4 * gchunk
                    cadd = p.op("vector", "tensor_add", scrA[:, 0:512], pss[:],
                                MSK[:, j * 512:(j + 1) * 512],
                                waits=(("pe", cpe), ("pl", MSK_PL),
                                       ("pe", mmy.get(kt - 2, (0, 0))[1])))
                    cex = p.op("scalar", "activation", pt[:], scrA[:, 0:512],
                               AF.Exp, waits=(("dv", cadd),
                                              ("dv", das.get(kt - 2, 0)),
                                              ("pe", mmy.get(kt - 2, (0, 0))[1])))
                else:
                    cex = p.op("scalar", "activation", pt[:], pss[:], AF.Exp,
                               waits=(("pe", cpe),
                                      ("dv", das.get(kt - 2, 0)),
                                      ("pe", mmy.get(kt - 2, (0, 0))[1])))
                exs[kt] = cex
                if first:
                    cda = p.op("vector", "tensor_copy", dacc[:], pt[:],
                               waits=(("ac", cex),))
                else:
                    cda = p.op("vector", "tensor_add", dacc[:], dacc[:], pt[:],
                               waits=(("ac", cex),))
                das[kt] = cda
                for e in range(2):
                    mmy[kt] = (e, p.op("tensor", "matmul", psY[e][:],
                                       vbuf[:, kt * 256 + e * 128:kt * 256 + (e + 1) * 128],
                                       pt[:], start=first, stop=(kt == nkt - 1),
                                       sem="pe",
                                       waits=(((f"ldv{hp % 2}", lv_cur), ("ac", cex),
                                               ("dv", yev if (first and g + qc > 0) else 0))
                                              if first and e == 0 else (("ac", cex),))))
            cdr = p.op("tensor", "matmul", psR[0][0:1, 0:512], onescol[:], dacc[:],
                       start=True, stop=True, sem="pe",
                       waits=(("dv", p.cnt["dv"]), ("ldc", LDC)))
            p.op("vector", "reciprocal", row(0, 0), psR[0][0:1, 0:512],
                 waits=(("pe", cdr),))
            lb = bc_to(row(0, 0), n=512)
            for e in range(2):
                yev = p.op("vector", "tensor_mul",
                           yv[v][:, (2 * qc + e) * 512:(2 * qc + e) * 512 + 512],
                           psY[e][:], bcR[:, 0:512],
                           waits=(("ac", lb), ("pe", mmy[nkt - 1][1])))
        unit_pe_end[g] = p.cnt["pe"]
        if v == 1:
            for e in range(2):
                ft = 2 * hp + e
                for qc in range(2):
                    slq = slice((2 * qc + e) * 512, (2 * qc + e) * 512 + 512)
                    qsl2 = slice(qc * 512, (qc + 1) * 512)
                    p.op("vector", "tensor_scalar", scrA[:, qsl2],
                         yv[1][:, slq], lamcols[:, hp:hp + 1], None, ALU.mult,
                         waits=(("ac", LAM_AC),))
                    p.op("vector", "tensor_sub", scrA[:, qsl2],
                         yv[0][:, slq], scrA[:, qsl2])
                    dst = yv[0][:, slq] if e == 0 else scrB[:, qsl2]
                    cg = p.op("vector", "tensor_mul", dst, scrA[:, qsl2],
                              G[:, ft * NTOK + qc * 512:ft * NTOK + (qc + 1) * 512],
                              waits=(("ldq", LDQ),))
                    csq = p.op("scalar", "square", sqt[:, qsl2], dst,
                               waits=(("dv", cg), ("pe", p.cnt["pe"])))
                    p.op("tensor", "matmul", psR[qc][0:1, 0:512], onescolb[:],
                         sqt[:, qsl2],
                         start=(e == 0), stop=(e == 1), sem="pe",
                         waits=(("ac", csq), ("dv", p.cnt["dv"]), ("ldc", LDC)))
            PE_RMS = p.cnt["pe"]
            for hh in range(2):
                p.op("vector", "tensor_scalar", row(1, hh), psR[hh][0:1, 0:512],
                     1.0 / 256, 1e-6, ALU.mult, ALU.add, waits=(("pe", PE_RMS),))
            cac = p.op("scalar", "sqrt", row(1), row(1), waits=(("dv", p.cnt["dv"]),))
            p.op("vector", "reciprocal", row(1), row(1), waits=(("ac", cac),))
            p.op("vector", "tensor_scalar", row(1), row(1),
                 float(1.0 - LAMBDA_INIT), None, ALU.mult)
            bc_to(row(1), n=NTOK)
            for e in range(2):
                ft = 2 * hp + e
                ldxt = p.dma("sync", xst[e][:], xin_d.ap()[ft * 128:(ft + 1) * 128, :],
                             sem="ldx", waits=(("sto", p.cnt["sto"]),))
                for qc in range(2):
                    qsl2 = slice(qc * 512, (qc + 1) * 512)
                    srcap = (yv[0][:, (2 * qc) * 512:(2 * qc) * 512 + 512]
                             if e == 0 else scrB[:, qsl2])
                    p.op("vector", "tensor_mul", scrA[:, qsl2], srcap, bcR[:, qsl2],
                         waits=(("ac", p.cnt["ac"]),))
                    yev = p.op("vector", "tensor_add", xst[e][:, qsl2],
                               xst[e][:, qsl2], scrA[:, qsl2],
                               waits=(("ldx", ldxt),))
                p.dma("gpsimd", xout_d.ap()[ft * 128:(ft + 1) * 128, :],
                      xst[e][:], sem="sto", waits=(("dv", p.cnt["dv"]),))
    p.wait("gpsimd", ("sto", p.cnt["sto"]))
    return p


def _pad_lam(a):
    out = np.zeros((128, 8), dtype=np.float32)
    out[:64, :] = np.float32(a).T
    return out


def build_l0_attn_bu(p=None, io=None, static_dbg=None,
                     dyn_parts=("q", "d", "y", "k", "v")):
    if p is None:
        p = Prog("l0attnbu")
    io = io or {}
    xin_d = p.dram_io(io, "xin", [D, NTOK], F32, kind="ExternalInput")
    qs_d = p.dram_io(io, "qs", [D, NTOK], BF16, kind="ExternalInput")
    kf_d = p.dram_io(io, "kf", [D, T], BF16, kind="ExternalInput")
    vf_d = p.dram_io(io, "vf", [T, D], BF16, kind="ExternalInput")
    g_d = p.dram_io(io, "go", [D, NTOK], BF16, kind="ExternalInput")
    lq1_d = p.dram_io(io, "lq1", [128, 8], F32, kind="ExternalInput")
    lk1_d = p.dram_io(io, "lk1", [128, 8], F32, kind="ExternalInput")
    lq2_d = p.dram_io(io, "lq2", [128, 8], F32, kind="ExternalInput")
    lk2_d = p.dram_io(io, "lk2", [128, 8], F32, kind="ExternalInput")
    tbl_d = p.dram_io(io, "tbl", [1, 96], I32, kind="ExternalInput")
    ones_d = p.dram_io(io, "onesc", [1, 128], F32, kind="ExternalInput")
    onescol_d = p.dram_io(io, "onescol", [128, 1], F32, kind="ExternalInput")
    onescolb_d = p.dram_io(io, "onescolb", [128, 1], BF16, kind="ExternalInput")
    xout_d = p.dram_io(io, "xout", [D, NTOK], F32, kind="ExternalOutput")
    kperm_d = p.dram_io(io, "kperm", [D, 36 * 128], BF16, kind="Internal")
    vperm_d = p.dram_io(io, "vperm", [36 * 128, D], BF16, kind="Internal")

    xst = [p.sbuf(f"xst{i}", [128, NTOK], F32) for i in range(2)]
    Qcur = p.sbuf("Qcur", [128, NTOK], BF16)
    G = p.sbuf("G", [128, NF * NTOK], BF16)
    KT = [p.sbuf(f"KT{i}", [128, 36 * 128], BF16) for i in range(2)]
    VT = [p.sbuf(f"VT{i}", [128, 36 * 256], BF16) for i in range(2)]
    PT = [p.sbuf(f"PT{i}", [128, 512], BF16) for i in range(2)]
    scrA = p.sbuf("scrA", [128, NTOK], F32)
    scrB = p.sbuf("scrB", [128, NTOK], F32)
    yv = [p.sbuf(f"yv{v}", [128, 4 * 512], F32) for v in range(2)]
    sqt = p.sbuf("sqt", [128, NTOK], BF16)
    MSK = p.sbuf("MSK", [128, 4 * 512], F32)
    lamt = p.sbuf("lamt", [128, 8], F32)
    lam2t = p.sbuf("lam2t", [128, 8], F32)
    lamr = p.sbuf("lamr", [1, 16], F32)
    lamcols = p.sbuf("lamcols", [128, 8], F32)
    tbls = p.sbuf("tbls", [1, 96], I32)
    ones1 = p.sbuf("ones1", [1, 128], F32)
    onescol = p.sbuf("onescolt", [128, 1], F32)
    onescolb = p.sbuf("onescolbt", [128, 1], BF16)
    rows = p.sbuf("rows", [1, 2 * NTOK], F32)
    bcR = p.sbuf("bcR", [128, NTOK], F32)

    psS = [p.psum(f"psS{i}", [128, 512], F32) for i in range(2)]
    psY2 = [p.psum(f"psY{e}", [128, 2 * 512], F32) for e in range(2)]
    psDEN = p.psum("psDEN", [1, 2 * 512], F32)

    def row(i, h=None):
        if h is None:
            return rows[0:1, i * NTOK:(i + 1) * NTOK]
        return rows[0:1, i * NTOK + h * 512:i * NTOK + (h + 1) * 512]

    p.dma("sync", ones1[:], ones_d.ap(), sem="ldc")
    p.dma("sync", onescol[:], onescol_d.ap(), sem="ldc")
    p.dma("sync", onescolb[:], onescolb_d.ap(), sem="ldc")
    p.dma("sync", tbls[:], tbl_d.ap(), sem="ldc")
    LDC = p.cnt["ldc"]

    def bc_to(row_ap, n=NTOK):
        last = 0
        for h in range(n // 512):
            cpe = p.op("tensor", "matmul", psS[0][:, 0:512], ones1[:],
                       row_ap[0:1, h * 512:(h + 1) * 512], start=True, stop=True,
                       sem="pe", waits=(("ac", p.cnt["ac"]), ("dv", p.cnt["dv"]),
                                        ("ldc", LDC)))
            last = p.op("scalar", "copy", bcR[:, h * 512:(h + 1) * 512],
                        psS[0][:, 0:512], waits=(("pe", cpe),))
        return last

    for i in range(NF):
        p.dma("scalar", G[:, i * NTOK:(i + 1) * NTOK],
              g_d.ap()[i * 128:(i + 1) * 128, :], sem="ldq")
    LDQ = p.cnt["ldq"]

    p.dma("sync", lamt[:], lq1_d.ap(), sem="ldl")
    p.dma("sync", lam2t[:], lk1_d.ap(), sem="ldl")
    cm = p.op("vector", "tensor_mul", lamt[:], lamt[:], lam2t[:],
              waits=(("ldl", p.cnt["ldl"]),))
    c1m = p.op("tensor", "matmul", psS[0][0:1, 0:8], onescol[:], lamt[:],
               start=True, stop=True, sem="pe",
               waits=(("dv", cm), ("ldc", LDC)))
    ce1 = p.op("scalar", "activation", lamr[0:1, 0:8], psS[0][0:1, 0:8], AF.Exp,
               waits=(("pe", c1m),))
    p.dma("sync", lamt[:], lq2_d.ap(), sem="ldl", waits=(("dv", p.cnt["dv"]), ("pe", c1m)))
    p.dma("sync", lam2t[:], lk2_d.ap(), sem="ldl")
    cm2 = p.op("vector", "tensor_mul", lamt[:], lamt[:], lam2t[:],
               waits=(("ldl", p.cnt["ldl"]),))
    c2m = p.op("tensor", "matmul", psS[0][0:1, 0:8], onescol[:], lamt[:],
               start=True, stop=True, sem="pe",
               waits=(("dv", cm2), ("ac", ce1)))
    ce2 = p.op("scalar", "activation", lamr[0:1, 8:16], psS[0][0:1, 0:8], AF.Exp,
               waits=(("pe", c2m),))
    p.op("vector", "tensor_sub", lamr[0:1, 0:8], lamr[0:1, 0:8], lamr[0:1, 8:16],
         waits=(("ac", ce2),))
    p.op("vector", "tensor_scalar", lamr[0:1, 0:8], lamr[0:1, 0:8],
         float(LAMBDA_INIT), None, ALU.add)
    cbl = p.op("tensor", "matmul", psS[0][:, 0:8], ones1[:], lamr[0:1, 0:8],
               start=True, stop=True, sem="pe", waits=(("dv", p.cnt["dv"]), ("ac", ce2)))
    LAM_AC = p.op("scalar", "copy", lamcols[:], psS[0][:, 0:8], waits=(("pe", cbl),))

    cz = p.op("vector", "memset", scrB[:, 0:512], 0.0)
    cmsk = 0
    for j in range(4):
        cmsk = p.op("gpsimd", "affine_select", MSK[:, j * 512:(j + 1) * 512],
                    scrB[:, 0:512], pattern=[[1, 512]], compare_op=ALU.is_ge,
                    fill=-1e30, channel_multiplier=-1, base=-128 * j,
                    waits=(("dv", cz),))
    MSK_PL = cmsk

    regs = {}

    def mkreg(eng, name):
        h = {}
        def fn(engine, _n=name, _h=h):
            _h["r"] = engine.register(_n).__enter__()
        p.raw(eng, fn)
        return h

    rS = mkreg("gpsimd", "rS")
    rSv = mkreg("sync", "rSv")

    zcolb = p.sbuf("zcolb", [128, 128], BF16)
    cz0 = p.op("vector", "memset", zcolb[:], 0.0)
    cz0 = p.op("vector", "memset", PT[0][:], 0.0)
    for q2 in range(2):
        p.op("tensor", "matmul", psDEN[0:1, q2 * 512:(q2 + 1) * 512],
             zcolb[:, 0:1], PT[0][:], start=True, stop=True, sem="pe",
             waits=(("dv", cz0),))
        for e2 in range(2):
            p.op("tensor", "matmul", psY2[e2][:, q2 * 512:(q2 + 1) * 512],
                 zcolb[:], PT[0][:], start=True, stop=True, sem="pe",
                 waits=(("dv", cz0),))
    ksrc_vals = {}
    vsrc_vals = {}

    stt = static_dbg[0] if static_dbg is not None else None
    dyn = (lambda part: stt is None or part in dyn_parts)

    kvg = io.get("kv_gathered", False)
    kmax = (4 * D * NTOK - 512) if kvg else (T - 512)
    vmax = (4 * NTOK - 512) if kvg else (T - 512)
    if dyn("k"):
        def gp_hoist(engine):
            engine.wait_ge(p.semh("ldc"), LDC)
            for i in range(9):
                engine.reg_load(rS["r"], tbls[0:1, 72 + i:72 + i + 1])
                ksrc_vals[i] = engine.snap(rS["r"], min_val=0, max_val=kmax)
        p.raw("gpsimd", gp_hoist)

    if dyn("v"):
        def sy_hoist(engine):
            engine.wait_ge(p.semh("ldc"), LDC)
            for i in range(9):
                engine.reg_load(rSv["r"], tbls[0:1, 81 + i:81 + i + 1])
                vsrc_vals[i] = engine.snap(rSv["r"], min_val=0, max_val=vmax)
        p.raw("sync", sy_hoist)
    for blk in range(9):
        if dyn("k"):
            if kvg:
                ksrc_ap = (lambda _b=blk: bass.AP(kf_d, ksrc_vals[_b],
                                                  [[NTOK, D], [1, 512]]))
            else:
                ksrc_ap = (lambda _b=blk: kf_d.ap()[:, bass.ds(ksrc_vals[_b], 512)])
        else:
            _s0 = int(stt[72 + blk])
            ksrc_ap = kf_d.ap()[:, _s0:_s0 + 512]
        if dyn("v"):
            vsrc_ap = (lambda _b=blk: vf_d.ap()[bass.ds(vsrc_vals[_b], 512), :])
        else:
            _s0 = int(stt[81 + blk])
            vsrc_ap = vf_d.ap()[_s0:_s0 + 512, :]
        p.dma("gpsimd", kperm_d.ap()[:, blk * 512:(blk + 1) * 512],
              ksrc_ap, sem="ldperm",
              waits=((("ldc", LDC),) if not dyn("k") and blk == 0 else ()))
        p.dma("sync", vperm_d.ap()[blk * 512:(blk + 1) * 512, :],
              vsrc_ap, sem="ldperm2",
              waits=((("ldc", LDC),) if not dyn("v") and blk == 0 else ()))
    LDPERM = p.cnt["ldperm"]
    LDPERM2 = p.cnt["ldperm2"]

    slot_off = {}

    def slot_bit_static(s_):
        if s_ < 4:
            return 0
        if s_ < 8 or s_ >= 20:
            return 512
        return None

    def slot_val(s_):
        return slot_off[(s_ - 8) // 4]

    def pe_hoist(engine):
        engine.wait_ge(p.semh("ldc"), LDC)
        for j in range(3):
            r = engine.alloc_register()
            engine.reg_load(r, tbls[0:1, 8 + 4 * j:9 + 4 * j])
            slot_off[j] = engine.snap(r, True, min_val=0, max_val=512)
    if stt is None or any(dyn(x) for x in ("q", "d", "y")):
        p.raw("tensor", pe_hoist)

    unit_pe_end = {}
    for g in range(DBG_UNITS):
        hp, v = g // 2, g % 2
        h = 2 * hp + v
        kbuf = KT[g % 2]
        lq_cur = p.dma("scalar", Qcur[:], qs_d.ap()[h * 128:(h + 1) * 128, :],
                       sem="ldqc", waits=(("pe", p.cnt["pe"]),))
        lk = p.dma("sync", kbuf[:], kperm_d.ap()[h * 128:(h + 1) * 128, :],
                   sem=f"ldk{g % 2}",
                   waits=(("pe", unit_pe_end.get(g - 2, 0)), ("ldperm", LDPERM)))
        if v == 0:
            vbuf = VT[hp % 2]
            lv_cur = p.dma("scalar",
                           vbuf.ap().rearrange("p (a o) -> p a o", o=256),
                           vperm_d.ap().rearrange("(a p) d -> p a d", p=128)[:, :, hp * 256:(hp + 1) * 256],
                           sem=f"ldv{hp % 2}",
                           waits=(("pe", unit_pe_end.get(g - 4, 0)),
                                  ("ldq", LDQ), ("ldperm2", LDPERM2)))
        mmy = {}
        exs = {}
        das = {}
        for s in range(36):
            first = (s == 0)
            pss = psS[s % 2]
            pt = PT[s % 2]
            diag = (s < 8) and ("diag" not in DBG_STRIP)

            sbit = slot_bit_static(s)
            if dyn("q") and sbit is None:
                def mk_qsl(_s=s):
                    return Qcur[:, bass.ds(slot_val(_s), 512)]
            else:
                _qv = sbit if sbit is not None else int(stt[s])
                mk_qsl = Qcur[:, _qv:_qv + 512]

            mk_lhsT = kbuf[:, s * 128:(s + 1) * 128]

            cpe = p.op("tensor", "matmul", pss[:], mk_lhsT, mk_qsl,
                       start=True, stop=True, sem="pe",
                       waits=(((f"ldk{g % 2}", lk), ("ldqc", lq_cur),
                               ("ac", p.cnt["ac"]), ("dv", p.cnt["dv"]),
                               ("ldc", LDC))
                              if first else
                              (("ac", exs.get(s - 2, 0)),)))
            if diag:
                j = s % 4
                cadd = p.op("vector", "tensor_add", scrA[:, 0:512], pss[:],
                            MSK[:, j * 512:(j + 1) * 512],
                            waits=(("pe", cpe), ("pl", MSK_PL),
                                   ("pe", mmy.get(s - 2, 0))))
                cex = p.op("scalar", "activation", pt[:], scrA[:, 0:512],
                           AF.Exp, waits=(("dv", cadd), ("dv", das.get(s - 2, 0)),
                                          ("pe", mmy.get(s - 2, 0))))
            else:
                cex = p.op("scalar", "activation", pt[:], pss[:], AF.Exp,
                           waits=(("pe", cpe), ("dv", das.get(s - 2, 0)),
                                  ("pe", mmy.get(s - 2, 0))))
            exs[s] = cex
            if dyn("d") and sbit is None:
                def mk_den(_s=s):
                    return psDEN[:, bass.ds(slot_val(_s), 512)]
            else:
                _dv = sbit if sbit is not None else int(stt[s])
                mk_den = psDEN[:, _dv:_dv + 512]

            p.op("tensor", "matmul", mk_den, onescolb[:], pt[:],
                 start=(s in (0, 4)), stop=False, sem="pe",
                 skip_group_check=True, waits=(("ac", cex), ("ldc", LDC)))
            das[s] = 0
            for e in range(2):
                if dyn("y") and sbit is None:
                    def mk_psy(_s=s, _e=e):
                        return psY2[_e][:, bass.ds(slot_val(_s), 512)]
                else:
                    _yv = sbit if sbit is not None else (int(stt[36 + s]) // 1024) * 512
                    mk_psy = psY2[e][:, _yv:_yv + 512]

                mk_vt = vbuf[:, s * 256 + e * 128:s * 256 + (e + 1) * 128]

                mmy[s] = p.op("tensor", "matmul", mk_psy, mk_vt, pt[:],
                              start=(s in (0, 4)), stop=False, sem="pe",
                              skip_group_check=True,
                              waits=(((f"ldv{hp % 2}", lv_cur), ("ac", cex),
                                      ("dv", p.cnt["dv"]))
                                     if s in (0, 1) else (("ac", cex),)))
        PE_SLOTS = p.cnt["pe"]
        for qc in ([] if "evac" in DBG_STRIP else range(2)):
            p.op("vector", "reciprocal", row(0, 0),
                 psDEN[0:1, qc * 512:(qc + 1) * 512],
                 waits=(("pe", PE_SLOTS),))
            lb = bc_to(row(0, 0), n=512)
            for e in range(2):
                p.op("vector", "tensor_mul",
                     yv[v][:, (2 * qc + e) * 512:(2 * qc + e) * 512 + 512],
                     psY2[e][:, qc * 512:qc * 512 + 512],
                     bcR[:, 0:512],
                     waits=(("ac", lb), ("pe", p.cnt["pe"])))
        unit_pe_end[g] = p.cnt["pe"]
        if v == 1 and "comb" not in DBG_STRIP:
            for e in range(2):
                ft = 2 * hp + e
                for qc in range(2):
                    slq = slice((2 * qc + e) * 512, (2 * qc + e) * 512 + 512)
                    qsl2 = slice(qc * 512, (qc + 1) * 512)
                    p.op("vector", "tensor_scalar", scrA[:, qsl2],
                         yv[1][:, slq], lamcols[:, hp:hp + 1], None, ALU.mult,
                         waits=(("ac", LAM_AC),))
                    p.op("vector", "tensor_sub", scrA[:, qsl2],
                         yv[0][:, slq], scrA[:, qsl2])
                    dst = yv[0][:, slq] if e == 0 else scrB[:, qsl2]
                    cg = p.op("vector", "tensor_mul", dst, scrA[:, qsl2],
                              G[:, ft * NTOK + qc * 512:ft * NTOK + (qc + 1) * 512],
                              waits=(("ldq", LDQ),))
                    csq = p.op("scalar", "square", sqt[:, qsl2], dst,
                               waits=(("dv", cg), ("pe", p.cnt["pe"])))
                    tgt = psDEN[0:1, 0:512] if qc == 0 else psDEN[0:1, 512:1024]
                    p.op("tensor", "matmul", tgt, onescolb[:], sqt[:, qsl2],
                         start=(e == 0), stop=(e == 1), sem="pe",
                         waits=(("ac", csq), ("dv", p.cnt["dv"]), ("ldc", LDC)))
            PE_RMS = p.cnt["pe"]
            p.op("vector", "tensor_scalar", row(1, 0), psDEN[0:1, 0:512],
                 1.0 / 256, 1e-6, ALU.mult, ALU.add, waits=(("pe", PE_RMS),))
            p.op("vector", "tensor_scalar", row(1, 1), psDEN[0:1, 512:1024],
                 1.0 / 256, 1e-6, ALU.mult, ALU.add)
            cac = p.op("scalar", "sqrt", row(1), row(1), waits=(("dv", p.cnt["dv"]),))
            p.op("vector", "reciprocal", row(1), row(1), waits=(("ac", cac),))
            p.op("vector", "tensor_scalar", row(1), row(1),
                 float(1.0 - LAMBDA_INIT), None, ALU.mult)
            bc_to(row(1), n=NTOK)
            for e in range(2):
                ft = 2 * hp + e
                ldxt = p.dma("sync", xst[e][:], xin_d.ap()[ft * 128:(ft + 1) * 128, :],
                             sem="ldx", waits=(("sto", p.cnt["sto"]),))
                for qc in range(2):
                    qsl2 = slice(qc * 512, (qc + 1) * 512)
                    srcap = (yv[0][:, (2 * qc) * 512:(2 * qc) * 512 + 512]
                             if e == 0 else scrB[:, qsl2])
                    p.op("vector", "tensor_mul", scrA[:, qsl2], srcap, bcR[:, qsl2],
                         waits=(("ac", p.cnt["ac"]),))
                    p.op("vector", "tensor_add", xst[e][:, qsl2],
                         xst[e][:, qsl2], scrA[:, qsl2],
                         waits=(("ldx", ldxt),))
                p.dma("gpsimd", xout_d.ap()[ft * 128:(ft + 1) * 128, :],
                      xst[e][:], sem="sto", waits=(("dv", p.cnt["dv"]),))
    p.wait("gpsimd", ("sto", p.cnt["sto"]))
    return p


def host_attn_b_table(c, gathered=False):
    cp = c % 4
    kA, kB = 4 * (cp + 1), 4 * (8 - cp)
    slots = []
    for s in range(4):
        slots.append(((kA - 4 + s) * 128, 0))
    for s in range(4):
        slots.append(((kB - 4 + s) * 128, 1))
    for k in range(kA - 4):
        slots.append((k * 128, 0))
    for k in range(kB - 4):
        slots.append((k * 128, 1))
    assert len(slots) == 36
    t = np.zeros((1, 96), dtype=np.int32)
    t[0, 0:36] = [b * 512 for _, b in slots]
    t[0, 36:72] = [b * 1024 for _, b in slots]
    gchunks = [cp, 7 - cp] + list(range(cp)) + list(range(7 - cp))
    if gathered:
        ks, vs = [], []
        for g in gchunks:
            cc, half = min(g, 7 - g), (0 if g < 4 else 1)
            ks.append(cc * D * NTOK + half * 512)
            vs.append(cc * NTOK + half * 512)
        t[0, 72:81] = ks
        t[0, 81:90] = vs
    else:
        srcs = [512 * g for g in gchunks]
        t[0, 72:81] = srcs
        t[0, 81:90] = srcs
    return t


def host_attn_b_inputs(x_fm_c, qs, kfull, vfull, g, ins):
    return {
        "xin": x_fm_c, "qs": qs, "kf": kfull, "vf": vfull, "go": g,
        "lq1": _pad_lam(ins["L0_attn_lambda_q1"]),
        "lk1": _pad_lam(ins["L0_attn_lambda_k1"]),
        "lq2": _pad_lam(ins["L0_attn_lambda_q2"]),
        "lk2": _pad_lam(ins["L0_attn_lambda_k2"]),
        "onesc": np.ones((1, 128), dtype=np.float32),
        "onescol": np.ones((128, 1), dtype=np.float32),
        "onescolb": np.ones((128, 1), dtype=BF),
    }


def build_l1_ffn(p=None, io=None):
    if p is None:
        p = Prog("l1ffn")
    io = io or {}
    STO0 = p.cnt["sto"]
    xin_d = p.dram_io(io, "xin", [D, NTOK], F32, kind="ExternalInput")
    fs_d = p.dram_io(io, "fsum", [D, D], BF16, kind="ExternalInput")
    bias_d = p.dram_io(io, "bias", [128, 16], F32, kind="ExternalInput")
    ones_d = p.dram_io(io, "onesc", [1, 128], F32, kind="ExternalInput")
    onescolb_d = p.dram_io(io, "onescolb", [128, 1], BF16, kind="ExternalInput")
    xout_d = p.dram_io(io, "xout", [D, NTOK], F32, kind="ExternalOutput")

    xt = [p.sbuf(f"xt{i}", [128, NTOK], F32) for i in range(2)]
    xb = p.sbuf("xb", [128, NTOK], BF16)
    sq = p.sbuf("sq", [128, NTOK], BF16)
    scrA = p.sbuf("scrA", [128, NTOK], F32)
    scrB = p.sbuf("scrB", [128, NTOK], F32)
    scrC = p.sbuf("scrC", [128, NTOK], F32)
    XLN = p.sbuf("XLN", [128, NF * NTOK], BF16)
    wb = [p.sbuf(f"wb{i}", [128, NF * 128], BF16) for i in range(2)]
    hb = p.sbuf("hb", [128, 16], F32)
    ones1 = p.sbuf("ones1", [1, 128], F32)
    onescolb = p.sbuf("onescolbt", [128, 1], BF16)
    rows = p.sbuf("rows", [1, 4 * NTOK], F32)
    bcA = p.sbuf("bcA", [128, NTOK], F32)
    bcB = p.sbuf("bcB", [128, NTOK], F32)

    psQ = [p.psum(f"psQ{i}", [128, 512], F32) for i in range(4)]
    psBC = p.psum("psBC", [128, 512], F32)

    C8 = float(np.float32(0.5 / np.sqrt(np.float64(8.0))))

    def row(i, h=None):
        if h is None:
            return rows[0:1, i * NTOK:(i + 1) * NTOK]
        return rows[0:1, i * NTOK + h * 512:i * NTOK + (h + 1) * 512]

    p.dma("sync", ones1[:], ones_d.ap(), sem="ldc")
    p.dma("sync", onescolb[:], onescolb_d.ap(), sem="ldc")
    lb0 = p.dma("sync", hb[:], bias_d.ap(), sem="ldc")
    LDC = p.cnt["ldc"]
    p.op("vector", "tensor_scalar", hb[:], hb[:], 0.5, None, ALU.mult,
         waits=(("ldc", LDC),))
    HB_DV = p.cnt["dv"]

    def bc_to(dst, row_ap, n=NTOK):
        last = 0
        for h in range(n // 512):
            cpe = p.op("tensor", "matmul", psBC[:, 0:512], ones1[:],
                       row_ap[0:1, h * 512:(h + 1) * 512], start=True, stop=True,
                       sem="pe", waits=(("ac", p.cnt["ac"]), ("dv", p.cnt["dv"]),
                                        ("ldc", LDC)))
            last = p.op("scalar", "copy", dst[:, h * 512:(h + 1) * 512],
                        psBC[:, 0:512], waits=(("pe", cpe),))
        return last

    for i in range(NF):
        buf = xt[i % 2]
        ld = p.dma("sync", buf[:], xin_d.ap()[i * 128:(i + 1) * 128, :], sem="ldx",
                   waits=(("ac", p.cnt["ac"]), ("dv", p.cnt["dv"])))
        cb = p.op("vector", "tensor_copy", xb[:], buf[:],
                  waits=(("ldx", ld), ("pe", p.cnt["pe"])))
        cs = p.op("scalar", "square", sq[:], buf[:],
                  waits=(("ldx", ld), ("pe", p.cnt["pe"])))
        for h in range(2):
            ps_s = psQ[0] if h == 0 else psQ[1]
            ps_q = psQ[2] if h == 0 else psQ[3]
            p.op("tensor", "matmul", ps_s[0:1, 0:512], onescolb[:],
                 xb[:, h * 512:(h + 1) * 512], start=(i == 0), stop=(i == NF - 1),
                 sem="pe", waits=(("dv", cb), ("ldc", LDC)))
            p.op("tensor", "matmul", ps_q[0:1, 0:512], onescolb[:],
                 sq[:, h * 512:(h + 1) * 512], start=(i == 0), stop=(i == NF - 1),
                 sem="pe", waits=(("ac", cs),))
    PE_STATS = p.cnt["pe"]
    for h in range(2):
        p.op("vector", "tensor_scalar", row(0, h), (psQ[0] if h == 0 else psQ[1])[0:1, 0:512],
             1.0 / D, None, ALU.mult, waits=(("pe", PE_STATS),))
        p.op("vector", "tensor_scalar", row(1, h), (psQ[2] if h == 0 else psQ[3])[0:1, 0:512],
             1.0 / D, None, ALU.mult, waits=(("pe", PE_STATS),))
    p.op("vector", "tensor_mul", row(2), row(0), row(0))
    p.op("vector", "tensor_sub", row(1), row(1), row(2))
    p.op("vector", "tensor_scalar", row(1), row(1), 1e-5, None, ALU.add)
    cac = p.op("scalar", "sqrt", row(1), row(1), waits=(("dv", p.cnt["dv"]),))
    p.op("vector", "reciprocal", row(2), row(1), waits=(("ac", cac),))
    p.op("vector", "tensor_mul", row(3), row(0), row(2))
    bc_to(bcA, row(2))
    bc_to(bcB, row(3))
    BC_DONE = p.cnt["ac"]
    for i in range(NF):
        buf = xt[i % 2]
        ld = p.dma("sync", buf[:], xin_d.ap()[i * 128:(i + 1) * 128, :], sem="ldx",
                   waits=(("dv", p.cnt["dv"]),))
        p.op("vector", "tensor_mul", scrA[:], buf[:], bcA[:],
             waits=(("ldx", ld), ("ac", BC_DONE)))
        p.op("vector", "tensor_sub", scrA[:], scrA[:], bcB[:])
        p.op("vector", "tensor_copy", XLN[:, i * NTOK:(i + 1) * NTOK], scrA[:])
    XLN_DONE = p.cnt["dv"]

    pe_end = {}
    ev = {}
    for c in range(NF):
        if io.get("gather_fs", False):
            fsrc = (fs_d.ap().rearrange("(A p) o -> p A o", p=128)
                    [:, (c // 2) * 16:(c // 2 + 1) * 16,
                     (c % 2) * 128:(c % 2 + 1) * 128])
        else:
            fsrc = (fs_d.ap().rearrange("(a p) o -> p a o", p=128)
                    [:, :, c * 128:(c + 1) * 128])
        lw = p.dma("gpsimd", wb[c % 2].ap().rearrange("p (a o) -> p a o", o=128),
                   fsrc, sem=f"ldw{c % 2}", waits=(("pe", pe_end.get(c - 2, 0)),))
        for h in range(2):
            ps = psQ[2 * (c % 2) + h]
            mm = 0
            for i in range(NF):
                mm = p.op("tensor", "matmul", ps[:],
                          wb[c % 2][:, i * 128:(i + 1) * 128],
                          XLN[:, i * NTOK + h * 512:i * NTOK + h * 512 + 512],
                          start=(i == 0), stop=(i == NF - 1), sem="pe",
                          waits=(((f"ldw{c % 2}", lw), ("dv", XLN_DONE),
                                  ("dv", ev.get((c - 2, h), 0))) if i == 0 else ()))
            n = 2 * c + h
            ldx2 = p.dma("sync", xt[c % 2][:, h * 512:(h + 1) * 512],
                         xin_d.ap()[c * 128:(c + 1) * 128, h * 512:(h + 1) * 512],
                         sem="ldx2", waits=(("sto", STO0 + 16 * n), ("dv", XLN_DONE)))
            c1 = p.op("vector", "tensor_scalar", scrB[:, 0:512], ps[:],
                      C8, hb[:, c:c + 1], ALU.mult, ALU.add,
                      waits=(("pe", mm), ("dv", HB_DV)))
            evd = scrC[:, (n % 2) * 512:(n % 2) * 512 + 512]
            c2 = p.op("vector", "tensor_add", evd, scrB[:, 0:512],
                      xt[c % 2][:, h * 512:(h + 1) * 512],
                      waits=(("ldx2", ldx2), ("sto", STO0 + 16 * n)))
            ev[(c, h)] = c2
            p.dma("gpsimd", xout_d.ap()[c * 128:(c + 1) * 128, h * 512:(h + 1) * 512],
                  evd, sem="sto", waits=(("dv", c2),))
        pe_end[c] = p.cnt["pe"]
    p.wait("gpsimd", ("sto", p.cnt["sto"]))
    return p


def host_l1_ffn_inputs(x_fm_c, fsum, ins):
    return {
        "xin": x_fm_c, "fsum": fsum,
        "bias": np.ascontiguousarray(np.float32(ins["L1_ffn_bias"]).reshape(16, 128).T),
        "onesc": np.ones((1, 128), dtype=np.float32),
        "onescolb": np.ones((128, 1), dtype=BF),
    }


def build_l1_attn_a(p=None, io=None):
    if p is None:
        p = Prog("l1attna")
    io = io or {}
    xin_d = p.dram_io(io, "xin", [D, NTOK], F32, kind="ExternalInput")
    wq_d = p.dram_io(io, "wq", [D, D], F32, kind="ExternalInput")
    wk_d = p.dram_io(io, "wk", [D, D], F32, kind="ExternalInput")
    wv_d = p.dram_io(io, "wv", [D, D], F32, kind="ExternalInput")
    bq_d = p.dram_io(io, "bq", [128, 16], F32, kind="ExternalInput")
    bk_d = p.dram_io(io, "bk", [128, 16], F32, kind="ExternalInput")
    bv_d = p.dram_io(io, "bv", [128, 16], F32, kind="ExternalInput")
    ones_d = p.dram_io(io, "onesc", [1, 128], F32, kind="ExternalInput")
    onescolb_d = p.dram_io(io, "onescolb", [128, 1], BF16, kind="ExternalInput")
    qs_o = p.dram_io(io, "qs", [D, NTOK], BF16, kind="ExternalOutput")
    k_o = p.dram_io(io, "ko", [D, NTOK], BF16, kind="ExternalOutput")
    vt_o = p.dram_io(io, "vt", [NTOK, D], BF16, kind="ExternalOutput")

    xt = [p.sbuf(f"xt{i}", [128, NTOK], F32) for i in range(2)]
    xb = p.sbuf("xb", [128, NTOK], BF16)
    sq = p.sbuf("sq", [128, NTOK], BF16)
    scrA = p.sbuf("scrA", [128, NTOK], F32)
    scrB = p.sbuf("scrB", [128, NTOK], F32)
    XLN = p.sbuf("XLN", [128, NF * NTOK], BF16)
    wb = [p.sbuf(f"wb{i}", [128, NF * 128], BF16) for i in range(2)]
    obh = p.sbuf("obh", [128, NTOK], BF16)
    vstage = p.sbuf("vstage", [128, NTOK], BF16)
    vtr = p.sbuf("vtr", [128, NTOK], BF16)
    bqs = p.sbuf("bqs", [128, 48], F32)
    ones1 = p.sbuf("ones1", [1, 128], F32)
    onescolb = p.sbuf("onescolbt", [128, 1], BF16)
    rows = p.sbuf("rows", [1, 4 * NTOK], F32)
    bcA = p.sbuf("bcA", [128, NTOK], F32)
    bcB = p.sbuf("bcB", [128, NTOK], F32)
    idn = p.sbuf("idnf", [128, 128], BF16)
    ones128 = p.sbuf("ones128", [128, 128], BF16)

    psQ = [p.psum(f"psQ{i}", [128, 512], F32) for i in range(4)]
    psBC = p.psum("psBC", [128, 512], F32)
    psTR = p.psum("psTR", [128, 128], BF16)

    def row(i, h=None):
        if h is None:
            return rows[0:1, i * NTOK:(i + 1) * NTOK]
        return rows[0:1, i * NTOK + h * 512:i * NTOK + (h + 1) * 512]

    p.dma("sync", ones1[:], ones_d.ap(), sem="ldc")
    p.dma("sync", onescolb[:], onescolb_d.ap(), sem="ldc")
    p.dma("sync", bqs[:, 0:16], bq_d.ap(), sem="ldc")
    p.dma("sync", bqs[:, 16:32], bk_d.ap(), sem="ldc")
    p.dma("sync", bqs[:, 32:48], bv_d.ap(), sem="ldc")
    LDC = p.cnt["ldc"]
    cmo = p.op("vector", "memset", ones128[:], 1.0, waits=(("ldc", LDC),))
    p.op("gpsimd", "affine_select", idn[:], ones128[:], pattern=[[-1, 128]],
         compare_op=ALU.is_equal, fill=0.0, channel_multiplier=1,
         waits=(("dv", cmo),))
    PL_ID = p.cnt["pl"]

    def bc_to(dst, row_ap, n=NTOK):
        last = 0
        for h in range(n // 512):
            cpe = p.op("tensor", "matmul", psBC[:, 0:512], ones1[:],
                       row_ap[0:1, h * 512:(h + 1) * 512], start=True, stop=True,
                       sem="pe", waits=(("ac", p.cnt["ac"]), ("dv", p.cnt["dv"]),
                                        ("ldc", LDC)))
            last = p.op("scalar", "copy", dst[:, h * 512:(h + 1) * 512],
                        psBC[:, 0:512], waits=(("pe", cpe),))
        return last

    for i in range(NF):
        buf = xt[i % 2]
        ld = p.dma("sync", buf[:], xin_d.ap()[i * 128:(i + 1) * 128, :], sem="ldx",
                   waits=(("ac", p.cnt["ac"]), ("dv", p.cnt["dv"])))
        cb = p.op("vector", "tensor_copy", xb[:], buf[:],
                  waits=(("ldx", ld), ("pe", p.cnt["pe"])))
        cs = p.op("scalar", "square", sq[:], buf[:],
                  waits=(("ldx", ld), ("pe", p.cnt["pe"])))
        for h in range(2):
            ps_s = psQ[0] if h == 0 else psQ[1]
            ps_q = psQ[2] if h == 0 else psQ[3]
            p.op("tensor", "matmul", ps_s[0:1, 0:512], onescolb[:],
                 xb[:, h * 512:(h + 1) * 512], start=(i == 0), stop=(i == NF - 1),
                 sem="pe", waits=(("dv", cb), ("ldc", LDC)))
            p.op("tensor", "matmul", ps_q[0:1, 0:512], onescolb[:],
                 sq[:, h * 512:(h + 1) * 512], start=(i == 0), stop=(i == NF - 1),
                 sem="pe", waits=(("ac", cs),))
    PE_STATS = p.cnt["pe"]
    for h in range(2):
        p.op("vector", "tensor_scalar", row(0, h), (psQ[0] if h == 0 else psQ[1])[0:1, 0:512],
             1.0 / D, None, ALU.mult, waits=(("pe", PE_STATS),))
        p.op("vector", "tensor_scalar", row(1, h), (psQ[2] if h == 0 else psQ[3])[0:1, 0:512],
             1.0 / D, None, ALU.mult, waits=(("pe", PE_STATS),))
    p.op("vector", "tensor_mul", row(2), row(0), row(0))
    p.op("vector", "tensor_sub", row(1), row(1), row(2))
    p.op("vector", "tensor_scalar", row(1), row(1), 1e-5, None, ALU.add)
    cac = p.op("scalar", "sqrt", row(1), row(1), waits=(("dv", p.cnt["dv"]),))
    p.op("vector", "reciprocal", row(2), row(1), waits=(("ac", cac),))
    p.op("vector", "tensor_mul", row(3), row(0), row(2))
    bc_to(bcA, row(2))
    bc_to(bcB, row(3))
    BC_DONE = p.cnt["ac"]
    for i in range(NF):
        buf = xt[i % 2]
        ld = p.dma("sync", buf[:], xin_d.ap()[i * 128:(i + 1) * 128, :], sem="ldx",
                   waits=(("dv", p.cnt["dv"]),))
        p.op("vector", "tensor_mul", scrA[:], buf[:], bcA[:],
             waits=(("ldx", ld), ("ac", BC_DONE)))
        p.op("vector", "tensor_sub", scrA[:], scrA[:], bcB[:])
        p.op("vector", "tensor_copy", XLN[:, i * NTOK:(i + 1) * NTOK], scrA[:])
    XLN_DONE = p.cnt["dv"]

    W = [wq_d, wk_d, wv_d]
    pe_end = {}
    pe_ev_ac = 0
    pe_ev_dv = 0
    for c in range(48):
        kind = c // 16
        head = c % 16
        lw = p.dma("gpsimd", wb[c % 2].ap().rearrange("p (a o) -> p a o", o=128),
                   W[kind].ap().rearrange("(a p) o -> p a o", p=128)[:, :, head * 128:(head + 1) * 128],
                   sem=f"ldw{c % 2}", waits=(("pe", pe_end.get(c - 2, 0)),))
        mm = {}
        for h in range(2):
            ps = psQ[2 * (c % 2) + h]
            for i in range(NF):
                mm[h] = p.op("tensor", "matmul", ps[:],
                             wb[c % 2][:, i * 128:(i + 1) * 128],
                             XLN[:, i * NTOK + h * 512:i * NTOK + h * 512 + 512],
                             start=(i == 0), stop=(i == NF - 1), sem="pe",
                             waits=(((f"ldw{c % 2}", lw), ("dv", XLN_DONE),
                                     ("ac", pe_ev_ac), ("dv", pe_ev_dv))
                                    if (i == 0 and h == 0) else ()))
        bcol = bqs[:, kind * 16 + head:kind * 16 + head + 1]
        if kind == 2:
            for h in range(2):
                ps = psQ[2 * (c % 2) + h]
                p.op("vector", "tensor_scalar", vstage[:, h * 512:(h + 1) * 512],
                     ps[:], bcol, None, ALU.add,
                     waits=(("pe", mm[h]), ("pe", p.cnt["pe"]), ("ldc", LDC)))
            for j in range(8):
                cpe = p.op("tensor", "transpose", psTR[:],
                           vstage[:, j * 128:(j + 1) * 128], idn[:], sem="pe",
                           waits=(("dv", p.cnt["dv"]), ("ac", p.cnt["ac"]),
                                  ("pl", PL_ID)))
                p.op("scalar", "copy", vtr[:, j * 128:(j + 1) * 128], psTR[:],
                     waits=(("pe", cpe), ("sto", p.cnt["sto"])))
            pe_ev_ac, pe_ev_dv = p.cnt["ac"], p.cnt["dv"]
            for j in range(8):
                p.dma("gpsimd", vt_o.ap()[j * 128:(j + 1) * 128,
                                          head * 128:(head + 1) * 128],
                      vtr[:, j * 128:(j + 1) * 128], sem="sto",
                      waits=(("ac", p.cnt["ac"]),))
        else:
            for h in range(2):
                ps = psQ[2 * (c % 2) + h]
                if kind == 0:
                    p.op("vector", "tensor_scalar", scrA[:, h * 512:(h + 1) * 512],
                         ps[:], bcol, None, ALU.add,
                         waits=(("pe", mm[h]), ("ldc", LDC), ("sto", p.cnt["sto"])))
                else:
                    p.op("vector", "tensor_scalar", obh[:, h * 512:(h + 1) * 512],
                         ps[:], bcol, None, ALU.add,
                         waits=(("pe", mm[h]), ("ldc", LDC), ("sto", p.cnt["sto"])))
            if kind == 0:
                p.op("vector", "tensor_scalar", scrB[:], scrA[:],
                     float(SC_HI), None, ALU.mult)
                p.op("vector", "scalar_tensor_tensor", obh[:], scrA[:],
                     float(SC_LO), scrB[:], ALU.mult, ALU.add)
            pe_ev_ac, pe_ev_dv = p.cnt["ac"], p.cnt["dv"]
            p.dma("gpsimd", (qs_o if kind == 0 else k_o).ap()[head * 128:(head + 1) * 128, :],
                  obh[:], sem="sto", waits=(("dv", p.cnt["dv"]),))
        pe_end[c] = p.cnt["pe"]
    p.wait("gpsimd", ("sto", p.cnt["sto"]))
    return p


def host_l1_attn_a_inputs(x_fm_c, ins):
    rb = lambda k: np.ascontiguousarray(np.float32(ins[k]).reshape(16, 128).T)
    return {
        "xin": x_fm_c,
        "wq": np.ascontiguousarray(np.float32(ins["L1_attn_Wq"])),
        "wk": np.ascontiguousarray(np.float32(ins["L1_attn_Wk"])),
        "wv": np.ascontiguousarray(np.float32(ins["L1_attn_Wv"])),
        "bq": rb("L1_attn_bq"), "bk": rb("L1_attn_bk"), "bv": rb("L1_attn_bv"),
        "onesc": np.ones((1, 128), dtype=np.float32),
        "onescolb": np.ones((128, 1), dtype=BF),
    }


def build_l1_attn_b(p=None, io=None):
    if p is None:
        p = Prog("l1attnb")
    io = io or {}
    qs_d = p.dram_io(io, "qs", [D, NTOK], BF16, kind="ExternalInput")
    kf_d = p.dram_io(io, "kf", [D, T], BF16, kind="ExternalInput")
    vf_d = p.dram_io(io, "vf", [T, D], BF16, kind="ExternalInput")
    ones_d = p.dram_io(io, "onesc", [1, 128], F32, kind="ExternalInput")
    onescol_d = p.dram_io(io, "onescol", [128, 1], F32, kind="ExternalInput")
    oc_o = p.dram_io(io, "oc", [128, NTOK * 16], BF16, kind="ExternalOutput")

    QS = p.sbuf("QS", [128, NF * NTOK], BF16)
    KT = [p.sbuf(f"KT{i}", [128, T], BF16) for i in range(2)]
    VT = [p.sbuf(f"VT{i}", [128, 32 * 128], BF16) for i in range(2)]
    PT = [p.sbuf(f"PT{i}", [128, 512], BF16) for i in range(2)]
    dacc = p.sbuf("dacc", [128, 512], F32)
    OC = p.sbuf("OC", [128, NTOK * 16], BF16)
    ones1 = p.sbuf("ones1", [1, 128], F32)
    onescol = p.sbuf("onescolt", [128, 1], F32)
    rows = p.sbuf("rows", [1, NTOK], F32)
    bcR = p.sbuf("bcR", [128, 512], F32)
    negc = p.sbuf("negc", [128, 1], F32)

    psS = [p.psum(f"psS{i}", [128, 512], F32) for i in range(2)]
    psY = [p.psum(f"psY{i}", [128, 512], F32) for i in range(2)]
    psBC = p.psum("psBC", [128, 512], F32)
    psR = p.psum("psR", [1, 512], F32)

    p.dma("sync", ones1[:], ones_d.ap(), sem="ldc")
    p.dma("sync", onescol[:], onescol_d.ap(), sem="ldc")
    LDC = p.cnt["ldc"]
    NEG_DV = p.op("vector", "memset", negc[:], -20.0)
    for i in range(NF):
        p.dma("scalar", QS[:, i * NTOK:(i + 1) * NTOK],
              qs_d.ap()[i * 128:(i + 1) * 128, :], sem="ldq")
    LDQ = p.cnt["ldq"]

    kvg = io.get("kv_gathered", False)
    unit_pe_end = {}
    for h in range(NH):
        kbuf = KT[h % 2]
        vbuf = VT[h % 2]
        if kvg:
            lk = lv = 0
            for g in range(8):
                cc, half = min(g, 7 - g), (0 if g < 4 else 1)
                lk = p.dma("sync", kbuf[:, g * 512:(g + 1) * 512],
                           kf_d.ap()[cc * D + h * 128:cc * D + (h + 1) * 128,
                                     half * 512:(half + 1) * 512],
                           sem=f"ldk{h % 2}",
                           waits=((("pe", unit_pe_end.get(h - 2, 0)),)
                                  if g == 0 else ()))
                lv = p.dma("scalar",
                           vbuf.ap().rearrange("p (a o) -> p a o", o=128)[:, 4 * g:4 * g + 4, :],
                           vf_d.ap().rearrange("(x p) d -> p x d", p=128)
                           [:, cc * 8 + half * 4:cc * 8 + half * 4 + 4,
                            h * 128:(h + 1) * 128],
                           sem=f"ldv{h % 2}",
                           waits=((("pe", unit_pe_end.get(h - 2, 0)), ("ldq", LDQ))
                                  if g == 0 else ()))
        else:
            lk = p.dma("sync", kbuf[:], kf_d.ap()[h * 128:(h + 1) * 128, :],
                       sem=f"ldk{h % 2}", waits=(("pe", unit_pe_end.get(h - 2, 0)),))
            lv = p.dma("scalar", vbuf.ap().rearrange("p (a o) -> p a o", o=128),
                       vf_d.ap().rearrange("(a p) d -> p a d", p=128)[:, :, h * 128:(h + 1) * 128],
                       sem=f"ldv{h % 2}", waits=(("pe", unit_pe_end.get(h - 2, 0)), ("ldq", LDQ)))
        qrow = QS[:, h * NTOK:(h + 1) * NTOK]
        for qc in range(2):
            qsl = qrow[:, qc * 512:(qc + 1) * 512]
            mmy = {}
            exs = {}
            das = {}
            yevs = {}
            for kt in range(32):
                first = (kt == 0)
                pss = psS[kt % 2]
                pt = PT[kt % 2]
                cpe = p.op("tensor", "matmul", pss[:],
                           kbuf[:, kt * 128:(kt + 1) * 128], qsl,
                           start=True, stop=True, sem="pe",
                           waits=(((f"ldk{h % 2}", lk), ("ldq", LDQ),
                                   ("ac", p.cnt["ac"]), ("dv", p.cnt["dv"]))
                                  if first else
                                  (("ac", exs.get(kt - 2, 0)),)))
                cex = p.op("scalar", "activation", pt[:], pss[:], AF.Exp,
                           bias=negc[:, 0:1], scale=1.0,
                           waits=(("pe", cpe), ("dv", das.get(kt - 2, 0)),
                                  ("pe", mmy.get(kt - 2, 0)), ("dv", NEG_DV)))
                exs[kt] = cex
                if first:
                    cda = p.op("vector", "tensor_copy", dacc[:], pt[:],
                               waits=(("ac", cex),))
                else:
                    cda = p.op("vector", "tensor_add", dacc[:], dacc[:], pt[:],
                               waits=(("ac", cex),))
                das[kt] = cda
                mmy[kt] = p.op("tensor", "matmul", psY[qc][:],
                               vbuf[:, kt * 128:(kt + 1) * 128],
                               pt[:], start=first, stop=(kt == 31), sem="pe",
                               waits=(((f"ldv{h % 2}", lv), ("ac", cex),
                                       ("dv", yevs.get("prev", 0)))
                                      if first else (("ac", cex),)))
            cdr = p.op("tensor", "matmul", psR[0:1, 0:512], onescol[:], dacc[:],
                       start=True, stop=True, sem="pe",
                       waits=(("dv", p.cnt["dv"]), ("ldc", LDC)))
            p.op("vector", "reciprocal", rows[0:1, 0:512], psR[0:1, 0:512],
                 waits=(("pe", cdr),))
            cpe2 = p.op("tensor", "matmul", psBC[:, 0:512], ones1[:],
                        rows[0:1, 0:512], start=True, stop=True, sem="pe",
                        waits=(("ac", p.cnt["ac"]), ("dv", p.cnt["dv"]), ("ldc", LDC)))
            cbc = p.op("scalar", "copy", bcR[:], psBC[:, 0:512], waits=(("pe", cpe2),))
            ocv = bass.AP(OC, h + qc * 512 * 16,
                          [[NTOK * 16, 128], [16, 512]])
            yevs["prev"] = p.op("vector", "tensor_mul", ocv, psY[qc][:], bcR[:],
                                waits=(("ac", cbc), ("pe", mmy[31]),
                                       ("sto", p.cnt["sto"])))
        unit_pe_end[h] = p.cnt["pe"]
    OCD = p.cnt["dv"]
    p.dma("gpsimd", oc_o.ap(), OC[:], sem="sto", waits=(("dv", OCD),))
    p.wait("gpsimd", ("sto", p.cnt["sto"]))
    return p


def host_l1_attn_b_inputs(qs, kfull, vfull):
    return {
        "qs": qs, "kf": kfull, "vf": vfull,
        "onesc": np.ones((1, 128), dtype=np.float32),
        "onescol": np.ones((128, 1), dtype=np.float32),
    }


def build_l1_attn_c(p=None, io=None):
    if p is None:
        p = Prog("l1attnc")
    io = io or {}
    STO0 = p.cnt["sto"]
    xin_d = p.dram_io(io, "xin", [D, NTOK], F32, kind="ExternalInput")
    y_d = p.dram_io(io, "yrows", [NTOK, D], BF16, kind="ExternalInput")
    wo_d = p.dram_io(io, "wo", [D, D], F32, kind="ExternalInput")
    bo_d = p.dram_io(io, "bo", [128, 16], F32, kind="ExternalInput")
    xout_d = p.dram_io(io, "xout", [D, NTOK], F32, kind="ExternalOutput")

    Y = p.sbuf("Y", [128, 8 * D], BF16)
    YT = p.sbuf("YT", [128, NF * NTOK], BF16)
    wb = [p.sbuf(f"wb{i}", [128, NF * 128], BF16) for i in range(2)]
    xt = [p.sbuf(f"xt{i}", [128, NTOK], F32) for i in range(2)]
    scrB = p.sbuf("scrB", [128, NTOK], F32)
    scrC = p.sbuf("scrC", [128, NTOK], F32)
    bos = p.sbuf("bos", [128, 16], F32)
    ones128 = p.sbuf("ones128", [128, 128], BF16)
    idn = p.sbuf("idnf", [128, 128], BF16)

    psQ = [p.psum(f"psQ{i}", [128, 512], F32) for i in range(4)]
    psTR = p.psum("psTR", [128, 128], BF16)

    p.dma("sync", bos[:], bo_d.ap(), sem="ldc")
    LDC = p.cnt["ldc"]
    cmo = p.op("vector", "memset", ones128[:], 1.0)
    p.op("gpsimd", "affine_select", idn[:], ones128[:], pattern=[[-1, 128]],
         compare_op=ALU.is_equal, fill=0.0, channel_multiplier=1,
         waits=(("dv", cmo),))
    PL_ID = p.cnt["pl"]
    for j in range(8):
        p.dma("sync", Y[:, j * D:(j + 1) * D], y_d.ap()[j * 128:(j + 1) * 128, :],
              sem="ldy")
    LDY = p.cnt["ldy"]
    for j in range(8):
        for i in range(NF):
            cpe = p.op("tensor", "transpose", psTR[:],
                       Y[:, j * D + i * 128:j * D + (i + 1) * 128], idn[:],
                       sem="pe", waits=(("ldy", LDY), ("pl", PL_ID),
                                        ("ac", p.cnt["ac"])))
            p.op("scalar", "copy", YT[:, i * NTOK + j * 128:i * NTOK + (j + 1) * 128],
                 psTR[:], waits=(("pe", cpe),))
    YT_AC = p.cnt["ac"]

    pe_end = {}
    ev = {}
    for c in range(NF):
        lw = p.dma("gpsimd", wb[c % 2].ap().rearrange("p (a o) -> p a o", o=128),
                   wo_d.ap().rearrange("(a p) o -> p a o", p=128)[:, :, c * 128:(c + 1) * 128],
                   sem=f"ldw{c % 2}", waits=(("pe", pe_end.get(c - 2, 0)),))
        for h in range(2):
            ps = psQ[2 * (c % 2) + h]
            mm = 0
            for i in range(NF):
                mm = p.op("tensor", "matmul", ps[:],
                          wb[c % 2][:, i * 128:(i + 1) * 128],
                          YT[:, i * NTOK + h * 512:i * NTOK + h * 512 + 512],
                          start=(i == 0), stop=(i == NF - 1), sem="pe",
                          waits=(((f"ldw{c % 2}", lw), ("ac", YT_AC),
                                  ("dv", ev.get((c - 2, h), 0))) if i == 0 else ()))
            n = 2 * c + h
            ldx2 = p.dma("sync", xt[c % 2][:, h * 512:(h + 1) * 512],
                         xin_d.ap()[c * 128:(c + 1) * 128, h * 512:(h + 1) * 512],
                         sem="ldx2", waits=(("sto", STO0 + 16 * n),))
            c1 = p.op("vector", "tensor_scalar", scrB[:, 0:512], ps[:],
                      bos[:, c:c + 1], None, ALU.add,
                      waits=(("pe", mm), ("ldc", LDC)))
            evd = scrC[:, (n % 2) * 512:(n % 2) * 512 + 512]
            c2 = p.op("vector", "tensor_add", evd, scrB[:, 0:512],
                      xt[c % 2][:, h * 512:(h + 1) * 512],
                      waits=(("ldx2", ldx2), ("sto", STO0 + 16 * n)))
            ev[(c, h)] = c2
            p.dma("gpsimd", xout_d.ap()[c * 128:(c + 1) * 128, h * 512:(h + 1) * 512],
                  evd, sem="sto", waits=(("dv", c2),))
        pe_end[c] = p.cnt["pe"]
    p.wait("gpsimd", ("sto", p.cnt["sto"]))
    return p


def host_l1_attn_c_inputs(x_fm_c, yrows, ins):
    return {
        "xin": x_fm_c, "yrows": yrows,
        "wo": np.ascontiguousarray(np.float32(ins["L1_attn_Wo"])),
        "bo": np.ascontiguousarray(np.float32(ins["L1_attn_bo"]).reshape(16, 128).T),
    }


def build_ydist(p, io):
    ocg_d = io["ocg"]
    y_d = io["yrows"]
    ytbl_d = p.dram_io(io, "ytbl", [1, 8], I32, kind="ExternalInput")
    ocx_d = p.dram("ocx", [128, NTOK * 16], BF16, kind="Internal")
    ytbl = p.sbuf("ytbl_t", [1, 8], I32)
    p.dma("scalar", ytbl[:], ytbl_d.ap(), sem="yld")
    YLD = p.cnt["yld"]

    vals = {}

    def sc_hoist(engine):
        engine.wait_ge(p.semh("yld"), YLD)
        for i in range(8):
            r = engine.alloc_register()
            engine.reg_load(r, ytbl[0:1, i:i + 1])
            vals[i] = engine.snap(r, True, min_val=0, max_val=4 * 128 - 16)
    p.raw("scalar", sc_hoist)
    h1 = 0
    for i in range(8):
        h1 = p.dma("scalar", ocx_d.ap()[i * 16:(i + 1) * 16, :],
                   (lambda _i=i: ocg_d.ap()[bass.ds(vals[_i], 16), :]),
                   sem="yx1")
    for half in range(2):
        for gqblk in range(32):
            g, qb = gqblk // 4, gqblk % 4
            cc, half2 = min(g, 7 - g), (0 if g < 4 else 1)
            lo = (half2 * 512 + qb * 128) * 16
            src = bass.AP(ocx_d, (half * 64 + cc * 16) * (NTOK * 16) + lo,
                          [[NTOK * 16, 16], [1, 2048]])
            dst = bass.AP(y_d, (half * 512 + gqblk) * D, [[32 * D, 16], [1, 2048]])
            p.dma("gpsimd", dst, src, sem="yx2",
                  waits=((("yx1", h1),) if (half, gqblk) == (0, 0) else ()))
    return p.cnt["yx2"]


def host_ydist_table(c):
    cp = c % 4
    t = np.zeros((1, 8), dtype=np.int32)
    for half in range(2):
        dblk = cp if half == 0 else (7 - cp)
        for cc in range(4):
            t[0, half * 4 + cc] = cc * 128 + dblk * 16
    return t


def host_assemble_Y(oc_shards):
    Ys = []
    for b in range(B):
        Y = np.empty((T, D), dtype=BF)
        for cc in range(4):
            c = b * 4 + cc
            oc = np.asarray(oc_shards[c])
            s1, s2 = core_token_slices(c)
            for half, sl in ((0, s1), (1, s2)):
                for qb in range(4):
                    gqblk = sl.start // 128 + qb
                    lo = (half * 512 + qb * 128) * 16
                    blk = oc[:, lo:lo + 128 * 16]
                    Y[gqblk::32, :][0:0] if False else None
                    Y[np.arange(128) * 32 + gqblk, :] = blk
        Ys.append(Y)
    return Ys


G4 = [[0, 1, 2, 3], [4, 5, 6, 7]]
G8 = [list(range(8))]


def build_fused():
    p = Prog("FUSED")
    p.begin_stage("pp")
    tg_s = p.dram("tg_s", [D, 1024], BF16)
    tu_s = p.dram("tu_s", [D, 1024], BF16)
    td_s = p.dram("td_s", [1024, D], BF16)
    fs_s = p.dram("fs_s", [D, 256], BF16)
    mv_d = p.dram("mv", [1, 4], F32)
    build_prep(p, {"tg_T": tg_s, "tu_T": tu_s, "td_T": td_s,
                   "fsum": fs_s, "mvals": mv_d})
    tg_g = p.dram("tg_g", [8 * D, 1024], BF16)
    tu_g = p.dram("tu_g", [8 * D, 1024], BF16)
    td_g = p.dram("td_g", [8 * 1024, D], BF16)
    fs_g = p.dram("fs_g", [8 * D, 256], BF16)
    p.allgather(tg_s, tg_g, G8, "ccw", waits=(("sto", p.cnt["sto"]),))
    p.allgather(tu_s, tu_g, G8, "ccw")
    p.allgather(td_s, td_g, G8, "ccw")
    p.allgather(fs_s, fs_g, G8, "ccw")
    wio = {"tgT": tg_g, "tuT": tu_g, "tdT": td_g, "mv": mv_d, "gather_w": True}
    p.begin_stage("f1")
    xh1 = p.dram("xh1", [D, NTOK], F32)
    build_l0_ffn(p, dict(wio, xout=xh1))
    p.begin_stage("a1")
    qs1 = p.dram("qs1", [D, NTOK], BF16)
    ko1 = p.dram("ko1", [D, NTOK], BF16)
    vt1 = p.dram("vt1", [NTOK, D], BF16)
    go1 = p.dram("go1", [D, NTOK], BF16)
    build_l0_attn_a(p, {"xin": xh1, "qs": qs1, "ko": ko1, "vt": vt1, "go": go1})
    kg1 = p.dram("kg1", [4 * D, NTOK], BF16)
    vg1 = p.dram("vg1", [4 * NTOK, D], BF16)
    p.allgather(ko1, kg1, G4, "cck", waits=(("sto", p.cnt["sto"]),))
    p.allgather(vt1, vg1, G4, "cck")
    p.begin_stage("ab")
    xh2 = p.dram("xh2", [D, NTOK], F32)
    build_l0_attn_bu(p, {"xin": xh1, "qs": qs1, "kf": kg1, "vf": vg1,
                         "go": go1, "xout": xh2, "kv_gathered": True})
    p.begin_stage("f2")
    xh3 = p.dram("xh3", [D, NTOK], F32)
    build_l0_ffn(p, dict(wio, xin=xh2, xout=xh3))
    p.begin_stage("f3")
    xh4 = p.dram("xh4", [D, NTOK], F32)
    build_l1_ffn(p, {"xin": xh3, "xout": xh4, "fsum": fs_g, "gather_fs": True})
    p.begin_stage("a2")
    qs2 = p.dram("qs2", [D, NTOK], BF16)
    ko2 = p.dram("ko2", [D, NTOK], BF16)
    vt2 = p.dram("vt2", [NTOK, D], BF16)
    build_l1_attn_a(p, {"xin": xh4, "qs": qs2, "ko": ko2, "vt": vt2})
    kg2 = p.dram("kg2", [4 * D, NTOK], BF16)
    vg2 = p.dram("vg2", [4 * NTOK, D], BF16)
    p.allgather(ko2, kg2, G4, "cck", waits=(("sto", p.cnt["sto"]),))
    p.allgather(vt2, vg2, G4, "cck")
    p.begin_stage("bb")
    oc_d = p.dram("oc", [128, NTOK * 16], BF16)
    build_l1_attn_b(p, {"qs": qs2, "kf": kg2, "vf": vg2, "oc": oc_d,
                        "kv_gathered": True})
    ocg = p.dram("ocg", [4 * 128, NTOK * 16], BF16)
    p.allgather(oc_d, ocg, G4, "cco", waits=(("sto", p.cnt["sto"]),))
    p.begin_stage("yy")
    y_d = p.dram("yrows", [NTOK, D], BF16)
    build_ydist(p, {"ocg": ocg, "yrows": y_d})
    p.begin_stage("ac")
    xh5 = p.dram("xh5", [D, NTOK], F32)
    build_l1_attn_c(p, {"xin": xh4, "yrows": y_d, "xout": xh5})
    p.begin_stage("f4")
    build_l1_ffn(p, {"xin": xh5, "fsum": fs_g, "gather_fs": True})
    return p


def _asf(a, dt=None):
    a = np.asarray(a)
    if a.dtype == np.uint16:
        a = a.view(BF)
    if dt is not None:
        a = np.asarray(a, dt)
    return a


def _pref(tag, m, skip=()):
    return {tag + k: v for k, v in m.items() if k not in skip}


def _run(build_fn, in_maps, core_ids=None):
    p = build_fn() if callable(build_fn) else build_fn
    nc = p.finalize()
    if core_ids is None:
        core_ids = list(range(len(in_maps)))
    res = run_bass_kernel_spmd(nc, in_maps, core_ids=core_ids)
    return res.results


def kernel(**ins):
    x0 = np.asarray(ins["L0_attn_hidden_states"], np.float32)
    lnw_dev = np.ascontiguousarray(
        np.float32(ins["L0_ffn_ln_weight"]).reshape(64, 128).T)

    p = build_fused()
    zq = np.zeros((1, 1), dtype=np.float32)
    im = []
    for c in range(NCORE):
        xfc = np.ascontiguousarray(fm(shard_tokens(x0, c)))
        base_ffn = {"lnw": lnw_dev,
                    "onesc": np.ones((1, 128), dtype=np.float32),
                    "onescolb": np.ones((128, 1), dtype=BF)}
        m = {}
        m.update(_pref("pp_", host_prep_inputs(ins, c)))
        m.update(_pref("f1_", dict(base_ffn, xin=xfc)))
        m.update(_pref("a1_", host_attn_a_inputs(xfc, ins, c), skip=("xin",)))
        ab = host_attn_b_inputs(zq, zq, zq, zq, zq, ins)
        m.update(_pref("ab_", ab, skip=("xin", "qs", "kf", "vf", "go")))
        m["ab_tbl"] = host_attn_b_table(c, gathered=True)
        m.update(_pref("f2_", base_ffn))
        l1f = host_l1_ffn_inputs(zq, zq, ins)
        m.update(_pref("f3_", l1f, skip=("xin", "fsum")))
        m.update(_pref("a2_", host_l1_attn_a_inputs(zq, ins), skip=("xin",)))
        m.update(_pref("bb_", host_l1_attn_b_inputs(zq, zq, zq),
                       skip=("qs", "kf", "vf")))
        m["yy_ytbl"] = host_ydist_table(c)
        m.update(_pref("ac_", host_l1_attn_c_inputs(zq, zq, ins),
                       skip=("xin", "yrows")))
        m.update(_pref("f4_", l1f, skip=("xin", "fsum")))
        im.append(m)
    res = _run(p, im)
    xs = [np.ascontiguousarray(_asf(res[c]["f4_xout"], np.float32).T)
          for c in range(NCORE)]
    out = unshard_tokens(xs)
    return np.ascontiguousarray(out.astype(np.float32))


def kernel_5launch(**ins):
    x0 = np.asarray(ins["L0_attn_hidden_states"], np.float32)

    prep_res = _run(build_prep, [host_prep_inputs(ins, c) for c in range(NCORE)])
    tgT = np.ascontiguousarray(np.concatenate(
        [_asf(prep_res[c]["tg_T"]) for c in range(NCORE)], axis=1).astype(BF))
    tuT = np.ascontiguousarray(np.concatenate(
        [_asf(prep_res[c]["tu_T"]) for c in range(NCORE)], axis=1).astype(BF))
    tdT = np.ascontiguousarray(np.concatenate(
        [_asf(prep_res[c]["td_T"]) for c in range(NCORE)], axis=0).astype(BF))
    fsum = np.ascontiguousarray(np.concatenate(
        [_asf(prep_res[c]["fsum"]) for c in range(NCORE)], axis=1).astype(BF))
    mvals = [np.ascontiguousarray(_asf(prep_res[c]["mvals"], np.float32))
             for c in range(NCORE)]
    lnw_dev = np.ascontiguousarray(
        np.float32(ins["L0_ffn_ln_weight"]).reshape(64, 128).T)

    xf = [np.ascontiguousarray(fm(shard_tokens(x0, c))) for c in range(NCORE)]

    def assemble_kv(res, kname, vname):
        kfull, vfull = [], []
        for b in range(B):
            kf = np.empty((D, T), dtype=BF)
            vf = np.empty((T, D), dtype=BF)
            for cc in range(4):
                c = b * 4 + cc
                s1, s2 = core_token_slices(c)
                ko = _asf(res[c][kname]).astype(BF)
                vt = _asf(res[c][vname]).astype(BF)
                kf[:, s1] = ko[:, :CH]; kf[:, s2] = ko[:, CH:]
                vf[s1, :] = vt[:CH]; vf[s2, :] = vt[CH:]
            kfull.append(np.ascontiguousarray(kf))
            vfull.append(np.ascontiguousarray(vf))
        return kfull, vfull

    p = Prog("L2")
    p.begin_stage("f1")
    xh1 = p.dram("xh", [D, NTOK], F32, kind="ExternalOutput")
    build_l0_ffn(p, {"xout": xh1})
    p.begin_stage("a1")
    build_l0_attn_a(p, {"xin": xh1})
    im = []
    for c in range(NCORE):
        m = _pref("f1_", host_ffn_inputs(xf[c], tgT, tuT, tdT, lnw_dev, mvals[c]))
        m.update(_pref("a1_", host_attn_a_inputs(xf[c], ins, c), skip=("xin",)))
        im.append(m)
    res_a = _run(p, im)
    for c in range(NCORE):
        xf[c] = np.ascontiguousarray(_asf(res_a[c]["f1_xh"], np.float32))

    kfull, vfull = assemble_kv(res_a, "a1_ko", "a1_vt")

    p = Prog("L3")
    p.begin_stage("ab")
    xh2 = p.dram("xh2", [D, NTOK], F32)
    build_l0_attn_bu(p, {"xout": xh2})
    p.begin_stage("f2")
    xh3 = p.dram("xh3", [D, NTOK], F32)
    build_l0_ffn(p, {"xin": xh2, "xout": xh3})
    p.begin_stage("f3")
    xh4 = p.dram("xh4", [D, NTOK], F32, kind="ExternalOutput")
    build_l1_ffn(p, {"xin": xh3, "xout": xh4})
    p.begin_stage("a2")
    build_l1_attn_a(p, {"xin": xh4})
    im = []
    for c in range(NCORE):
        b = c // 4
        m = host_attn_b_inputs(
            xf[c], np.ascontiguousarray(_asf(res_a[c]["a1_qs"]).astype(BF)),
            kfull[b], vfull[b],
            np.ascontiguousarray(_asf(res_a[c]["a1_go"]).astype(BF)), ins)
        m["tbl"] = host_attn_b_table(c)
        m = _pref("ab_", m)
        m.update(_pref("f2_", host_ffn_inputs(xf[c], tgT, tuT, tdT, lnw_dev,
                                              mvals[c]), skip=("xin",)))
        m.update(_pref("f3_", host_l1_ffn_inputs(xf[c], fsum, ins),
                       skip=("xin",)))
        m.update(_pref("a2_", host_l1_attn_a_inputs(xf[c], ins), skip=("xin",)))
        im.append(m)
    res_a = _run(p, im)
    for c in range(NCORE):
        xf[c] = np.ascontiguousarray(_asf(res_a[c]["f3_xh4"], np.float32))

    kfull, vfull = assemble_kv(res_a, "a2_ko", "a2_vt")
    im = []
    for c in range(NCORE):
        b = c // 4
        im.append(host_l1_attn_b_inputs(
            np.ascontiguousarray(_asf(res_a[c]["a2_qs"]).astype(BF)),
            kfull[b], vfull[b]))
    res_b = _run(build_l1_attn_b, im)
    Ys = host_assemble_Y([_asf(res_b[c]["oc"]).astype(BF) for c in range(NCORE)])
    Ybtd = np.stack(Ys).reshape(B, T, D)

    p = Prog("L5")
    p.begin_stage("ac")
    xh5 = p.dram("xh5", [D, NTOK], F32)
    build_l1_attn_c(p, {"xout": xh5})
    p.begin_stage("f4")
    build_l1_ffn(p, {"xin": xh5})
    im = []
    for c in range(NCORE):
        m = _pref("ac_", host_l1_attn_c_inputs(
            xf[c], np.ascontiguousarray(shard_tokens(Ybtd, c)), ins))
        m.update(_pref("f4_", host_l1_ffn_inputs(xf[c], fsum, ins),
                       skip=("xin",)))
        im.append(m)
    res = _run(p, im)
    for c in range(NCORE):
        xf[c] = np.ascontiguousarray(_asf(res[c]["f4_xout"], np.float32))

    out = unshard_tokens([np.ascontiguousarray(xf[c].T) for c in range(NCORE)])
    return np.ascontiguousarray(out.astype(np.float32))


# version 08
# speedup vs baseline: 1.0549x; 1.0549x over previous
import math
from collections import defaultdict
from contextlib import ExitStack

import numpy as np
import ml_dtypes

import concourse.bass as bass
import concourse.mybir as mybir
from concourse.bass_utils import run_bass_kernel_spmd

F32 = mybir.dt.float32
BF16 = mybir.dt.bfloat16
I32 = mybir.dt.int32
AF = mybir.ActivationFunctionType
ALU = mybir.AluOpType
AXX = mybir.AxisListType.X

BF = ml_dtypes.bfloat16

D = 2048
T = 4096
B = 2
NH = 16
HD = 128
HP = 8
DFF = 8192
NCORE = 8
NTOK = 1024
CH = 512
LAMBDA_INIT = 0.8 - 0.6 * float(np.exp(-0.3 * 6))
SC_HI = np.float32(np.float64(HD) ** -0.5)
SC_LO = np.float32(np.float64(HD) ** -0.5 - np.float64(SC_HI))
PI_LIM = np.float32(3.1415925)

FIXUP_ROUND = False


def core_chunks(c):
    p = c % 4
    return (p, 7 - p)


def core_token_slices(c):
    a, b_ = core_chunks(c)
    return (slice(a * CH, (a + 1) * CH), slice(b_ * CH, (b_ + 1) * CH))


class Prog:
    ENGINES = ("sync", "scalar", "vector", "tensor", "gpsimd")

    def __init__(self, name="prog"):
        self.nc = bass.Bass("TRN2", target_bir_lowering=False)
        self._stack = ExitStack()
        self._stack.__enter__()
        self.ops = {e: [] for e in self.ENGINES}
        self._sems = {}
        self._semnames = set()
        self.cnt = defaultdict(int)
        self.name = name
        self.tag = ""
        self._stage_stack = None

    def begin_stage(self, tag):
        if self._stage_stack is not None:
            snap = [(s, v) for s, v in self.cnt.items() if v > 0]
            for eng in self.ENGINES:
                self.wait(eng, *snap)
            self._stage_stack.close()
        self._stage_stack = ExitStack()
        self.tag = tag + "_"

    def sbuf(self, name, shape, dt=F32):
        st = self._stage_stack if self._stage_stack is not None else self._stack
        return st.enter_context(self.nc.sbuf_tensor(self.tag + name, shape, dt))

    def psum(self, name, shape, dt=F32):
        st = self._stage_stack if self._stage_stack is not None else self._stack
        return st.enter_context(self.nc.psum_tensor(self.tag + name, shape, dt))

    def dram(self, name, shape, dt=F32, kind="Internal"):
        return self.nc.dram_tensor(self.tag + name, shape, dt, kind=kind)

    def dram_io(self, io, name, shape, dt=F32, kind="Internal"):
        if io is not None and name in io:
            return io[name]
        return self.dram(name, shape, dt, kind=kind)

    def semh(self, name):
        if name not in self._sems:
            self._sems[name] = self._stack.enter_context(self.nc.semaphore(name))
        return self._sems[name]

    def raw(self, eng, fn):
        self.ops[eng].append((self.tag, fn))

    AUTO = {"vector": "dv", "scalar": "ac", "gpsimd": "pl"}

    def op(self, eng, method, *args, sem=None, inc=1, waits=(), **kw):
        waits = tuple(waits)
        for (ws, _) in waits:
            self._semnames.add(ws)
        auto = self.AUTO.get(eng)
        if auto is not None and method != "dma_start":
            if sem is None:
                sem = auto
            if sem == auto and self.cnt[auto] > 0:
                waits = waits + ((auto, self.cnt[auto]),)
        if sem is not None:
            self._semnames.add(sem)
            self.cnt[sem] += inc
            target = self.cnt[sem]
        else:
            target = None
        semname = sem
        myinc = inc

        _opidx = len(self.ops[eng])

        def fn(engine, _m=method, _a=args, _k=kw, _w=waits, _s=semname, _i=myinc,
               _oi=_opidx):
            for (ws, wv) in _w:
                if wv > 0:
                    engine.wait_ge(self.semh(ws), wv)
            aa = [a() if callable(a) else a for a in _a]
            kk = {k2: (v() if callable(v) else v) for k2, v in _k.items()}
            try:
                instr = getattr(engine, _m)(*aa, **kk)
            except Exception as e:
                import traceback as _tb
                det = [str(getattr(a, "shape", a))[:60] for a in aa]
                raise RuntimeError(f"op#{_oi} {_m} on {engine} failed: {e} args={det}\n"
                                   f"INNER:\n{_tb.format_exc()}") from e
            if _s is not None:
                instr.then_inc(self.semh(_s), _i)
            return instr

        self.ops[eng].append((self.tag, fn))
        return target

    def wait(self, eng, *waits):
        for (ws, _) in waits:
            self._semnames.add(ws)
        def fn(engine, _w=tuple(waits)):
            for (ws, wv) in _w:
                if wv > 0:
                    engine.wait_ge(self.semh(ws), wv)
        self.ops[eng].append((self.tag, fn))

    def dma(self, queue, out, in_, sem=None, waits=(), **kw):
        return self.op(queue, "dma_start", sem=sem, inc=16, waits=waits,
                       out=out, in_=in_, **kw)

    def allgather(self, src_d, dst_d, groups, sem, waits=()):
        self._semnames.add(sem)
        for (ws, _) in waits:
            self._semnames.add(ws)
        self.cnt[sem] += 1
        target = self.cnt[sem]

        def fn(engine, _w=tuple(waits), _t=target):
            for (ws, wv) in _w:
                if wv > 0:
                    engine.wait_ge(self.semh(ws), wv)
            engine.collective_compute(
                "AllGather", ALU.bypass, replica_groups=groups,
                ins=[src_d.ap().opt()], outs=[dst_d.ap().opt()],
            ).then_inc(self.semh(sem))
        self.ops["gpsimd"].append((self.tag, fn))
        return target

    def finalize(self):
        for n in sorted(self._semnames):
            self.semh(n)
        with self.nc.Block() as block:
            for eng in self.ENGINES:
                lst = self.ops[eng]
                if not lst:
                    continue
                deco = getattr(block, eng)

                def make(lst):
                    def body(engine):
                        for tg, fn in lst:
                            with self.nc.named_scope(tg or "main"):
                                fn(engine)
                    return body

                deco(make(lst))
        return self.nc


NF = D // 128
DBG_STRIP = ()
DBG_UNITS = 16
HALF = NTOK // 512


def bcast_row(p, row_ap, out_psum, ones1, sem_pe, waits):
    return p.op("tensor", "matmul", out_psum, ones1, row_ap,
                sem=sem_pe, waits=waits, start=True, stop=True)


def emit_round(p, src_fp32, dst_bf16, int_t, scr, eng="vector", waits=(), sem=None,
               fixup=None):
    if fixup is None:
        fixup = FIXUP_ROUND
    if sem is None:
        sem = Prog.AUTO.get(eng)
    c = p.op(eng, "tensor_copy", int_t, src_fp32, sem=sem, waits=waits)
    if not fixup:
        return p.op(eng, "tensor_copy", dst_bf16, int_t, sem=sem,
                    waits=((sem, c),))
    c = p.op(eng, "tensor_copy", scr, int_t, sem=sem, waits=((sem, c),))
    n = src_fp32.shape[-1]
    fu = p._fix_u[:, 0:n]
    fv = p._fix_v[:, 0:n]
    cu = p.op(eng, "scalar_tensor_tensor", fu, scr, 0.5, src_fp32,
              ALU.add, ALU.is_le, sem=sem, waits=((sem, c),))
    cv = p.op(eng, "scalar_tensor_tensor", fv, scr, -0.5, src_fp32,
              ALU.add, ALU.is_ge, sem=sem, waits=((sem, cu),))
    cw = p.op(eng, "tensor_add", scr, scr, fu, sem=sem, waits=((sem, cv),))
    return p.op(eng, "tensor_sub", dst_bf16, scr, fv, sem=sem,
                waits=((sem, cw),))


def alloc_fix(p, shape):
    p._fix_u = p.sbuf("fix_u", shape, F32)
    p._fix_v = p.sbuf("fix_v", shape, F32)


def shard_tokens(x_btd, c):
    b = c // 4
    s1, s2 = core_token_slices(c)
    return np.concatenate([x_btd[b, s1], x_btd[b, s2]], axis=0)


def unshard_tokens(shards):
    out = np.empty((B, T, shards[0].shape[1]), dtype=shards[0].dtype)
    for c in range(NCORE):
        b = c // 4
        s1, s2 = core_token_slices(c)
        out[b, s1] = shards[c][:CH]
        out[b, s2] = shards[c][CH:]
    return out


def fm(x_nd):
    return np.ascontiguousarray(x_nd.T)


def rope_host_tables(c):
    s1, s2 = core_token_slices(c)
    tvals = np.concatenate([np.arange(s1.start, s1.stop), np.arange(s2.start, s2.stop)])
    ln1e4 = np.log(np.float64(10000.0))
    hi = np.float32(ln1e4); lo = np.float32(ln1e4 - np.float64(hi))
    i = np.arange(64, dtype=np.float64)
    invf = np.exp(-(i / 64) * np.float64(hi)) * np.exp(-(i / 64) * np.float64(lo))
    ang = np.outer(invf, np.float64(tvals))
    twopi = 2 * np.pi
    k_s = np.rint(ang / twopi).astype(np.float32)
    k_c = np.rint((ang + np.pi / 2) / twopi).astype(np.float32)
    return (tvals.astype(np.float32)[None, :], k_s, k_c)


def build_prep(p=None, io=None):
    if p is None:
        p = Prog("prep")
    io = io or {}
    nc = p.nc
    ones_d = p.dram_io(io, "onesc", [1, 128], F32, kind="ExternalInput")
    onescol_d = p.dram_io(io, "onescol", [128, 1], F32, kind="ExternalInput")
    slice_d = p.dram_io(io, "slc", [1, 1], F32, kind="ExternalInput")

    tg_o = p.dram_io(io, "tg_T", [D, 1024], BF16, kind="ExternalOutput")
    tu_o = p.dram_io(io, "tu_T", [D, 1024], BF16, kind="ExternalOutput")
    td_o = p.dram_io(io, "td_T", [1024, D], BF16, kind="ExternalOutput")
    fs_o = p.dram_io(io, "fsum", [D, 256], BF16, kind="ExternalOutput")
    m_o = p.dram_io(io, "mvals", [1, 4], F32, kind="ExternalOutput")

    wgs_d = p.dram_io(io, "wgs", [1024, D], F32, kind="ExternalInput")
    wus_d = p.dram_io(io, "wus", [1024, D], F32, kind="ExternalInput")
    wds_d = p.dram_io(io, "wds", [D, 1024], F32, kind="ExternalInput")
    facs_d = p.dram_io(io, "facs", [8, D, 256], F32, kind="ExternalInput")

    ones1 = p.sbuf("ones1", [1, 128], F32)
    onescol = p.sbuf("onescol_t", [128, 1], F32)
    idn = p.sbuf("idn", [128, 128], BF16)
    asum = p.sbuf("asum", [128, 3], F32)
    arow = p.sbuf("arow", [1, 3], F32)
    mrow = p.sbuf("mrow", [1, 4], F32)
    minv = p.sbuf("minv", [128, 3], F32)
    wt = [p.sbuf(f"wt{i}", [128, D], F32) for i in range(2)]
    tt_ = [p.sbuf(f"tt{i}", [128, D], BF16) for i in range(2)]
    tvt = p.sbuf("tvt", [128, 2048], BF16)
    red = p.psum("red", [1, 512], F32)
    trp = p.psum("trp", [128, 128], BF16)
    redsc = p.sbuf("redsc", [128, 4], F32)
    fsl = [p.sbuf(f"fsl{i}", [128, 256], F32) for i in range(2)]
    facc = p.sbuf("facc", [128, 256], F32)
    fbf = p.sbuf("fbf", [128, 256], BF16)

    def absmean_slice(w_dram, nrows, ncols, col):
        ntile = nrows // 128
        for i in range(ntile):
            buf = wt[i % 2]
            ld = p.dma("sync", buf[:, 0:ncols], w_dram.ap()[i * 128:(i + 1) * 128, :],
                       sem="ldw", waits=(("dv", p.cnt["dv"]),))
            c1 = p.op("vector", "reduce_sum", redsc[:, 0:1], buf[:, 0:ncols], AXX,
                      apply_absolute_value=True, sem="dv", waits=(("ldw", ld),))
            p.op("vector", "tensor_add", asum[:, col:col + 1], asum[:, col:col + 1],
                 redsc[:, 0:1], sem="dv", waits=(("dv", c1),))

    c0 = p.op("vector", "memset", asum[:], 0.0, sem="dv")
    p.dma("sync", ones1[:], ones_d.ap(), sem="ldc")
    p.dma("sync", onescol[:], onescol_d.ap(), sem="ldc")
    absmean_slice(wgs_d, 1024, D, 0)
    absmean_slice(wus_d, 1024, D, 1)
    absmean_slice(wds_d, D, 1024, 2)
    part_s = p.dram("part_s", [1, 4], F32)
    part_g = p.dram("part_g", [8, 4], F32)
    pgz = p.sbuf("pgz", [128, 4], F32)
    cpe = p.op("tensor", "matmul", red[0:1, 0:3], onescol[:], asum[:, 0:3],
               start=True, stop=True, sem="pe",
               waits=(("dv", p.cnt["dv"]), ("ldc", 32)))
    cpr = p.op("vector", "tensor_copy", mrow[0:1, 0:3], red[0:1, 0:3], sem="dv",
               waits=(("pe", cpe),))
    cpr = p.op("vector", "memset", mrow[0:1, 3:4], 0.0, sem="dv", waits=(("dv", cpr),))
    p.dma("gpsimd", part_s.ap(), mrow[:], sem="stp", waits=(("dv", cpr),))
    p.allgather(part_s, part_g, G8, "ccp", waits=(("stp", p.cnt["stp"]),))
    czp = p.op("vector", "memset", pgz[:], 0.0, sem="dv")
    ldp = p.dma("sync", pgz[0:8, :], part_g.ap(), sem="ldpg",
                waits=(("ccp", p.cnt["ccp"]), ("dv", czp)))
    cpe = p.op("tensor", "matmul", red[0:1, 0:4], onescol[:], pgz[:],
               start=True, stop=True, sem="pe",
               waits=(("ldpg", ldp), ("dv", p.cnt["dv"])))
    c1 = p.op("vector", "tensor_scalar", arow[:], red[0:1, 0:3], 1.0 / (1 << 24), 1e-5,
              ALU.mult, ALU.max, sem="dv", waits=(("pe", cpe),))
    c1 = p.op("vector", "tensor_copy", mrow[0:1, 0:3], arow[:], sem="dv", waits=(("dv", c1),))
    c1 = p.op("vector", "memset", mrow[0:1, 3:4], 0.0, sem="dv", waits=(("dv", c1),))
    crec = p.op("vector", "reciprocal", arow[:], arow[:], sem="dv", waits=(("dv", c1),))
    bc3 = p.psum("bc3", [128, 4], F32)
    cb = p.op("tensor", "matmul", bc3[0:128, 0:3], ones1[:],
              arow[:], start=True, stop=True, sem="pe", waits=(("dv", crec),))
    cmi = p.op("vector", "tensor_copy", minv[:], bc3[0:128, 0:3], sem="dv",
               waits=(("pe", cb),))
    p.dma("gpsimd", m_o.ap(), mrow[:], sem="sto", waits=(("dv", p.cnt["dv"]),))

    ci = p.op("vector", "memset", idn[:], 1.0, sem="dv")
    cid = p.op("gpsimd", "affine_select", idn[:], idn[:], pattern=[[-1, 128]],
               compare_op=ALU.is_equal, fill=0.0, channel_multiplier=1,
               sem="pl", waits=(("dv", ci),))

    def ternarize_slice(w_slice_d, out_d, nrows, col, transpose_out):
        ntile = nrows // 128
        ncols = w_slice_d.shape[1] if hasattr(w_slice_d, "shape") else None
        for i in range(ntile):
            sto_before = p.cnt["sto"]
            buf = wt[i % 2]
            tb = tt_[i % 2]
            ld = p.dma("scalar", buf[:], w_slice_d.ap()[i * 128:(i + 1) * 128, :],
                       sem="lds", waits=(("dv", p.cnt["dv"]),))
            c1 = p.op("vector", "tensor_scalar", p._t1[:], buf[:], minv[:, col:col + 1],
                      0.5, ALU.mult, ALU.is_ge, sem="dv",
                      waits=(("lds", ld), ("pe2", p.cnt["pe2"])))
            c2 = p.op("vector", "tensor_scalar", p._t2[:], buf[:], minv[:, col:col + 1],
                      -0.5, ALU.mult, ALU.is_le, sem="dv", waits=(("dv", c1),))
            c3 = p.op("vector", "tensor_sub", tb[:], p._t1[:], p._t2[:], sem="dv",
                      waits=(("dv", c2),))
            nb = tb.shape[1] // 128
            for j in range(nb):
                cpe2 = p.op("tensor", "transpose", trp[:], tb[:, j * 128:(j + 1) * 128],
                            idn[:], sem="pe2",
                            waits=(("dv", c3), ("pl", cid), ("ac", p.cnt["ac"])))
                p.op("scalar", "copy", tvt[:, j * 128:(j + 1) * 128], trp[:],
                     sem="ac", waits=(("pe2", cpe2), ("sto", sto_before)))
            for j in range(nb):
                p.dma("gpsimd", out_d.ap()[j * 128:(j + 1) * 128, i * 128:(i + 1) * 128],
                      tvt[:, j * 128:(j + 1) * 128], sem="sto",
                      waits=(("ac", p.cnt["ac"]),))

    p._t1 = p.sbuf("t1s", [128, D], F32)
    p._t2 = p.sbuf("t2s", [128, D], F32)
    ternarize_slice(wgs_d, tg_o, 1024, 0, True)
    ternarize_slice(wus_d, tu_o, 1024, 1, True)

    ntile = D // 128
    for i in range(ntile):
        sto_before = p.cnt["sto"]
        buf = wt[i % 2]
        tb = tt_[i % 2]
        ld = p.dma("scalar", buf[:, 0:1024], wds_d.ap()[i * 128:(i + 1) * 128, :],
                   sem="lds", waits=(("dv", p.cnt["dv"]),))
        c1 = p.op("vector", "tensor_scalar", p._t1[:, 0:1024], buf[:, 0:1024],
                  minv[:, 2:3], 0.5, ALU.mult, ALU.is_ge, sem="dv",
                  waits=(("lds", ld), ("pe2", p.cnt["pe2"])))
        c2 = p.op("vector", "tensor_scalar", p._t2[:, 0:1024], buf[:, 0:1024],
                  minv[:, 2:3], -0.5, ALU.mult, ALU.is_le, sem="dv", waits=(("dv", c1),))
        c3 = p.op("vector", "tensor_sub", tb[:, 0:1024], p._t1[:, 0:1024],
                  p._t2[:, 0:1024], sem="dv", waits=(("dv", c2),))
        for j in range(8):
            cpe2 = p.op("tensor", "transpose", trp[:], tb[:, j * 128:(j + 1) * 128],
                        idn[:], sem="pe2",
                        waits=(("dv", c3), ("pl", cid), ("ac", p.cnt["ac"])))
            p.op("scalar", "copy", tvt[:, j * 128:(j + 1) * 128], trp[:],
                 sem="ac", waits=(("pe2", cpe2), ("sto", sto_before)))
        for j in range(8):
            p.dma("gpsimd", td_o.ap()[j * 128:(j + 1) * 128, i * 128:(i + 1) * 128],
                  tvt[:, j * 128:(j + 1) * 128], sem="sto",
                  waits=(("ac", p.cnt["ac"]),))

    prev_cb = 0
    for i in range(D // 128):
        cz = p.op("vector", "memset", facc[:], 0.0, sem="dv",
                  waits=((("sto", p.cnt["sto"]),) if i == 0 else (("dv", prev_cb),)))
        for r in range(8):
            buf = fsl[r % 2]
            ld = p.dma("sync", buf[:], facs_d.ap()[r, i * 128:(i + 1) * 128, :],
                       sem="ldf", waits=(("dv", p.cnt["dv"]),))
            cz = p.op("vector", "tensor_add", facc[:], facc[:], buf[:], sem="dv",
                      waits=(("ldf", ld), ("dv", cz)))
        cb = p.op("vector", "tensor_copy", fbf[:], facc[:], sem="dv",
                  waits=(("dv", cz), ("sto", p.cnt["sto"])))
        prev_cb = cb
        p.dma("gpsimd", fs_o.ap()[i * 128:(i + 1) * 128, :], fbf[:], sem="sto",
              waits=(("dv", cb),))

    p.wait("gpsimd", ("sto", p.cnt["sto"]))
    return p


def host_prep_inputs(ins, c):
    wg = ins["L0_ffn_w_gate"]; wu = ins["L0_ffn_w_up"]; wd = ins["L0_ffn_w_down"]
    fac = ins["L1_ffn_factors"]
    sl = slice(c * 1024, (c + 1) * 1024)
    se = slice(c * 256, (c + 1) * 256)
    return {
        "onesc": np.ones((1, 128), dtype=np.float32),
        "onescol": np.ones((128, 1), dtype=np.float32),
        "slc": np.zeros((1, 1), dtype=np.float32),
        "wgs": np.ascontiguousarray(wg[sl]), "wus": np.ascontiguousarray(wu[sl]),
        "wds": np.ascontiguousarray(wd[:, sl]),
        "facs": np.ascontiguousarray(fac[:, :, se]),
    }


def build_l0_ffn(p=None, io=None):
    if p is None:
        p = Prog("l0ffn")
    io = io or {}
    xin_d = p.dram_io(io, "xin", [D, NTOK], F32, kind="ExternalInput")
    tg_d = p.dram_io(io, "tgT", [D, DFF], BF16, kind="ExternalInput")
    tu_d = p.dram_io(io, "tuT", [D, DFF], BF16, kind="ExternalInput")
    td_d = p.dram_io(io, "tdT", [DFF, D], BF16, kind="ExternalInput")
    lnw_d = p.dram_io(io, "lnw", [128, 64], F32, kind="ExternalInput")
    mv_d = p.dram_io(io, "mv", [1, 4], F32, kind="ExternalInput")
    ones_d = p.dram_io(io, "onesc", [1, 128], F32, kind="ExternalInput")
    onescolb_d = p.dram_io(io, "onescolb", [128, 1], BF16, kind="ExternalInput")
    xout_d = p.dram_io(io, "xout", [D, NTOK], F32, kind="ExternalOutput")
    u2_d = p.dram_io(io, "u2scr", [DFF, NTOK], BF16, kind="Internal")
    STO0 = p.cnt["sto"]
    STU0 = p.cnt["stu"]

    R1 = p.sbuf("R1", [128, NF * NTOK], BF16)
    wg = [p.sbuf(f"wgb{i}", [128, NF * 128], BF16) for i in range(2)]
    wu = [p.sbuf(f"wub{i}", [128, NF * 128], BF16) for i in range(2)]
    macc = p.sbuf("macc", [128, NTOK], BF16)
    mincc = p.sbuf("mincc", [128, NTOK], BF16)
    xb = p.sbuf("xb", [128, NTOK], BF16)
    sq = p.sbuf("sq", [128, NTOK], BF16)
    u2st = [p.sbuf(f"u2st{i}", [128, NTOK], BF16) for i in range(2)]
    pad0 = p.sbuf("pad0", [128, NTOK], F32)
    intt = p.sbuf("intt", [128, NTOK], I32)
    r1_addr = p.nc.lookup_mloc(R1).addr
    R2 = p.nc.alloc_sbuf_tensor_at(p.tag + "R2", [128, 64 * 512], BF16, offset=r1_addr)

    xt = [p.sbuf(f"xt{i}", [128, NTOK], F32) for i in range(2)]
    scrA = p.sbuf("scrA", [128, NTOK], F32)
    scrB = p.sbuf("scrB", [128, NTOK], F32)
    scrC = p.sbuf("scrC", [128, NTOK], F32)
    scrD = p.sbuf("scrD", [128, NTOK], F32)
    wd = [p.sbuf(f"wdb{i}", [128, 64 * 128], BF16) for i in range(2)]
    u2ld = [p.sbuf(f"u2ld{i}", [128, 512], BF16) for i in range(2)]
    ones1 = p.sbuf("ones1", [1, 128], F32)
    onescolb = p.sbuf("onescolbt", [128, 1], BF16)
    lnw = p.sbuf("lnwt", [128, 64], F32)
    mv = p.sbuf("mvt", [1, 4], F32)
    rows = p.sbuf("rows", [1, 9 * NTOK], F32)
    sc = p.sbuf("sc", [1, 8], F32)
    tmcol = p.sbuf("tmcol", [128, 8], F32)
    bcA = p.sbuf("bcA", [128, NTOK], F32)
    bcB = p.sbuf("bcB", [128, NTOK], F32)
    bcCG = p.sbuf("bcCG", [128, NTOK], F32)
    bcS2 = p.sbuf("bcS2", [128, NTOK], F32)
    bcFS = p.sbuf("bcFS", [128, NTOK], F32)
    idn = p.sbuf("idnf", [128, 128], F32)
    alloc_fix(p, [128, NTOK])

    psG = p.psum("psG", [128, 512], F32)
    psU = p.psum("psU", [128, 512], F32)
    psG2 = p.psum("psG2", [128, 512], F32)
    psU2 = p.psum("psU2", [128, 512], F32)
    msH = [p.psum(f"msH{h}", [1, 512], F32) for h in range(2)]
    psBC = p.psum("psBC", [128, 512], F32)
    psD = p.psum("psD", [128, 512], F32)

    def row(i):
        return rows[0:1, i * NTOK:(i + 1) * NTOK]

    def rowh(i, h):
        return rows[0:1, i * NTOK + h * 512:i * NTOK + (h + 1) * 512]

    p.dma("sync", ones1[:], ones_d.ap(), sem="ldc")
    p.dma("sync", onescolb[:], onescolb_d.ap(), sem="ldc")
    p.dma("sync", lnw[:], lnw_d.ap(), sem="ldc")
    p.dma("sync", mv[:], mv_d.ap(), sem="ldc")
    LDC = p.cnt["ldc"]
    cmi = p.op("vector", "memset", idn[:], 1.0, waits=(("ldc", LDC),))
    p.op("gpsimd", "affine_select", idn[:], idn[:], pattern=[[-1, 128]],
         compare_op=ALU.is_equal, fill=0.0, channel_multiplier=1,
         waits=(("dv", cmi),))
    PL_ID = p.cnt["pl"]

    def bc_to(dst, row_ap, n=NTOK):
        last = 0
        for h in range(n // 512):
            cpe = p.op("tensor", "matmul", psBC[:, 0:512], ones1[:],
                       row_ap[0:1, h * 512:(h + 1) * 512], start=True, stop=True,
                       sem="pe", waits=(("ac", p.cnt["ac"]), ("dv", p.cnt["dv"]),
                                        ("ldc", LDC)))
            last = p.op("scalar", "copy", dst[:, h * 512:(h + 1) * 512],
                        psBC[:, 0:512], waits=(("pe", cpe),))
        return last

    def part_max_row(row_out):
        lastac = 0
        for j in range(NTOK // 128):
            cpe = p.op("tensor", "transpose", psD[:, 0:128],
                       scrA[:, j * 128:(j + 1) * 128], idn[:], sem="pe",
                       waits=(("dv", p.cnt["dv"]), ("pl", PL_ID),
                              ("dv2", p.cnt["dv2"]), ("ac", p.cnt["ac"])))
            p.op("vector", "reduce_max", tmcol[:, j:j + 1], psD[:, 0:128], AXX,
                 sem="dv2", waits=(("pe", cpe), ("dv", p.cnt["dv"])))
        for j in range(NTOK // 128):
            cpe = p.op("tensor", "transpose", psD[0:1, 128:256], tmcol[:, j:j + 1],
                       idn[:], sem="pe",
                       waits=(("dv2", p.cnt["dv2"]), ("ac", p.cnt["ac"])))
            lastac = p.op("scalar", "copy", row_out[0:1, j * 128:(j + 1) * 128],
                          psD[0:1, 128:256], waits=(("pe", cpe),))
        return lastac

    for i in range(NF):
        buf = xt[i % 2]
        ld = p.dma("sync", buf[:], xin_d.ap()[i * 128:(i + 1) * 128, :], sem="ldx",
                   waits=(("ac", p.cnt["ac"]), ("dv", p.cnt["dv"])))
        cb = p.op("vector", "tensor_copy", xb[:], buf[:],
                  waits=(("ldx", ld), ("pe", p.cnt["pe"])))
        cs = p.op("scalar", "square", sq[:], buf[:],
                  waits=(("ldx", ld), ("pe", p.cnt["pe"])))
        for h in range(2):
            ps_s = psG if h == 0 else psG2
            ps_q = psU if h == 0 else psU2
            p.op("tensor", "matmul", ps_s[0:1, 0:512], onescolb[:],
                 xb[:, h * 512:(h + 1) * 512], start=(i == 0), stop=(i == NF - 1),
                 sem="pe", waits=(("dv", cb), ("ldc", LDC)))
            p.op("tensor", "matmul", ps_q[0:1, 0:512], onescolb[:],
                 sq[:, h * 512:(h + 1) * 512], start=(i == 0), stop=(i == NF - 1),
                 sem="pe", waits=(("ac", cs),))
    PE_STATS = p.cnt["pe"]
    for h in range(2):
        ps_s = psG if h == 0 else psG2
        ps_q = psU if h == 0 else psU2
        p.op("vector", "tensor_scalar", rowh(0, h), ps_s[0:1, 0:512],
             1.0 / D, None, ALU.mult, waits=(("pe", PE_STATS),))
        p.op("vector", "tensor_scalar", rowh(1, h), ps_q[0:1, 0:512],
             1.0 / D, None, ALU.mult, waits=(("pe", PE_STATS),))
    p.op("vector", "tensor_mul", row(2), row(0), row(0))
    p.op("vector", "tensor_sub", row(1), row(1), row(2))
    p.op("vector", "tensor_scalar", row(1), row(1), 1e-5, None, ALU.add)
    cac = p.op("scalar", "sqrt", row(1), row(1), waits=(("dv", p.cnt["dv"]),))
    p.op("vector", "reciprocal", row(2), row(1), waits=(("ac", cac),))
    p.op("vector", "tensor_mul", row(3), row(0), row(2))
    bc_to(bcA, row(2))
    bc_to(bcB, row(3))

    BC_DONE = p.cnt["ac"]
    for i in range(NF):
        buf = xt[i % 2]
        ld = p.dma("sync", buf[:], xin_d.ap()[i * 128:(i + 1) * 128, :], sem="ldx",
                   waits=(("dv", p.cnt["dv"]),))
        c1 = p.op("vector", "tensor_mul", scrA[:], buf[:], bcA[:],
                  waits=(("ldx", ld), ("ac", BC_DONE), ("pl", p.cnt["pl"])))
        c2 = p.op("vector", "tensor_sub", scrA[:], scrA[:], bcB[:])
        if i == 0:
            p.op("vector", "tensor_copy", scrB[:], scrA[:])
            p.op("vector", "tensor_copy", scrC[:], scrA[:])
        else:
            p.op("vector", "tensor_max", scrB[:], scrB[:], scrA[:])
            p.op("vector", "tensor_tensor", scrC[:], scrC[:], scrA[:], ALU.min)
    p.op("vector", "tensor_scalar", scrD[:], scrC[:], -1.0, None, ALU.mult,
         waits=(("pl", p.cnt["pl"]),))
    p.op("vector", "tensor_max", scrA[:], scrB[:], scrD[:])
    part_max_row(row(4))
    p.op("vector", "tensor_scalar", row(4), row(4), 1e-5, None, ALU.max,
         waits=(("ac", p.cnt["ac"]),))
    p.op("vector", "reciprocal", row(5), row(4))
    p.op("vector", "tensor_scalar", row(5), row(5), 127.0, None, ALU.mult)
    p.op("vector", "tensor_mul", row(6), row(2), row(5))
    p.op("vector", "tensor_mul", row(7), row(3), row(5))
    bc_to(bcA, row(6))
    bc_to(bcB, row(7))
    p.op("vector", "tensor_scalar", sc[0:1, 0:1], mv[0:1, 0:1], 1.0 / 127, None,
         ALU.mult, waits=(("ldc", LDC),))
    p.op("vector", "tensor_scalar", sc[0:1, 1:2], mv[0:1, 1:2], 1.0 / 127, None,
         ALU.mult)
    p.op("vector", "tensor_scalar", sc[0:1, 2:3], mv[0:1, 2:3], 0.5 / 127, None,
         ALU.mult)
    p.op("vector", "tensor_scalar", row(7), row(4), sc[0:1, 0:1], None, ALU.mult,
         waits=(("pe", p.cnt["pe"]),))
    p.op("vector", "tensor_scalar", row(8), row(4), sc[0:1, 1:2], None, ALU.mult)
    bc_to(bcCG, row(7))

    BC2 = p.cnt["ac"]
    for i in range(NF):
        buf = xt[i % 2]
        ld = p.dma("sync", buf[:], xin_d.ap()[i * 128:(i + 1) * 128, :], sem="ldx",
                   waits=(("dv", p.cnt["dv"]),))
        p.op("vector", "tensor_mul", scrA[:], buf[:], bcA[:],
             waits=(("ldx", ld), ("ac", BC2), ("pl", p.cnt["pl"])))
        p.op("vector", "tensor_sub", scrA[:], scrA[:], bcB[:])
        for h in range(2):
            sl = slice(h * 512, (h + 1) * 512)
            emit_round(p, scrA[:, sl], R1[:, i * NTOK + h * 512:i * NTOK + (h + 1) * 512],
                       intt[:, 0:512], scrD[:, 0:512])
    R1_DONE = p.cnt["dv"]

    gathered_w = io.get("gather_w", False)

    def load_w(dst, src_d, j, pewait):
        if gathered_w:
            co = j // 8
            src = (src_d.ap().rearrange("(A p) o -> p A o", p=128)
                   [:, co * 16:(co + 1) * 16, (j % 8) * 128:(j % 8 + 1) * 128])
        else:
            src = (src_d.ap().rearrange("(a p) o -> p a o", p=128)
                   [:, :, j * 128:(j + 1) * 128])
        return p.dma("scalar", dst.ap().rearrange("p (a o) -> p a o", o=128),
                     src, sem=f"ldw{j % 2}", waits=(("pe", pewait),))

    lam_ev = {}
    pe_end = {}
    for j in range(64):
        lw1 = load_w(wg[j % 2], tg_d, j, pe_end.get(j - 2, 0))
        lw2 = load_w(wu[j % 2], tu_d, j, pe_end.get(j - 2, 0))
        lwsem = f"ldw{j % 2}"
        for h in range(2):
            g = j * 2 + h
            ps_g = psG if g % 2 == 0 else psG2
            ps_u = psU if g % 2 == 0 else psU2
            mm = 0
            for i in range(NF):
                mm = p.op("tensor", "matmul", ps_g[:],
                          wg[j % 2][:, i * 128:(i + 1) * 128],
                          R1[:, i * NTOK + h * 512:i * NTOK + h * 512 + 512],
                          start=(i == 0), stop=(i == NF - 1), sem="pe",
                          waits=(((lwsem, lw2), ("dv", R1_DONE),
                                  ("dv", lam_ev.get(g - 2, 0))) if i == 0 else ()))
            for i in range(NF):
                mm = p.op("tensor", "matmul", ps_u[:],
                          wu[j % 2][:, i * 128:(i + 1) * 128],
                          R1[:, i * NTOK + h * 512:i * NTOK + h * 512 + 512],
                          start=(i == 0), stop=(i == NF - 1), sem="pe")
            sl = slice(h * 512, (h + 1) * 512)
            c1 = p.op("vector", "tensor_mul", scrA[:, sl], ps_g[:], bcCG[:, sl],
                      waits=(("pe", mm),))
            c2 = p.op("scalar", "activation", scrB[:, sl], scrA[:, sl], AF.Silu,
                      waits=(("dv", c1),))
            c3 = p.op("vector", "tensor_mul", scrC[:, sl], scrB[:, sl], ps_u[:],
                      waits=(("ac", c2),))
            lam_ev[g] = c3
            c4 = p.op("scalar", "square", sq[:, sl], scrC[:, sl],
                      waits=(("dv", c3),))
            p.op("tensor", "matmul", msH[h][0:1, 0:512], onescolb[:], sq[:, sl],
                 start=(j == 0), stop=(j == 63), sem="pe", waits=(("ac", c4),))
            p.op("vector", "tensor_scalar", u2st[j % 2][:, sl], scrC[:, sl],
                 lnw[:, j:j + 1], None, ALU.mult,
                 waits=(("stu", STU0 + 16 * j),))
            if j == 0:
                p.op("vector", "tensor_copy", macc[:, sl], u2st[j % 2][:, sl])
                p.op("vector", "tensor_copy", mincc[:, sl], u2st[j % 2][:, sl])
            else:
                p.op("vector", "tensor_max", macc[:, sl], macc[:, sl],
                     u2st[j % 2][:, sl])
                p.op("vector", "tensor_tensor", mincc[:, sl], mincc[:, sl],
                     u2st[j % 2][:, sl], ALU.min)
        p.dma("gpsimd", u2_d.ap()[j * 128:(j + 1) * 128, :], u2st[j % 2][:],
              sem="stu", waits=(("dv", p.cnt["dv"]), ("pl", p.cnt["pl"])))
        pe_end[j] = p.cnt["pe"]
    GU_PE = p.cnt["pe"]

    p.op("vector", "tensor_copy", scrA[:], macc[:],
         waits=(("pe", GU_PE), ("ac", p.cnt["ac"]), ("stu", STU0 + 16 * 64)))
    p.op("vector", "tensor_scalar", scrB[:], mincc[:], -1.0, None, ALU.mult,
         waits=(("pl", p.cnt["pl"]),))
    p.op("vector", "tensor_max", scrA[:], scrA[:], scrB[:])
    part_max_row(row(4))
    for h in range(2):
        p.op("vector", "tensor_scalar", rowh(0, h), msH[h][0:1, 0:512],
             1.0 / DFF, None, ALU.mult, waits=(("pe", GU_PE), ("ac", p.cnt["ac"])))
    p.op("vector", "tensor_mul", row(1), row(8), row(8))
    p.op("vector", "tensor_mul", row(1), row(1), row(0))
    p.op("vector", "tensor_scalar", row(1), row(1), 1e-6, None, ALU.add)
    cac = p.op("scalar", "sqrt", row(1), row(1), waits=(("dv", p.cnt["dv"]),))
    p.op("vector", "reciprocal", row(2), row(1), waits=(("ac", cac),))
    p.op("vector", "tensor_mul", row(2), row(8), row(2))
    p.op("vector", "reciprocal", row(6), row(2))
    p.op("vector", "tensor_scalar", row(6), row(6), 1e-5, None, ALU.mult)
    p.op("vector", "tensor_max", row(3), row(4), row(6))
    p.op("vector", "reciprocal", row(5), row(3))
    p.op("vector", "tensor_scalar", row(5), row(5), 127.0, None, ALU.mult)
    p.op("vector", "tensor_mul", row(6), row(2), row(3))
    p.op("vector", "tensor_scalar", row(6), row(6), sc[0:1, 2:3], None, ALU.mult)
    bc_to(bcS2, row(5))
    bc_to(bcFS, row(6))
    BC3 = p.cnt["ac"]

    pe_end2 = {}
    for h in range(2):
        sl = slice(h * 512, (h + 1) * 512)
        r2last = 0
        for j in range(64):
            lb = u2ld[j % 2]
            ld = p.dma("sync", lb[:], u2_d.ap()[j * 128:(j + 1) * 128, sl],
                       sem="ldu", waits=(("dv", p.cnt["dv"]), ("stu", STU0 + 16 * 64)))
            p.op("vector", "tensor_mul", scrA[:, 0:512], lb[:], bcS2[:, sl],
                 waits=(("ldu", ld), ("ac", BC3), ("pe", p.cnt["pe"]),
                        ("pl", p.cnt["pl"])))
            r2last = emit_round(p, scrA[:, 0:512], R2[:, j * 512:(j + 1) * 512],
                                intt[:, 0:512], scrD[:, 0:512])
        prev_ev = 0
        for d in range(NF):
            lw = p.dma("scalar", wd[d % 2].ap().rearrange("p (a o) -> p a o", o=128),
                       td_d.ap().rearrange("(a p) d -> p a d", p=128)[:, :, d * 128:(d + 1) * 128],
                       sem=f"ldd{d % 2}", waits=(("pe", pe_end2.get((h, d - 2), pe_end2.get((h - 1, NF - 2 + d), 0))),))
            mm = 0
            for j in range(64):
                mm = p.op("tensor", "matmul", psD[:],
                          wd[d % 2][:, j * 128:(j + 1) * 128],
                          R2[:, j * 512:(j + 1) * 512],
                          start=(j == 0), stop=(j == 63), sem="pe",
                          waits=(((f"ldd{d % 2}", lw), ("dv", r2last),
                                  ("dv", prev_ev), ("ac", p.cnt["ac"])) if j == 0 else ()))
            pe_end2[(h, d)] = p.cnt["pe"]
            n = h * NF + d
            ldx2 = p.dma("sync", xt[d % 2][:, 0:512],
                         xin_d.ap()[d * 128:(d + 1) * 128, sl], sem="ldx2",
                         waits=(("sto", STO0 + 16 * n), ("dv", R1_DONE)))
            ev = scrC[:, (n % 2) * 512:(n % 2) * 512 + 512]
            c1 = p.op("vector", "tensor_mul", scrB[:, 0:512], psD[:], bcFS[:, sl],
                      waits=(("pe", mm),))
            prev_ev = c1
            c2 = p.op("vector", "tensor_add", ev, scrB[:, 0:512],
                      xt[d % 2][:, 0:512],
                      waits=(("ldx2", ldx2), ("sto", STO0 + 16 * n)))
            p.dma("gpsimd", xout_d.ap()[d * 128:(d + 1) * 128, sl],
                  ev, sem="sto", waits=(("dv", c2),))
    p.wait("gpsimd", ("sto", p.cnt["sto"]))
    return p


def host_ffn_inputs(x_fm_c, tgT, tuT, tdT, lnw_dev, mval):
    return {
        "xin": x_fm_c, "tgT": tgT, "tuT": tuT, "tdT": tdT,
        "lnw": lnw_dev, "mv": mval,
        "onesc": np.ones((1, 128), dtype=np.float32),
        "onescolb": np.ones((128, 1), dtype=BF),
    }


def build_l0_attn_a(p=None, io=None):
    if p is None:
        p = Prog("l0attna")
    io = io or {}
    xin_d = p.dram_io(io, "xin", [D, NTOK], F32, kind="ExternalInput")
    wqkv_d = p.dram_io(io, "wqkv", [D, 3 * D], F32, kind="ExternalInput")
    wgm_d = p.dram_io(io, "wgm", [D, D], F32, kind="ExternalInput")
    trow_d = p.dram_io(io, "trow", [1, NTOK], F32, kind="ExternalInput")
    ks_d = p.dram_io(io, "ks", [64, NTOK], F32, kind="ExternalInput")
    kc_d = p.dram_io(io, "kc", [64, NTOK], F32, kind="ExternalInput")
    ones_d = p.dram_io(io, "onesc", [1, 128], F32, kind="ExternalInput")
    onescolb_d = p.dram_io(io, "onescolb", [128, 1], BF16, kind="ExternalInput")
    qs_o = p.dram_io(io, "qs", [D, NTOK], BF16, kind="ExternalOutput")
    k_o = p.dram_io(io, "ko", [D, NTOK], BF16, kind="ExternalOutput")
    vt_o = p.dram_io(io, "vt", [NTOK, D], BF16, kind="ExternalOutput")
    g_o = p.dram_io(io, "go", [D, NTOK], BF16, kind="ExternalOutput")

    xt = [p.sbuf(f"xt{i}", [128, NTOK], F32) for i in range(2)]
    xb = p.sbuf("xb", [128, NTOK], BF16)
    sq = p.sbuf("sq", [128, NTOK], BF16)
    scrA = p.sbuf("scrA", [128, NTOK], F32)
    scrB = p.sbuf("scrB", [128, NTOK], F32)
    scrC = p.sbuf("scrC", [128, NTOK], F32)
    scrD = p.sbuf("scrD", [128, NTOK], F32)
    XLN = p.sbuf("XLN", [128, NF * NTOK], BF16)
    wb = [p.sbuf(f"wb{i}", [128, NF * 128], BF16) for i in range(2)]
    qbh = p.sbuf("qbh", [128, NTOK], BF16)
    obh = p.sbuf("obh", [128, NTOK], BF16)
    vstage = p.sbuf("vstage", [128, NTOK], BF16)
    vtr = p.sbuf("vtr", [128, NTOK], BF16)
    ones1 = p.sbuf("ones1", [1, 128], F32)
    onescolb = p.sbuf("onescolbt", [128, 1], BF16)
    trow = p.sbuf("trowt", [1, NTOK], F32)
    kst = p.sbuf("kst", [64, NTOK], F32)
    kct = p.sbuf("kct", [64, NTOK], F32)
    invf = p.sbuf("invf", [1, 64], F32)
    invf2 = p.sbuf("invf2", [1, 64], F32)
    SS2 = p.sbuf("SS2", [128, NTOK], F32)
    CC2 = p.sbuf("CC2", [128, NTOK], F32)
    rows = p.sbuf("rows", [1, 4 * NTOK], F32)
    bcA = p.sbuf("bcA", [128, NTOK], F32)
    bcB = p.sbuf("bcB", [128, NTOK], F32)
    bcR = p.sbuf("bcR", [128, NTOK], F32)
    idn = p.sbuf("idnf", [128, 128], BF16)
    ones128 = p.sbuf("ones128", [128, 128], BF16)

    psQ = [p.psum(f"psQ{i}", [128, 512], F32) for i in range(4)]
    psM = [p.psum(f"psM{h}", [1, 512], F32) for h in range(2)]
    psBC = p.psum("psBC", [128, 512], F32)
    psTR = p.psum("psTR", [128, 128], BF16)

    def row(i, h=None):
        if h is None:
            return rows[0:1, i * NTOK:(i + 1) * NTOK]
        return rows[0:1, i * NTOK + h * 512:i * NTOK + (h + 1) * 512]

    p.dma("sync", ones1[:], ones_d.ap(), sem="ldc")
    p.dma("sync", onescolb[:], onescolb_d.ap(), sem="ldc")
    p.dma("sync", trow[:], trow_d.ap(), sem="ldc")
    p.dma("sync", kst[:], ks_d.ap(), sem="ldc")
    p.dma("sync", kct[:], kc_d.ap(), sem="ldc")
    LDC = p.cnt["ldc"]
    cmo = p.op("vector", "memset", ones128[:], 1.0, waits=(("ldc", LDC),))
    p.op("gpsimd", "affine_select", idn[:], ones128[:], pattern=[[-1, 128]],
         compare_op=ALU.is_equal, fill=0.0, channel_multiplier=1,
         waits=(("dv", cmo),))
    PL_ID = p.cnt["pl"]

    def bc_to(dst, row_ap, n=NTOK):
        last = 0
        for h in range(n // 512):
            cpe = p.op("tensor", "matmul", psBC[:, 0:512], ones1[:],
                       row_ap[0:1, h * 512:(h + 1) * 512], start=True, stop=True,
                       sem="pe", waits=(("ac", p.cnt["ac"]), ("dv", p.cnt["dv"]),
                                        ("ldc", LDC)))
            last = p.op("scalar", "copy", dst[:, h * 512:(h + 1) * 512],
                        psBC[:, 0:512], waits=(("pe", cpe),))
        return last

    _ln4 = np.log(np.float64(10000.0))
    HI_LN = float(np.float32(_ln4))
    LO_LN = float(np.float32(_ln4 - np.float64(np.float32(_ln4))))
    TWO_PI_HI = float(np.float32(2 * np.pi))
    TWO_PI_LO = float(np.float32(2 * np.pi - np.float64(np.float32(2 * np.pi))))
    PIH = float(np.float32(np.pi / 2))
    ci = p.op("gpsimd", "iota", invf[:], pattern=[[1, 64]], base=0,
              channel_multiplier=0, allow_small_or_imprecise_dtypes=True)
    c1 = p.op("scalar", "activation", invf2[:], invf[:], AF.Exp, bias=0.0,
              scale=-HI_LN / 64, waits=(("pl", ci),))
    c2 = p.op("scalar", "activation", invf[:], invf[:], AF.Exp, bias=0.0,
              scale=-LO_LN / 64, waits=(("ac", c1),))
    c3 = p.op("vector", "tensor_mul", invf[:], invf[:], invf2[:],
              waits=(("ac", c2),))
    for h in range(2):
        p.op("tensor", "matmul", psQ[h][0:64, 0:512],
             invf[0:1, :], trow[0:1, h * 512:(h + 1) * 512],
             start=True, stop=True, sem="pe",
             waits=(("dv", c3), ("ldc", LDC)))
    PE_ANG = p.cnt["pe"]
    for h in range(2):
        sl = slice(h * 512, (h + 1) * 512)
        p.op("vector", "scalar_tensor_tensor", scrA[0:64, sl], kst[:, sl], -TWO_PI_HI,
             psQ[h][0:64, 0:512], ALU.mult, ALU.add, waits=(("pe", PE_ANG),))
        p.op("vector", "scalar_tensor_tensor", scrA[0:64, sl], kst[:, sl], -TWO_PI_LO,
             scrA[0:64, sl], ALU.mult, ALU.add)
    p.op("vector", "tensor_scalar", scrA[0:64, :], scrA[0:64, :], float(PI_LIM),
         -float(PI_LIM), ALU.min, ALU.max)
    cs4 = p.op("scalar", "activation", SS2[0:64, :], scrA[0:64, :], AF.Sin,
               waits=(("dv", p.cnt["dv"]),))
    for h in range(2):
        sl = slice(h * 512, (h + 1) * 512)
        p.op("vector", "tensor_scalar", scrB[0:64, sl], psQ[h][0:64, 0:512], PIH, None,
             ALU.add, waits=(("pe", PE_ANG),))
    p.op("vector", "scalar_tensor_tensor", scrB[0:64, :], kct[:], -TWO_PI_HI,
         scrB[0:64, :], ALU.mult, ALU.add)
    p.op("vector", "scalar_tensor_tensor", scrB[0:64, :], kct[:], -TWO_PI_LO,
         scrB[0:64, :], ALU.mult, ALU.add)
    p.op("vector", "tensor_scalar", scrB[0:64, :], scrB[0:64, :], float(PI_LIM),
         -float(PI_LIM), ALU.min, ALU.max)
    cc5 = p.op("scalar", "activation", CC2[0:64, :], scrB[0:64, :], AF.Sin,
               waits=(("dv", p.cnt["dv"]),))
    p.op("vector", "tensor_scalar", SS2[64:128, :], SS2[0:64, :], -1.0, None,
         ALU.mult, waits=(("ac", cs4),))
    cdup = p.op("scalar", "copy", CC2[64:128, :], CC2[0:64, :], waits=(("ac", cc5),))
    ROPE_DV = p.cnt["dv"]
    ROPE_AC = p.cnt["ac"]

    for i in range(NF):
        buf = xt[i % 2]
        ld = p.dma("sync", buf[:], xin_d.ap()[i * 128:(i + 1) * 128, :], sem="ldx",
                   waits=(("ac", p.cnt["ac"]), ("dv", p.cnt["dv"])))
        cb = p.op("vector", "tensor_copy", xb[:], buf[:],
                  waits=(("ldx", ld), ("pe", p.cnt["pe"])))
        cs = p.op("scalar", "square", sq[:], buf[:],
                  waits=(("ldx", ld), ("pe", p.cnt["pe"])))
        for h in range(2):
            ps_s = psQ[0] if h == 0 else psQ[1]
            ps_q = psQ[2] if h == 0 else psQ[3]
            p.op("tensor", "matmul", ps_s[0:1, 0:512], onescolb[:],
                 xb[:, h * 512:(h + 1) * 512], start=(i == 0), stop=(i == NF - 1),
                 sem="pe", waits=(("dv", cb), ("ldc", LDC)))
            p.op("tensor", "matmul", ps_q[0:1, 0:512], onescolb[:],
                 sq[:, h * 512:(h + 1) * 512], start=(i == 0), stop=(i == NF - 1),
                 sem="pe", waits=(("ac", cs),))
    PE_STATS = p.cnt["pe"]
    for h in range(2):
        p.op("vector", "tensor_scalar", row(0, h), (psQ[0] if h == 0 else psQ[1])[0:1, 0:512],
             1.0 / D, None, ALU.mult, waits=(("pe", PE_STATS),))
        p.op("vector", "tensor_scalar", row(1, h), (psQ[2] if h == 0 else psQ[3])[0:1, 0:512],
             1.0 / D, None, ALU.mult, waits=(("pe", PE_STATS),))
    p.op("vector", "tensor_mul", row(2), row(0), row(0))
    p.op("vector", "tensor_sub", row(1), row(1), row(2))
    p.op("vector", "tensor_scalar", row(1), row(1), 1e-5, None, ALU.add)
    cac = p.op("scalar", "sqrt", row(1), row(1), waits=(("dv", p.cnt["dv"]),))
    p.op("vector", "reciprocal", row(2), row(1), waits=(("ac", cac),))
    p.op("vector", "tensor_mul", row(3), row(0), row(2))
    bc_to(bcA, row(2))
    bc_to(bcB, row(3))
    BC_DONE = p.cnt["ac"]
    for i in range(NF):
        buf = xt[i % 2]
        ld = p.dma("sync", buf[:], xin_d.ap()[i * 128:(i + 1) * 128, :], sem="ldx",
                   waits=(("dv", p.cnt["dv"]),))
        p.op("vector", "tensor_mul", scrA[:], buf[:], bcA[:],
             waits=(("ldx", ld), ("ac", BC_DONE)))
        p.op("vector", "tensor_sub", scrA[:], scrA[:], bcB[:])
        p.op("vector", "tensor_copy", XLN[:, i * NTOK:(i + 1) * NTOK], scrA[:])
    XLN_DONE = p.cnt["dv"]

    def load_wchunk(dst, src_d, c, pewait):
        return p.dma("gpsimd",
                     dst.ap().rearrange("p (a o) -> p a o", o=128),
                     src_d.ap().rearrange("(a p) o -> p a o", p=128)[:, :, c * 128:(c + 1) * 128],
                     sem=f"ldw{c % 2}", waits=(("pe", pewait),))

    pe_end = {}
    pe_ev_ac = 0
    pe_ev_dv = 0
    for c in range(48 + 16):
        is_g = c >= 48
        lw = load_wchunk(wb[c % 2], wgm_d if is_g else wqkv_d,
                         c - 48 if is_g else c, pe_end.get(c - 2, 0))
        head = c // 3
        kind = 3 if is_g else (c % 3)
        mm = {}
        for h in range(2):
            ps = psQ[2 * (c % 2) + h]
            for i in range(NF):
                mm[h] = p.op("tensor", "matmul", ps[:],
                             wb[c % 2][:, i * 128:(i + 1) * 128],
                             XLN[:, i * NTOK + h * 512:i * NTOK + h * 512 + 512],
                             start=(i == 0), stop=(i == NF - 1), sem="pe",
                             waits=(((f"ldw{c % 2}", lw), ("dv", XLN_DONE),
                                     ("ac", pe_ev_ac), ("dv", pe_ev_dv))
                                    if (i == 0 and h == 0) else ()))
        if kind == 3:
            gi = c - 48
            for h in range(2):
                ps = psQ[2 * (c % 2) + h]
                p.op("scalar", "activation", obh[:, h * 512:(h + 1) * 512],
                     ps[:], AF.Silu, waits=(("pe", mm[h]), ("sto", p.cnt["sto"])))
            pe_ev_ac, pe_ev_dv = p.cnt["ac"], p.cnt["dv"]
            p.dma("gpsimd", g_o.ap()[gi * 128:(gi + 1) * 128, :],
                  obh[:], sem="sto", waits=(("ac", p.cnt["ac"]),))
        elif kind == 2:
            for h in range(2):
                ps = psQ[2 * (c % 2) + h]
                p.op("scalar", "copy", vstage[:, h * 512:(h + 1) * 512], ps[:],
                     waits=(("pe", mm[h]), ("pe", p.cnt["pe"])))
            for j in range(8):
                cpe = p.op("tensor", "transpose", psTR[:],
                           vstage[:, j * 128:(j + 1) * 128], idn[:], sem="pe",
                           waits=(("ac", p.cnt["ac"]), ("pl", PL_ID)))
                p.op("scalar", "copy", vtr[:, j * 128:(j + 1) * 128], psTR[:],
                     waits=(("pe", cpe), ("sto", p.cnt["sto"])))
            pe_ev_ac, pe_ev_dv = p.cnt["ac"], p.cnt["dv"]
            for j in range(8):
                p.dma("gpsimd", vt_o.ap()[j * 128:(j + 1) * 128,
                                          head * 128:(head + 1) * 128],
                      vtr[:, j * 128:(j + 1) * 128], sem="sto",
                      waits=(("ac", p.cnt["ac"]),))
        else:
            for h in range(2):
                ps = psQ[2 * (c % 2) + h]
                cqe = p.op("scalar", "copy", qbh[:, h * 512:(h + 1) * 512], ps[:],
                           waits=(("pe", mm[h]), ("sto", p.cnt["sto"]),
                                  ("dv", p.cnt["dv"])))
                cqs = p.op("scalar", "square", sq[:, h * 512:(h + 1) * 512],
                           qbh[:, h * 512:(h + 1) * 512], waits=(("ac", cqe),))
                p.op("tensor", "matmul", psM[h][0:1, 0:512], onescolb[:],
                     sq[:, h * 512:(h + 1) * 512], start=True, stop=True,
                     sem="pe", waits=(("ac", cqs), ("dv", p.cnt["dv"])))
            PE_MS = p.cnt["pe"]
            p.op("vector", "tensor_scalar", row(0, 0), psM[0][0:1, 0:512],
                 1.0 / 128, 1e-6, ALU.mult, ALU.add, waits=(("pe", PE_MS),))
            p.op("vector", "tensor_scalar", row(0, 1), psM[1][0:1, 0:512],
                 1.0 / 128, 1e-6, ALU.mult, ALU.add)
            cac = p.op("scalar", "sqrt", row(0), row(0), waits=(("dv", p.cnt["dv"]),))
            p.op("vector", "reciprocal", row(0), row(0), waits=(("ac", cac),))
            bc_to(bcR, row(0))
            c5 = p.op("vector", "tensor_mul", scrA[:], qbh[:], bcR[:],
                      waits=(("ac", p.cnt["ac"]),))
            c6 = p.op("scalar", "copy", scrB[0:64, :], scrA[64:128, :],
                      waits=(("dv", c5),))
            c7 = p.op("scalar", "copy", scrB[64:128, :], scrA[0:64, :])
            p.op("vector", "tensor_mul", scrC[:], scrA[:], CC2[:],
                 waits=(("ac", ROPE_AC), ("dv", ROPE_DV)))
            p.op("vector", "tensor_mul", scrD[:], scrB[:], SS2[:],
                 waits=(("ac", c7),))
            p.op("vector", "tensor_add", scrC[:], scrC[:], scrD[:])
            if kind == 0:
                p.op("vector", "tensor_scalar", scrD[:], scrC[:],
                     float(SC_HI), None, ALU.mult)
                p.op("vector", "scalar_tensor_tensor", obh[:], scrC[:],
                     float(SC_LO), scrD[:], ALU.mult, ALU.add,
                     waits=(("sto", p.cnt["sto"]),))
                pe_ev_ac, pe_ev_dv = p.cnt["ac"], p.cnt["dv"]
                p.dma("gpsimd", qs_o.ap()[head * 128:(head + 1) * 128, :],
                      obh[:], sem="sto", waits=(("dv", p.cnt["dv"]),))
            else:
                p.op("vector", "tensor_copy", obh[:], scrC[:],
                     waits=(("sto", p.cnt["sto"]),))
                pe_ev_ac, pe_ev_dv = p.cnt["ac"], p.cnt["dv"]
                p.dma("gpsimd", k_o.ap()[head * 128:(head + 1) * 128, :],
                      obh[:], sem="sto", waits=(("dv", p.cnt["dv"]),))
        pe_end[c] = p.cnt["pe"]
    p.wait("gpsimd", ("sto", p.cnt["sto"]))
    return p


def host_attn_a_inputs(x_fm_c, ins, c):
    trow, ks, kc = rope_host_tables(c)
    return {
        "xin": x_fm_c,
        "wqkv": np.ascontiguousarray(np.float32(ins["L0_attn_W_qkv"])),
        "wgm": np.ascontiguousarray(np.float32(ins["L0_attn_W_g"])),
        "trow": trow, "ks": np.ascontiguousarray(ks), "kc": np.ascontiguousarray(kc),
        "onesc": np.ones((1, 128), dtype=np.float32),
        "onescolb": np.ones((128, 1), dtype=BF),
    }


def build_l0_attn_b(cpos):
    chunkA, chunkB = cpos, 7 - cpos
    p = Prog(f"l0attnb{cpos}")
    io = {}
    xin_d = p.dram_io(io, "xin", [D, NTOK], F32, kind="ExternalInput")
    qs_d = p.dram_io(io, "qs", [D, NTOK], BF16, kind="ExternalInput")
    kf_d = p.dram_io(io, "kf", [D, T], BF16, kind="ExternalInput")
    vf_d = p.dram_io(io, "vf", [T, D], BF16, kind="ExternalInput")
    g_d = p.dram_io(io, "go", [D, NTOK], BF16, kind="ExternalInput")
    lq1_d = p.dram_io(io, "lq1", [128, 8], F32, kind="ExternalInput")
    lk1_d = p.dram_io(io, "lk1", [128, 8], F32, kind="ExternalInput")
    lq2_d = p.dram_io(io, "lq2", [128, 8], F32, kind="ExternalInput")
    lk2_d = p.dram_io(io, "lk2", [128, 8], F32, kind="ExternalInput")
    ones_d = p.dram_io(io, "onesc", [1, 128], F32, kind="ExternalInput")
    onescol_d = p.dram_io(io, "onescol", [128, 1], F32, kind="ExternalInput")
    onescolb_d = p.dram_io(io, "onescolb", [128, 1], BF16, kind="ExternalInput")
    xout_d = p.dram_io(io, "xout", [D, NTOK], F32, kind="ExternalOutput")

    xst = [p.sbuf(f"xst{i}", [128, NTOK], F32) for i in range(2)]
    QS = p.sbuf("QS", [128, NF * NTOK], BF16)
    G = p.sbuf("G", [128, NF * NTOK], BF16)
    KT = [p.sbuf(f"KT{i}", [128, T], BF16) for i in range(2)]
    VT = [p.sbuf(f"VT{i}", [128, 32 * 256], BF16) for i in range(2)]
    PT = [p.sbuf(f"PT{i}", [128, 512], BF16) for i in range(2)]
    dacc = p.sbuf("dacc", [128, 512], F32)
    scrA = p.sbuf("scrA", [128, NTOK], F32)
    scrB = p.sbuf("scrB", [128, NTOK], F32)
    yv = [p.sbuf(f"yv{v}", [128, 4 * 512], F32) for v in range(2)]
    sqt = p.sbuf("sqt", [128, NTOK], BF16)
    MSK = p.sbuf("MSK", [128, 4 * 512], F32)
    lamt = p.sbuf("lamt", [128, 8], F32)
    lam2t = p.sbuf("lam2t", [128, 8], F32)
    lamr = p.sbuf("lamr", [1, 16], F32)
    lamcols = p.sbuf("lamcols", [128, 8], F32)
    onescol = None
    ones1 = p.sbuf("ones1", [1, 128], F32)
    onescol = p.sbuf("onescolt", [128, 1], F32)
    onescolb = p.sbuf("onescolbt", [128, 1], BF16)
    rows = p.sbuf("rows", [1, 2 * NTOK], F32)
    bcR = p.sbuf("bcR", [128, NTOK], F32)

    psS = [p.psum(f"psS{i}", [128, 512], F32) for i in range(2)]
    psY = [p.psum(f"psY{i}", [128, 512], F32) for i in range(2)]
    psBC = p.psum("psBC", [128, 512], F32)
    psR = [p.psum(f"psR{i}", [1, 512], F32) for i in range(2)]
    psL = p.psum("psL", [128, 8], F32)

    def row(i, h=None):
        if h is None:
            return rows[0:1, i * NTOK:(i + 1) * NTOK]
        return rows[0:1, i * NTOK + h * 512:i * NTOK + (h + 1) * 512]

    p.dma("sync", ones1[:], ones_d.ap(), sem="ldc")
    p.dma("sync", onescol[:], onescol_d.ap(), sem="ldc")
    p.dma("sync", onescolb[:], onescolb_d.ap(), sem="ldc")
    LDC = p.cnt["ldc"]

    def bc_to(row_ap, n=NTOK):
        last = 0
        for h in range(n // 512):
            cpe = p.op("tensor", "matmul", psBC[:, 0:512], ones1[:],
                       row_ap[0:1, h * 512:(h + 1) * 512], start=True, stop=True,
                       sem="pe", waits=(("ac", p.cnt["ac"]), ("dv", p.cnt["dv"]),
                                        ("ldc", LDC)))
            last = p.op("scalar", "copy", bcR[:, h * 512:(h + 1) * 512],
                        psBC[:, 0:512], waits=(("pe", cpe),))
        return last

    for i in range(NF):
        p.dma("scalar", QS[:, i * NTOK:(i + 1) * NTOK],
              qs_d.ap()[i * 128:(i + 1) * 128, :], sem="ldq")
        p.dma("scalar", G[:, i * NTOK:(i + 1) * NTOK],
              g_d.ap()[i * 128:(i + 1) * 128, :], sem="ldq")
    LDQ = p.cnt["ldq"]

    p.dma("sync", lamt[:], lq1_d.ap(), sem="ldl")
    p.dma("sync", lam2t[:], lk1_d.ap(), sem="ldl")
    cm = p.op("vector", "tensor_mul", lamt[:], lamt[:], lam2t[:],
              waits=(("ldl", p.cnt["ldl"]),))
    c1m = p.op("tensor", "matmul", psL[0:1, 0:8], onescol[:], lamt[:],
               start=True, stop=True, sem="pe",
               waits=(("dv", cm), ("ldc", LDC)))
    ce1 = p.op("scalar", "activation", lamr[0:1, 0:8], psL[0:1, 0:8], AF.Exp,
               waits=(("pe", c1m),))
    p.dma("sync", lamt[:], lq2_d.ap(), sem="ldl", waits=(("dv", p.cnt["dv"]), ("pe", c1m)))
    p.dma("sync", lam2t[:], lk2_d.ap(), sem="ldl")
    cm2 = p.op("vector", "tensor_mul", lamt[:], lamt[:], lam2t[:],
               waits=(("ldl", p.cnt["ldl"]),))
    c2m = p.op("tensor", "matmul", psL[0:1, 0:8], onescol[:], lamt[:],
               start=True, stop=True, sem="pe",
               waits=(("dv", cm2), ("ac", ce1)))
    ce2 = p.op("scalar", "activation", lamr[0:1, 8:16], psL[0:1, 0:8], AF.Exp,
               waits=(("pe", c2m),))
    p.op("vector", "tensor_sub", lamr[0:1, 0:8], lamr[0:1, 0:8], lamr[0:1, 8:16],
         waits=(("ac", ce2),))
    p.op("vector", "tensor_scalar", lamr[0:1, 0:8], lamr[0:1, 0:8],
         float(LAMBDA_INIT), None, ALU.add)
    cbl = p.op("tensor", "matmul", psL[:, 0:8], ones1[:], lamr[0:1, 0:8],
               start=True, stop=True, sem="pe", waits=(("dv", p.cnt["dv"]),))
    LAM_AC = p.op("scalar", "copy", lamcols[:], psL[:, 0:8], waits=(("pe", cbl),))

    cz = p.op("vector", "memset", scrB[:, 0:512], 0.0)
    cmsk = 0
    for j in range(4):
        cmsk = p.op("gpsimd", "affine_select", MSK[:, j * 512:(j + 1) * 512], scrB[:, 0:512],
                    pattern=[[1, 512]], compare_op=ALU.is_ge, fill=-1e30,
                    channel_multiplier=-1, base=-128 * j, waits=(("dv", cz),))
    MSK_PL = cmsk

    unit_pe_end = {}
    for g in range(16):
        hp, v = g // 2, g % 2
        h = 2 * hp + v
        kbuf = KT[g % 2]
        lk = p.dma("sync", kbuf[:], kf_d.ap()[h * 128:(h + 1) * 128, :],
                   sem=f"ldk{g % 2}", waits=(("pe", unit_pe_end.get(g - 2, 0)),))
        if v == 0:
            vbuf = VT[hp % 2]
            lv = p.dma("scalar", vbuf.ap().rearrange("p (a o) -> p a o", o=256),
                       vf_d.ap().rearrange("(a p) d -> p a d", p=128)[:, :, hp * 256:(hp + 1) * 256],
                       sem=f"ldv{hp % 2}",
                       waits=(("pe", unit_pe_end.get(g - 4, 0)), ("ldq", LDQ)))
            lv_cur = lv
        qrow = QS[:, h * NTOK:(h + 1) * NTOK]
        for qc, (gchunk, nkt) in enumerate(((chunkA, 4 * (chunkA + 1)),
                                            (chunkB, 4 * (chunkB + 1)))):
            qsl = qrow[:, qc * 512:(qc + 1) * 512]
            mmy = {}
            exs = {}
            das = {}
            for kt in range(nkt):
                first = (kt == 0)
                pss = psS[kt % 2]
                pt = PT[kt % 2]
                diag = (kt >= 4 * gchunk)
                cpe = p.op("tensor", "matmul", pss[:],
                           kbuf[:, kt * 128:(kt + 1) * 128], qsl,
                           start=True, stop=True, sem="pe",
                           waits=(((f"ldk{g % 2}", lk), ("ldq", LDQ),
                                   ("ac", p.cnt["ac"]), ("dv", p.cnt["dv"]))
                                  if first else
                                  (("ac", exs.get(kt - 2, 0)),)))
                if diag:
                    j = kt - 4 * gchunk
                    cadd = p.op("vector", "tensor_add", scrA[:, 0:512], pss[:],
                                MSK[:, j * 512:(j + 1) * 512],
                                waits=(("pe", cpe), ("pl", MSK_PL),
                                       ("pe", mmy.get(kt - 2, (0, 0))[1])))
                    cex = p.op("scalar", "activation", pt[:], scrA[:, 0:512],
                               AF.Exp, waits=(("dv", cadd),
                                              ("dv", das.get(kt - 2, 0)),
                                              ("pe", mmy.get(kt - 2, (0, 0))[1])))
                else:
                    cex = p.op("scalar", "activation", pt[:], pss[:], AF.Exp,
                               waits=(("pe", cpe),
                                      ("dv", das.get(kt - 2, 0)),
                                      ("pe", mmy.get(kt - 2, (0, 0))[1])))
                exs[kt] = cex
                if first:
                    cda = p.op("vector", "tensor_copy", dacc[:], pt[:],
                               waits=(("ac", cex),))
                else:
                    cda = p.op("vector", "tensor_add", dacc[:], dacc[:], pt[:],
                               waits=(("ac", cex),))
                das[kt] = cda
                for e in range(2):
                    mmy[kt] = (e, p.op("tensor", "matmul", psY[e][:],
                                       vbuf[:, kt * 256 + e * 128:kt * 256 + (e + 1) * 128],
                                       pt[:], start=first, stop=(kt == nkt - 1),
                                       sem="pe",
                                       waits=(((f"ldv{hp % 2}", lv_cur), ("ac", cex),
                                               ("dv", yev if (first and g + qc > 0) else 0))
                                              if first and e == 0 else (("ac", cex),))))
            cdr = p.op("tensor", "matmul", psR[0][0:1, 0:512], onescol[:], dacc[:],
                       start=True, stop=True, sem="pe",
                       waits=(("dv", p.cnt["dv"]), ("ldc", LDC)))
            p.op("vector", "reciprocal", row(0, 0), psR[0][0:1, 0:512],
                 waits=(("pe", cdr),))
            lb = bc_to(row(0, 0), n=512)
            for e in range(2):
                yev = p.op("vector", "tensor_mul",
                           yv[v][:, (2 * qc + e) * 512:(2 * qc + e) * 512 + 512],
                           psY[e][:], bcR[:, 0:512],
                           waits=(("ac", lb), ("pe", mmy[nkt - 1][1])))
        unit_pe_end[g] = p.cnt["pe"]
        if v == 1:
            for e in range(2):
                ft = 2 * hp + e
                for qc in range(2):
                    slq = slice((2 * qc + e) * 512, (2 * qc + e) * 512 + 512)
                    qsl2 = slice(qc * 512, (qc + 1) * 512)
                    p.op("vector", "tensor_scalar", scrA[:, qsl2],
                         yv[1][:, slq], lamcols[:, hp:hp + 1], None, ALU.mult,
                         waits=(("ac", LAM_AC),))
                    p.op("vector", "tensor_sub", scrA[:, qsl2],
                         yv[0][:, slq], scrA[:, qsl2])
                    dst = yv[0][:, slq] if e == 0 else scrB[:, qsl2]
                    cg = p.op("vector", "tensor_mul", dst, scrA[:, qsl2],
                              G[:, ft * NTOK + qc * 512:ft * NTOK + (qc + 1) * 512],
                              waits=(("ldq", LDQ),))
                    csq = p.op("scalar", "square", sqt[:, qsl2], dst,
                               waits=(("dv", cg), ("pe", p.cnt["pe"])))
                    p.op("tensor", "matmul", psR[qc][0:1, 0:512], onescolb[:],
                         sqt[:, qsl2],
                         start=(e == 0), stop=(e == 1), sem="pe",
                         waits=(("ac", csq), ("dv", p.cnt["dv"]), ("ldc", LDC)))
            PE_RMS = p.cnt["pe"]
            for hh in range(2):
                p.op("vector", "tensor_scalar", row(1, hh), psR[hh][0:1, 0:512],
                     1.0 / 256, 1e-6, ALU.mult, ALU.add, waits=(("pe", PE_RMS),))
            cac = p.op("scalar", "sqrt", row(1), row(1), waits=(("dv", p.cnt["dv"]),))
            p.op("vector", "reciprocal", row(1), row(1), waits=(("ac", cac),))
            p.op("vector", "tensor_scalar", row(1), row(1),
                 float(1.0 - LAMBDA_INIT), None, ALU.mult)
            bc_to(row(1), n=NTOK)
            for e in range(2):
                ft = 2 * hp + e
                ldxt = p.dma("sync", xst[e][:], xin_d.ap()[ft * 128:(ft + 1) * 128, :],
                             sem="ldx", waits=(("sto", p.cnt["sto"]),))
                for qc in range(2):
                    qsl2 = slice(qc * 512, (qc + 1) * 512)
                    srcap = (yv[0][:, (2 * qc) * 512:(2 * qc) * 512 + 512]
                             if e == 0 else scrB[:, qsl2])
                    p.op("vector", "tensor_mul", scrA[:, qsl2], srcap, bcR[:, qsl2],
                         waits=(("ac", p.cnt["ac"]),))
                    yev = p.op("vector", "tensor_add", xst[e][:, qsl2],
                               xst[e][:, qsl2], scrA[:, qsl2],
                               waits=(("ldx", ldxt),))
                p.dma("gpsimd", xout_d.ap()[ft * 128:(ft + 1) * 128, :],
                      xst[e][:], sem="sto", waits=(("dv", p.cnt["dv"]),))
    p.wait("gpsimd", ("sto", p.cnt["sto"]))
    return p


def _pad_lam(a):
    out = np.zeros((128, 8), dtype=np.float32)
    out[:64, :] = np.float32(a).T
    return out


def build_l0_attn_bu(p=None, io=None, static_dbg=None,
                     dyn_parts=("q", "d", "y", "k", "v")):
    if p is None:
        p = Prog("l0attnbu")
    io = io or {}
    xin_d = p.dram_io(io, "xin", [D, NTOK], F32, kind="ExternalInput")
    qs_d = p.dram_io(io, "qs", [D, NTOK], BF16, kind="ExternalInput")
    kf_d = p.dram_io(io, "kf", [D, T], BF16, kind="ExternalInput")
    vf_d = p.dram_io(io, "vf", [T, D], BF16, kind="ExternalInput")
    g_d = p.dram_io(io, "go", [D, NTOK], BF16, kind="ExternalInput")
    lq1_d = p.dram_io(io, "lq1", [128, 8], F32, kind="ExternalInput")
    lk1_d = p.dram_io(io, "lk1", [128, 8], F32, kind="ExternalInput")
    lq2_d = p.dram_io(io, "lq2", [128, 8], F32, kind="ExternalInput")
    lk2_d = p.dram_io(io, "lk2", [128, 8], F32, kind="ExternalInput")
    tbl_d = p.dram_io(io, "tbl", [1, 96], I32, kind="ExternalInput")
    ones_d = p.dram_io(io, "onesc", [1, 128], F32, kind="ExternalInput")
    onescol_d = p.dram_io(io, "onescol", [128, 1], F32, kind="ExternalInput")
    onescolb_d = p.dram_io(io, "onescolb", [128, 1], BF16, kind="ExternalInput")
    xout_d = p.dram_io(io, "xout", [D, NTOK], F32, kind="ExternalOutput")
    kperm_d = p.dram_io(io, "kperm", [D, 36 * 128], BF16, kind="Internal")
    vperm_d = p.dram_io(io, "vperm", [36 * 128, D], BF16, kind="Internal")

    xst = [p.sbuf(f"xst{i}", [128, NTOK], F32) for i in range(2)]
    Qcur = p.sbuf("Qcur", [128, NTOK], BF16)
    G = p.sbuf("G", [128, NF * NTOK], BF16)
    KT = [p.sbuf(f"KT{i}", [128, 36 * 128], BF16) for i in range(2)]
    VT = [p.sbuf(f"VT{i}", [128, 36 * 256], BF16) for i in range(2)]
    PT = [p.sbuf(f"PT{i}", [128, 512], BF16) for i in range(2)]
    scrA = p.sbuf("scrA", [128, NTOK], F32)
    scrB = p.sbuf("scrB", [128, NTOK], F32)
    yv = [p.sbuf(f"yv{v}", [128, 4 * 512], F32) for v in range(2)]
    sqt = p.sbuf("sqt", [128, NTOK], BF16)
    MSK = p.sbuf("MSK", [128, 4 * 512], F32)
    lamt = p.sbuf("lamt", [128, 8], F32)
    lam2t = p.sbuf("lam2t", [128, 8], F32)
    lamr = p.sbuf("lamr", [1, 16], F32)
    lamcols = p.sbuf("lamcols", [128, 8], F32)
    tbls = p.sbuf("tbls", [1, 96], I32)
    ones1 = p.sbuf("ones1", [1, 128], F32)
    onescol = p.sbuf("onescolt", [128, 1], F32)
    onescolb = p.sbuf("onescolbt", [128, 1], BF16)
    rows = p.sbuf("rows", [1, 2 * NTOK], F32)
    bcR = p.sbuf("bcR", [128, NTOK], F32)

    psS = [p.psum(f"psS{i}", [128, 512], F32) for i in range(2)]
    psY2 = [p.psum(f"psY{e}", [128, 2 * 512], F32) for e in range(2)]
    psDEN = p.psum("psDEN", [1, 2 * 512], F32)

    def row(i, h=None):
        if h is None:
            return rows[0:1, i * NTOK:(i + 1) * NTOK]
        return rows[0:1, i * NTOK + h * 512:i * NTOK + (h + 1) * 512]

    p.dma("sync", ones1[:], ones_d.ap(), sem="ldc")
    p.dma("sync", onescol[:], onescol_d.ap(), sem="ldc")
    p.dma("sync", onescolb[:], onescolb_d.ap(), sem="ldc")
    p.dma("sync", tbls[:], tbl_d.ap(), sem="ldc")
    LDC = p.cnt["ldc"]

    def bc_to(row_ap, n=NTOK):
        last = 0
        for h in range(n // 512):
            cpe = p.op("tensor", "matmul", psS[0][:, 0:512], ones1[:],
                       row_ap[0:1, h * 512:(h + 1) * 512], start=True, stop=True,
                       sem="pe", waits=(("ac", p.cnt["ac"]), ("dv", p.cnt["dv"]),
                                        ("ldc", LDC)))
            last = p.op("scalar", "copy", bcR[:, h * 512:(h + 1) * 512],
                        psS[0][:, 0:512], waits=(("pe", cpe),))
        return last

    for i in range(NF):
        p.dma("scalar", G[:, i * NTOK:(i + 1) * NTOK],
              g_d.ap()[i * 128:(i + 1) * 128, :], sem="ldq")
    LDQ = p.cnt["ldq"]

    p.dma("sync", lamt[:], lq1_d.ap(), sem="ldl")
    p.dma("sync", lam2t[:], lk1_d.ap(), sem="ldl")
    cm = p.op("vector", "tensor_mul", lamt[:], lamt[:], lam2t[:],
              waits=(("ldl", p.cnt["ldl"]),))
    c1m = p.op("tensor", "matmul", psS[0][0:1, 0:8], onescol[:], lamt[:],
               start=True, stop=True, sem="pe",
               waits=(("dv", cm), ("ldc", LDC)))
    ce1 = p.op("scalar", "activation", lamr[0:1, 0:8], psS[0][0:1, 0:8], AF.Exp,
               waits=(("pe", c1m),))
    p.dma("sync", lamt[:], lq2_d.ap(), sem="ldl", waits=(("dv", p.cnt["dv"]), ("pe", c1m)))
    p.dma("sync", lam2t[:], lk2_d.ap(), sem="ldl")
    cm2 = p.op("vector", "tensor_mul", lamt[:], lamt[:], lam2t[:],
               waits=(("ldl", p.cnt["ldl"]),))
    c2m = p.op("tensor", "matmul", psS[0][0:1, 0:8], onescol[:], lamt[:],
               start=True, stop=True, sem="pe",
               waits=(("dv", cm2), ("ac", ce1)))
    ce2 = p.op("scalar", "activation", lamr[0:1, 8:16], psS[0][0:1, 0:8], AF.Exp,
               waits=(("pe", c2m),))
    p.op("vector", "tensor_sub", lamr[0:1, 0:8], lamr[0:1, 0:8], lamr[0:1, 8:16],
         waits=(("ac", ce2),))
    p.op("vector", "tensor_scalar", lamr[0:1, 0:8], lamr[0:1, 0:8],
         float(LAMBDA_INIT), None, ALU.add)
    cbl = p.op("tensor", "matmul", psS[0][:, 0:8], ones1[:], lamr[0:1, 0:8],
               start=True, stop=True, sem="pe", waits=(("dv", p.cnt["dv"]), ("ac", ce2)))
    LAM_AC = p.op("scalar", "copy", lamcols[:], psS[0][:, 0:8], waits=(("pe", cbl),))

    cz = p.op("vector", "memset", scrB[:, 0:512], 0.0)
    cmsk = 0
    for j in range(4):
        cmsk = p.op("gpsimd", "affine_select", MSK[:, j * 512:(j + 1) * 512],
                    scrB[:, 0:512], pattern=[[1, 512]], compare_op=ALU.is_ge,
                    fill=-1e30, channel_multiplier=-1, base=-128 * j,
                    waits=(("dv", cz),))
    MSK_PL = cmsk

    regs = {}

    def mkreg(eng, name):
        h = {}
        def fn(engine, _n=name, _h=h):
            _h["r"] = engine.register(_n).__enter__()
        p.raw(eng, fn)
        return h

    rS = mkreg("gpsimd", "rS")
    rSv = mkreg("sync", "rSv")

    zcolb = p.sbuf("zcolb", [128, 128], BF16)
    cz0 = p.op("vector", "memset", zcolb[:], 0.0)
    cz0 = p.op("vector", "memset", PT[0][:], 0.0)
    for q2 in range(2):
        p.op("tensor", "matmul", psDEN[0:1, q2 * 512:(q2 + 1) * 512],
             zcolb[:, 0:1], PT[0][:], start=True, stop=True, sem="pe",
             waits=(("dv", cz0),))
        for e2 in range(2):
            p.op("tensor", "matmul", psY2[e2][:, q2 * 512:(q2 + 1) * 512],
                 zcolb[:], PT[0][:], start=True, stop=True, sem="pe",
                 waits=(("dv", cz0),))
    ksrc_vals = {}
    vsrc_vals = {}

    stt = static_dbg[0] if static_dbg is not None else None
    dyn = (lambda part: stt is None or part in dyn_parts)

    kvg = io.get("kv_gathered", False)
    kmax = (4 * D * NTOK - 512) if kvg else (T - 512)
    vmax = (4 * NTOK - 512) if kvg else (T - 512)
    if dyn("k"):
        def gp_hoist(engine):
            engine.wait_ge(p.semh("ldc"), LDC)
            for i in range(9):
                engine.reg_load(rS["r"], tbls[0:1, 72 + i:72 + i + 1])
                ksrc_vals[i] = engine.snap(rS["r"], min_val=0, max_val=kmax)
        p.raw("gpsimd", gp_hoist)

    if dyn("v"):
        def sy_hoist(engine):
            engine.wait_ge(p.semh("ldc"), LDC)
            for i in range(9):
                engine.reg_load(rSv["r"], tbls[0:1, 81 + i:81 + i + 1])
                vsrc_vals[i] = engine.snap(rSv["r"], min_val=0, max_val=vmax)
        p.raw("sync", sy_hoist)
    for blk in range(9):
        if dyn("k"):
            if kvg:
                ksrc_ap = (lambda _b=blk: bass.AP(kf_d, ksrc_vals[_b],
                                                  [[NTOK, D], [1, 512]]))
            else:
                ksrc_ap = (lambda _b=blk: kf_d.ap()[:, bass.ds(ksrc_vals[_b], 512)])
        else:
            _s0 = int(stt[72 + blk])
            ksrc_ap = kf_d.ap()[:, _s0:_s0 + 512]
        if dyn("v"):
            vsrc_ap = (lambda _b=blk: vf_d.ap()[bass.ds(vsrc_vals[_b], 512), :])
        else:
            _s0 = int(stt[81 + blk])
            vsrc_ap = vf_d.ap()[_s0:_s0 + 512, :]
        p.dma("gpsimd", kperm_d.ap()[:, blk * 512:(blk + 1) * 512],
              ksrc_ap, sem="ldperm",
              waits=((("ldc", LDC),) if not dyn("k") and blk == 0 else ()))
        p.dma("sync", vperm_d.ap()[blk * 512:(blk + 1) * 512, :],
              vsrc_ap, sem="ldperm2",
              waits=((("ldc", LDC),) if not dyn("v") and blk == 0 else ()))
    LDPERM = p.cnt["ldperm"]
    LDPERM2 = p.cnt["ldperm2"]

    slot_off = {}

    def slot_bit_static(s_):
        if s_ < 4:
            return 0
        if s_ < 8 or s_ >= 20:
            return 512
        return None

    def slot_val(s_):
        return slot_off[(s_ - 8) // 4]

    def pe_hoist(engine):
        engine.wait_ge(p.semh("ldc"), LDC)
        for j in range(3):
            r = engine.alloc_register()
            engine.reg_load(r, tbls[0:1, 8 + 4 * j:9 + 4 * j])
            slot_off[j] = engine.snap(r, True, min_val=0, max_val=512)
    if stt is None or any(dyn(x) for x in ("q", "d", "y")):
        p.raw("tensor", pe_hoist)

    unit_pe_end = {}
    for g in range(DBG_UNITS):
        hp, v = g // 2, g % 2
        h = 2 * hp + v
        kbuf = KT[g % 2]
        lq_cur = p.dma("scalar", Qcur[:], qs_d.ap()[h * 128:(h + 1) * 128, :],
                       sem="ldqc", waits=(("pe", p.cnt["pe"]),))
        lk = p.dma("sync", kbuf[:], kperm_d.ap()[h * 128:(h + 1) * 128, :],
                   sem=f"ldk{g % 2}",
                   waits=(("pe", unit_pe_end.get(g - 2, 0)), ("ldperm", LDPERM)))
        if v == 0:
            vbuf = VT[hp % 2]
            lv_cur = p.dma("scalar",
                           vbuf.ap().rearrange("p (a o) -> p a o", o=256),
                           vperm_d.ap().rearrange("(a p) d -> p a d", p=128)[:, :, hp * 256:(hp + 1) * 256],
                           sem=f"ldv{hp % 2}",
                           waits=(("pe", unit_pe_end.get(g - 4, 0)),
                                  ("ldq", LDQ), ("ldperm2", LDPERM2)))
        mmy = {}
        exs = {}
        das = {}
        for s in range(36):
            first = (s == 0)
            pss = psS[s % 2]
            pt = PT[s % 2]
            diag = (s < 8) and ("diag" not in DBG_STRIP)

            sbit = slot_bit_static(s)
            if dyn("q") and sbit is None:
                def mk_qsl(_s=s):
                    return Qcur[:, bass.ds(slot_val(_s), 512)]
            else:
                _qv = sbit if sbit is not None else int(stt[s])
                mk_qsl = Qcur[:, _qv:_qv + 512]

            mk_lhsT = kbuf[:, s * 128:(s + 1) * 128]

            cpe = p.op("tensor", "matmul", pss[:], mk_lhsT, mk_qsl,
                       start=True, stop=True, sem="pe",
                       waits=(((f"ldk{g % 2}", lk), ("ldqc", lq_cur),
                               ("ac", p.cnt["ac"]), ("dv", p.cnt["dv"]),
                               ("ldc", LDC))
                              if first else
                              (("ac", exs.get(s - 2, 0)),)))
            if diag:
                j = s % 4
                cadd = p.op("vector", "tensor_add", scrA[:, 0:512], pss[:],
                            MSK[:, j * 512:(j + 1) * 512],
                            waits=(("pe", cpe), ("pl", MSK_PL),
                                   ("pe", mmy.get(s - 2, 0))))
                cex = p.op("scalar", "activation", pt[:], scrA[:, 0:512],
                           AF.Exp, waits=(("dv", cadd), ("dv", das.get(s - 2, 0)),
                                          ("pe", mmy.get(s - 2, 0))))
            else:
                cex = p.op("scalar", "activation", pt[:], pss[:], AF.Exp,
                           waits=(("pe", cpe), ("dv", das.get(s - 2, 0)),
                                  ("pe", mmy.get(s - 2, 0))))
            exs[s] = cex
            if dyn("d") and sbit is None:
                def mk_den(_s=s):
                    return psDEN[:, bass.ds(slot_val(_s), 512)]
            else:
                _dv = sbit if sbit is not None else int(stt[s])
                mk_den = psDEN[:, _dv:_dv + 512]

            p.op("tensor", "matmul", mk_den, onescolb[:], pt[:],
                 start=(s in (0, 4)), stop=False, sem="pe",
                 skip_group_check=True, waits=(("ac", cex), ("ldc", LDC)))
            das[s] = 0
            for e in range(2):
                if dyn("y") and sbit is None:
                    def mk_psy(_s=s, _e=e):
                        return psY2[_e][:, bass.ds(slot_val(_s), 512)]
                else:
                    _yv = sbit if sbit is not None else (int(stt[36 + s]) // 1024) * 512
                    mk_psy = psY2[e][:, _yv:_yv + 512]

                mk_vt = vbuf[:, s * 256 + e * 128:s * 256 + (e + 1) * 128]

                mmy[s] = p.op("tensor", "matmul", mk_psy, mk_vt, pt[:],
                              start=(s in (0, 4)), stop=False, sem="pe",
                              skip_group_check=True,
                              waits=(((f"ldv{hp % 2}", lv_cur), ("ac", cex),
                                      ("dv", p.cnt["dv"]))
                                     if s in (0, 1) else (("ac", cex),)))
        PE_SLOTS = p.cnt["pe"]
        for qc in ([] if "evac" in DBG_STRIP else range(2)):
            p.op("vector", "reciprocal", row(0, 0),
                 psDEN[0:1, qc * 512:(qc + 1) * 512],
                 waits=(("pe", PE_SLOTS),))
            lb = bc_to(row(0, 0), n=512)
            for e in range(2):
                p.op("vector", "tensor_mul",
                     yv[v][:, (2 * qc + e) * 512:(2 * qc + e) * 512 + 512],
                     psY2[e][:, qc * 512:qc * 512 + 512],
                     bcR[:, 0:512],
                     waits=(("ac", lb), ("pe", p.cnt["pe"])))
        unit_pe_end[g] = p.cnt["pe"]
        if v == 1 and "comb" not in DBG_STRIP:
            for e in range(2):
                ft = 2 * hp + e
                for qc in range(2):
                    slq = slice((2 * qc + e) * 512, (2 * qc + e) * 512 + 512)
                    qsl2 = slice(qc * 512, (qc + 1) * 512)
                    p.op("vector", "tensor_scalar", scrA[:, qsl2],
                         yv[1][:, slq], lamcols[:, hp:hp + 1], None, ALU.mult,
                         waits=(("ac", LAM_AC),))
                    p.op("vector", "tensor_sub", scrA[:, qsl2],
                         yv[0][:, slq], scrA[:, qsl2])
                    dst = yv[0][:, slq] if e == 0 else scrB[:, qsl2]
                    cg = p.op("vector", "tensor_mul", dst, scrA[:, qsl2],
                              G[:, ft * NTOK + qc * 512:ft * NTOK + (qc + 1) * 512],
                              waits=(("ldq", LDQ),))
                    csq = p.op("scalar", "square", sqt[:, qsl2], dst,
                               waits=(("dv", cg), ("pe", p.cnt["pe"])))
                    tgt = psDEN[0:1, 0:512] if qc == 0 else psDEN[0:1, 512:1024]
                    p.op("tensor", "matmul", tgt, onescolb[:], sqt[:, qsl2],
                         start=(e == 0), stop=(e == 1), sem="pe",
                         waits=(("ac", csq), ("dv", p.cnt["dv"]), ("ldc", LDC)))
            PE_RMS = p.cnt["pe"]
            p.op("vector", "tensor_scalar", row(1, 0), psDEN[0:1, 0:512],
                 1.0 / 256, 1e-6, ALU.mult, ALU.add, waits=(("pe", PE_RMS),))
            p.op("vector", "tensor_scalar", row(1, 1), psDEN[0:1, 512:1024],
                 1.0 / 256, 1e-6, ALU.mult, ALU.add)
            cac = p.op("scalar", "sqrt", row(1), row(1), waits=(("dv", p.cnt["dv"]),))
            p.op("vector", "reciprocal", row(1), row(1), waits=(("ac", cac),))
            p.op("vector", "tensor_scalar", row(1), row(1),
                 float(1.0 - LAMBDA_INIT), None, ALU.mult)
            bc_to(row(1), n=NTOK)
            for e in range(2):
                ft = 2 * hp + e
                ldxt = p.dma("sync", xst[e][:], xin_d.ap()[ft * 128:(ft + 1) * 128, :],
                             sem="ldx", waits=(("sto", p.cnt["sto"]),))
                for qc in range(2):
                    qsl2 = slice(qc * 512, (qc + 1) * 512)
                    srcap = (yv[0][:, (2 * qc) * 512:(2 * qc) * 512 + 512]
                             if e == 0 else scrB[:, qsl2])
                    p.op("vector", "tensor_mul", scrA[:, qsl2], srcap, bcR[:, qsl2],
                         waits=(("ac", p.cnt["ac"]),))
                    p.op("vector", "tensor_add", xst[e][:, qsl2],
                         xst[e][:, qsl2], scrA[:, qsl2],
                         waits=(("ldx", ldxt),))
                p.dma("gpsimd", xout_d.ap()[ft * 128:(ft + 1) * 128, :],
                      xst[e][:], sem="sto", waits=(("dv", p.cnt["dv"]),))
    p.wait("gpsimd", ("sto", p.cnt["sto"]))
    return p


def host_attn_b_table(c, gathered=False):
    cp = c % 4
    kA, kB = 4 * (cp + 1), 4 * (8 - cp)
    slots = []
    for s in range(4):
        slots.append(((kA - 4 + s) * 128, 0))
    for s in range(4):
        slots.append(((kB - 4 + s) * 128, 1))
    for k in range(kA - 4):
        slots.append((k * 128, 0))
    for k in range(kB - 4):
        slots.append((k * 128, 1))
    assert len(slots) == 36
    t = np.zeros((1, 96), dtype=np.int32)
    t[0, 0:36] = [b * 512 for _, b in slots]
    t[0, 36:72] = [b * 1024 for _, b in slots]
    gchunks = [cp, 7 - cp] + list(range(cp)) + list(range(7 - cp))
    if gathered:
        ks, vs = [], []
        for g in gchunks:
            cc, half = min(g, 7 - g), (0 if g < 4 else 1)
            ks.append(cc * D * NTOK + half * 512)
            vs.append(cc * NTOK + half * 512)
        t[0, 72:81] = ks
        t[0, 81:90] = vs
    else:
        srcs = [512 * g for g in gchunks]
        t[0, 72:81] = srcs
        t[0, 81:90] = srcs
    return t


def host_attn_b_inputs(x_fm_c, qs, kfull, vfull, g, ins):
    return {
        "xin": x_fm_c, "qs": qs, "kf": kfull, "vf": vfull, "go": g,
        "lq1": _pad_lam(ins["L0_attn_lambda_q1"]),
        "lk1": _pad_lam(ins["L0_attn_lambda_k1"]),
        "lq2": _pad_lam(ins["L0_attn_lambda_q2"]),
        "lk2": _pad_lam(ins["L0_attn_lambda_k2"]),
        "onesc": np.ones((1, 128), dtype=np.float32),
        "onescol": np.ones((128, 1), dtype=np.float32),
        "onescolb": np.ones((128, 1), dtype=BF),
    }


def build_l1_ffn(p=None, io=None):
    if p is None:
        p = Prog("l1ffn")
    io = io or {}
    STO0 = p.cnt["sto"]
    xin_d = p.dram_io(io, "xin", [D, NTOK], F32, kind="ExternalInput")
    fs_d = p.dram_io(io, "fsum", [D, D], BF16, kind="ExternalInput")
    bias_d = p.dram_io(io, "bias", [128, 16], F32, kind="ExternalInput")
    ones_d = p.dram_io(io, "onesc", [1, 128], F32, kind="ExternalInput")
    onescolb_d = p.dram_io(io, "onescolb", [128, 1], BF16, kind="ExternalInput")
    xout_d = p.dram_io(io, "xout", [D, NTOK], F32, kind="ExternalOutput")

    xt = [p.sbuf(f"xt{i}", [128, NTOK], F32) for i in range(2)]
    xb = p.sbuf("xb", [128, NTOK], BF16)
    sq = p.sbuf("sq", [128, NTOK], BF16)
    scrA = p.sbuf("scrA", [128, NTOK], F32)
    scrB = p.sbuf("scrB", [128, NTOK], F32)
    scrC = p.sbuf("scrC", [128, NTOK], F32)
    XLN = p.sbuf("XLN", [128, NF * NTOK], BF16)
    wb = [p.sbuf(f"wb{i}", [128, NF * 128], BF16) for i in range(2)]
    hb = p.sbuf("hb", [128, 16], F32)
    ones1 = p.sbuf("ones1", [1, 128], F32)
    onescolb = p.sbuf("onescolbt", [128, 1], BF16)
    rows = p.sbuf("rows", [1, 4 * NTOK], F32)
    bcA = p.sbuf("bcA", [128, NTOK], F32)
    bcB = p.sbuf("bcB", [128, NTOK], F32)

    psQ = [p.psum(f"psQ{i}", [128, 512], F32) for i in range(4)]
    psBC = p.psum("psBC", [128, 512], F32)

    C8 = float(np.float32(0.5 / np.sqrt(np.float64(8.0))))

    def row(i, h=None):
        if h is None:
            return rows[0:1, i * NTOK:(i + 1) * NTOK]
        return rows[0:1, i * NTOK + h * 512:i * NTOK + (h + 1) * 512]

    p.dma("sync", ones1[:], ones_d.ap(), sem="ldc")
    p.dma("sync", onescolb[:], onescolb_d.ap(), sem="ldc")
    lb0 = p.dma("sync", hb[:], bias_d.ap(), sem="ldc")
    LDC = p.cnt["ldc"]
    p.op("vector", "tensor_scalar", hb[:], hb[:], 0.5, None, ALU.mult,
         waits=(("ldc", LDC),))
    HB_DV = p.cnt["dv"]

    def bc_to(dst, row_ap, n=NTOK):
        last = 0
        for h in range(n // 512):
            cpe = p.op("tensor", "matmul", psBC[:, 0:512], ones1[:],
                       row_ap[0:1, h * 512:(h + 1) * 512], start=True, stop=True,
                       sem="pe", waits=(("ac", p.cnt["ac"]), ("dv", p.cnt["dv"]),
                                        ("ldc", LDC)))
            last = p.op("scalar", "copy", dst[:, h * 512:(h + 1) * 512],
                        psBC[:, 0:512], waits=(("pe", cpe),))
        return last

    for i in range(NF):
        buf = xt[i % 2]
        ld = p.dma("sync", buf[:], xin_d.ap()[i * 128:(i + 1) * 128, :], sem="ldx",
                   waits=(("ac", p.cnt["ac"]), ("dv", p.cnt["dv"])))
        cb = p.op("vector", "tensor_copy", xb[:], buf[:],
                  waits=(("ldx", ld), ("pe", p.cnt["pe"])))
        cs = p.op("scalar", "square", sq[:], buf[:],
                  waits=(("ldx", ld), ("pe", p.cnt["pe"])))
        for h in range(2):
            ps_s = psQ[0] if h == 0 else psQ[1]
            ps_q = psQ[2] if h == 0 else psQ[3]
            p.op("tensor", "matmul", ps_s[0:1, 0:512], onescolb[:],
                 xb[:, h * 512:(h + 1) * 512], start=(i == 0), stop=(i == NF - 1),
                 sem="pe", waits=(("dv", cb), ("ldc", LDC)))
            p.op("tensor", "matmul", ps_q[0:1, 0:512], onescolb[:],
                 sq[:, h * 512:(h + 1) * 512], start=(i == 0), stop=(i == NF - 1),
                 sem="pe", waits=(("ac", cs),))
    PE_STATS = p.cnt["pe"]
    for h in range(2):
        p.op("vector", "tensor_scalar", row(0, h), (psQ[0] if h == 0 else psQ[1])[0:1, 0:512],
             1.0 / D, None, ALU.mult, waits=(("pe", PE_STATS),))
        p.op("vector", "tensor_scalar", row(1, h), (psQ[2] if h == 0 else psQ[3])[0:1, 0:512],
             1.0 / D, None, ALU.mult, waits=(("pe", PE_STATS),))
    p.op("vector", "tensor_mul", row(2), row(0), row(0))
    p.op("vector", "tensor_sub", row(1), row(1), row(2))
    p.op("vector", "tensor_scalar", row(1), row(1), 1e-5, None, ALU.add)
    cac = p.op("scalar", "sqrt", row(1), row(1), waits=(("dv", p.cnt["dv"]),))
    p.op("vector", "reciprocal", row(2), row(1), waits=(("ac", cac),))
    p.op("vector", "tensor_mul", row(3), row(0), row(2))
    bc_to(bcA, row(2))
    bc_to(bcB, row(3))
    BC_DONE = p.cnt["ac"]
    for i in range(NF):
        buf = xt[i % 2]
        ld = p.dma("sync", buf[:], xin_d.ap()[i * 128:(i + 1) * 128, :], sem="ldx",
                   waits=(("dv", p.cnt["dv"]),))
        p.op("vector", "tensor_mul", scrA[:], buf[:], bcA[:],
             waits=(("ldx", ld), ("ac", BC_DONE)))
        p.op("vector", "tensor_sub", scrA[:], scrA[:], bcB[:])
        p.op("vector", "tensor_copy", XLN[:, i * NTOK:(i + 1) * NTOK], scrA[:])
    XLN_DONE = p.cnt["dv"]

    pe_end = {}
    ev = {}
    for c in range(NF):
        if io.get("gather_fs", False):
            fsrc = (fs_d.ap().rearrange("(A p) o -> p A o", p=128)
                    [:, (c // 2) * 16:(c // 2 + 1) * 16,
                     (c % 2) * 128:(c % 2 + 1) * 128])
        else:
            fsrc = (fs_d.ap().rearrange("(a p) o -> p a o", p=128)
                    [:, :, c * 128:(c + 1) * 128])
        lw = p.dma("gpsimd", wb[c % 2].ap().rearrange("p (a o) -> p a o", o=128),
                   fsrc, sem=f"ldw{c % 2}", waits=(("pe", pe_end.get(c - 2, 0)),))
        for h in range(2):
            ps = psQ[2 * (c % 2) + h]
            mm = 0
            for i in range(NF):
                mm = p.op("tensor", "matmul", ps[:],
                          wb[c % 2][:, i * 128:(i + 1) * 128],
                          XLN[:, i * NTOK + h * 512:i * NTOK + h * 512 + 512],
                          start=(i == 0), stop=(i == NF - 1), sem="pe",
                          waits=(((f"ldw{c % 2}", lw), ("dv", XLN_DONE),
                                  ("dv", ev.get((c - 2, h), 0))) if i == 0 else ()))
            n = 2 * c + h
            ldx2 = p.dma("sync", xt[c % 2][:, h * 512:(h + 1) * 512],
                         xin_d.ap()[c * 128:(c + 1) * 128, h * 512:(h + 1) * 512],
                         sem="ldx2", waits=(("sto", STO0 + 16 * n), ("dv", XLN_DONE)))
            c1 = p.op("vector", "tensor_scalar", scrB[:, 0:512], ps[:],
                      C8, hb[:, c:c + 1], ALU.mult, ALU.add,
                      waits=(("pe", mm), ("dv", HB_DV)))
            evd = scrC[:, (n % 2) * 512:(n % 2) * 512 + 512]
            c2 = p.op("vector", "tensor_add", evd, scrB[:, 0:512],
                      xt[c % 2][:, h * 512:(h + 1) * 512],
                      waits=(("ldx2", ldx2), ("sto", STO0 + 16 * n)))
            ev[(c, h)] = c2
            p.dma("gpsimd", xout_d.ap()[c * 128:(c + 1) * 128, h * 512:(h + 1) * 512],
                  evd, sem="sto", waits=(("dv", c2),))
        pe_end[c] = p.cnt["pe"]
    p.wait("gpsimd", ("sto", p.cnt["sto"]))
    return p


def host_l1_ffn_inputs(x_fm_c, fsum, ins):
    return {
        "xin": x_fm_c, "fsum": fsum,
        "bias": np.ascontiguousarray(np.float32(ins["L1_ffn_bias"]).reshape(16, 128).T),
        "onesc": np.ones((1, 128), dtype=np.float32),
        "onescolb": np.ones((128, 1), dtype=BF),
    }


def build_l1_attn_a(p=None, io=None):
    if p is None:
        p = Prog("l1attna")
    io = io or {}
    xin_d = p.dram_io(io, "xin", [D, NTOK], F32, kind="ExternalInput")
    wq_d = p.dram_io(io, "wq", [D, D], F32, kind="ExternalInput")
    wk_d = p.dram_io(io, "wk", [D, D], F32, kind="ExternalInput")
    wv_d = p.dram_io(io, "wv", [D, D], F32, kind="ExternalInput")
    bq_d = p.dram_io(io, "bq", [128, 16], F32, kind="ExternalInput")
    bk_d = p.dram_io(io, "bk", [128, 16], F32, kind="ExternalInput")
    bv_d = p.dram_io(io, "bv", [128, 16], F32, kind="ExternalInput")
    ones_d = p.dram_io(io, "onesc", [1, 128], F32, kind="ExternalInput")
    onescolb_d = p.dram_io(io, "onescolb", [128, 1], BF16, kind="ExternalInput")
    qs_o = p.dram_io(io, "qs", [D, NTOK], BF16, kind="ExternalOutput")
    k_o = p.dram_io(io, "ko", [D, NTOK], BF16, kind="ExternalOutput")
    vt_o = p.dram_io(io, "vt", [NTOK, D], BF16, kind="ExternalOutput")

    xt = [p.sbuf(f"xt{i}", [128, NTOK], F32) for i in range(2)]
    xb = p.sbuf("xb", [128, NTOK], BF16)
    sq = p.sbuf("sq", [128, NTOK], BF16)
    scrA = p.sbuf("scrA", [128, NTOK], F32)
    scrB = p.sbuf("scrB", [128, NTOK], F32)
    XLN = p.sbuf("XLN", [128, NF * NTOK], BF16)
    wb = [p.sbuf(f"wb{i}", [128, NF * 128], BF16) for i in range(2)]
    obh = p.sbuf("obh", [128, NTOK], BF16)
    vstage = p.sbuf("vstage", [128, NTOK], BF16)
    vtr = p.sbuf("vtr", [128, NTOK], BF16)
    bqs = p.sbuf("bqs", [128, 48], F32)
    ones1 = p.sbuf("ones1", [1, 128], F32)
    onescolb = p.sbuf("onescolbt", [128, 1], BF16)
    rows = p.sbuf("rows", [1, 4 * NTOK], F32)
    bcA = p.sbuf("bcA", [128, NTOK], F32)
    bcB = p.sbuf("bcB", [128, NTOK], F32)
    idn = p.sbuf("idnf", [128, 128], BF16)
    ones128 = p.sbuf("ones128", [128, 128], BF16)

    psQ = [p.psum(f"psQ{i}", [128, 512], F32) for i in range(4)]
    psBC = p.psum("psBC", [128, 512], F32)
    psTR = p.psum("psTR", [128, 128], BF16)

    def row(i, h=None):
        if h is None:
            return rows[0:1, i * NTOK:(i + 1) * NTOK]
        return rows[0:1, i * NTOK + h * 512:i * NTOK + (h + 1) * 512]

    p.dma("sync", ones1[:], ones_d.ap(), sem="ldc")
    p.dma("sync", onescolb[:], onescolb_d.ap(), sem="ldc")
    p.dma("sync", bqs[:, 0:16], bq_d.ap(), sem="ldc")
    p.dma("sync", bqs[:, 16:32], bk_d.ap(), sem="ldc")
    p.dma("sync", bqs[:, 32:48], bv_d.ap(), sem="ldc")
    LDC = p.cnt["ldc"]
    cmo = p.op("vector", "memset", ones128[:], 1.0, waits=(("ldc", LDC),))
    p.op("gpsimd", "affine_select", idn[:], ones128[:], pattern=[[-1, 128]],
         compare_op=ALU.is_equal, fill=0.0, channel_multiplier=1,
         waits=(("dv", cmo),))
    PL_ID = p.cnt["pl"]

    def bc_to(dst, row_ap, n=NTOK):
        last = 0
        for h in range(n // 512):
            cpe = p.op("tensor", "matmul", psBC[:, 0:512], ones1[:],
                       row_ap[0:1, h * 512:(h + 1) * 512], start=True, stop=True,
                       sem="pe", waits=(("ac", p.cnt["ac"]), ("dv", p.cnt["dv"]),
                                        ("ldc", LDC)))
            last = p.op("scalar", "copy", dst[:, h * 512:(h + 1) * 512],
                        psBC[:, 0:512], waits=(("pe", cpe),))
        return last

    for i in range(NF):
        buf = xt[i % 2]
        ld = p.dma("sync", buf[:], xin_d.ap()[i * 128:(i + 1) * 128, :], sem="ldx",
                   waits=(("ac", p.cnt["ac"]), ("dv", p.cnt["dv"])))
        cb = p.op("vector", "tensor_copy", xb[:], buf[:],
                  waits=(("ldx", ld), ("pe", p.cnt["pe"])))
        cs = p.op("scalar", "square", sq[:], buf[:],
                  waits=(("ldx", ld), ("pe", p.cnt["pe"])))
        for h in range(2):
            ps_s = psQ[0] if h == 0 else psQ[1]
            ps_q = psQ[2] if h == 0 else psQ[3]
            p.op("tensor", "matmul", ps_s[0:1, 0:512], onescolb[:],
                 xb[:, h * 512:(h + 1) * 512], start=(i == 0), stop=(i == NF - 1),
                 sem="pe", waits=(("dv", cb), ("ldc", LDC)))
            p.op("tensor", "matmul", ps_q[0:1, 0:512], onescolb[:],
                 sq[:, h * 512:(h + 1) * 512], start=(i == 0), stop=(i == NF - 1),
                 sem="pe", waits=(("ac", cs),))
    PE_STATS = p.cnt["pe"]
    for h in range(2):
        p.op("vector", "tensor_scalar", row(0, h), (psQ[0] if h == 0 else psQ[1])[0:1, 0:512],
             1.0 / D, None, ALU.mult, waits=(("pe", PE_STATS),))
        p.op("vector", "tensor_scalar", row(1, h), (psQ[2] if h == 0 else psQ[3])[0:1, 0:512],
             1.0 / D, None, ALU.mult, waits=(("pe", PE_STATS),))
    p.op("vector", "tensor_mul", row(2), row(0), row(0))
    p.op("vector", "tensor_sub", row(1), row(1), row(2))
    p.op("vector", "tensor_scalar", row(1), row(1), 1e-5, None, ALU.add)
    cac = p.op("scalar", "sqrt", row(1), row(1), waits=(("dv", p.cnt["dv"]),))
    p.op("vector", "reciprocal", row(2), row(1), waits=(("ac", cac),))
    p.op("vector", "tensor_mul", row(3), row(0), row(2))
    bc_to(bcA, row(2))
    bc_to(bcB, row(3))
    BC_DONE = p.cnt["ac"]
    for i in range(NF):
        buf = xt[i % 2]
        ld = p.dma("sync", buf[:], xin_d.ap()[i * 128:(i + 1) * 128, :], sem="ldx",
                   waits=(("dv", p.cnt["dv"]),))
        p.op("vector", "tensor_mul", scrA[:], buf[:], bcA[:],
             waits=(("ldx", ld), ("ac", BC_DONE)))
        p.op("vector", "tensor_sub", scrA[:], scrA[:], bcB[:])
        p.op("vector", "tensor_copy", XLN[:, i * NTOK:(i + 1) * NTOK], scrA[:])
    XLN_DONE = p.cnt["dv"]

    W = [wq_d, wk_d, wv_d]
    pe_end = {}
    pe_ev_ac = 0
    pe_ev_dv = 0
    for c in range(48):
        kind = c // 16
        head = c % 16
        lw = p.dma("gpsimd", wb[c % 2].ap().rearrange("p (a o) -> p a o", o=128),
                   W[kind].ap().rearrange("(a p) o -> p a o", p=128)[:, :, head * 128:(head + 1) * 128],
                   sem=f"ldw{c % 2}", waits=(("pe", pe_end.get(c - 2, 0)),))
        mm = {}
        for h in range(2):
            ps = psQ[2 * (c % 2) + h]
            for i in range(NF):
                mm[h] = p.op("tensor", "matmul", ps[:],
                             wb[c % 2][:, i * 128:(i + 1) * 128],
                             XLN[:, i * NTOK + h * 512:i * NTOK + h * 512 + 512],
                             start=(i == 0), stop=(i == NF - 1), sem="pe",
                             waits=(((f"ldw{c % 2}", lw), ("dv", XLN_DONE),
                                     ("ac", pe_ev_ac), ("dv", pe_ev_dv))
                                    if (i == 0 and h == 0) else ()))
        bcol = bqs[:, kind * 16 + head:kind * 16 + head + 1]
        if kind == 2:
            for h in range(2):
                ps = psQ[2 * (c % 2) + h]
                p.op("vector", "tensor_scalar", vstage[:, h * 512:(h + 1) * 512],
                     ps[:], bcol, None, ALU.add,
                     waits=(("pe", mm[h]), ("pe", p.cnt["pe"]), ("ldc", LDC)))
            for j in range(8):
                cpe = p.op("tensor", "transpose", psTR[:],
                           vstage[:, j * 128:(j + 1) * 128], idn[:], sem="pe",
                           waits=(("dv", p.cnt["dv"]), ("ac", p.cnt["ac"]),
                                  ("pl", PL_ID)))
                p.op("scalar", "copy", vtr[:, j * 128:(j + 1) * 128], psTR[:],
                     waits=(("pe", cpe), ("sto", p.cnt["sto"])))
            pe_ev_ac, pe_ev_dv = p.cnt["ac"], p.cnt["dv"]
            for j in range(8):
                p.dma("gpsimd", vt_o.ap()[j * 128:(j + 1) * 128,
                                          head * 128:(head + 1) * 128],
                      vtr[:, j * 128:(j + 1) * 128], sem="sto",
                      waits=(("ac", p.cnt["ac"]),))
        else:
            for h in range(2):
                ps = psQ[2 * (c % 2) + h]
                if kind == 0:
                    p.op("vector", "tensor_scalar", scrA[:, h * 512:(h + 1) * 512],
                         ps[:], bcol, None, ALU.add,
                         waits=(("pe", mm[h]), ("ldc", LDC), ("sto", p.cnt["sto"])))
                else:
                    p.op("vector", "tensor_scalar", obh[:, h * 512:(h + 1) * 512],
                         ps[:], bcol, None, ALU.add,
                         waits=(("pe", mm[h]), ("ldc", LDC), ("sto", p.cnt["sto"])))
            if kind == 0:
                p.op("vector", "tensor_scalar", scrB[:], scrA[:],
                     float(SC_HI), None, ALU.mult)
                p.op("vector", "scalar_tensor_tensor", obh[:], scrA[:],
                     float(SC_LO), scrB[:], ALU.mult, ALU.add)
            pe_ev_ac, pe_ev_dv = p.cnt["ac"], p.cnt["dv"]
            p.dma("gpsimd", (qs_o if kind == 0 else k_o).ap()[head * 128:(head + 1) * 128, :],
                  obh[:], sem="sto", waits=(("dv", p.cnt["dv"]),))
        pe_end[c] = p.cnt["pe"]
    p.wait("gpsimd", ("sto", p.cnt["sto"]))
    return p


def host_l1_attn_a_inputs(x_fm_c, ins):
    rb = lambda k: np.ascontiguousarray(np.float32(ins[k]).reshape(16, 128).T)
    return {
        "xin": x_fm_c,
        "wq": np.ascontiguousarray(np.float32(ins["L1_attn_Wq"])),
        "wk": np.ascontiguousarray(np.float32(ins["L1_attn_Wk"])),
        "wv": np.ascontiguousarray(np.float32(ins["L1_attn_Wv"])),
        "bq": rb("L1_attn_bq"), "bk": rb("L1_attn_bk"), "bv": rb("L1_attn_bv"),
        "onesc": np.ones((1, 128), dtype=np.float32),
        "onescolb": np.ones((128, 1), dtype=BF),
    }


def build_l1_attn_b(p=None, io=None):
    if p is None:
        p = Prog("l1attnb")
    io = io or {}
    qs_d = p.dram_io(io, "qs", [D, NTOK], BF16, kind="ExternalInput")
    kf_d = p.dram_io(io, "kf", [D, T], BF16, kind="ExternalInput")
    vf_d = p.dram_io(io, "vf", [T, D], BF16, kind="ExternalInput")
    ones_d = p.dram_io(io, "onesc", [1, 128], F32, kind="ExternalInput")
    onescol_d = p.dram_io(io, "onescol", [128, 1], F32, kind="ExternalInput")
    oc_o = p.dram_io(io, "oc", [128, NTOK * 16], BF16, kind="ExternalOutput")

    QS = p.sbuf("QS", [128, NF * NTOK], BF16)
    KT = [p.sbuf(f"KT{i}", [128, T], BF16) for i in range(2)]
    VT = [p.sbuf(f"VT{i}", [128, 32 * 128], BF16) for i in range(2)]
    PT = [p.sbuf(f"PT{i}", [128, 512], BF16) for i in range(2)]
    dacc = p.sbuf("dacc", [128, 512], F32)
    OC = p.sbuf("OC", [128, NTOK * 16], BF16)
    ones1 = p.sbuf("ones1", [1, 128], F32)
    onescol = p.sbuf("onescolt", [128, 1], F32)
    rows = p.sbuf("rows", [1, NTOK], F32)
    bcR = p.sbuf("bcR", [128, 512], F32)
    negc = p.sbuf("negc", [128, 1], F32)

    psS = [p.psum(f"psS{i}", [128, 512], F32) for i in range(2)]
    psY = [p.psum(f"psY{i}", [128, 512], F32) for i in range(2)]
    psBC = p.psum("psBC", [128, 512], F32)
    psR = p.psum("psR", [1, 512], F32)

    p.dma("sync", ones1[:], ones_d.ap(), sem="ldc")
    p.dma("sync", onescol[:], onescol_d.ap(), sem="ldc")
    LDC = p.cnt["ldc"]
    NEG_DV = p.op("vector", "memset", negc[:], -20.0)
    for i in range(NF):
        p.dma("scalar", QS[:, i * NTOK:(i + 1) * NTOK],
              qs_d.ap()[i * 128:(i + 1) * 128, :], sem="ldq")
    LDQ = p.cnt["ldq"]

    kvg = io.get("kv_gathered", False)
    unit_pe_end = {}
    for h in range(NH):
        kbuf = KT[h % 2]
        vbuf = VT[h % 2]
        if kvg:
            lk = lv = 0
            for g in range(8):
                cc, half = min(g, 7 - g), (0 if g < 4 else 1)
                lk = p.dma("sync", kbuf[:, g * 512:(g + 1) * 512],
                           kf_d.ap()[cc * D + h * 128:cc * D + (h + 1) * 128,
                                     half * 512:(half + 1) * 512],
                           sem=f"ldk{h % 2}",
                           waits=((("pe", unit_pe_end.get(h - 2, 0)),)
                                  if g == 0 else ()))
                lv = p.dma("scalar",
                           vbuf.ap().rearrange("p (a o) -> p a o", o=128)[:, 4 * g:4 * g + 4, :],
                           vf_d.ap().rearrange("(x p) d -> p x d", p=128)
                           [:, cc * 8 + half * 4:cc * 8 + half * 4 + 4,
                            h * 128:(h + 1) * 128],
                           sem=f"ldv{h % 2}",
                           waits=((("pe", unit_pe_end.get(h - 2, 0)), ("ldq", LDQ))
                                  if g == 0 else ()))
        else:
            lk = p.dma("sync", kbuf[:], kf_d.ap()[h * 128:(h + 1) * 128, :],
                       sem=f"ldk{h % 2}", waits=(("pe", unit_pe_end.get(h - 2, 0)),))
            lv = p.dma("scalar", vbuf.ap().rearrange("p (a o) -> p a o", o=128),
                       vf_d.ap().rearrange("(a p) d -> p a d", p=128)[:, :, h * 128:(h + 1) * 128],
                       sem=f"ldv{h % 2}", waits=(("pe", unit_pe_end.get(h - 2, 0)), ("ldq", LDQ)))
        qrow = QS[:, h * NTOK:(h + 1) * NTOK]
        for qc in range(2):
            qsl = qrow[:, qc * 512:(qc + 1) * 512]
            mmy = {}
            exs = {}
            das = {}
            yevs = {}
            for kt in range(32):
                first = (kt == 0)
                pss = psS[kt % 2]
                pt = PT[kt % 2]
                cpe = p.op("tensor", "matmul", pss[:],
                           kbuf[:, kt * 128:(kt + 1) * 128], qsl,
                           start=True, stop=True, sem="pe",
                           waits=(((f"ldk{h % 2}", lk), ("ldq", LDQ),
                                   ("ac", p.cnt["ac"]), ("dv", p.cnt["dv"]))
                                  if first else
                                  (("ac", exs.get(kt - 2, 0)),)))
                cex = p.op("scalar", "activation", pt[:], pss[:], AF.Exp,
                           bias=negc[:, 0:1], scale=1.0,
                           waits=(("pe", cpe), ("dv", das.get(kt - 2, 0)),
                                  ("pe", mmy.get(kt - 2, 0)), ("dv", NEG_DV)))
                exs[kt] = cex
                if first:
                    cda = p.op("vector", "tensor_copy", dacc[:], pt[:],
                               waits=(("ac", cex),))
                else:
                    cda = p.op("vector", "tensor_add", dacc[:], dacc[:], pt[:],
                               waits=(("ac", cex),))
                das[kt] = cda
                mmy[kt] = p.op("tensor", "matmul", psY[qc][:],
                               vbuf[:, kt * 128:(kt + 1) * 128],
                               pt[:], start=first, stop=(kt == 31), sem="pe",
                               waits=(((f"ldv{h % 2}", lv), ("ac", cex),
                                       ("dv", yevs.get("prev", 0)))
                                      if first else (("ac", cex),)))
            cdr = p.op("tensor", "matmul", psR[0:1, 0:512], onescol[:], dacc[:],
                       start=True, stop=True, sem="pe",
                       waits=(("dv", p.cnt["dv"]), ("ldc", LDC)))
            p.op("vector", "reciprocal", rows[0:1, 0:512], psR[0:1, 0:512],
                 waits=(("pe", cdr),))
            cpe2 = p.op("tensor", "matmul", psBC[:, 0:512], ones1[:],
                        rows[0:1, 0:512], start=True, stop=True, sem="pe",
                        waits=(("ac", p.cnt["ac"]), ("dv", p.cnt["dv"]), ("ldc", LDC)))
            cbc = p.op("scalar", "copy", bcR[:], psBC[:, 0:512], waits=(("pe", cpe2),))
            ocv = bass.AP(OC, h + qc * 512 * 16,
                          [[NTOK * 16, 128], [16, 512]])
            yevs["prev"] = p.op("vector", "tensor_mul", ocv, psY[qc][:], bcR[:],
                                waits=(("ac", cbc), ("pe", mmy[31]),
                                       ("sto", p.cnt["sto"])))
        unit_pe_end[h] = p.cnt["pe"]
    OCD = p.cnt["dv"]
    p.dma("gpsimd", oc_o.ap(), OC[:], sem="sto", waits=(("dv", OCD),))
    p.wait("gpsimd", ("sto", p.cnt["sto"]))
    return p


def host_l1_attn_b_inputs(qs, kfull, vfull):
    return {
        "qs": qs, "kf": kfull, "vf": vfull,
        "onesc": np.ones((1, 128), dtype=np.float32),
        "onescol": np.ones((128, 1), dtype=np.float32),
    }


def build_l1_attn_c(p=None, io=None):
    if p is None:
        p = Prog("l1attnc")
    io = io or {}
    STO0 = p.cnt["sto"]
    xin_d = p.dram_io(io, "xin", [D, NTOK], F32, kind="ExternalInput")
    y_d = p.dram_io(io, "yrows", [NTOK, D], BF16, kind="ExternalInput")
    wo_d = p.dram_io(io, "wo", [D, D], F32, kind="ExternalInput")
    bo_d = p.dram_io(io, "bo", [128, 16], F32, kind="ExternalInput")
    xout_d = p.dram_io(io, "xout", [D, NTOK], F32, kind="ExternalOutput")

    Y = p.sbuf("Y", [128, 8 * D], BF16)
    YT = p.sbuf("YT", [128, NF * NTOK], BF16)
    wb = [p.sbuf(f"wb{i}", [128, NF * 128], BF16) for i in range(2)]
    xt = [p.sbuf(f"xt{i}", [128, NTOK], F32) for i in range(2)]
    scrB = p.sbuf("scrB", [128, NTOK], F32)
    scrC = p.sbuf("scrC", [128, NTOK], F32)
    bos = p.sbuf("bos", [128, 16], F32)
    ones128 = p.sbuf("ones128", [128, 128], BF16)
    idn = p.sbuf("idnf", [128, 128], BF16)

    psQ = [p.psum(f"psQ{i}", [128, 512], F32) for i in range(4)]
    psTR = p.psum("psTR", [128, 128], BF16)

    p.dma("sync", bos[:], bo_d.ap(), sem="ldc")
    LDC = p.cnt["ldc"]
    cmo = p.op("vector", "memset", ones128[:], 1.0)
    p.op("gpsimd", "affine_select", idn[:], ones128[:], pattern=[[-1, 128]],
         compare_op=ALU.is_equal, fill=0.0, channel_multiplier=1,
         waits=(("dv", cmo),))
    PL_ID = p.cnt["pl"]
    for j in range(8):
        p.dma("sync", Y[:, j * D:(j + 1) * D], y_d.ap()[j * 128:(j + 1) * 128, :],
              sem="ldy")
    LDY = p.cnt["ldy"]
    for j in range(8):
        for i in range(NF):
            cpe = p.op("tensor", "transpose", psTR[:],
                       Y[:, j * D + i * 128:j * D + (i + 1) * 128], idn[:],
                       sem="pe", waits=(("ldy", LDY), ("pl", PL_ID),
                                        ("ac", p.cnt["ac"])))
            p.op("scalar", "copy", YT[:, i * NTOK + j * 128:i * NTOK + (j + 1) * 128],
                 psTR[:], waits=(("pe", cpe),))
    YT_AC = p.cnt["ac"]

    pe_end = {}
    ev = {}
    for c in range(NF):
        lw = p.dma("gpsimd", wb[c % 2].ap().rearrange("p (a o) -> p a o", o=128),
                   wo_d.ap().rearrange("(a p) o -> p a o", p=128)[:, :, c * 128:(c + 1) * 128],
                   sem=f"ldw{c % 2}", waits=(("pe", pe_end.get(c - 2, 0)),))
        for h in range(2):
            ps = psQ[2 * (c % 2) + h]
            mm = 0
            for i in range(NF):
                mm = p.op("tensor", "matmul", ps[:],
                          wb[c % 2][:, i * 128:(i + 1) * 128],
                          YT[:, i * NTOK + h * 512:i * NTOK + h * 512 + 512],
                          start=(i == 0), stop=(i == NF - 1), sem="pe",
                          waits=(((f"ldw{c % 2}", lw), ("ac", YT_AC),
                                  ("dv", ev.get((c - 2, h), 0))) if i == 0 else ()))
            n = 2 * c + h
            ldx2 = p.dma("sync", xt[c % 2][:, h * 512:(h + 1) * 512],
                         xin_d.ap()[c * 128:(c + 1) * 128, h * 512:(h + 1) * 512],
                         sem="ldx2", waits=(("sto", STO0 + 16 * n),))
            c1 = p.op("vector", "tensor_scalar", scrB[:, 0:512], ps[:],
                      bos[:, c:c + 1], None, ALU.add,
                      waits=(("pe", mm), ("ldc", LDC)))
            evd = scrC[:, (n % 2) * 512:(n % 2) * 512 + 512]
            c2 = p.op("vector", "tensor_add", evd, scrB[:, 0:512],
                      xt[c % 2][:, h * 512:(h + 1) * 512],
                      waits=(("ldx2", ldx2), ("sto", STO0 + 16 * n)))
            ev[(c, h)] = c2
            p.dma("gpsimd", xout_d.ap()[c * 128:(c + 1) * 128, h * 512:(h + 1) * 512],
                  evd, sem="sto", waits=(("dv", c2),))
        pe_end[c] = p.cnt["pe"]
    p.wait("gpsimd", ("sto", p.cnt["sto"]))
    return p


def host_l1_attn_c_inputs(x_fm_c, yrows, ins):
    return {
        "xin": x_fm_c, "yrows": yrows,
        "wo": np.ascontiguousarray(np.float32(ins["L1_attn_Wo"])),
        "bo": np.ascontiguousarray(np.float32(ins["L1_attn_bo"]).reshape(16, 128).T),
    }


def build_ydist(p, io):
    ocg_d = io["ocg"]
    y_d = io["yrows"]
    ytbl_d = p.dram_io(io, "ytbl", [1, 8], I32, kind="ExternalInput")
    ocx_d = p.dram("ocx", [128, NTOK * 16], BF16, kind="Internal")
    ytbl = p.sbuf("ytbl_t", [1, 8], I32)
    p.dma("scalar", ytbl[:], ytbl_d.ap(), sem="yld")
    YLD = p.cnt["yld"]

    vals = {}

    def sc_hoist(engine):
        engine.wait_ge(p.semh("yld"), YLD)
        for i in range(8):
            r = engine.alloc_register()
            engine.reg_load(r, ytbl[0:1, i:i + 1])
            vals[i] = engine.snap(r, True, min_val=0, max_val=4 * 128 - 16)
    p.raw("scalar", sc_hoist)
    h1 = 0
    for i in range(8):
        h1 = p.dma("scalar", ocx_d.ap()[i * 16:(i + 1) * 16, :],
                   (lambda _i=i: ocg_d.ap()[bass.ds(vals[_i], 16), :]),
                   sem="yx1")
    for half in range(2):
        for gqblk in range(32):
            g, qb = gqblk // 4, gqblk % 4
            cc, half2 = min(g, 7 - g), (0 if g < 4 else 1)
            lo = (half2 * 512 + qb * 128) * 16
            src = bass.AP(ocx_d, (half * 64 + cc * 16) * (NTOK * 16) + lo,
                          [[NTOK * 16, 16], [1, 2048]])
            dst = bass.AP(y_d, (half * 512 + gqblk) * D, [[32 * D, 16], [1, 2048]])
            p.dma("gpsimd", dst, src, sem="yx2",
                  waits=((("yx1", h1),) if (half, gqblk) == (0, 0) else ()))
    return p.cnt["yx2"]


def host_ydist_table(c):
    cp = c % 4
    t = np.zeros((1, 8), dtype=np.int32)
    for half in range(2):
        dblk = cp if half == 0 else (7 - cp)
        for cc in range(4):
            t[0, half * 4 + cc] = cc * 128 + dblk * 16
    return t


def host_assemble_Y(oc_shards):
    Ys = []
    for b in range(B):
        Y = np.empty((T, D), dtype=BF)
        for cc in range(4):
            c = b * 4 + cc
            oc = np.asarray(oc_shards[c])
            s1, s2 = core_token_slices(c)
            for half, sl in ((0, s1), (1, s2)):
                for qb in range(4):
                    gqblk = sl.start // 128 + qb
                    lo = (half * 512 + qb * 128) * 16
                    blk = oc[:, lo:lo + 128 * 16]
                    Y[gqblk::32, :][0:0] if False else None
                    Y[np.arange(128) * 32 + gqblk, :] = blk
        Ys.append(Y)
    return Ys


G4 = [[0, 1, 2, 3], [4, 5, 6, 7]]
G8 = [list(range(8))]


def build_fused():
    p = Prog("FUSED")
    p.begin_stage("pp")
    tg_s = p.dram("tg_s", [D, 1024], BF16)
    tu_s = p.dram("tu_s", [D, 1024], BF16)
    td_s = p.dram("td_s", [1024, D], BF16)
    fs_s = p.dram("fs_s", [D, 256], BF16)
    mv_d = p.dram("mv", [1, 4], F32)
    build_prep(p, {"tg_T": tg_s, "tu_T": tu_s, "td_T": td_s,
                   "fsum": fs_s, "mvals": mv_d})
    tg_g = p.dram("tg_g", [8 * D, 1024], BF16)
    tu_g = p.dram("tu_g", [8 * D, 1024], BF16)
    td_g = p.dram("td_g", [8 * 1024, D], BF16)
    fs_g = p.dram("fs_g", [8 * D, 256], BF16)
    p.allgather(tg_s, tg_g, G8, "ccw", waits=(("sto", p.cnt["sto"]),))
    p.allgather(tu_s, tu_g, G8, "ccw")
    p.allgather(td_s, td_g, G8, "ccw")
    p.allgather(fs_s, fs_g, G8, "ccw")
    wio = {"tgT": tg_g, "tuT": tu_g, "tdT": td_g, "mv": mv_d, "gather_w": True}
    p.begin_stage("f1")
    xh1 = p.dram("xh1", [D, NTOK], F32)
    build_l0_ffn(p, dict(wio, xout=xh1))
    p.begin_stage("a1")
    qs1 = p.dram("qs1", [D, NTOK], BF16)
    ko1 = p.dram("ko1", [D, NTOK], BF16)
    vt1 = p.dram("vt1", [NTOK, D], BF16)
    go1 = p.dram("go1", [D, NTOK], BF16)
    build_l0_attn_a(p, {"xin": xh1, "qs": qs1, "ko": ko1, "vt": vt1, "go": go1})
    kg1 = p.dram("kg1", [4 * D, NTOK], BF16)
    vg1 = p.dram("vg1", [4 * NTOK, D], BF16)
    p.allgather(ko1, kg1, G4, "cck", waits=(("sto", p.cnt["sto"]),))
    p.allgather(vt1, vg1, G4, "cck")
    p.begin_stage("ab")
    xh2 = p.dram("xh2", [D, NTOK], F32)
    build_l0_attn_bu(p, {"xin": xh1, "qs": qs1, "kf": kg1, "vf": vg1,
                         "go": go1, "xout": xh2, "kv_gathered": True})
    p.begin_stage("f2")
    xh3 = p.dram("xh3", [D, NTOK], F32)
    build_l0_ffn(p, dict(wio, xin=xh2, xout=xh3))
    p.begin_stage("f3")
    xh4 = p.dram("xh4", [D, NTOK], F32)
    build_l1_ffn(p, {"xin": xh3, "xout": xh4, "fsum": fs_g, "gather_fs": True})
    p.begin_stage("a2")
    qs2 = p.dram("qs2", [D, NTOK], BF16)
    ko2 = p.dram("ko2", [D, NTOK], BF16)
    vt2 = p.dram("vt2", [NTOK, D], BF16)
    build_l1_attn_a(p, {"xin": xh4, "qs": qs2, "ko": ko2, "vt": vt2})
    kg2 = p.dram("kg2", [4 * D, NTOK], BF16)
    vg2 = p.dram("vg2", [4 * NTOK, D], BF16)
    p.allgather(ko2, kg2, G4, "cck", waits=(("sto", p.cnt["sto"]),))
    p.allgather(vt2, vg2, G4, "cck")
    p.begin_stage("bb")
    oc_d = p.dram("oc", [128, NTOK * 16], BF16)
    build_l1_attn_b(p, {"qs": qs2, "kf": kg2, "vf": vg2, "oc": oc_d,
                        "kv_gathered": True})
    ocg = p.dram("ocg", [4 * 128, NTOK * 16], BF16)
    p.allgather(oc_d, ocg, G4, "cco", waits=(("sto", p.cnt["sto"]),))
    p.begin_stage("yy")
    y_d = p.dram("yrows", [NTOK, D], BF16)
    build_ydist(p, {"ocg": ocg, "yrows": y_d})
    p.begin_stage("ac")
    xh5 = p.dram("xh5", [D, NTOK], F32)
    build_l1_attn_c(p, {"xin": xh4, "yrows": y_d, "xout": xh5})
    p.begin_stage("f4")
    build_l1_ffn(p, {"xin": xh5, "fsum": fs_g, "gather_fs": True})
    return p


def _asf(a, dt=None):
    a = np.asarray(a)
    if a.dtype == np.uint16:
        a = a.view(BF)
    if dt is not None:
        a = np.asarray(a, dt)
    return a


def _pref(tag, m, skip=()):
    return {tag + k: v for k, v in m.items() if k not in skip}


def _run(build_fn, in_maps, core_ids=None):
    p = build_fn() if callable(build_fn) else build_fn
    nc = p.finalize()
    if core_ids is None:
        core_ids = list(range(len(in_maps)))
    res = run_bass_kernel_spmd(nc, in_maps, core_ids=core_ids)
    return res.results


def kernel(**ins):
    x0 = np.asarray(ins["L0_attn_hidden_states"], np.float32)
    lnw_dev = np.ascontiguousarray(
        np.float32(ins["L0_ffn_ln_weight"]).reshape(64, 128).T)

    p = build_fused()
    zq = np.zeros((1, 1), dtype=np.float32)
    im = []
    for c in range(NCORE):
        xfc = np.ascontiguousarray(fm(shard_tokens(x0, c)))
        base_ffn = {"lnw": lnw_dev,
                    "onesc": np.ones((1, 128), dtype=np.float32),
                    "onescolb": np.ones((128, 1), dtype=BF)}
        m = {}
        m.update(_pref("pp_", host_prep_inputs(ins, c)))
        m.update(_pref("f1_", dict(base_ffn, xin=xfc)))
        m.update(_pref("a1_", host_attn_a_inputs(xfc, ins, c), skip=("xin",)))
        ab = host_attn_b_inputs(zq, zq, zq, zq, zq, ins)
        m.update(_pref("ab_", ab, skip=("xin", "qs", "kf", "vf", "go")))
        m["ab_tbl"] = host_attn_b_table(c, gathered=True)
        m.update(_pref("f2_", base_ffn))
        l1f = host_l1_ffn_inputs(zq, zq, ins)
        m.update(_pref("f3_", l1f, skip=("xin", "fsum")))
        m.update(_pref("a2_", host_l1_attn_a_inputs(zq, ins), skip=("xin",)))
        m.update(_pref("bb_", host_l1_attn_b_inputs(zq, zq, zq),
                       skip=("qs", "kf", "vf")))
        m["yy_ytbl"] = host_ydist_table(c)
        m.update(_pref("ac_", host_l1_attn_c_inputs(zq, zq, ins),
                       skip=("xin", "yrows")))
        m.update(_pref("f4_", l1f, skip=("xin", "fsum")))
        im.append(m)
    res = _run(p, im)
    xs = [np.ascontiguousarray(_asf(res[c]["f4_xout"], np.float32).T)
          for c in range(NCORE)]
    out = unshard_tokens(xs)
    return np.ascontiguousarray(out.astype(np.float32))


def kernel_5launch(**ins):
    x0 = np.asarray(ins["L0_attn_hidden_states"], np.float32)

    prep_res = _run(build_prep, [host_prep_inputs(ins, c) for c in range(NCORE)])
    tgT = np.ascontiguousarray(np.concatenate(
        [_asf(prep_res[c]["tg_T"]) for c in range(NCORE)], axis=1).astype(BF))
    tuT = np.ascontiguousarray(np.concatenate(
        [_asf(prep_res[c]["tu_T"]) for c in range(NCORE)], axis=1).astype(BF))
    tdT = np.ascontiguousarray(np.concatenate(
        [_asf(prep_res[c]["td_T"]) for c in range(NCORE)], axis=0).astype(BF))
    fsum = np.ascontiguousarray(np.concatenate(
        [_asf(prep_res[c]["fsum"]) for c in range(NCORE)], axis=1).astype(BF))
    mvals = [np.ascontiguousarray(_asf(prep_res[c]["mvals"], np.float32))
             for c in range(NCORE)]
    lnw_dev = np.ascontiguousarray(
        np.float32(ins["L0_ffn_ln_weight"]).reshape(64, 128).T)

    xf = [np.ascontiguousarray(fm(shard_tokens(x0, c))) for c in range(NCORE)]

    def assemble_kv(res, kname, vname):
        kfull, vfull = [], []
        for b in range(B):
            kf = np.empty((D, T), dtype=BF)
            vf = np.empty((T, D), dtype=BF)
            for cc in range(4):
                c = b * 4 + cc
                s1, s2 = core_token_slices(c)
                ko = _asf(res[c][kname]).astype(BF)
                vt = _asf(res[c][vname]).astype(BF)
                kf[:, s1] = ko[:, :CH]; kf[:, s2] = ko[:, CH:]
                vf[s1, :] = vt[:CH]; vf[s2, :] = vt[CH:]
            kfull.append(np.ascontiguousarray(kf))
            vfull.append(np.ascontiguousarray(vf))
        return kfull, vfull

    p = Prog("L2")
    p.begin_stage("f1")
    xh1 = p.dram("xh", [D, NTOK], F32, kind="ExternalOutput")
    build_l0_ffn(p, {"xout": xh1})
    p.begin_stage("a1")
    build_l0_attn_a(p, {"xin": xh1})
    im = []
    for c in range(NCORE):
        m = _pref("f1_", host_ffn_inputs(xf[c], tgT, tuT, tdT, lnw_dev, mvals[c]))
        m.update(_pref("a1_", host_attn_a_inputs(xf[c], ins, c), skip=("xin",)))
        im.append(m)
    res_a = _run(p, im)
    for c in range(NCORE):
        xf[c] = np.ascontiguousarray(_asf(res_a[c]["f1_xh"], np.float32))

    kfull, vfull = assemble_kv(res_a, "a1_ko", "a1_vt")

    p = Prog("L3")
    p.begin_stage("ab")
    xh2 = p.dram("xh2", [D, NTOK], F32)
    build_l0_attn_bu(p, {"xout": xh2})
    p.begin_stage("f2")
    xh3 = p.dram("xh3", [D, NTOK], F32)
    build_l0_ffn(p, {"xin": xh2, "xout": xh3})
    p.begin_stage("f3")
    xh4 = p.dram("xh4", [D, NTOK], F32, kind="ExternalOutput")
    build_l1_ffn(p, {"xin": xh3, "xout": xh4})
    p.begin_stage("a2")
    build_l1_attn_a(p, {"xin": xh4})
    im = []
    for c in range(NCORE):
        b = c // 4
        m = host_attn_b_inputs(
            xf[c], np.ascontiguousarray(_asf(res_a[c]["a1_qs"]).astype(BF)),
            kfull[b], vfull[b],
            np.ascontiguousarray(_asf(res_a[c]["a1_go"]).astype(BF)), ins)
        m["tbl"] = host_attn_b_table(c)
        m = _pref("ab_", m)
        m.update(_pref("f2_", host_ffn_inputs(xf[c], tgT, tuT, tdT, lnw_dev,
                                              mvals[c]), skip=("xin",)))
        m.update(_pref("f3_", host_l1_ffn_inputs(xf[c], fsum, ins),
                       skip=("xin",)))
        m.update(_pref("a2_", host_l1_attn_a_inputs(xf[c], ins), skip=("xin",)))
        im.append(m)
    res_a = _run(p, im)
    for c in range(NCORE):
        xf[c] = np.ascontiguousarray(_asf(res_a[c]["f3_xh4"], np.float32))

    kfull, vfull = assemble_kv(res_a, "a2_ko", "a2_vt")
    im = []
    for c in range(NCORE):
        b = c // 4
        im.append(host_l1_attn_b_inputs(
            np.ascontiguousarray(_asf(res_a[c]["a2_qs"]).astype(BF)),
            kfull[b], vfull[b]))
    res_b = _run(build_l1_attn_b, im)
    Ys = host_assemble_Y([_asf(res_b[c]["oc"]).astype(BF) for c in range(NCORE)])
    Ybtd = np.stack(Ys).reshape(B, T, D)

    p = Prog("L5")
    p.begin_stage("ac")
    xh5 = p.dram("xh5", [D, NTOK], F32)
    build_l1_attn_c(p, {"xout": xh5})
    p.begin_stage("f4")
    build_l1_ffn(p, {"xin": xh5})
    im = []
    for c in range(NCORE):
        m = _pref("ac_", host_l1_attn_c_inputs(
            xf[c], np.ascontiguousarray(shard_tokens(Ybtd, c)), ins))
        m.update(_pref("f4_", host_l1_ffn_inputs(xf[c], fsum, ins),
                       skip=("xin",)))
        im.append(m)
    res = _run(p, im)
    for c in range(NCORE):
        xf[c] = np.ascontiguousarray(_asf(res[c]["f4_xout"], np.float32))

    out = unshard_tokens([np.ascontiguousarray(xf[c].T) for c in range(NCORE)])
    return np.ascontiguousarray(out.astype(np.float32))


# version 09 (final)
# speedup vs baseline: 1.0790x; 1.0228x over previous
import math
from collections import defaultdict
from contextlib import ExitStack

import numpy as np
import ml_dtypes

import concourse.bass as bass
import concourse.mybir as mybir
from concourse.bass_utils import run_bass_kernel_spmd

F32 = mybir.dt.float32
BF16 = mybir.dt.bfloat16
I32 = mybir.dt.int32
AF = mybir.ActivationFunctionType
ALU = mybir.AluOpType
AXX = mybir.AxisListType.X

BF = ml_dtypes.bfloat16

D = 2048
T = 4096
B = 2
NH = 16
HD = 128
HP = 8
DFF = 8192
NCORE = 8
NTOK = 1024
CH = 512
LAMBDA_INIT = 0.8 - 0.6 * float(np.exp(-0.3 * 6))
SC_HI = np.float32(np.float64(HD) ** -0.5)
SC_LO = np.float32(np.float64(HD) ** -0.5 - np.float64(SC_HI))
PI_LIM = np.float32(3.1415925)

FIXUP_ROUND = False


def core_chunks(c):
    p = c % 4
    return (p, 7 - p)


def core_token_slices(c):
    a, b_ = core_chunks(c)
    return (slice(a * CH, (a + 1) * CH), slice(b_ * CH, (b_ + 1) * CH))


class Prog:
    ENGINES = ("sync", "scalar", "vector", "tensor", "gpsimd")

    def __init__(self, name="prog"):
        self.nc = bass.Bass("TRN2", target_bir_lowering=False)
        self._stack = ExitStack()
        self._stack.__enter__()
        self.ops = {e: [] for e in self.ENGINES}
        self._sems = {}
        self._semnames = set()
        self.cnt = defaultdict(int)
        self.name = name
        self.tag = ""
        self._stage_stack = None

    def begin_stage(self, tag):
        if self._stage_stack is not None:
            snap = [(s, v) for s, v in self.cnt.items() if v > 0]
            for eng in self.ENGINES:
                self.wait(eng, *snap)
            self._stage_stack.close()
        self._stage_stack = ExitStack()
        self.tag = tag + "_"

    def sbuf(self, name, shape, dt=F32):
        st = self._stage_stack if self._stage_stack is not None else self._stack
        return st.enter_context(self.nc.sbuf_tensor(self.tag + name, shape, dt))

    def psum(self, name, shape, dt=F32):
        st = self._stage_stack if self._stage_stack is not None else self._stack
        return st.enter_context(self.nc.psum_tensor(self.tag + name, shape, dt))

    def dram(self, name, shape, dt=F32, kind="Internal"):
        return self.nc.dram_tensor(self.tag + name, shape, dt, kind=kind)

    def dram_io(self, io, name, shape, dt=F32, kind="Internal"):
        if io is not None and name in io:
            return io[name]
        return self.dram(name, shape, dt, kind=kind)

    def semh(self, name):
        if name not in self._sems:
            self._sems[name] = self._stack.enter_context(self.nc.semaphore(name))
        return self._sems[name]

    def raw(self, eng, fn):
        self.ops[eng].append((self.tag, fn))

    AUTO = {"vector": "dv", "scalar": "ac", "gpsimd": "pl"}

    def op(self, eng, method, *args, sem=None, inc=1, waits=(), **kw):
        waits = tuple(waits)
        for (ws, _) in waits:
            self._semnames.add(ws)
        auto = self.AUTO.get(eng)
        if auto is not None and method != "dma_start":
            if sem is None:
                sem = auto
            if sem == auto and self.cnt[auto] > 0:
                waits = waits + ((auto, self.cnt[auto]),)
        if sem is not None:
            self._semnames.add(sem)
            self.cnt[sem] += inc
            target = self.cnt[sem]
        else:
            target = None
        semname = sem
        myinc = inc

        _opidx = len(self.ops[eng])

        def fn(engine, _m=method, _a=args, _k=kw, _w=waits, _s=semname, _i=myinc,
               _oi=_opidx):
            for (ws, wv) in _w:
                if wv > 0:
                    engine.wait_ge(self.semh(ws), wv)
            aa = [a() if callable(a) else a for a in _a]
            kk = {k2: (v() if callable(v) else v) for k2, v in _k.items()}
            try:
                instr = getattr(engine, _m)(*aa, **kk)
            except Exception as e:
                import traceback as _tb
                det = [str(getattr(a, "shape", a))[:60] for a in aa]
                raise RuntimeError(f"op#{_oi} {_m} on {engine} failed: {e} args={det}\n"
                                   f"INNER:\n{_tb.format_exc()}") from e
            if _s is not None:
                instr.then_inc(self.semh(_s), _i)
            return instr

        self.ops[eng].append((self.tag, fn))
        return target

    def wait(self, eng, *waits):
        for (ws, _) in waits:
            self._semnames.add(ws)
        def fn(engine, _w=tuple(waits)):
            for (ws, wv) in _w:
                if wv > 0:
                    engine.wait_ge(self.semh(ws), wv)
        self.ops[eng].append((self.tag, fn))

    def dma(self, queue, out, in_, sem=None, waits=(), **kw):
        return self.op(queue, "dma_start", sem=sem, inc=16, waits=waits,
                       out=out, in_=in_, **kw)

    def allgather(self, src_d, dst_d, groups, sem, waits=()):
        self._semnames.add(sem)
        for (ws, _) in waits:
            self._semnames.add(ws)
        self.cnt[sem] += 1
        target = self.cnt[sem]

        def fn(engine, _w=tuple(waits), _t=target):
            for (ws, wv) in _w:
                if wv > 0:
                    engine.wait_ge(self.semh(ws), wv)
            engine.collective_compute(
                "AllGather", ALU.bypass, replica_groups=groups,
                ins=[src_d.ap().opt()], outs=[dst_d.ap().opt()],
            ).then_inc(self.semh(sem))
        self.ops["gpsimd"].append((self.tag, fn))
        return target

    def finalize(self):
        for n in sorted(self._semnames):
            self.semh(n)
        with self.nc.Block() as block:
            for eng in self.ENGINES:
                lst = self.ops[eng]
                if not lst:
                    continue
                deco = getattr(block, eng)

                def make(lst):
                    def body(engine):
                        for tg, fn in lst:
                            with self.nc.named_scope(tg or "main"):
                                fn(engine)
                    return body

                deco(make(lst))
        return self.nc


NF = D // 128
DBG_STRIP = ()
DBG_UNITS = 16
HALF = NTOK // 512


def bcast_row(p, row_ap, out_psum, ones1, sem_pe, waits):
    return p.op("tensor", "matmul", out_psum, ones1, row_ap,
                sem=sem_pe, waits=waits, start=True, stop=True)


def emit_round(p, src_fp32, dst_bf16, int_t, scr, eng="vector", waits=(), sem=None,
               fixup=None):
    if fixup is None:
        fixup = FIXUP_ROUND
    if sem is None:
        sem = Prog.AUTO.get(eng)
    c = p.op(eng, "tensor_copy", int_t, src_fp32, sem=sem, waits=waits)
    if not fixup:
        return p.op(eng, "tensor_copy", dst_bf16, int_t, sem=sem,
                    waits=((sem, c),))
    c = p.op(eng, "tensor_copy", scr, int_t, sem=sem, waits=((sem, c),))
    n = src_fp32.shape[-1]
    fu = p._fix_u[:, 0:n]
    fv = p._fix_v[:, 0:n]
    cu = p.op(eng, "scalar_tensor_tensor", fu, scr, 0.5, src_fp32,
              ALU.add, ALU.is_le, sem=sem, waits=((sem, c),))
    cv = p.op(eng, "scalar_tensor_tensor", fv, scr, -0.5, src_fp32,
              ALU.add, ALU.is_ge, sem=sem, waits=((sem, cu),))
    cw = p.op(eng, "tensor_add", scr, scr, fu, sem=sem, waits=((sem, cv),))
    return p.op(eng, "tensor_sub", dst_bf16, scr, fv, sem=sem,
                waits=((sem, cw),))


def alloc_fix(p, shape):
    p._fix_u = p.sbuf("fix_u", shape, F32)
    p._fix_v = p.sbuf("fix_v", shape, F32)


def shard_tokens(x_btd, c):
    b = c // 4
    s1, s2 = core_token_slices(c)
    return np.concatenate([x_btd[b, s1], x_btd[b, s2]], axis=0)


def unshard_tokens(shards):
    out = np.empty((B, T, shards[0].shape[1]), dtype=shards[0].dtype)
    for c in range(NCORE):
        b = c // 4
        s1, s2 = core_token_slices(c)
        out[b, s1] = shards[c][:CH]
        out[b, s2] = shards[c][CH:]
    return out


def fm(x_nd):
    return np.ascontiguousarray(x_nd.T)


def rope_host_tables(c):
    s1, s2 = core_token_slices(c)
    tvals = np.concatenate([np.arange(s1.start, s1.stop), np.arange(s2.start, s2.stop)])
    ln1e4 = np.log(np.float64(10000.0))
    hi = np.float32(ln1e4); lo = np.float32(ln1e4 - np.float64(hi))
    i = np.arange(64, dtype=np.float64)
    invf = np.exp(-(i / 64) * np.float64(hi)) * np.exp(-(i / 64) * np.float64(lo))
    ang = np.outer(invf, np.float64(tvals))
    twopi = 2 * np.pi
    k_s = np.rint(ang / twopi).astype(np.float32)
    k_c = np.rint((ang + np.pi / 2) / twopi).astype(np.float32)
    return (tvals.astype(np.float32)[None, :], k_s, k_c)


def build_prep(p=None, io=None):
    if p is None:
        p = Prog("prep")
    io = io or {}
    nc = p.nc
    ones_d = p.dram_io(io, "onesc", [1, 128], F32, kind="ExternalInput")
    onescol_d = p.dram_io(io, "onescol", [128, 1], F32, kind="ExternalInput")
    slice_d = p.dram_io(io, "slc", [1, 1], F32, kind="ExternalInput")

    tg_o = p.dram_io(io, "tg_T", [D, 1024], BF16, kind="ExternalOutput")
    tu_o = p.dram_io(io, "tu_T", [D, 1024], BF16, kind="ExternalOutput")
    td_o = p.dram_io(io, "td_T", [1024, D], BF16, kind="ExternalOutput")
    fs_o = p.dram_io(io, "fsum", [D, 256], BF16, kind="ExternalOutput")
    m_o = p.dram_io(io, "mvals", [1, 4], F32, kind="ExternalOutput")

    wgs_d = p.dram_io(io, "wgs", [1024, D], F32, kind="ExternalInput")
    wus_d = p.dram_io(io, "wus", [1024, D], F32, kind="ExternalInput")
    wds_d = p.dram_io(io, "wds", [D, 1024], F32, kind="ExternalInput")
    facs_d = p.dram_io(io, "facs", [8, D, 256], F32, kind="ExternalInput")

    ones1 = p.sbuf("ones1", [1, 128], F32)
    onescol = p.sbuf("onescol_t", [128, 1], F32)
    idn = p.sbuf("idn", [128, 128], BF16)
    asum = p.sbuf("asum", [128, 3], F32)
    arow = p.sbuf("arow", [1, 3], F32)
    mrow = p.sbuf("mrow", [1, 4], F32)
    minv = p.sbuf("minv", [128, 3], F32)
    wt = [p.sbuf(f"wt{i}", [128, D], F32) for i in range(2)]
    tt_ = [p.sbuf(f"tt{i}", [128, D], BF16) for i in range(2)]
    tvt = p.sbuf("tvt", [128, 2048], BF16)
    red = p.psum("red", [1, 512], F32)
    trp = p.psum("trp", [128, 128], BF16)
    redsc = p.sbuf("redsc", [128, 4], F32)
    fsl = [p.sbuf(f"fsl{i}", [128, 256], F32) for i in range(2)]
    facc = p.sbuf("facc", [128, 256], F32)
    fbf = p.sbuf("fbf", [128, 256], BF16)

    def absmean_slice(w_dram, nrows, ncols, col):
        ntile = nrows // 128
        for i in range(ntile):
            buf = wt[i % 2]
            ld = p.dma("sync", buf[:, 0:ncols], w_dram.ap()[i * 128:(i + 1) * 128, :],
                       sem="ldw", waits=(("dv", p.cnt["dv"]),))
            c1 = p.op("vector", "reduce_sum", redsc[:, 0:1], buf[:, 0:ncols], AXX,
                      apply_absolute_value=True, sem="dv", waits=(("ldw", ld),))
            p.op("vector", "tensor_add", asum[:, col:col + 1], asum[:, col:col + 1],
                 redsc[:, 0:1], sem="dv", waits=(("dv", c1),))

    c0 = p.op("vector", "memset", asum[:], 0.0, sem="dv")
    p.dma("sync", ones1[:], ones_d.ap(), sem="ldc")
    p.dma("sync", onescol[:], onescol_d.ap(), sem="ldc")
    absmean_slice(wgs_d, 1024, D, 0)
    absmean_slice(wus_d, 1024, D, 1)
    absmean_slice(wds_d, D, 1024, 2)
    part_s = p.dram("part_s", [1, 4], F32)
    part_g = p.dram("part_g", [8, 4], F32)
    pgz = p.sbuf("pgz", [128, 4], F32)
    cpe = p.op("tensor", "matmul", red[0:1, 0:3], onescol[:], asum[:, 0:3],
               start=True, stop=True, sem="pe",
               waits=(("dv", p.cnt["dv"]), ("ldc", 32)))
    cpr = p.op("vector", "tensor_copy", mrow[0:1, 0:3], red[0:1, 0:3], sem="dv",
               waits=(("pe", cpe),))
    cpr = p.op("vector", "memset", mrow[0:1, 3:4], 0.0, sem="dv", waits=(("dv", cpr),))
    p.dma("gpsimd", part_s.ap(), mrow[:], sem="stp", waits=(("dv", cpr),))
    p.allgather(part_s, part_g, G8, "ccp", waits=(("stp", p.cnt["stp"]),))
    czp = p.op("vector", "memset", pgz[:], 0.0, sem="dv")
    ldp = p.dma("sync", pgz[0:8, :], part_g.ap(), sem="ldpg",
                waits=(("ccp", p.cnt["ccp"]), ("dv", czp)))
    cpe = p.op("tensor", "matmul", red[0:1, 0:4], onescol[:], pgz[:],
               start=True, stop=True, sem="pe",
               waits=(("ldpg", ldp), ("dv", p.cnt["dv"])))
    c1 = p.op("vector", "tensor_scalar", arow[:], red[0:1, 0:3], 1.0 / (1 << 24), 1e-5,
              ALU.mult, ALU.max, sem="dv", waits=(("pe", cpe),))
    c1 = p.op("vector", "tensor_copy", mrow[0:1, 0:3], arow[:], sem="dv", waits=(("dv", c1),))
    c1 = p.op("vector", "memset", mrow[0:1, 3:4], 0.0, sem="dv", waits=(("dv", c1),))
    crec = p.op("vector", "reciprocal", arow[:], arow[:], sem="dv", waits=(("dv", c1),))
    bc3 = p.psum("bc3", [128, 4], F32)
    cb = p.op("tensor", "matmul", bc3[0:128, 0:3], ones1[:],
              arow[:], start=True, stop=True, sem="pe", waits=(("dv", crec),))
    cmi = p.op("vector", "tensor_copy", minv[:], bc3[0:128, 0:3], sem="dv",
               waits=(("pe", cb),))
    p.dma("gpsimd", m_o.ap(), mrow[:], sem="sto", waits=(("dv", p.cnt["dv"]),))

    ci = p.op("vector", "memset", idn[:], 1.0, sem="dv")
    cid = p.op("gpsimd", "affine_select", idn[:], idn[:], pattern=[[-1, 128]],
               compare_op=ALU.is_equal, fill=0.0, channel_multiplier=1,
               sem="pl", waits=(("dv", ci),))

    def ternarize_slice(w_slice_d, out_d, nrows, col, transpose_out):
        ntile = nrows // 128
        ncols = w_slice_d.shape[1] if hasattr(w_slice_d, "shape") else None
        for i in range(ntile):
            sto_before = p.cnt["sto"]
            buf = wt[i % 2]
            tb = tt_[i % 2]
            ld = p.dma("scalar", buf[:], w_slice_d.ap()[i * 128:(i + 1) * 128, :],
                       sem="lds", waits=(("dv", p.cnt["dv"]),))
            c1 = p.op("vector", "tensor_scalar", p._t1[:], buf[:], minv[:, col:col + 1],
                      0.5, ALU.mult, ALU.is_ge, sem="dv",
                      waits=(("lds", ld), ("pe2", p.cnt["pe2"])))
            c2 = p.op("vector", "tensor_scalar", p._t2[:], buf[:], minv[:, col:col + 1],
                      -0.5, ALU.mult, ALU.is_le, sem="dv", waits=(("dv", c1),))
            c3 = p.op("vector", "tensor_sub", tb[:], p._t1[:], p._t2[:], sem="dv",
                      waits=(("dv", c2),))
            nb = tb.shape[1] // 128
            for j in range(nb):
                cpe2 = p.op("tensor", "transpose", trp[:], tb[:, j * 128:(j + 1) * 128],
                            idn[:], sem="pe2",
                            waits=(("dv", c3), ("pl", cid), ("ac", p.cnt["ac"])))
                p.op("scalar", "copy", tvt[:, j * 128:(j + 1) * 128], trp[:],
                     sem="ac", waits=(("pe2", cpe2), ("sto", sto_before)))
            for j in range(nb):
                p.dma("gpsimd", out_d.ap()[j * 128:(j + 1) * 128, i * 128:(i + 1) * 128],
                      tvt[:, j * 128:(j + 1) * 128], sem="sto",
                      waits=(("ac", p.cnt["ac"]),))

    p._t1 = p.sbuf("t1s", [128, D], F32)
    p._t2 = p.sbuf("t2s", [128, D], F32)
    ternarize_slice(wgs_d, tg_o, 1024, 0, True)
    ternarize_slice(wus_d, tu_o, 1024, 1, True)

    ntile = D // 128
    for i in range(ntile):
        sto_before = p.cnt["sto"]
        buf = wt[i % 2]
        tb = tt_[i % 2]
        ld = p.dma("scalar", buf[:, 0:1024], wds_d.ap()[i * 128:(i + 1) * 128, :],
                   sem="lds", waits=(("dv", p.cnt["dv"]),))
        c1 = p.op("vector", "tensor_scalar", p._t1[:, 0:1024], buf[:, 0:1024],
                  minv[:, 2:3], 0.5, ALU.mult, ALU.is_ge, sem="dv",
                  waits=(("lds", ld), ("pe2", p.cnt["pe2"])))
        c2 = p.op("vector", "tensor_scalar", p._t2[:, 0:1024], buf[:, 0:1024],
                  minv[:, 2:3], -0.5, ALU.mult, ALU.is_le, sem="dv", waits=(("dv", c1),))
        c3 = p.op("vector", "tensor_sub", tb[:, 0:1024], p._t1[:, 0:1024],
                  p._t2[:, 0:1024], sem="dv", waits=(("dv", c2),))
        for j in range(8):
            cpe2 = p.op("tensor", "transpose", trp[:], tb[:, j * 128:(j + 1) * 128],
                        idn[:], sem="pe2",
                        waits=(("dv", c3), ("pl", cid), ("ac", p.cnt["ac"])))
            p.op("scalar", "copy", tvt[:, j * 128:(j + 1) * 128], trp[:],
                 sem="ac", waits=(("pe2", cpe2), ("sto", sto_before)))
        for j in range(8):
            p.dma("gpsimd", td_o.ap()[j * 128:(j + 1) * 128, i * 128:(i + 1) * 128],
                  tvt[:, j * 128:(j + 1) * 128], sem="sto",
                  waits=(("ac", p.cnt["ac"]),))

    prev_cb = 0
    for i in range(D // 128):
        cz = p.op("vector", "memset", facc[:], 0.0, sem="dv",
                  waits=((("sto", p.cnt["sto"]),) if i == 0 else (("dv", prev_cb),)))
        for r in range(8):
            buf = fsl[r % 2]
            ld = p.dma("sync", buf[:], facs_d.ap()[r, i * 128:(i + 1) * 128, :],
                       sem="ldf", waits=(("dv", p.cnt["dv"]),))
            cz = p.op("vector", "tensor_add", facc[:], facc[:], buf[:], sem="dv",
                      waits=(("ldf", ld), ("dv", cz)))
        cb = p.op("vector", "tensor_copy", fbf[:], facc[:], sem="dv",
                  waits=(("dv", cz), ("sto", p.cnt["sto"])))
        prev_cb = cb
        p.dma("gpsimd", fs_o.ap()[i * 128:(i + 1) * 128, :], fbf[:], sem="sto",
              waits=(("dv", cb),))

    p.wait("gpsimd", ("sto", p.cnt["sto"]))
    return p


def host_prep_inputs(ins, c):
    wg = ins["L0_ffn_w_gate"]; wu = ins["L0_ffn_w_up"]; wd = ins["L0_ffn_w_down"]
    fac = ins["L1_ffn_factors"]
    sl = slice(c * 1024, (c + 1) * 1024)
    se = slice(c * 256, (c + 1) * 256)
    return {
        "onesc": np.ones((1, 128), dtype=np.float32),
        "onescol": np.ones((128, 1), dtype=np.float32),
        "slc": np.zeros((1, 1), dtype=np.float32),
        "wgs": np.ascontiguousarray(wg[sl]), "wus": np.ascontiguousarray(wu[sl]),
        "wds": np.ascontiguousarray(wd[:, sl]),
        "facs": np.ascontiguousarray(fac[:, :, se]),
    }


def build_l0_ffn(p=None, io=None):
    if p is None:
        p = Prog("l0ffn")
    io = io or {}
    xin_d = p.dram_io(io, "xin", [D, NTOK], F32, kind="ExternalInput")
    tg_d = p.dram_io(io, "tgT", [D, DFF], BF16, kind="ExternalInput")
    tu_d = p.dram_io(io, "tuT", [D, DFF], BF16, kind="ExternalInput")
    td_d = p.dram_io(io, "tdT", [DFF, D], BF16, kind="ExternalInput")
    lnw_d = p.dram_io(io, "lnw", [128, 64], F32, kind="ExternalInput")
    mv_d = p.dram_io(io, "mv", [1, 4], F32, kind="ExternalInput")
    ones_d = p.dram_io(io, "onesc", [1, 128], F32, kind="ExternalInput")
    onescolb_d = p.dram_io(io, "onescolb", [128, 1], BF16, kind="ExternalInput")
    xout_d = p.dram_io(io, "xout", [D, NTOK], F32, kind="ExternalOutput")
    u2_d = p.dram_io(io, "u2scr", [DFF, NTOK], BF16, kind="Internal")
    STO0 = p.cnt["sto"]
    STU0 = p.cnt["stu"]

    R1 = p.sbuf("R1", [128, NF * NTOK], BF16)
    wg = [p.sbuf(f"wgb{i}", [128, NF * 128], BF16) for i in range(2)]
    wu = [p.sbuf(f"wub{i}", [128, NF * 128], BF16) for i in range(2)]
    macc = p.sbuf("macc", [128, NTOK], BF16)
    mincc = p.sbuf("mincc", [128, NTOK], BF16)
    xb = p.sbuf("xb", [128, NTOK], BF16)
    sq = p.sbuf("sq", [128, NTOK], BF16)
    u2st = [p.sbuf(f"u2st{i}", [128, NTOK], BF16) for i in range(2)]
    pad0 = p.sbuf("pad0", [128, NTOK], F32)
    intt = p.sbuf("intt", [128, NTOK], I32)
    r1_addr = p.nc.lookup_mloc(R1).addr
    R2 = p.nc.alloc_sbuf_tensor_at(p.tag + "R2", [128, 64 * 512], BF16, offset=r1_addr)

    xt = [p.sbuf(f"xt{i}", [128, NTOK], F32) for i in range(2)]
    scrA = p.sbuf("scrA", [128, NTOK], F32)
    scrB = p.sbuf("scrB", [128, NTOK], F32)
    scrC = p.sbuf("scrC", [128, NTOK], F32)
    scrD = p.sbuf("scrD", [128, NTOK], F32)
    wd = [p.sbuf(f"wdb{i}", [128, 64 * 128], BF16) for i in range(2)]
    u2ld = [p.sbuf(f"u2ld{i}", [128, 512], BF16) for i in range(2)]
    ones1 = p.sbuf("ones1", [1, 128], F32)
    onescolb = p.sbuf("onescolbt", [128, 1], BF16)
    lnw = p.sbuf("lnwt", [128, 64], F32)
    mv = p.sbuf("mvt", [1, 4], F32)
    rows = p.sbuf("rows", [1, 9 * NTOK], F32)
    sc = p.sbuf("sc", [1, 8], F32)
    tmcol = p.sbuf("tmcol", [128, 8], F32)
    bcA = p.sbuf("bcA", [128, NTOK], F32)
    bcB = p.sbuf("bcB", [128, NTOK], F32)
    bcCG = p.sbuf("bcCG", [128, NTOK], F32)
    bcS2 = p.sbuf("bcS2", [128, NTOK], F32)
    bcFS = p.sbuf("bcFS", [128, NTOK], F32)
    idn = p.sbuf("idnf", [128, 128], F32)
    alloc_fix(p, [128, NTOK])

    psG = p.psum("psG", [128, 512], F32)
    psU = p.psum("psU", [128, 512], F32)
    psG2 = p.psum("psG2", [128, 512], F32)
    psU2 = p.psum("psU2", [128, 512], F32)
    msH = [p.psum(f"msH{h}", [1, 512], F32) for h in range(2)]
    psBC = p.psum("psBC", [128, 512], F32)
    psD = p.psum("psD", [128, 512], F32)

    def row(i):
        return rows[0:1, i * NTOK:(i + 1) * NTOK]

    def rowh(i, h):
        return rows[0:1, i * NTOK + h * 512:i * NTOK + (h + 1) * 512]

    p.dma("sync", ones1[:], ones_d.ap(), sem="ldc")
    p.dma("sync", onescolb[:], onescolb_d.ap(), sem="ldc")
    p.dma("sync", lnw[:], lnw_d.ap(), sem="ldc")
    p.dma("sync", mv[:], mv_d.ap(), sem="ldc")
    LDC = p.cnt["ldc"]
    cmi = p.op("vector", "memset", idn[:], 1.0, waits=(("ldc", LDC),))
    p.op("gpsimd", "affine_select", idn[:], idn[:], pattern=[[-1, 128]],
         compare_op=ALU.is_equal, fill=0.0, channel_multiplier=1,
         waits=(("dv", cmi),))
    PL_ID = p.cnt["pl"]

    def bc_to(dst, row_ap, n=NTOK):
        last = 0
        for h in range(n // 512):
            cpe = p.op("tensor", "matmul", psBC[:, 0:512], ones1[:],
                       row_ap[0:1, h * 512:(h + 1) * 512], start=True, stop=True,
                       sem="pe", waits=(("ac", p.cnt["ac"]), ("dv", p.cnt["dv"]),
                                        ("ldc", LDC)))
            last = p.op("scalar", "copy", dst[:, h * 512:(h + 1) * 512],
                        psBC[:, 0:512], waits=(("pe", cpe),))
        return last

    def part_max_row(row_out):
        lastac = 0
        for j in range(NTOK // 128):
            cpe = p.op("tensor", "transpose", psD[:, 0:128],
                       scrA[:, j * 128:(j + 1) * 128], idn[:], sem="pe",
                       waits=(("dv", p.cnt["dv"]), ("pl", PL_ID),
                              ("dv2", p.cnt["dv2"]), ("ac", p.cnt["ac"])))
            p.op("vector", "reduce_max", tmcol[:, j:j + 1], psD[:, 0:128], AXX,
                 sem="dv2", waits=(("pe", cpe), ("dv", p.cnt["dv"])))
        for j in range(NTOK // 128):
            cpe = p.op("tensor", "transpose", psD[0:1, 128:256], tmcol[:, j:j + 1],
                       idn[:], sem="pe",
                       waits=(("dv2", p.cnt["dv2"]), ("ac", p.cnt["ac"])))
            lastac = p.op("scalar", "copy", row_out[0:1, j * 128:(j + 1) * 128],
                          psD[0:1, 128:256], waits=(("pe", cpe),))
        return lastac

    for i in range(NF):
        buf = xt[i % 2]
        ld = p.dma("sync", buf[:], xin_d.ap()[i * 128:(i + 1) * 128, :], sem="ldx",
                   waits=(("ac", p.cnt["ac"]), ("dv", p.cnt["dv"])))
        cb = p.op("vector", "tensor_copy", xb[:], buf[:],
                  waits=(("ldx", ld), ("pe", p.cnt["pe"])))
        cs = p.op("scalar", "square", sq[:], buf[:],
                  waits=(("ldx", ld), ("pe", p.cnt["pe"])))
        for h in range(2):
            ps_s = psG if h == 0 else psG2
            ps_q = psU if h == 0 else psU2
            p.op("tensor", "matmul", ps_s[0:1, 0:512], onescolb[:],
                 xb[:, h * 512:(h + 1) * 512], start=(i == 0), stop=(i == NF - 1),
                 sem="pe", waits=(("dv", cb), ("ldc", LDC)))
            p.op("tensor", "matmul", ps_q[0:1, 0:512], onescolb[:],
                 sq[:, h * 512:(h + 1) * 512], start=(i == 0), stop=(i == NF - 1),
                 sem="pe", waits=(("ac", cs),))
    PE_STATS = p.cnt["pe"]
    for h in range(2):
        ps_s = psG if h == 0 else psG2
        ps_q = psU if h == 0 else psU2
        p.op("vector", "tensor_scalar", rowh(0, h), ps_s[0:1, 0:512],
             1.0 / D, None, ALU.mult, waits=(("pe", PE_STATS),))
        p.op("vector", "tensor_scalar", rowh(1, h), ps_q[0:1, 0:512],
             1.0 / D, None, ALU.mult, waits=(("pe", PE_STATS),))
    p.op("vector", "tensor_mul", row(2), row(0), row(0))
    p.op("vector", "tensor_sub", row(1), row(1), row(2))
    p.op("vector", "tensor_scalar", row(1), row(1), 1e-5, None, ALU.add)
    cac = p.op("scalar", "sqrt", row(1), row(1), waits=(("dv", p.cnt["dv"]),))
    p.op("vector", "reciprocal", row(2), row(1), waits=(("ac", cac),))
    p.op("vector", "tensor_mul", row(3), row(0), row(2))
    bc_to(bcA, row(2))
    bc_to(bcB, row(3))

    BC_DONE = p.cnt["ac"]
    for i in range(NF):
        buf = xt[i % 2]
        ld = p.dma("sync", buf[:], xin_d.ap()[i * 128:(i + 1) * 128, :], sem="ldx",
                   waits=(("dv", p.cnt["dv"]),))
        c1 = p.op("vector", "tensor_mul", scrA[:], buf[:], bcA[:],
                  waits=(("ldx", ld), ("ac", BC_DONE), ("pl", p.cnt["pl"])))
        c2 = p.op("vector", "tensor_sub", scrA[:], scrA[:], bcB[:])
        if i == 0:
            p.op("vector", "tensor_copy", scrB[:], scrA[:])
            p.op("vector", "tensor_copy", scrC[:], scrA[:])
        else:
            p.op("vector", "tensor_max", scrB[:], scrB[:], scrA[:])
            p.op("vector", "tensor_tensor", scrC[:], scrC[:], scrA[:], ALU.min)
    p.op("vector", "tensor_scalar", scrD[:], scrC[:], -1.0, None, ALU.mult,
         waits=(("pl", p.cnt["pl"]),))
    p.op("vector", "tensor_max", scrA[:], scrB[:], scrD[:])
    part_max_row(row(4))
    p.op("vector", "tensor_scalar", row(4), row(4), 1e-5, None, ALU.max,
         waits=(("ac", p.cnt["ac"]),))
    p.op("vector", "reciprocal", row(5), row(4))
    p.op("vector", "tensor_scalar", row(5), row(5), 127.0, None, ALU.mult)
    p.op("vector", "tensor_mul", row(6), row(2), row(5))
    p.op("vector", "tensor_mul", row(7), row(3), row(5))
    bc_to(bcA, row(6))
    bc_to(bcB, row(7))
    p.op("vector", "tensor_scalar", sc[0:1, 0:1], mv[0:1, 0:1], 1.0 / 127, None,
         ALU.mult, waits=(("ldc", LDC),))
    p.op("vector", "tensor_scalar", sc[0:1, 1:2], mv[0:1, 1:2], 1.0 / 127, None,
         ALU.mult)
    p.op("vector", "tensor_scalar", sc[0:1, 2:3], mv[0:1, 2:3], 0.5 / 127, None,
         ALU.mult)
    p.op("vector", "tensor_scalar", row(7), row(4), sc[0:1, 0:1], None, ALU.mult,
         waits=(("pe", p.cnt["pe"]),))
    p.op("vector", "tensor_scalar", row(8), row(4), sc[0:1, 1:2], None, ALU.mult)
    bc_to(bcCG, row(7))

    BC2 = p.cnt["ac"]
    for i in range(NF):
        buf = xt[i % 2]
        ld = p.dma("sync", buf[:], xin_d.ap()[i * 128:(i + 1) * 128, :], sem="ldx",
                   waits=(("dv", p.cnt["dv"]),))
        p.op("vector", "tensor_mul", scrA[:], buf[:], bcA[:],
             waits=(("ldx", ld), ("ac", BC2), ("pl", p.cnt["pl"])))
        p.op("vector", "tensor_sub", scrA[:], scrA[:], bcB[:])
        for h in range(2):
            sl = slice(h * 512, (h + 1) * 512)
            emit_round(p, scrA[:, sl], R1[:, i * NTOK + h * 512:i * NTOK + (h + 1) * 512],
                       intt[:, 0:512], scrD[:, 0:512])
    R1_DONE = p.cnt["dv"]

    gathered_w = io.get("gather_w", False)

    def load_w(dst, src_d, j, pewait, queue="scalar", semname=None):
        if gathered_w:
            co = j // 8
            src = (src_d.ap().rearrange("(A p) o -> p A o", p=128)
                   [:, co * 16:(co + 1) * 16, (j % 8) * 128:(j % 8 + 1) * 128])
        else:
            src = (src_d.ap().rearrange("(a p) o -> p a o", p=128)
                   [:, :, j * 128:(j + 1) * 128])
        return p.dma(queue, dst.ap().rearrange("p (a o) -> p a o", o=128),
                     src, sem=semname or f"ldw{j % 2}", waits=(("pe", pewait),))

    lam_ev = {}
    pe_end = {}
    for j in range(64):
        lw1 = load_w(wg[j % 2], tg_d, j, pe_end.get(j - 2, 0), queue="scalar")
        lw2 = load_w(wu[j % 2], tu_d, j, pe_end.get(j - 2, 0), queue="sync",
                     semname=f"ldwu{j % 2}")
        lwsem = f"ldw{j % 2}"
        for h in range(2):
            g = j * 2 + h
            ps_g = psG if g % 2 == 0 else psG2
            ps_u = psU if g % 2 == 0 else psU2
            mm = 0
            for i in range(NF):
                mm = p.op("tensor", "matmul", ps_g[:],
                          wg[j % 2][:, i * 128:(i + 1) * 128],
                          R1[:, i * NTOK + h * 512:i * NTOK + h * 512 + 512],
                          start=(i == 0), stop=(i == NF - 1), sem="pe",
                          waits=(((lwsem, lw1), (f"ldwu{j % 2}", lw2),
                                  ("dv", R1_DONE),
                                  ("dv", lam_ev.get(g - 2, 0))) if i == 0 else ()))
            for i in range(NF):
                mm = p.op("tensor", "matmul", ps_u[:],
                          wu[j % 2][:, i * 128:(i + 1) * 128],
                          R1[:, i * NTOK + h * 512:i * NTOK + h * 512 + 512],
                          start=(i == 0), stop=(i == NF - 1), sem="pe")
            sl = slice(h * 512, (h + 1) * 512)
            c1 = p.op("vector", "tensor_mul", scrA[:, sl], ps_g[:], bcCG[:, sl],
                      waits=(("pe", mm),))
            c2 = p.op("scalar", "activation", scrB[:, sl], scrA[:, sl], AF.Silu,
                      waits=(("dv", c1),))
            c3 = p.op("vector", "tensor_mul", scrC[:, sl], scrB[:, sl], ps_u[:],
                      waits=(("ac", c2),))
            lam_ev[g] = c3
            c4 = p.op("scalar", "square", sq[:, sl], scrC[:, sl],
                      waits=(("dv", c3),))
            p.op("tensor", "matmul", msH[h][0:1, 0:512], onescolb[:], sq[:, sl],
                 start=(j == 0), stop=(j == 63), sem="pe", waits=(("ac", c4),))
            p.op("vector", "tensor_scalar", u2st[j % 2][:, sl], scrC[:, sl],
                 lnw[:, j:j + 1], None, ALU.mult,
                 waits=(("stu", STU0 + 16 * j),))
            if j == 0:
                p.op("vector", "tensor_copy", macc[:, sl], u2st[j % 2][:, sl])
                p.op("vector", "tensor_copy", mincc[:, sl], u2st[j % 2][:, sl])
            else:
                p.op("vector", "tensor_max", macc[:, sl], macc[:, sl],
                     u2st[j % 2][:, sl])
                p.op("vector", "tensor_tensor", mincc[:, sl], mincc[:, sl],
                     u2st[j % 2][:, sl], ALU.min)
        p.dma("gpsimd", u2_d.ap()[j * 128:(j + 1) * 128, :], u2st[j % 2][:],
              sem="stu", waits=(("dv", p.cnt["dv"]), ("pl", p.cnt["pl"])))
        pe_end[j] = p.cnt["pe"]
    GU_PE = p.cnt["pe"]

    p.op("vector", "tensor_copy", scrA[:], macc[:],
         waits=(("pe", GU_PE), ("ac", p.cnt["ac"]), ("stu", STU0 + 16 * 64)))
    p.op("vector", "tensor_scalar", scrB[:], mincc[:], -1.0, None, ALU.mult,
         waits=(("pl", p.cnt["pl"]),))
    p.op("vector", "tensor_max", scrA[:], scrA[:], scrB[:])
    part_max_row(row(4))
    for h in range(2):
        p.op("vector", "tensor_scalar", rowh(0, h), msH[h][0:1, 0:512],
             1.0 / DFF, None, ALU.mult, waits=(("pe", GU_PE), ("ac", p.cnt["ac"])))
    p.op("vector", "tensor_mul", row(1), row(8), row(8))
    p.op("vector", "tensor_mul", row(1), row(1), row(0))
    p.op("vector", "tensor_scalar", row(1), row(1), 1e-6, None, ALU.add)
    cac = p.op("scalar", "sqrt", row(1), row(1), waits=(("dv", p.cnt["dv"]),))
    p.op("vector", "reciprocal", row(2), row(1), waits=(("ac", cac),))
    p.op("vector", "tensor_mul", row(2), row(8), row(2))
    p.op("vector", "reciprocal", row(6), row(2))
    p.op("vector", "tensor_scalar", row(6), row(6), 1e-5, None, ALU.mult)
    p.op("vector", "tensor_max", row(3), row(4), row(6))
    p.op("vector", "reciprocal", row(5), row(3))
    p.op("vector", "tensor_scalar", row(5), row(5), 127.0, None, ALU.mult)
    p.op("vector", "tensor_mul", row(6), row(2), row(3))
    p.op("vector", "tensor_scalar", row(6), row(6), sc[0:1, 2:3], None, ALU.mult)
    bc_to(bcS2, row(5))
    bc_to(bcFS, row(6))
    BC3 = p.cnt["ac"]

    pe_end2 = {}
    for h in range(2):
        sl = slice(h * 512, (h + 1) * 512)
        r2last = 0
        for j in range(64):
            lb = u2ld[j % 2]
            ld = p.dma("sync", lb[:], u2_d.ap()[j * 128:(j + 1) * 128, sl],
                       sem="ldu", waits=(("dv", p.cnt["dv"]), ("stu", STU0 + 16 * 64)))
            p.op("vector", "tensor_mul", scrA[:, 0:512], lb[:], bcS2[:, sl],
                 waits=(("ldu", ld), ("ac", BC3), ("pe", p.cnt["pe"]),
                        ("pl", p.cnt["pl"])))
            r2last = emit_round(p, scrA[:, 0:512], R2[:, j * 512:(j + 1) * 512],
                                intt[:, 0:512], scrD[:, 0:512])
        prev_ev = 0
        for d in range(NF):
            lw = p.dma("scalar", wd[d % 2].ap().rearrange("p (a o) -> p a o", o=128),
                       td_d.ap().rearrange("(a p) d -> p a d", p=128)[:, :, d * 128:(d + 1) * 128],
                       sem=f"ldd{d % 2}", waits=(("pe", pe_end2.get((h, d - 2), pe_end2.get((h - 1, NF - 2 + d), 0))),))
            mm = 0
            for j in range(64):
                mm = p.op("tensor", "matmul", psD[:],
                          wd[d % 2][:, j * 128:(j + 1) * 128],
                          R2[:, j * 512:(j + 1) * 512],
                          start=(j == 0), stop=(j == 63), sem="pe",
                          waits=(((f"ldd{d % 2}", lw), ("dv", r2last),
                                  ("dv", prev_ev), ("ac", p.cnt["ac"])) if j == 0 else ()))
            pe_end2[(h, d)] = p.cnt["pe"]
            n = h * NF + d
            ldx2 = p.dma("sync", xt[d % 2][:, 0:512],
                         xin_d.ap()[d * 128:(d + 1) * 128, sl], sem="ldx2",
                         waits=(("sto", STO0 + 16 * n), ("dv", R1_DONE)))
            ev = scrC[:, (n % 2) * 512:(n % 2) * 512 + 512]
            c1 = p.op("vector", "tensor_mul", scrB[:, 0:512], psD[:], bcFS[:, sl],
                      waits=(("pe", mm),))
            prev_ev = c1
            c2 = p.op("vector", "tensor_add", ev, scrB[:, 0:512],
                      xt[d % 2][:, 0:512],
                      waits=(("ldx2", ldx2), ("sto", STO0 + 16 * n)))
            p.dma("gpsimd", xout_d.ap()[d * 128:(d + 1) * 128, sl],
                  ev, sem="sto", waits=(("dv", c2),))
    p.wait("gpsimd", ("sto", p.cnt["sto"]))
    return p


def host_ffn_inputs(x_fm_c, tgT, tuT, tdT, lnw_dev, mval):
    return {
        "xin": x_fm_c, "tgT": tgT, "tuT": tuT, "tdT": tdT,
        "lnw": lnw_dev, "mv": mval,
        "onesc": np.ones((1, 128), dtype=np.float32),
        "onescolb": np.ones((128, 1), dtype=BF),
    }


def build_l0_attn_a(p=None, io=None):
    if p is None:
        p = Prog("l0attna")
    io = io or {}
    xin_d = p.dram_io(io, "xin", [D, NTOK], F32, kind="ExternalInput")
    wqkv_d = p.dram_io(io, "wqkv", [D, 3 * D], F32, kind="ExternalInput")
    wgm_d = p.dram_io(io, "wgm", [D, D], F32, kind="ExternalInput")
    trow_d = p.dram_io(io, "trow", [1, NTOK], F32, kind="ExternalInput")
    ks_d = p.dram_io(io, "ks", [64, NTOK], F32, kind="ExternalInput")
    kc_d = p.dram_io(io, "kc", [64, NTOK], F32, kind="ExternalInput")
    ones_d = p.dram_io(io, "onesc", [1, 128], F32, kind="ExternalInput")
    onescolb_d = p.dram_io(io, "onescolb", [128, 1], BF16, kind="ExternalInput")
    qs_o = p.dram_io(io, "qs", [D, NTOK], BF16, kind="ExternalOutput")
    k_o = p.dram_io(io, "ko", [D, NTOK], BF16, kind="ExternalOutput")
    vt_o = p.dram_io(io, "vt", [NTOK, D], BF16, kind="ExternalOutput")
    g_o = p.dram_io(io, "go", [D, NTOK], BF16, kind="ExternalOutput")

    xt = [p.sbuf(f"xt{i}", [128, NTOK], F32) for i in range(2)]
    xb = p.sbuf("xb", [128, NTOK], BF16)
    sq = p.sbuf("sq", [128, NTOK], BF16)
    scrA = p.sbuf("scrA", [128, NTOK], F32)
    scrB = p.sbuf("scrB", [128, NTOK], F32)
    scrC = p.sbuf("scrC", [128, NTOK], F32)
    scrD = p.sbuf("scrD", [128, NTOK], F32)
    XLN = p.sbuf("XLN", [128, NF * NTOK], BF16)
    wb = [p.sbuf(f"wb{i}", [128, NF * 128], BF16) for i in range(2)]
    qbh = p.sbuf("qbh", [128, NTOK], BF16)
    obh = p.sbuf("obh", [128, NTOK], BF16)
    vstage = p.sbuf("vstage", [128, NTOK], BF16)
    vtr = p.sbuf("vtr", [128, NTOK], BF16)
    ones1 = p.sbuf("ones1", [1, 128], F32)
    onescolb = p.sbuf("onescolbt", [128, 1], BF16)
    trow = p.sbuf("trowt", [1, NTOK], F32)
    kst = p.sbuf("kst", [64, NTOK], F32)
    kct = p.sbuf("kct", [64, NTOK], F32)
    invf = p.sbuf("invf", [1, 64], F32)
    invf2 = p.sbuf("invf2", [1, 64], F32)
    SS2 = p.sbuf("SS2", [128, NTOK], F32)
    CC2 = p.sbuf("CC2", [128, NTOK], F32)
    rows = p.sbuf("rows", [1, 4 * NTOK], F32)
    bcA = p.sbuf("bcA", [128, NTOK], F32)
    bcB = p.sbuf("bcB", [128, NTOK], F32)
    bcR = p.sbuf("bcR", [128, NTOK], F32)
    idn = p.sbuf("idnf", [128, 128], BF16)
    ones128 = p.sbuf("ones128", [128, 128], BF16)

    psQ = [p.psum(f"psQ{i}", [128, 512], F32) for i in range(4)]
    psM = [p.psum(f"psM{h}", [1, 512], F32) for h in range(2)]
    psBC = p.psum("psBC", [128, 512], F32)
    psTR = p.psum("psTR", [128, 128], BF16)

    def row(i, h=None):
        if h is None:
            return rows[0:1, i * NTOK:(i + 1) * NTOK]
        return rows[0:1, i * NTOK + h * 512:i * NTOK + (h + 1) * 512]

    p.dma("sync", ones1[:], ones_d.ap(), sem="ldc")
    p.dma("sync", onescolb[:], onescolb_d.ap(), sem="ldc")
    p.dma("sync", trow[:], trow_d.ap(), sem="ldc")
    p.dma("sync", kst[:], ks_d.ap(), sem="ldc")
    p.dma("sync", kct[:], kc_d.ap(), sem="ldc")
    LDC = p.cnt["ldc"]
    cmo = p.op("vector", "memset", ones128[:], 1.0, waits=(("ldc", LDC),))
    p.op("gpsimd", "affine_select", idn[:], ones128[:], pattern=[[-1, 128]],
         compare_op=ALU.is_equal, fill=0.0, channel_multiplier=1,
         waits=(("dv", cmo),))
    PL_ID = p.cnt["pl"]

    def bc_to(dst, row_ap, n=NTOK):
        last = 0
        for h in range(n // 512):
            cpe = p.op("tensor", "matmul", psBC[:, 0:512], ones1[:],
                       row_ap[0:1, h * 512:(h + 1) * 512], start=True, stop=True,
                       sem="pe", waits=(("ac", p.cnt["ac"]), ("dv", p.cnt["dv"]),
                                        ("ldc", LDC)))
            last = p.op("scalar", "copy", dst[:, h * 512:(h + 1) * 512],
                        psBC[:, 0:512], waits=(("pe", cpe),))
        return last

    _ln4 = np.log(np.float64(10000.0))
    HI_LN = float(np.float32(_ln4))
    LO_LN = float(np.float32(_ln4 - np.float64(np.float32(_ln4))))
    TWO_PI_HI = float(np.float32(2 * np.pi))
    TWO_PI_LO = float(np.float32(2 * np.pi - np.float64(np.float32(2 * np.pi))))
    PIH = float(np.float32(np.pi / 2))
    ci = p.op("gpsimd", "iota", invf[:], pattern=[[1, 64]], base=0,
              channel_multiplier=0, allow_small_or_imprecise_dtypes=True)
    c1 = p.op("scalar", "activation", invf2[:], invf[:], AF.Exp, bias=0.0,
              scale=-HI_LN / 64, waits=(("pl", ci),))
    c2 = p.op("scalar", "activation", invf[:], invf[:], AF.Exp, bias=0.0,
              scale=-LO_LN / 64, waits=(("ac", c1),))
    c3 = p.op("vector", "tensor_mul", invf[:], invf[:], invf2[:],
              waits=(("ac", c2),))
    for h in range(2):
        p.op("tensor", "matmul", psQ[h][0:64, 0:512],
             invf[0:1, :], trow[0:1, h * 512:(h + 1) * 512],
             start=True, stop=True, sem="pe",
             waits=(("dv", c3), ("ldc", LDC)))
    PE_ANG = p.cnt["pe"]
    for h in range(2):
        sl = slice(h * 512, (h + 1) * 512)
        p.op("vector", "scalar_tensor_tensor", scrA[0:64, sl], kst[:, sl], -TWO_PI_HI,
             psQ[h][0:64, 0:512], ALU.mult, ALU.add, waits=(("pe", PE_ANG),))
        p.op("vector", "scalar_tensor_tensor", scrA[0:64, sl], kst[:, sl], -TWO_PI_LO,
             scrA[0:64, sl], ALU.mult, ALU.add)
    p.op("vector", "tensor_scalar", scrA[0:64, :], scrA[0:64, :], float(PI_LIM),
         -float(PI_LIM), ALU.min, ALU.max)
    cs4 = p.op("scalar", "activation", SS2[0:64, :], scrA[0:64, :], AF.Sin,
               waits=(("dv", p.cnt["dv"]),))
    for h in range(2):
        sl = slice(h * 512, (h + 1) * 512)
        p.op("vector", "tensor_scalar", scrB[0:64, sl], psQ[h][0:64, 0:512], PIH, None,
             ALU.add, waits=(("pe", PE_ANG),))
    p.op("vector", "scalar_tensor_tensor", scrB[0:64, :], kct[:], -TWO_PI_HI,
         scrB[0:64, :], ALU.mult, ALU.add)
    p.op("vector", "scalar_tensor_tensor", scrB[0:64, :], kct[:], -TWO_PI_LO,
         scrB[0:64, :], ALU.mult, ALU.add)
    p.op("vector", "tensor_scalar", scrB[0:64, :], scrB[0:64, :], float(PI_LIM),
         -float(PI_LIM), ALU.min, ALU.max)
    cc5 = p.op("scalar", "activation", CC2[0:64, :], scrB[0:64, :], AF.Sin,
               waits=(("dv", p.cnt["dv"]),))
    p.op("vector", "tensor_scalar", SS2[64:128, :], SS2[0:64, :], -1.0, None,
         ALU.mult, waits=(("ac", cs4),))
    cdup = p.op("scalar", "copy", CC2[64:128, :], CC2[0:64, :], waits=(("ac", cc5),))
    ROPE_DV = p.cnt["dv"]
    ROPE_AC = p.cnt["ac"]

    for i in range(NF):
        buf = xt[i % 2]
        ld = p.dma("sync", buf[:], xin_d.ap()[i * 128:(i + 1) * 128, :], sem="ldx",
                   waits=(("ac", p.cnt["ac"]), ("dv", p.cnt["dv"])))
        cb = p.op("vector", "tensor_copy", xb[:], buf[:],
                  waits=(("ldx", ld), ("pe", p.cnt["pe"])))
        cs = p.op("scalar", "square", sq[:], buf[:],
                  waits=(("ldx", ld), ("pe", p.cnt["pe"])))
        for h in range(2):
            ps_s = psQ[0] if h == 0 else psQ[1]
            ps_q = psQ[2] if h == 0 else psQ[3]
            p.op("tensor", "matmul", ps_s[0:1, 0:512], onescolb[:],
                 xb[:, h * 512:(h + 1) * 512], start=(i == 0), stop=(i == NF - 1),
                 sem="pe", waits=(("dv", cb), ("ldc", LDC)))
            p.op("tensor", "matmul", ps_q[0:1, 0:512], onescolb[:],
                 sq[:, h * 512:(h + 1) * 512], start=(i == 0), stop=(i == NF - 1),
                 sem="pe", waits=(("ac", cs),))
    PE_STATS = p.cnt["pe"]
    for h in range(2):
        p.op("vector", "tensor_scalar", row(0, h), (psQ[0] if h == 0 else psQ[1])[0:1, 0:512],
             1.0 / D, None, ALU.mult, waits=(("pe", PE_STATS),))
        p.op("vector", "tensor_scalar", row(1, h), (psQ[2] if h == 0 else psQ[3])[0:1, 0:512],
             1.0 / D, None, ALU.mult, waits=(("pe", PE_STATS),))
    p.op("vector", "tensor_mul", row(2), row(0), row(0))
    p.op("vector", "tensor_sub", row(1), row(1), row(2))
    p.op("vector", "tensor_scalar", row(1), row(1), 1e-5, None, ALU.add)
    cac = p.op("scalar", "sqrt", row(1), row(1), waits=(("dv", p.cnt["dv"]),))
    p.op("vector", "reciprocal", row(2), row(1), waits=(("ac", cac),))
    p.op("vector", "tensor_mul", row(3), row(0), row(2))
    bc_to(bcA, row(2))
    bc_to(bcB, row(3))
    BC_DONE = p.cnt["ac"]
    for i in range(NF):
        buf = xt[i % 2]
        ld = p.dma("sync", buf[:], xin_d.ap()[i * 128:(i + 1) * 128, :], sem="ldx",
                   waits=(("dv", p.cnt["dv"]),))
        p.op("vector", "tensor_mul", scrA[:], buf[:], bcA[:],
             waits=(("ldx", ld), ("ac", BC_DONE)))
        p.op("vector", "tensor_sub", scrA[:], scrA[:], bcB[:])
        p.op("vector", "tensor_copy", XLN[:, i * NTOK:(i + 1) * NTOK], scrA[:])
    XLN_DONE = p.cnt["dv"]

    def load_wchunk(dst, src_d, c, pewait):
        return p.dma("gpsimd",
                     dst.ap().rearrange("p (a o) -> p a o", o=128),
                     src_d.ap().rearrange("(a p) o -> p a o", p=128)[:, :, c * 128:(c + 1) * 128],
                     sem=f"ldw{c % 2}", waits=(("pe", pewait),))

    pe_end = {}
    pe_ev_ac = 0
    pe_ev_dv = 0
    for c in range(48 + 16):
        is_g = c >= 48
        lw = load_wchunk(wb[c % 2], wgm_d if is_g else wqkv_d,
                         c - 48 if is_g else c, pe_end.get(c - 2, 0))
        head = c // 3
        kind = 3 if is_g else (c % 3)
        mm = {}
        for h in range(2):
            ps = psQ[2 * (c % 2) + h]
            for i in range(NF):
                mm[h] = p.op("tensor", "matmul", ps[:],
                             wb[c % 2][:, i * 128:(i + 1) * 128],
                             XLN[:, i * NTOK + h * 512:i * NTOK + h * 512 + 512],
                             start=(i == 0), stop=(i == NF - 1), sem="pe",
                             waits=(((f"ldw{c % 2}", lw), ("dv", XLN_DONE),
                                     ("ac", pe_ev_ac), ("dv", pe_ev_dv))
                                    if (i == 0 and h == 0) else ()))
        if kind == 3:
            gi = c - 48
            for h in range(2):
                ps = psQ[2 * (c % 2) + h]
                p.op("scalar", "activation", obh[:, h * 512:(h + 1) * 512],
                     ps[:], AF.Silu, waits=(("pe", mm[h]), ("sto", p.cnt["sto"])))
            pe_ev_ac, pe_ev_dv = p.cnt["ac"], p.cnt["dv"]
            p.dma("gpsimd", g_o.ap()[gi * 128:(gi + 1) * 128, :],
                  obh[:], sem="sto", waits=(("ac", p.cnt["ac"]),))
        elif kind == 2:
            for h in range(2):
                ps = psQ[2 * (c % 2) + h]
                p.op("scalar", "copy", vstage[:, h * 512:(h + 1) * 512], ps[:],
                     waits=(("pe", mm[h]), ("pe", p.cnt["pe"])))
            for j in range(8):
                cpe = p.op("tensor", "transpose", psTR[:],
                           vstage[:, j * 128:(j + 1) * 128], idn[:], sem="pe",
                           waits=(("ac", p.cnt["ac"]), ("pl", PL_ID)))
                p.op("scalar", "copy", vtr[:, j * 128:(j + 1) * 128], psTR[:],
                     waits=(("pe", cpe), ("sto", p.cnt["sto"])))
            pe_ev_ac, pe_ev_dv = p.cnt["ac"], p.cnt["dv"]
            for j in range(8):
                p.dma("gpsimd", vt_o.ap()[j * 128:(j + 1) * 128,
                                          head * 128:(head + 1) * 128],
                      vtr[:, j * 128:(j + 1) * 128], sem="sto",
                      waits=(("ac", p.cnt["ac"]),))
        else:
            for h in range(2):
                ps = psQ[2 * (c % 2) + h]
                cqe = p.op("scalar", "copy", qbh[:, h * 512:(h + 1) * 512], ps[:],
                           waits=(("pe", mm[h]), ("sto", p.cnt["sto"]),
                                  ("dv", p.cnt["dv"])))
                cqs = p.op("scalar", "square", sq[:, h * 512:(h + 1) * 512],
                           qbh[:, h * 512:(h + 1) * 512], waits=(("ac", cqe),))
                p.op("tensor", "matmul", psM[h][0:1, 0:512], onescolb[:],
                     sq[:, h * 512:(h + 1) * 512], start=True, stop=True,
                     sem="pe", waits=(("ac", cqs), ("dv", p.cnt["dv"])))
            PE_MS = p.cnt["pe"]
            p.op("vector", "tensor_scalar", row(0, 0), psM[0][0:1, 0:512],
                 1.0 / 128, 1e-6, ALU.mult, ALU.add, waits=(("pe", PE_MS),))
            p.op("vector", "tensor_scalar", row(0, 1), psM[1][0:1, 0:512],
                 1.0 / 128, 1e-6, ALU.mult, ALU.add)
            cac = p.op("scalar", "sqrt", row(0), row(0), waits=(("dv", p.cnt["dv"]),))
            p.op("vector", "reciprocal", row(0), row(0), waits=(("ac", cac),))
            bc_to(bcR, row(0))
            c5 = p.op("vector", "tensor_mul", scrA[:], qbh[:], bcR[:],
                      waits=(("ac", p.cnt["ac"]),))
            c6 = p.op("scalar", "copy", scrB[0:64, :], scrA[64:128, :],
                      waits=(("dv", c5),))
            c7 = p.op("scalar", "copy", scrB[64:128, :], scrA[0:64, :])
            p.op("vector", "tensor_mul", scrC[:], scrA[:], CC2[:],
                 waits=(("ac", ROPE_AC), ("dv", ROPE_DV)))
            p.op("vector", "tensor_mul", scrD[:], scrB[:], SS2[:],
                 waits=(("ac", c7),))
            p.op("vector", "tensor_add", scrC[:], scrC[:], scrD[:])
            if kind == 0:
                p.op("vector", "tensor_scalar", scrD[:], scrC[:],
                     float(SC_HI), None, ALU.mult)
                p.op("vector", "scalar_tensor_tensor", obh[:], scrC[:],
                     float(SC_LO), scrD[:], ALU.mult, ALU.add,
                     waits=(("sto", p.cnt["sto"]),))
                pe_ev_ac, pe_ev_dv = p.cnt["ac"], p.cnt["dv"]
                p.dma("gpsimd", qs_o.ap()[head * 128:(head + 1) * 128, :],
                      obh[:], sem="sto", waits=(("dv", p.cnt["dv"]),))
            else:
                p.op("vector", "tensor_copy", obh[:], scrC[:],
                     waits=(("sto", p.cnt["sto"]),))
                pe_ev_ac, pe_ev_dv = p.cnt["ac"], p.cnt["dv"]
                p.dma("gpsimd", k_o.ap()[head * 128:(head + 1) * 128, :],
                      obh[:], sem="sto", waits=(("dv", p.cnt["dv"]),))
        pe_end[c] = p.cnt["pe"]
    p.wait("gpsimd", ("sto", p.cnt["sto"]))
    return p


def host_attn_a_inputs(x_fm_c, ins, c):
    trow, ks, kc = rope_host_tables(c)
    return {
        "xin": x_fm_c,
        "wqkv": np.ascontiguousarray(np.float32(ins["L0_attn_W_qkv"])),
        "wgm": np.ascontiguousarray(np.float32(ins["L0_attn_W_g"])),
        "trow": trow, "ks": np.ascontiguousarray(ks), "kc": np.ascontiguousarray(kc),
        "onesc": np.ones((1, 128), dtype=np.float32),
        "onescolb": np.ones((128, 1), dtype=BF),
    }


def build_l0_attn_b(cpos):
    chunkA, chunkB = cpos, 7 - cpos
    p = Prog(f"l0attnb{cpos}")
    io = {}
    xin_d = p.dram_io(io, "xin", [D, NTOK], F32, kind="ExternalInput")
    qs_d = p.dram_io(io, "qs", [D, NTOK], BF16, kind="ExternalInput")
    kf_d = p.dram_io(io, "kf", [D, T], BF16, kind="ExternalInput")
    vf_d = p.dram_io(io, "vf", [T, D], BF16, kind="ExternalInput")
    g_d = p.dram_io(io, "go", [D, NTOK], BF16, kind="ExternalInput")
    lq1_d = p.dram_io(io, "lq1", [128, 8], F32, kind="ExternalInput")
    lk1_d = p.dram_io(io, "lk1", [128, 8], F32, kind="ExternalInput")
    lq2_d = p.dram_io(io, "lq2", [128, 8], F32, kind="ExternalInput")
    lk2_d = p.dram_io(io, "lk2", [128, 8], F32, kind="ExternalInput")
    ones_d = p.dram_io(io, "onesc", [1, 128], F32, kind="ExternalInput")
    onescol_d = p.dram_io(io, "onescol", [128, 1], F32, kind="ExternalInput")
    onescolb_d = p.dram_io(io, "onescolb", [128, 1], BF16, kind="ExternalInput")
    xout_d = p.dram_io(io, "xout", [D, NTOK], F32, kind="ExternalOutput")

    xst = [p.sbuf(f"xst{i}", [128, NTOK], F32) for i in range(2)]
    QS = p.sbuf("QS", [128, NF * NTOK], BF16)
    G = p.sbuf("G", [128, NF * NTOK], BF16)
    KT = [p.sbuf(f"KT{i}", [128, T], BF16) for i in range(2)]
    VT = [p.sbuf(f"VT{i}", [128, 32 * 256], BF16) for i in range(2)]
    PT = [p.sbuf(f"PT{i}", [128, 512], BF16) for i in range(2)]
    dacc = p.sbuf("dacc", [128, 512], F32)
    scrA = p.sbuf("scrA", [128, NTOK], F32)
    scrB = p.sbuf("scrB", [128, NTOK], F32)
    yv = [p.sbuf(f"yv{v}", [128, 4 * 512], F32) for v in range(2)]
    sqt = p.sbuf("sqt", [128, NTOK], BF16)
    MSK = p.sbuf("MSK", [128, 4 * 512], F32)
    lamt = p.sbuf("lamt", [128, 8], F32)
    lam2t = p.sbuf("lam2t", [128, 8], F32)
    lamr = p.sbuf("lamr", [1, 16], F32)
    lamcols = p.sbuf("lamcols", [128, 8], F32)
    onescol = None
    ones1 = p.sbuf("ones1", [1, 128], F32)
    onescol = p.sbuf("onescolt", [128, 1], F32)
    onescolb = p.sbuf("onescolbt", [128, 1], BF16)
    rows = p.sbuf("rows", [1, 2 * NTOK], F32)
    bcR = p.sbuf("bcR", [128, NTOK], F32)

    psS = [p.psum(f"psS{i}", [128, 512], F32) for i in range(2)]
    psY = [p.psum(f"psY{i}", [128, 512], F32) for i in range(2)]
    psBC = p.psum("psBC", [128, 512], F32)
    psR = [p.psum(f"psR{i}", [1, 512], F32) for i in range(2)]
    psL = p.psum("psL", [128, 8], F32)

    def row(i, h=None):
        if h is None:
            return rows[0:1, i * NTOK:(i + 1) * NTOK]
        return rows[0:1, i * NTOK + h * 512:i * NTOK + (h + 1) * 512]

    p.dma("sync", ones1[:], ones_d.ap(), sem="ldc")
    p.dma("sync", onescol[:], onescol_d.ap(), sem="ldc")
    p.dma("sync", onescolb[:], onescolb_d.ap(), sem="ldc")
    LDC = p.cnt["ldc"]

    def bc_to(row_ap, n=NTOK):
        last = 0
        for h in range(n // 512):
            cpe = p.op("tensor", "matmul", psBC[:, 0:512], ones1[:],
                       row_ap[0:1, h * 512:(h + 1) * 512], start=True, stop=True,
                       sem="pe", waits=(("ac", p.cnt["ac"]), ("dv", p.cnt["dv"]),
                                        ("ldc", LDC)))
            last = p.op("scalar", "copy", bcR[:, h * 512:(h + 1) * 512],
                        psBC[:, 0:512], waits=(("pe", cpe),))
        return last

    for i in range(NF):
        p.dma("scalar", QS[:, i * NTOK:(i + 1) * NTOK],
              qs_d.ap()[i * 128:(i + 1) * 128, :], sem="ldq")
        p.dma("scalar", G[:, i * NTOK:(i + 1) * NTOK],
              g_d.ap()[i * 128:(i + 1) * 128, :], sem="ldq")
    LDQ = p.cnt["ldq"]

    p.dma("sync", lamt[:], lq1_d.ap(), sem="ldl")
    p.dma("sync", lam2t[:], lk1_d.ap(), sem="ldl")
    cm = p.op("vector", "tensor_mul", lamt[:], lamt[:], lam2t[:],
              waits=(("ldl", p.cnt["ldl"]),))
    c1m = p.op("tensor", "matmul", psL[0:1, 0:8], onescol[:], lamt[:],
               start=True, stop=True, sem="pe",
               waits=(("dv", cm), ("ldc", LDC)))
    ce1 = p.op("scalar", "activation", lamr[0:1, 0:8], psL[0:1, 0:8], AF.Exp,
               waits=(("pe", c1m),))
    p.dma("sync", lamt[:], lq2_d.ap(), sem="ldl", waits=(("dv", p.cnt["dv"]), ("pe", c1m)))
    p.dma("sync", lam2t[:], lk2_d.ap(), sem="ldl")
    cm2 = p.op("vector", "tensor_mul", lamt[:], lamt[:], lam2t[:],
               waits=(("ldl", p.cnt["ldl"]),))
    c2m = p.op("tensor", "matmul", psL[0:1, 0:8], onescol[:], lamt[:],
               start=True, stop=True, sem="pe",
               waits=(("dv", cm2), ("ac", ce1)))
    ce2 = p.op("scalar", "activation", lamr[0:1, 8:16], psL[0:1, 0:8], AF.Exp,
               waits=(("pe", c2m),))
    p.op("vector", "tensor_sub", lamr[0:1, 0:8], lamr[0:1, 0:8], lamr[0:1, 8:16],
         waits=(("ac", ce2),))
    p.op("vector", "tensor_scalar", lamr[0:1, 0:8], lamr[0:1, 0:8],
         float(LAMBDA_INIT), None, ALU.add)
    cbl = p.op("tensor", "matmul", psL[:, 0:8], ones1[:], lamr[0:1, 0:8],
               start=True, stop=True, sem="pe", waits=(("dv", p.cnt["dv"]),))
    LAM_AC = p.op("scalar", "copy", lamcols[:], psL[:, 0:8], waits=(("pe", cbl),))

    cz = p.op("vector", "memset", scrB[:, 0:512], 0.0)
    cmsk = 0
    for j in range(4):
        cmsk = p.op("gpsimd", "affine_select", MSK[:, j * 512:(j + 1) * 512], scrB[:, 0:512],
                    pattern=[[1, 512]], compare_op=ALU.is_ge, fill=-1e30,
                    channel_multiplier=-1, base=-128 * j, waits=(("dv", cz),))
    MSK_PL = cmsk

    unit_pe_end = {}
    for g in range(16):
        hp, v = g // 2, g % 2
        h = 2 * hp + v
        kbuf = KT[g % 2]
        lk = p.dma("sync", kbuf[:], kf_d.ap()[h * 128:(h + 1) * 128, :],
                   sem=f"ldk{g % 2}", waits=(("pe", unit_pe_end.get(g - 2, 0)),))
        if v == 0:
            vbuf = VT[hp % 2]
            lv = p.dma("scalar", vbuf.ap().rearrange("p (a o) -> p a o", o=256),
                       vf_d.ap().rearrange("(a p) d -> p a d", p=128)[:, :, hp * 256:(hp + 1) * 256],
                       sem=f"ldv{hp % 2}",
                       waits=(("pe", unit_pe_end.get(g - 4, 0)), ("ldq", LDQ)))
            lv_cur = lv
        qrow = QS[:, h * NTOK:(h + 1) * NTOK]
        for qc, (gchunk, nkt) in enumerate(((chunkA, 4 * (chunkA + 1)),
                                            (chunkB, 4 * (chunkB + 1)))):
            qsl = qrow[:, qc * 512:(qc + 1) * 512]
            mmy = {}
            exs = {}
            das = {}
            for kt in range(nkt):
                first = (kt == 0)
                pss = psS[kt % 2]
                pt = PT[kt % 2]
                diag = (kt >= 4 * gchunk)
                cpe = p.op("tensor", "matmul", pss[:],
                           kbuf[:, kt * 128:(kt + 1) * 128], qsl,
                           start=True, stop=True, sem="pe",
                           waits=(((f"ldk{g % 2}", lk), ("ldq", LDQ),
                                   ("ac", p.cnt["ac"]), ("dv", p.cnt["dv"]))
                                  if first else
                                  (("ac", exs.get(kt - 2, 0)),)))
                if diag:
                    j = kt - 4 * gchunk
                    cadd = p.op("vector", "tensor_add", scrA[:, 0:512], pss[:],
                                MSK[:, j * 512:(j + 1) * 512],
                                waits=(("pe", cpe), ("pl", MSK_PL),
                                       ("pe", mmy.get(kt - 2, (0, 0))[1])))
                    cex = p.op("scalar", "activation", pt[:], scrA[:, 0:512],
                               AF.Exp, waits=(("dv", cadd),
                                              ("dv", das.get(kt - 2, 0)),
                                              ("pe", mmy.get(kt - 2, (0, 0))[1])))
                else:
                    cex = p.op("scalar", "activation", pt[:], pss[:], AF.Exp,
                               waits=(("pe", cpe),
                                      ("dv", das.get(kt - 2, 0)),
                                      ("pe", mmy.get(kt - 2, (0, 0))[1])))
                exs[kt] = cex
                if first:
                    cda = p.op("vector", "tensor_copy", dacc[:], pt[:],
                               waits=(("ac", cex),))
                else:
                    cda = p.op("vector", "tensor_add", dacc[:], dacc[:], pt[:],
                               waits=(("ac", cex),))
                das[kt] = cda
                for e in range(2):
                    mmy[kt] = (e, p.op("tensor", "matmul", psY[e][:],
                                       vbuf[:, kt * 256 + e * 128:kt * 256 + (e + 1) * 128],
                                       pt[:], start=first, stop=(kt == nkt - 1),
                                       sem="pe",
                                       waits=(((f"ldv{hp % 2}", lv_cur), ("ac", cex),
                                               ("dv", yev if (first and g + qc > 0) else 0))
                                              if first and e == 0 else (("ac", cex),))))
            cdr = p.op("tensor", "matmul", psR[0][0:1, 0:512], onescol[:], dacc[:],
                       start=True, stop=True, sem="pe",
                       waits=(("dv", p.cnt["dv"]), ("ldc", LDC)))
            p.op("vector", "reciprocal", row(0, 0), psR[0][0:1, 0:512],
                 waits=(("pe", cdr),))
            lb = bc_to(row(0, 0), n=512)
            for e in range(2):
                yev = p.op("vector", "tensor_mul",
                           yv[v][:, (2 * qc + e) * 512:(2 * qc + e) * 512 + 512],
                           psY[e][:], bcR[:, 0:512],
                           waits=(("ac", lb), ("pe", mmy[nkt - 1][1])))
        unit_pe_end[g] = p.cnt["pe"]
        if v == 1:
            for e in range(2):
                ft = 2 * hp + e
                for qc in range(2):
                    slq = slice((2 * qc + e) * 512, (2 * qc + e) * 512 + 512)
                    qsl2 = slice(qc * 512, (qc + 1) * 512)
                    p.op("vector", "tensor_scalar", scrA[:, qsl2],
                         yv[1][:, slq], lamcols[:, hp:hp + 1], None, ALU.mult,
                         waits=(("ac", LAM_AC),))
                    p.op("vector", "tensor_sub", scrA[:, qsl2],
                         yv[0][:, slq], scrA[:, qsl2])
                    dst = yv[0][:, slq] if e == 0 else scrB[:, qsl2]
                    cg = p.op("vector", "tensor_mul", dst, scrA[:, qsl2],
                              G[:, ft * NTOK + qc * 512:ft * NTOK + (qc + 1) * 512],
                              waits=(("ldq", LDQ),))
                    csq = p.op("scalar", "square", sqt[:, qsl2], dst,
                               waits=(("dv", cg), ("pe", p.cnt["pe"])))
                    p.op("tensor", "matmul", psR[qc][0:1, 0:512], onescolb[:],
                         sqt[:, qsl2],
                         start=(e == 0), stop=(e == 1), sem="pe",
                         waits=(("ac", csq), ("dv", p.cnt["dv"]), ("ldc", LDC)))
            PE_RMS = p.cnt["pe"]
            for hh in range(2):
                p.op("vector", "tensor_scalar", row(1, hh), psR[hh][0:1, 0:512],
                     1.0 / 256, 1e-6, ALU.mult, ALU.add, waits=(("pe", PE_RMS),))
            cac = p.op("scalar", "sqrt", row(1), row(1), waits=(("dv", p.cnt["dv"]),))
            p.op("vector", "reciprocal", row(1), row(1), waits=(("ac", cac),))
            p.op("vector", "tensor_scalar", row(1), row(1),
                 float(1.0 - LAMBDA_INIT), None, ALU.mult)
            bc_to(row(1), n=NTOK)
            for e in range(2):
                ft = 2 * hp + e
                ldxt = p.dma("sync", xst[e][:], xin_d.ap()[ft * 128:(ft + 1) * 128, :],
                             sem="ldx", waits=(("sto", p.cnt["sto"]),))
                for qc in range(2):
                    qsl2 = slice(qc * 512, (qc + 1) * 512)
                    srcap = (yv[0][:, (2 * qc) * 512:(2 * qc) * 512 + 512]
                             if e == 0 else scrB[:, qsl2])
                    p.op("vector", "tensor_mul", scrA[:, qsl2], srcap, bcR[:, qsl2],
                         waits=(("ac", p.cnt["ac"]),))
                    yev = p.op("vector", "tensor_add", xst[e][:, qsl2],
                               xst[e][:, qsl2], scrA[:, qsl2],
                               waits=(("ldx", ldxt),))
                p.dma("gpsimd", xout_d.ap()[ft * 128:(ft + 1) * 128, :],
                      xst[e][:], sem="sto", waits=(("dv", p.cnt["dv"]),))
    p.wait("gpsimd", ("sto", p.cnt["sto"]))
    return p


def _pad_lam(a):
    out = np.zeros((128, 8), dtype=np.float32)
    out[:64, :] = np.float32(a).T
    return out


def build_l0_attn_bu(p=None, io=None, static_dbg=None,
                     dyn_parts=("q", "d", "y", "k", "v")):
    if p is None:
        p = Prog("l0attnbu")
    io = io or {}
    xin_d = p.dram_io(io, "xin", [D, NTOK], F32, kind="ExternalInput")
    qs_d = p.dram_io(io, "qs", [D, NTOK], BF16, kind="ExternalInput")
    kf_d = p.dram_io(io, "kf", [D, T], BF16, kind="ExternalInput")
    vf_d = p.dram_io(io, "vf", [T, D], BF16, kind="ExternalInput")
    g_d = p.dram_io(io, "go", [D, NTOK], BF16, kind="ExternalInput")
    lq1_d = p.dram_io(io, "lq1", [128, 8], F32, kind="ExternalInput")
    lk1_d = p.dram_io(io, "lk1", [128, 8], F32, kind="ExternalInput")
    lq2_d = p.dram_io(io, "lq2", [128, 8], F32, kind="ExternalInput")
    lk2_d = p.dram_io(io, "lk2", [128, 8], F32, kind="ExternalInput")
    tbl_d = p.dram_io(io, "tbl", [1, 96], I32, kind="ExternalInput")
    ones_d = p.dram_io(io, "onesc", [1, 128], F32, kind="ExternalInput")
    onescol_d = p.dram_io(io, "onescol", [128, 1], F32, kind="ExternalInput")
    onescolb_d = p.dram_io(io, "onescolb", [128, 1], BF16, kind="ExternalInput")
    xout_d = p.dram_io(io, "xout", [D, NTOK], F32, kind="ExternalOutput")
    kperm_d = p.dram_io(io, "kperm", [D, 36 * 128], BF16, kind="Internal")
    vperm_d = p.dram_io(io, "vperm", [36 * 128, D], BF16, kind="Internal")

    xst = [p.sbuf(f"xst{i}", [128, NTOK], F32) for i in range(2)]
    Qcur = p.sbuf("Qcur", [128, NTOK], BF16)
    G = p.sbuf("G", [128, NF * NTOK], BF16)
    KT = [p.sbuf(f"KT{i}", [128, 36 * 128], BF16) for i in range(2)]
    VT = [p.sbuf(f"VT{i}", [128, 36 * 256], BF16) for i in range(2)]
    PT = [p.sbuf(f"PT{i}", [128, 512], BF16) for i in range(2)]
    scrA = p.sbuf("scrA", [128, NTOK], F32)
    scrB = p.sbuf("scrB", [128, NTOK], F32)
    yv = [p.sbuf(f"yv{v}", [128, 4 * 512], F32) for v in range(2)]
    sqt = p.sbuf("sqt", [128, NTOK], BF16)
    MSK = p.sbuf("MSK", [128, 4 * 512], F32)
    lamt = p.sbuf("lamt", [128, 8], F32)
    lam2t = p.sbuf("lam2t", [128, 8], F32)
    lamr = p.sbuf("lamr", [1, 16], F32)
    lamcols = p.sbuf("lamcols", [128, 8], F32)
    tbls = p.sbuf("tbls", [1, 96], I32)
    ones1 = p.sbuf("ones1", [1, 128], F32)
    onescol = p.sbuf("onescolt", [128, 1], F32)
    onescolb = p.sbuf("onescolbt", [128, 1], BF16)
    rows = p.sbuf("rows", [1, 2 * NTOK], F32)
    bcR = p.sbuf("bcR", [128, NTOK], F32)

    psS = [p.psum(f"psS{i}", [128, 512], F32) for i in range(2)]
    psY2 = [p.psum(f"psY{e}", [128, 2 * 512], F32) for e in range(2)]
    psDEN = p.psum("psDEN", [1, 2 * 512], F32)

    def row(i, h=None):
        if h is None:
            return rows[0:1, i * NTOK:(i + 1) * NTOK]
        return rows[0:1, i * NTOK + h * 512:i * NTOK + (h + 1) * 512]

    p.dma("sync", ones1[:], ones_d.ap(), sem="ldc")
    p.dma("sync", onescol[:], onescol_d.ap(), sem="ldc")
    p.dma("sync", onescolb[:], onescolb_d.ap(), sem="ldc")
    p.dma("sync", tbls[:], tbl_d.ap(), sem="ldc")
    LDC = p.cnt["ldc"]

    def bc_to(row_ap, n=NTOK):
        last = 0
        for h in range(n // 512):
            cpe = p.op("tensor", "matmul", psS[0][:, 0:512], ones1[:],
                       row_ap[0:1, h * 512:(h + 1) * 512], start=True, stop=True,
                       sem="pe", waits=(("ac", p.cnt["ac"]), ("dv", p.cnt["dv"]),
                                        ("ldc", LDC)))
            last = p.op("scalar", "copy", bcR[:, h * 512:(h + 1) * 512],
                        psS[0][:, 0:512], waits=(("pe", cpe),))
        return last

    for i in range(NF):
        p.dma("scalar", G[:, i * NTOK:(i + 1) * NTOK],
              g_d.ap()[i * 128:(i + 1) * 128, :], sem="ldq")
    LDQ = p.cnt["ldq"]

    p.dma("sync", lamt[:], lq1_d.ap(), sem="ldl")
    p.dma("sync", lam2t[:], lk1_d.ap(), sem="ldl")
    cm = p.op("vector", "tensor_mul", lamt[:], lamt[:], lam2t[:],
              waits=(("ldl", p.cnt["ldl"]),))
    c1m = p.op("tensor", "matmul", psS[0][0:1, 0:8], onescol[:], lamt[:],
               start=True, stop=True, sem="pe",
               waits=(("dv", cm), ("ldc", LDC)))
    ce1 = p.op("scalar", "activation", lamr[0:1, 0:8], psS[0][0:1, 0:8], AF.Exp,
               waits=(("pe", c1m),))
    p.dma("sync", lamt[:], lq2_d.ap(), sem="ldl", waits=(("dv", p.cnt["dv"]), ("pe", c1m)))
    p.dma("sync", lam2t[:], lk2_d.ap(), sem="ldl")
    cm2 = p.op("vector", "tensor_mul", lamt[:], lamt[:], lam2t[:],
               waits=(("ldl", p.cnt["ldl"]),))
    c2m = p.op("tensor", "matmul", psS[0][0:1, 0:8], onescol[:], lamt[:],
               start=True, stop=True, sem="pe",
               waits=(("dv", cm2), ("ac", ce1)))
    ce2 = p.op("scalar", "activation", lamr[0:1, 8:16], psS[0][0:1, 0:8], AF.Exp,
               waits=(("pe", c2m),))
    p.op("vector", "tensor_sub", lamr[0:1, 0:8], lamr[0:1, 0:8], lamr[0:1, 8:16],
         waits=(("ac", ce2),))
    p.op("vector", "tensor_scalar", lamr[0:1, 0:8], lamr[0:1, 0:8],
         float(LAMBDA_INIT), None, ALU.add)
    cbl = p.op("tensor", "matmul", psS[0][:, 0:8], ones1[:], lamr[0:1, 0:8],
               start=True, stop=True, sem="pe", waits=(("dv", p.cnt["dv"]), ("ac", ce2)))
    LAM_AC = p.op("scalar", "copy", lamcols[:], psS[0][:, 0:8], waits=(("pe", cbl),))

    cz = p.op("vector", "memset", scrB[:, 0:512], 0.0)
    cmsk = 0
    for j in range(4):
        cmsk = p.op("gpsimd", "affine_select", MSK[:, j * 512:(j + 1) * 512],
                    scrB[:, 0:512], pattern=[[1, 512]], compare_op=ALU.is_ge,
                    fill=-1e30, channel_multiplier=-1, base=-128 * j,
                    waits=(("dv", cz),))
    MSK_PL = cmsk

    regs = {}

    def mkreg(eng, name):
        h = {}
        def fn(engine, _n=name, _h=h):
            _h["r"] = engine.register(_n).__enter__()
        p.raw(eng, fn)
        return h

    rS = mkreg("gpsimd", "rS")
    rSv = mkreg("sync", "rSv")

    zcolb = p.sbuf("zcolb", [128, 128], BF16)
    cz0 = p.op("vector", "memset", zcolb[:], 0.0)
    cz0 = p.op("vector", "memset", PT[0][:], 0.0)
    for q2 in range(2):
        p.op("tensor", "matmul", psDEN[0:1, q2 * 512:(q2 + 1) * 512],
             zcolb[:, 0:1], PT[0][:], start=True, stop=True, sem="pe",
             waits=(("dv", cz0),))
        for e2 in range(2):
            p.op("tensor", "matmul", psY2[e2][:, q2 * 512:(q2 + 1) * 512],
                 zcolb[:], PT[0][:], start=True, stop=True, sem="pe",
                 waits=(("dv", cz0),))
    ksrc_vals = {}
    vsrc_vals = {}

    stt = static_dbg[0] if static_dbg is not None else None
    dyn = (lambda part: stt is None or part in dyn_parts)

    kvg = io.get("kv_gathered", False)
    kmax = (4 * D * NTOK - 512) if kvg else (T - 512)
    vmax = (4 * NTOK - 512) if kvg else (T - 512)
    if dyn("k"):
        def gp_hoist(engine):
            engine.wait_ge(p.semh("ldc"), LDC)
            for i in range(9):
                engine.reg_load(rS["r"], tbls[0:1, 72 + i:72 + i + 1])
                ksrc_vals[i] = engine.snap(rS["r"], min_val=0, max_val=kmax)
        p.raw("gpsimd", gp_hoist)

    if dyn("v"):
        def sy_hoist(engine):
            engine.wait_ge(p.semh("ldc"), LDC)
            for i in range(9):
                engine.reg_load(rSv["r"], tbls[0:1, 81 + i:81 + i + 1])
                vsrc_vals[i] = engine.snap(rSv["r"], min_val=0, max_val=vmax)
        p.raw("sync", sy_hoist)
    for blk in range(9):
        if dyn("k"):
            if kvg:
                ksrc_ap = (lambda _b=blk: bass.AP(kf_d, ksrc_vals[_b],
                                                  [[NTOK, D], [1, 512]]))
            else:
                ksrc_ap = (lambda _b=blk: kf_d.ap()[:, bass.ds(ksrc_vals[_b], 512)])
        else:
            _s0 = int(stt[72 + blk])
            ksrc_ap = kf_d.ap()[:, _s0:_s0 + 512]
        if dyn("v"):
            vsrc_ap = (lambda _b=blk: vf_d.ap()[bass.ds(vsrc_vals[_b], 512), :])
        else:
            _s0 = int(stt[81 + blk])
            vsrc_ap = vf_d.ap()[_s0:_s0 + 512, :]
        p.dma("gpsimd", kperm_d.ap()[:, blk * 512:(blk + 1) * 512],
              ksrc_ap, sem="ldperm",
              waits=((("ldc", LDC),) if not dyn("k") and blk == 0 else ()))
        p.dma("sync", vperm_d.ap()[blk * 512:(blk + 1) * 512, :],
              vsrc_ap, sem="ldperm2",
              waits=((("ldc", LDC),) if not dyn("v") and blk == 0 else ()))
    LDPERM = p.cnt["ldperm"]
    LDPERM2 = p.cnt["ldperm2"]

    slot_off = {}

    def slot_bit_static(s_):
        if s_ < 4:
            return 0
        if s_ < 8 or s_ >= 20:
            return 512
        return None

    def slot_val(s_):
        return slot_off[(s_ - 8) // 4]

    def pe_hoist(engine):
        engine.wait_ge(p.semh("ldc"), LDC)
        for j in range(3):
            r = engine.alloc_register()
            engine.reg_load(r, tbls[0:1, 8 + 4 * j:9 + 4 * j])
            slot_off[j] = engine.snap(r, True, min_val=0, max_val=512)
    if stt is None or any(dyn(x) for x in ("q", "d", "y")):
        p.raw("tensor", pe_hoist)

    unit_pe_end = {}
    for g in range(DBG_UNITS):
        hp, v = g // 2, g % 2
        h = 2 * hp + v
        kbuf = KT[g % 2]
        lq_cur = p.dma("scalar", Qcur[:], qs_d.ap()[h * 128:(h + 1) * 128, :],
                       sem="ldqc", waits=(("pe", p.cnt["pe"]),))
        lk = p.dma("sync", kbuf[:], kperm_d.ap()[h * 128:(h + 1) * 128, :],
                   sem=f"ldk{g % 2}",
                   waits=(("pe", unit_pe_end.get(g - 2, 0)), ("ldperm", LDPERM)))
        if v == 0:
            vbuf = VT[hp % 2]
            lv_cur = p.dma("scalar",
                           vbuf.ap().rearrange("p (a o) -> p a o", o=256),
                           vperm_d.ap().rearrange("(a p) d -> p a d", p=128)[:, :, hp * 256:(hp + 1) * 256],
                           sem=f"ldv{hp % 2}",
                           waits=(("pe", unit_pe_end.get(g - 4, 0)),
                                  ("ldq", LDQ), ("ldperm2", LDPERM2)))
        mmy = {}
        exs = {}
        das = {}
        for s in range(36):
            first = (s == 0)
            pss = psS[s % 2]
            pt = PT[s % 2]
            diag = (s < 8) and ("diag" not in DBG_STRIP)

            sbit = slot_bit_static(s)
            if dyn("q") and sbit is None:
                def mk_qsl(_s=s):
                    return Qcur[:, bass.ds(slot_val(_s), 512)]
            else:
                _qv = sbit if sbit is not None else int(stt[s])
                mk_qsl = Qcur[:, _qv:_qv + 512]

            mk_lhsT = kbuf[:, s * 128:(s + 1) * 128]

            cpe = p.op("tensor", "matmul", pss[:], mk_lhsT, mk_qsl,
                       start=True, stop=True, sem="pe",
                       waits=(((f"ldk{g % 2}", lk), ("ldqc", lq_cur),
                               ("ac", p.cnt["ac"]), ("dv", p.cnt["dv"]),
                               ("ldc", LDC))
                              if first else
                              (("ac", exs.get(s - 2, 0)),)))
            if diag:
                j = s % 4
                cadd = p.op("vector", "tensor_add", scrA[:, 0:512], pss[:],
                            MSK[:, j * 512:(j + 1) * 512],
                            waits=(("pe", cpe), ("pl", MSK_PL),
                                   ("pe", mmy.get(s - 2, 0))))
                cex = p.op("scalar", "activation", pt[:], scrA[:, 0:512],
                           AF.Exp, waits=(("dv", cadd), ("dv", das.get(s - 2, 0)),
                                          ("pe", mmy.get(s - 2, 0))))
            else:
                cex = p.op("scalar", "activation", pt[:], pss[:], AF.Exp,
                           waits=(("pe", cpe), ("dv", das.get(s - 2, 0)),
                                  ("pe", mmy.get(s - 2, 0))))
            exs[s] = cex
            if dyn("d") and sbit is None:
                def mk_den(_s=s):
                    return psDEN[:, bass.ds(slot_val(_s), 512)]
            else:
                _dv = sbit if sbit is not None else int(stt[s])
                mk_den = psDEN[:, _dv:_dv + 512]

            p.op("tensor", "matmul", mk_den, onescolb[:], pt[:],
                 start=(s in (0, 4)), stop=False, sem="pe",
                 skip_group_check=True, waits=(("ac", cex), ("ldc", LDC)))
            das[s] = 0
            for e in range(2):
                if dyn("y") and sbit is None:
                    def mk_psy(_s=s, _e=e):
                        return psY2[_e][:, bass.ds(slot_val(_s), 512)]
                else:
                    _yv = sbit if sbit is not None else (int(stt[36 + s]) // 1024) * 512
                    mk_psy = psY2[e][:, _yv:_yv + 512]

                mk_vt = vbuf[:, s * 256 + e * 128:s * 256 + (e + 1) * 128]

                mmy[s] = p.op("tensor", "matmul", mk_psy, mk_vt, pt[:],
                              start=(s in (0, 4)), stop=False, sem="pe",
                              skip_group_check=True,
                              waits=(((f"ldv{hp % 2}", lv_cur), ("ac", cex),
                                      ("dv", p.cnt["dv"]))
                                     if s in (0, 1) else (("ac", cex),)))
        PE_SLOTS = p.cnt["pe"]
        for qc in ([] if "evac" in DBG_STRIP else range(2)):
            p.op("vector", "reciprocal", row(0, 0),
                 psDEN[0:1, qc * 512:(qc + 1) * 512],
                 waits=(("pe", PE_SLOTS),))
            lb = bc_to(row(0, 0), n=512)
            for e in range(2):
                p.op("vector", "tensor_mul",
                     yv[v][:, (2 * qc + e) * 512:(2 * qc + e) * 512 + 512],
                     psY2[e][:, qc * 512:qc * 512 + 512],
                     bcR[:, 0:512],
                     waits=(("ac", lb), ("pe", p.cnt["pe"])))
        unit_pe_end[g] = p.cnt["pe"]
        if v == 1 and "comb" not in DBG_STRIP:
            for e in range(2):
                ft = 2 * hp + e
                for qc in range(2):
                    slq = slice((2 * qc + e) * 512, (2 * qc + e) * 512 + 512)
                    qsl2 = slice(qc * 512, (qc + 1) * 512)
                    p.op("vector", "tensor_scalar", scrA[:, qsl2],
                         yv[1][:, slq], lamcols[:, hp:hp + 1], None, ALU.mult,
                         waits=(("ac", LAM_AC),))
                    p.op("vector", "tensor_sub", scrA[:, qsl2],
                         yv[0][:, slq], scrA[:, qsl2])
                    dst = yv[0][:, slq] if e == 0 else scrB[:, qsl2]
                    cg = p.op("vector", "tensor_mul", dst, scrA[:, qsl2],
                              G[:, ft * NTOK + qc * 512:ft * NTOK + (qc + 1) * 512],
                              waits=(("ldq", LDQ),))
                    csq = p.op("scalar", "square", sqt[:, qsl2], dst,
                               waits=(("dv", cg), ("pe", p.cnt["pe"])))
                    tgt = psDEN[0:1, 0:512] if qc == 0 else psDEN[0:1, 512:1024]
                    p.op("tensor", "matmul", tgt, onescolb[:], sqt[:, qsl2],
                         start=(e == 0), stop=(e == 1), sem="pe",
                         waits=(("ac", csq), ("dv", p.cnt["dv"]), ("ldc", LDC)))
            PE_RMS = p.cnt["pe"]
            p.op("vector", "tensor_scalar", row(1, 0), psDEN[0:1, 0:512],
                 1.0 / 256, 1e-6, ALU.mult, ALU.add, waits=(("pe", PE_RMS),))
            p.op("vector", "tensor_scalar", row(1, 1), psDEN[0:1, 512:1024],
                 1.0 / 256, 1e-6, ALU.mult, ALU.add)
            cac = p.op("scalar", "sqrt", row(1), row(1), waits=(("dv", p.cnt["dv"]),))
            p.op("vector", "reciprocal", row(1), row(1), waits=(("ac", cac),))
            p.op("vector", "tensor_scalar", row(1), row(1),
                 float(1.0 - LAMBDA_INIT), None, ALU.mult)
            bc_to(row(1), n=NTOK)
            for e in range(2):
                ft = 2 * hp + e
                ldxt = p.dma("sync", xst[e][:], xin_d.ap()[ft * 128:(ft + 1) * 128, :],
                             sem="ldx", waits=(("sto", p.cnt["sto"]),))
                for qc in range(2):
                    qsl2 = slice(qc * 512, (qc + 1) * 512)
                    srcap = (yv[0][:, (2 * qc) * 512:(2 * qc) * 512 + 512]
                             if e == 0 else scrB[:, qsl2])
                    p.op("vector", "tensor_mul", scrA[:, qsl2], srcap, bcR[:, qsl2],
                         waits=(("ac", p.cnt["ac"]),))
                    p.op("vector", "tensor_add", xst[e][:, qsl2],
                         xst[e][:, qsl2], scrA[:, qsl2],
                         waits=(("ldx", ldxt),))
                p.dma("gpsimd", xout_d.ap()[ft * 128:(ft + 1) * 128, :],
                      xst[e][:], sem="sto", waits=(("dv", p.cnt["dv"]),))
    p.wait("gpsimd", ("sto", p.cnt["sto"]))
    return p


def host_attn_b_table(c, gathered=False):
    cp = c % 4
    kA, kB = 4 * (cp + 1), 4 * (8 - cp)
    slots = []
    for s in range(4):
        slots.append(((kA - 4 + s) * 128, 0))
    for s in range(4):
        slots.append(((kB - 4 + s) * 128, 1))
    for k in range(kA - 4):
        slots.append((k * 128, 0))
    for k in range(kB - 4):
        slots.append((k * 128, 1))
    assert len(slots) == 36
    t = np.zeros((1, 96), dtype=np.int32)
    t[0, 0:36] = [b * 512 for _, b in slots]
    t[0, 36:72] = [b * 1024 for _, b in slots]
    gchunks = [cp, 7 - cp] + list(range(cp)) + list(range(7 - cp))
    if gathered:
        ks, vs = [], []
        for g in gchunks:
            cc, half = min(g, 7 - g), (0 if g < 4 else 1)
            ks.append(cc * D * NTOK + half * 512)
            vs.append(cc * NTOK + half * 512)
        t[0, 72:81] = ks
        t[0, 81:90] = vs
    else:
        srcs = [512 * g for g in gchunks]
        t[0, 72:81] = srcs
        t[0, 81:90] = srcs
    return t


def host_attn_b_inputs(x_fm_c, qs, kfull, vfull, g, ins):
    return {
        "xin": x_fm_c, "qs": qs, "kf": kfull, "vf": vfull, "go": g,
        "lq1": _pad_lam(ins["L0_attn_lambda_q1"]),
        "lk1": _pad_lam(ins["L0_attn_lambda_k1"]),
        "lq2": _pad_lam(ins["L0_attn_lambda_q2"]),
        "lk2": _pad_lam(ins["L0_attn_lambda_k2"]),
        "onesc": np.ones((1, 128), dtype=np.float32),
        "onescol": np.ones((128, 1), dtype=np.float32),
        "onescolb": np.ones((128, 1), dtype=BF),
    }


def build_l1_ffn(p=None, io=None):
    if p is None:
        p = Prog("l1ffn")
    io = io or {}
    STO0 = p.cnt["sto"]
    xin_d = p.dram_io(io, "xin", [D, NTOK], F32, kind="ExternalInput")
    fs_d = p.dram_io(io, "fsum", [D, D], BF16, kind="ExternalInput")
    bias_d = p.dram_io(io, "bias", [128, 16], F32, kind="ExternalInput")
    ones_d = p.dram_io(io, "onesc", [1, 128], F32, kind="ExternalInput")
    onescolb_d = p.dram_io(io, "onescolb", [128, 1], BF16, kind="ExternalInput")
    xout_d = p.dram_io(io, "xout", [D, NTOK], F32, kind="ExternalOutput")

    xt = [p.sbuf(f"xt{i}", [128, NTOK], F32) for i in range(2)]
    xb = p.sbuf("xb", [128, NTOK], BF16)
    sq = p.sbuf("sq", [128, NTOK], BF16)
    scrA = p.sbuf("scrA", [128, NTOK], F32)
    scrB = p.sbuf("scrB", [128, NTOK], F32)
    scrC = p.sbuf("scrC", [128, NTOK], F32)
    XLN = p.sbuf("XLN", [128, NF * NTOK], BF16)
    wb = [p.sbuf(f"wb{i}", [128, NF * 128], BF16) for i in range(2)]
    hb = p.sbuf("hb", [128, 16], F32)
    ones1 = p.sbuf("ones1", [1, 128], F32)
    onescolb = p.sbuf("onescolbt", [128, 1], BF16)
    rows = p.sbuf("rows", [1, 4 * NTOK], F32)
    bcA = p.sbuf("bcA", [128, NTOK], F32)
    bcB = p.sbuf("bcB", [128, NTOK], F32)

    psQ = [p.psum(f"psQ{i}", [128, 512], F32) for i in range(4)]
    psBC = p.psum("psBC", [128, 512], F32)

    C8 = float(np.float32(0.5 / np.sqrt(np.float64(8.0))))

    def row(i, h=None):
        if h is None:
            return rows[0:1, i * NTOK:(i + 1) * NTOK]
        return rows[0:1, i * NTOK + h * 512:i * NTOK + (h + 1) * 512]

    p.dma("sync", ones1[:], ones_d.ap(), sem="ldc")
    p.dma("sync", onescolb[:], onescolb_d.ap(), sem="ldc")
    lb0 = p.dma("sync", hb[:], bias_d.ap(), sem="ldc")
    LDC = p.cnt["ldc"]
    p.op("vector", "tensor_scalar", hb[:], hb[:], 0.5, None, ALU.mult,
         waits=(("ldc", LDC),))
    HB_DV = p.cnt["dv"]

    def bc_to(dst, row_ap, n=NTOK):
        last = 0
        for h in range(n // 512):
            cpe = p.op("tensor", "matmul", psBC[:, 0:512], ones1[:],
                       row_ap[0:1, h * 512:(h + 1) * 512], start=True, stop=True,
                       sem="pe", waits=(("ac", p.cnt["ac"]), ("dv", p.cnt["dv"]),
                                        ("ldc", LDC)))
            last = p.op("scalar", "copy", dst[:, h * 512:(h + 1) * 512],
                        psBC[:, 0:512], waits=(("pe", cpe),))
        return last

    for i in range(NF):
        buf = xt[i % 2]
        ld = p.dma("sync", buf[:], xin_d.ap()[i * 128:(i + 1) * 128, :], sem="ldx",
                   waits=(("ac", p.cnt["ac"]), ("dv", p.cnt["dv"])))
        cb = p.op("vector", "tensor_copy", xb[:], buf[:],
                  waits=(("ldx", ld), ("pe", p.cnt["pe"])))
        cs = p.op("scalar", "square", sq[:], buf[:],
                  waits=(("ldx", ld), ("pe", p.cnt["pe"])))
        for h in range(2):
            ps_s = psQ[0] if h == 0 else psQ[1]
            ps_q = psQ[2] if h == 0 else psQ[3]
            p.op("tensor", "matmul", ps_s[0:1, 0:512], onescolb[:],
                 xb[:, h * 512:(h + 1) * 512], start=(i == 0), stop=(i == NF - 1),
                 sem="pe", waits=(("dv", cb), ("ldc", LDC)))
            p.op("tensor", "matmul", ps_q[0:1, 0:512], onescolb[:],
                 sq[:, h * 512:(h + 1) * 512], start=(i == 0), stop=(i == NF - 1),
                 sem="pe", waits=(("ac", cs),))
    PE_STATS = p.cnt["pe"]
    for h in range(2):
        p.op("vector", "tensor_scalar", row(0, h), (psQ[0] if h == 0 else psQ[1])[0:1, 0:512],
             1.0 / D, None, ALU.mult, waits=(("pe", PE_STATS),))
        p.op("vector", "tensor_scalar", row(1, h), (psQ[2] if h == 0 else psQ[3])[0:1, 0:512],
             1.0 / D, None, ALU.mult, waits=(("pe", PE_STATS),))
    p.op("vector", "tensor_mul", row(2), row(0), row(0))
    p.op("vector", "tensor_sub", row(1), row(1), row(2))
    p.op("vector", "tensor_scalar", row(1), row(1), 1e-5, None, ALU.add)
    cac = p.op("scalar", "sqrt", row(1), row(1), waits=(("dv", p.cnt["dv"]),))
    p.op("vector", "reciprocal", row(2), row(1), waits=(("ac", cac),))
    p.op("vector", "tensor_mul", row(3), row(0), row(2))
    bc_to(bcA, row(2))
    bc_to(bcB, row(3))
    BC_DONE = p.cnt["ac"]
    for i in range(NF):
        buf = xt[i % 2]
        ld = p.dma("sync", buf[:], xin_d.ap()[i * 128:(i + 1) * 128, :], sem="ldx",
                   waits=(("dv", p.cnt["dv"]),))
        p.op("vector", "tensor_mul", scrA[:], buf[:], bcA[:],
             waits=(("ldx", ld), ("ac", BC_DONE)))
        p.op("vector", "tensor_sub", scrA[:], scrA[:], bcB[:])
        p.op("vector", "tensor_copy", XLN[:, i * NTOK:(i + 1) * NTOK], scrA[:])
    XLN_DONE = p.cnt["dv"]

    pe_end = {}
    ev = {}
    for c in range(NF):
        if io.get("gather_fs", False):
            fsrc = (fs_d.ap().rearrange("(A p) o -> p A o", p=128)
                    [:, (c // 2) * 16:(c // 2 + 1) * 16,
                     (c % 2) * 128:(c % 2 + 1) * 128])
        else:
            fsrc = (fs_d.ap().rearrange("(a p) o -> p a o", p=128)
                    [:, :, c * 128:(c + 1) * 128])
        lw = p.dma("gpsimd", wb[c % 2].ap().rearrange("p (a o) -> p a o", o=128),
                   fsrc, sem=f"ldw{c % 2}", waits=(("pe", pe_end.get(c - 2, 0)),))
        for h in range(2):
            ps = psQ[2 * (c % 2) + h]
            mm = 0
            for i in range(NF):
                mm = p.op("tensor", "matmul", ps[:],
                          wb[c % 2][:, i * 128:(i + 1) * 128],
                          XLN[:, i * NTOK + h * 512:i * NTOK + h * 512 + 512],
                          start=(i == 0), stop=(i == NF - 1), sem="pe",
                          waits=(((f"ldw{c % 2}", lw), ("dv", XLN_DONE),
                                  ("dv", ev.get((c - 2, h), 0))) if i == 0 else ()))
            n = 2 * c + h
            ldx2 = p.dma("sync", xt[c % 2][:, h * 512:(h + 1) * 512],
                         xin_d.ap()[c * 128:(c + 1) * 128, h * 512:(h + 1) * 512],
                         sem="ldx2", waits=(("sto", STO0 + 16 * n), ("dv", XLN_DONE)))
            c1 = p.op("vector", "tensor_scalar", scrB[:, 0:512], ps[:],
                      C8, hb[:, c:c + 1], ALU.mult, ALU.add,
                      waits=(("pe", mm), ("dv", HB_DV)))
            evd = scrC[:, (n % 2) * 512:(n % 2) * 512 + 512]
            c2 = p.op("vector", "tensor_add", evd, scrB[:, 0:512],
                      xt[c % 2][:, h * 512:(h + 1) * 512],
                      waits=(("ldx2", ldx2), ("sto", STO0 + 16 * n)))
            ev[(c, h)] = c2
            p.dma("gpsimd", xout_d.ap()[c * 128:(c + 1) * 128, h * 512:(h + 1) * 512],
                  evd, sem="sto", waits=(("dv", c2),))
        pe_end[c] = p.cnt["pe"]
    p.wait("gpsimd", ("sto", p.cnt["sto"]))
    return p


def host_l1_ffn_inputs(x_fm_c, fsum, ins):
    return {
        "xin": x_fm_c, "fsum": fsum,
        "bias": np.ascontiguousarray(np.float32(ins["L1_ffn_bias"]).reshape(16, 128).T),
        "onesc": np.ones((1, 128), dtype=np.float32),
        "onescolb": np.ones((128, 1), dtype=BF),
    }


def build_l1_attn_a(p=None, io=None):
    if p is None:
        p = Prog("l1attna")
    io = io or {}
    xin_d = p.dram_io(io, "xin", [D, NTOK], F32, kind="ExternalInput")
    wq_d = p.dram_io(io, "wq", [D, D], F32, kind="ExternalInput")
    wk_d = p.dram_io(io, "wk", [D, D], F32, kind="ExternalInput")
    wv_d = p.dram_io(io, "wv", [D, D], F32, kind="ExternalInput")
    bq_d = p.dram_io(io, "bq", [128, 16], F32, kind="ExternalInput")
    bk_d = p.dram_io(io, "bk", [128, 16], F32, kind="ExternalInput")
    bv_d = p.dram_io(io, "bv", [128, 16], F32, kind="ExternalInput")
    ones_d = p.dram_io(io, "onesc", [1, 128], F32, kind="ExternalInput")
    onescolb_d = p.dram_io(io, "onescolb", [128, 1], BF16, kind="ExternalInput")
    qs_o = p.dram_io(io, "qs", [D, NTOK], BF16, kind="ExternalOutput")
    k_o = p.dram_io(io, "ko", [D, NTOK], BF16, kind="ExternalOutput")
    vt_o = p.dram_io(io, "vt", [NTOK, D], BF16, kind="ExternalOutput")

    xt = [p.sbuf(f"xt{i}", [128, NTOK], F32) for i in range(2)]
    xb = p.sbuf("xb", [128, NTOK], BF16)
    sq = p.sbuf("sq", [128, NTOK], BF16)
    scrA = p.sbuf("scrA", [128, NTOK], F32)
    scrB = p.sbuf("scrB", [128, NTOK], F32)
    XLN = p.sbuf("XLN", [128, NF * NTOK], BF16)
    wb = [p.sbuf(f"wb{i}", [128, NF * 128], BF16) for i in range(2)]
    obh = p.sbuf("obh", [128, NTOK], BF16)
    vstage = p.sbuf("vstage", [128, NTOK], BF16)
    vtr = p.sbuf("vtr", [128, NTOK], BF16)
    bqs = p.sbuf("bqs", [128, 48], F32)
    ones1 = p.sbuf("ones1", [1, 128], F32)
    onescolb = p.sbuf("onescolbt", [128, 1], BF16)
    rows = p.sbuf("rows", [1, 4 * NTOK], F32)
    bcA = p.sbuf("bcA", [128, NTOK], F32)
    bcB = p.sbuf("bcB", [128, NTOK], F32)
    idn = p.sbuf("idnf", [128, 128], BF16)
    ones128 = p.sbuf("ones128", [128, 128], BF16)

    psQ = [p.psum(f"psQ{i}", [128, 512], F32) for i in range(4)]
    psBC = p.psum("psBC", [128, 512], F32)
    psTR = p.psum("psTR", [128, 128], BF16)

    def row(i, h=None):
        if h is None:
            return rows[0:1, i * NTOK:(i + 1) * NTOK]
        return rows[0:1, i * NTOK + h * 512:i * NTOK + (h + 1) * 512]

    p.dma("sync", ones1[:], ones_d.ap(), sem="ldc")
    p.dma("sync", onescolb[:], onescolb_d.ap(), sem="ldc")
    p.dma("sync", bqs[:, 0:16], bq_d.ap(), sem="ldc")
    p.dma("sync", bqs[:, 16:32], bk_d.ap(), sem="ldc")
    p.dma("sync", bqs[:, 32:48], bv_d.ap(), sem="ldc")
    LDC = p.cnt["ldc"]
    cmo = p.op("vector", "memset", ones128[:], 1.0, waits=(("ldc", LDC),))
    p.op("gpsimd", "affine_select", idn[:], ones128[:], pattern=[[-1, 128]],
         compare_op=ALU.is_equal, fill=0.0, channel_multiplier=1,
         waits=(("dv", cmo),))
    PL_ID = p.cnt["pl"]

    def bc_to(dst, row_ap, n=NTOK):
        last = 0
        for h in range(n // 512):
            cpe = p.op("tensor", "matmul", psBC[:, 0:512], ones1[:],
                       row_ap[0:1, h * 512:(h + 1) * 512], start=True, stop=True,
                       sem="pe", waits=(("ac", p.cnt["ac"]), ("dv", p.cnt["dv"]),
                                        ("ldc", LDC)))
            last = p.op("scalar", "copy", dst[:, h * 512:(h + 1) * 512],
                        psBC[:, 0:512], waits=(("pe", cpe),))
        return last

    for i in range(NF):
        buf = xt[i % 2]
        ld = p.dma("sync", buf[:], xin_d.ap()[i * 128:(i + 1) * 128, :], sem="ldx",
                   waits=(("ac", p.cnt["ac"]), ("dv", p.cnt["dv"])))
        cb = p.op("vector", "tensor_copy", xb[:], buf[:],
                  waits=(("ldx", ld), ("pe", p.cnt["pe"])))
        cs = p.op("scalar", "square", sq[:], buf[:],
                  waits=(("ldx", ld), ("pe", p.cnt["pe"])))
        for h in range(2):
            ps_s = psQ[0] if h == 0 else psQ[1]
            ps_q = psQ[2] if h == 0 else psQ[3]
            p.op("tensor", "matmul", ps_s[0:1, 0:512], onescolb[:],
                 xb[:, h * 512:(h + 1) * 512], start=(i == 0), stop=(i == NF - 1),
                 sem="pe", waits=(("dv", cb), ("ldc", LDC)))
            p.op("tensor", "matmul", ps_q[0:1, 0:512], onescolb[:],
                 sq[:, h * 512:(h + 1) * 512], start=(i == 0), stop=(i == NF - 1),
                 sem="pe", waits=(("ac", cs),))
    PE_STATS = p.cnt["pe"]
    for h in range(2):
        p.op("vector", "tensor_scalar", row(0, h), (psQ[0] if h == 0 else psQ[1])[0:1, 0:512],
             1.0 / D, None, ALU.mult, waits=(("pe", PE_STATS),))
        p.op("vector", "tensor_scalar", row(1, h), (psQ[2] if h == 0 else psQ[3])[0:1, 0:512],
             1.0 / D, None, ALU.mult, waits=(("pe", PE_STATS),))
    p.op("vector", "tensor_mul", row(2), row(0), row(0))
    p.op("vector", "tensor_sub", row(1), row(1), row(2))
    p.op("vector", "tensor_scalar", row(1), row(1), 1e-5, None, ALU.add)
    cac = p.op("scalar", "sqrt", row(1), row(1), waits=(("dv", p.cnt["dv"]),))
    p.op("vector", "reciprocal", row(2), row(1), waits=(("ac", cac),))
    p.op("vector", "tensor_mul", row(3), row(0), row(2))
    bc_to(bcA, row(2))
    bc_to(bcB, row(3))
    BC_DONE = p.cnt["ac"]
    for i in range(NF):
        buf = xt[i % 2]
        ld = p.dma("sync", buf[:], xin_d.ap()[i * 128:(i + 1) * 128, :], sem="ldx",
                   waits=(("dv", p.cnt["dv"]),))
        p.op("vector", "tensor_mul", scrA[:], buf[:], bcA[:],
             waits=(("ldx", ld), ("ac", BC_DONE)))
        p.op("vector", "tensor_sub", scrA[:], scrA[:], bcB[:])
        p.op("vector", "tensor_copy", XLN[:, i * NTOK:(i + 1) * NTOK], scrA[:])
    XLN_DONE = p.cnt["dv"]

    W = [wq_d, wk_d, wv_d]
    pe_end = {}
    pe_ev_ac = 0
    pe_ev_dv = 0
    for c in range(48):
        kind = c // 16
        head = c % 16
        lw = p.dma("gpsimd", wb[c % 2].ap().rearrange("p (a o) -> p a o", o=128),
                   W[kind].ap().rearrange("(a p) o -> p a o", p=128)[:, :, head * 128:(head + 1) * 128],
                   sem=f"ldw{c % 2}", waits=(("pe", pe_end.get(c - 2, 0)),))
        mm = {}
        for h in range(2):
            ps = psQ[2 * (c % 2) + h]
            for i in range(NF):
                mm[h] = p.op("tensor", "matmul", ps[:],
                             wb[c % 2][:, i * 128:(i + 1) * 128],
                             XLN[:, i * NTOK + h * 512:i * NTOK + h * 512 + 512],
                             start=(i == 0), stop=(i == NF - 1), sem="pe",
                             waits=(((f"ldw{c % 2}", lw), ("dv", XLN_DONE),
                                     ("ac", pe_ev_ac), ("dv", pe_ev_dv))
                                    if (i == 0 and h == 0) else ()))
        bcol = bqs[:, kind * 16 + head:kind * 16 + head + 1]
        if kind == 2:
            for h in range(2):
                ps = psQ[2 * (c % 2) + h]
                p.op("vector", "tensor_scalar", vstage[:, h * 512:(h + 1) * 512],
                     ps[:], bcol, None, ALU.add,
                     waits=(("pe", mm[h]), ("pe", p.cnt["pe"]), ("ldc", LDC)))
            for j in range(8):
                cpe = p.op("tensor", "transpose", psTR[:],
                           vstage[:, j * 128:(j + 1) * 128], idn[:], sem="pe",
                           waits=(("dv", p.cnt["dv"]), ("ac", p.cnt["ac"]),
                                  ("pl", PL_ID)))
                p.op("scalar", "copy", vtr[:, j * 128:(j + 1) * 128], psTR[:],
                     waits=(("pe", cpe), ("sto", p.cnt["sto"])))
            pe_ev_ac, pe_ev_dv = p.cnt["ac"], p.cnt["dv"]
            for j in range(8):
                p.dma("gpsimd", vt_o.ap()[j * 128:(j + 1) * 128,
                                          head * 128:(head + 1) * 128],
                      vtr[:, j * 128:(j + 1) * 128], sem="sto",
                      waits=(("ac", p.cnt["ac"]),))
        else:
            for h in range(2):
                ps = psQ[2 * (c % 2) + h]
                if kind == 0:
                    p.op("vector", "tensor_scalar", scrA[:, h * 512:(h + 1) * 512],
                         ps[:], bcol, None, ALU.add,
                         waits=(("pe", mm[h]), ("ldc", LDC), ("sto", p.cnt["sto"])))
                else:
                    p.op("vector", "tensor_scalar", obh[:, h * 512:(h + 1) * 512],
                         ps[:], bcol, None, ALU.add,
                         waits=(("pe", mm[h]), ("ldc", LDC), ("sto", p.cnt["sto"])))
            if kind == 0:
                p.op("vector", "tensor_scalar", scrB[:], scrA[:],
                     float(SC_HI), None, ALU.mult)
                p.op("vector", "scalar_tensor_tensor", obh[:], scrA[:],
                     float(SC_LO), scrB[:], ALU.mult, ALU.add)
            pe_ev_ac, pe_ev_dv = p.cnt["ac"], p.cnt["dv"]
            p.dma("gpsimd", (qs_o if kind == 0 else k_o).ap()[head * 128:(head + 1) * 128, :],
                  obh[:], sem="sto", waits=(("dv", p.cnt["dv"]),))
        pe_end[c] = p.cnt["pe"]
    p.wait("gpsimd", ("sto", p.cnt["sto"]))
    return p


def host_l1_attn_a_inputs(x_fm_c, ins):
    rb = lambda k: np.ascontiguousarray(np.float32(ins[k]).reshape(16, 128).T)
    return {
        "xin": x_fm_c,
        "wq": np.ascontiguousarray(np.float32(ins["L1_attn_Wq"])),
        "wk": np.ascontiguousarray(np.float32(ins["L1_attn_Wk"])),
        "wv": np.ascontiguousarray(np.float32(ins["L1_attn_Wv"])),
        "bq": rb("L1_attn_bq"), "bk": rb("L1_attn_bk"), "bv": rb("L1_attn_bv"),
        "onesc": np.ones((1, 128), dtype=np.float32),
        "onescolb": np.ones((128, 1), dtype=BF),
    }


def build_l1_attn_b(p=None, io=None):
    if p is None:
        p = Prog("l1attnb")
    io = io or {}
    qs_d = p.dram_io(io, "qs", [D, NTOK], BF16, kind="ExternalInput")
    kf_d = p.dram_io(io, "kf", [D, T], BF16, kind="ExternalInput")
    vf_d = p.dram_io(io, "vf", [T, D], BF16, kind="ExternalInput")
    ones_d = p.dram_io(io, "onesc", [1, 128], F32, kind="ExternalInput")
    onescol_d = p.dram_io(io, "onescol", [128, 1], F32, kind="ExternalInput")
    oc_o = p.dram_io(io, "oc", [128, NTOK * 16], BF16, kind="ExternalOutput")

    QS = p.sbuf("QS", [128, NF * NTOK], BF16)
    KT = [p.sbuf(f"KT{i}", [128, T], BF16) for i in range(2)]
    VT = [p.sbuf(f"VT{i}", [128, 32 * 128], BF16) for i in range(2)]
    PT = [p.sbuf(f"PT{i}", [128, 512], BF16) for i in range(2)]
    dacc = p.sbuf("dacc", [128, 512], F32)
    OC = p.sbuf("OC", [128, NTOK * 16], BF16)
    ones1 = p.sbuf("ones1", [1, 128], F32)
    onescol = p.sbuf("onescolt", [128, 1], F32)
    rows = p.sbuf("rows", [1, NTOK], F32)
    bcR = p.sbuf("bcR", [128, 512], F32)
    negc = p.sbuf("negc", [128, 1], F32)

    psS = [p.psum(f"psS{i}", [128, 512], F32) for i in range(2)]
    psY = [p.psum(f"psY{i}", [128, 512], F32) for i in range(2)]
    psBC = p.psum("psBC", [128, 512], F32)
    psR = p.psum("psR", [1, 512], F32)

    p.dma("sync", ones1[:], ones_d.ap(), sem="ldc")
    p.dma("sync", onescol[:], onescol_d.ap(), sem="ldc")
    LDC = p.cnt["ldc"]
    NEG_DV = p.op("vector", "memset", negc[:], -20.0)
    for i in range(NF):
        p.dma("scalar", QS[:, i * NTOK:(i + 1) * NTOK],
              qs_d.ap()[i * 128:(i + 1) * 128, :], sem="ldq")
    LDQ = p.cnt["ldq"]

    kvg = io.get("kv_gathered", False)
    unit_pe_end = {}
    for h in range(NH):
        kbuf = KT[h % 2]
        vbuf = VT[h % 2]
        if kvg:
            lk = lv = 0
            for g in range(8):
                cc, half = min(g, 7 - g), (0 if g < 4 else 1)
                lk = p.dma("sync", kbuf[:, g * 512:(g + 1) * 512],
                           kf_d.ap()[cc * D + h * 128:cc * D + (h + 1) * 128,
                                     half * 512:(half + 1) * 512],
                           sem=f"ldk{h % 2}",
                           waits=((("pe", unit_pe_end.get(h - 2, 0)),)
                                  if g == 0 else ()))
                lv = p.dma("scalar",
                           vbuf.ap().rearrange("p (a o) -> p a o", o=128)[:, 4 * g:4 * g + 4, :],
                           vf_d.ap().rearrange("(x p) d -> p x d", p=128)
                           [:, cc * 8 + half * 4:cc * 8 + half * 4 + 4,
                            h * 128:(h + 1) * 128],
                           sem=f"ldv{h % 2}",
                           waits=((("pe", unit_pe_end.get(h - 2, 0)), ("ldq", LDQ))
                                  if g == 0 else ()))
        else:
            lk = p.dma("sync", kbuf[:], kf_d.ap()[h * 128:(h + 1) * 128, :],
                       sem=f"ldk{h % 2}", waits=(("pe", unit_pe_end.get(h - 2, 0)),))
            lv = p.dma("scalar", vbuf.ap().rearrange("p (a o) -> p a o", o=128),
                       vf_d.ap().rearrange("(a p) d -> p a d", p=128)[:, :, h * 128:(h + 1) * 128],
                       sem=f"ldv{h % 2}", waits=(("pe", unit_pe_end.get(h - 2, 0)), ("ldq", LDQ)))
        qrow = QS[:, h * NTOK:(h + 1) * NTOK]
        for qc in range(2):
            qsl = qrow[:, qc * 512:(qc + 1) * 512]
            mmy = {}
            exs = {}
            das = {}
            yevs = {}
            for kt in range(32):
                first = (kt == 0)
                pss = psS[kt % 2]
                pt = PT[kt % 2]
                cpe = p.op("tensor", "matmul", pss[:],
                           kbuf[:, kt * 128:(kt + 1) * 128], qsl,
                           start=True, stop=True, sem="pe",
                           waits=(((f"ldk{h % 2}", lk), ("ldq", LDQ),
                                   ("ac", p.cnt["ac"]), ("dv", p.cnt["dv"]))
                                  if first else
                                  (("ac", exs.get(kt - 2, 0)),)))
                cex = p.op("scalar", "activation", pt[:], pss[:], AF.Exp,
                           bias=negc[:, 0:1], scale=1.0,
                           waits=(("pe", cpe), ("dv", das.get(kt - 2, 0)),
                                  ("pe", mmy.get(kt - 2, 0)), ("dv", NEG_DV)))
                exs[kt] = cex
                if first:
                    cda = p.op("vector", "tensor_copy", dacc[:], pt[:],
                               waits=(("ac", cex),))
                else:
                    cda = p.op("vector", "tensor_add", dacc[:], dacc[:], pt[:],
                               waits=(("ac", cex),))
                das[kt] = cda
                mmy[kt] = p.op("tensor", "matmul", psY[qc][:],
                               vbuf[:, kt * 128:(kt + 1) * 128],
                               pt[:], start=first, stop=(kt == 31), sem="pe",
                               waits=(((f"ldv{h % 2}", lv), ("ac", cex),
                                       ("dv", yevs.get("prev", 0)))
                                      if first else (("ac", cex),)))
            cdr = p.op("tensor", "matmul", psR[0:1, 0:512], onescol[:], dacc[:],
                       start=True, stop=True, sem="pe",
                       waits=(("dv", p.cnt["dv"]), ("ldc", LDC)))
            p.op("vector", "reciprocal", rows[0:1, 0:512], psR[0:1, 0:512],
                 waits=(("pe", cdr),))
            cpe2 = p.op("tensor", "matmul", psBC[:, 0:512], ones1[:],
                        rows[0:1, 0:512], start=True, stop=True, sem="pe",
                        waits=(("ac", p.cnt["ac"]), ("dv", p.cnt["dv"]), ("ldc", LDC)))
            cbc = p.op("scalar", "copy", bcR[:], psBC[:, 0:512], waits=(("pe", cpe2),))
            ocv = bass.AP(OC, h + qc * 512 * 16,
                          [[NTOK * 16, 128], [16, 512]])
            yevs["prev"] = p.op("vector", "tensor_mul", ocv, psY[qc][:], bcR[:],
                                waits=(("ac", cbc), ("pe", mmy[31]),
                                       ("sto", p.cnt["sto"])))
        unit_pe_end[h] = p.cnt["pe"]
    OCD = p.cnt["dv"]
    p.dma("gpsimd", oc_o.ap(), OC[:], sem="sto", waits=(("dv", OCD),))
    p.wait("gpsimd", ("sto", p.cnt["sto"]))
    return p


def host_l1_attn_b_inputs(qs, kfull, vfull):
    return {
        "qs": qs, "kf": kfull, "vf": vfull,
        "onesc": np.ones((1, 128), dtype=np.float32),
        "onescol": np.ones((128, 1), dtype=np.float32),
    }


def build_l1_attn_c(p=None, io=None):
    if p is None:
        p = Prog("l1attnc")
    io = io or {}
    STO0 = p.cnt["sto"]
    xin_d = p.dram_io(io, "xin", [D, NTOK], F32, kind="ExternalInput")
    y_d = p.dram_io(io, "yrows", [NTOK, D], BF16, kind="ExternalInput")
    wo_d = p.dram_io(io, "wo", [D, D], F32, kind="ExternalInput")
    bo_d = p.dram_io(io, "bo", [128, 16], F32, kind="ExternalInput")
    xout_d = p.dram_io(io, "xout", [D, NTOK], F32, kind="ExternalOutput")

    Y = p.sbuf("Y", [128, 8 * D], BF16)
    YT = p.sbuf("YT", [128, NF * NTOK], BF16)
    wb = [p.sbuf(f"wb{i}", [128, NF * 128], BF16) for i in range(2)]
    xt = [p.sbuf(f"xt{i}", [128, NTOK], F32) for i in range(2)]
    scrB = p.sbuf("scrB", [128, NTOK], F32)
    scrC = p.sbuf("scrC", [128, NTOK], F32)
    bos = p.sbuf("bos", [128, 16], F32)
    ones128 = p.sbuf("ones128", [128, 128], BF16)
    idn = p.sbuf("idnf", [128, 128], BF16)

    psQ = [p.psum(f"psQ{i}", [128, 512], F32) for i in range(4)]
    psTR = p.psum("psTR", [128, 128], BF16)

    p.dma("sync", bos[:], bo_d.ap(), sem="ldc")
    LDC = p.cnt["ldc"]
    cmo = p.op("vector", "memset", ones128[:], 1.0)
    p.op("gpsimd", "affine_select", idn[:], ones128[:], pattern=[[-1, 128]],
         compare_op=ALU.is_equal, fill=0.0, channel_multiplier=1,
         waits=(("dv", cmo),))
    PL_ID = p.cnt["pl"]
    for j in range(8):
        p.dma("sync", Y[:, j * D:(j + 1) * D], y_d.ap()[j * 128:(j + 1) * 128, :],
              sem="ldy")
    LDY = p.cnt["ldy"]
    for j in range(8):
        for i in range(NF):
            cpe = p.op("tensor", "transpose", psTR[:],
                       Y[:, j * D + i * 128:j * D + (i + 1) * 128], idn[:],
                       sem="pe", waits=(("ldy", LDY), ("pl", PL_ID),
                                        ("ac", p.cnt["ac"])))
            p.op("scalar", "copy", YT[:, i * NTOK + j * 128:i * NTOK + (j + 1) * 128],
                 psTR[:], waits=(("pe", cpe),))
    YT_AC = p.cnt["ac"]

    pe_end = {}
    ev = {}
    for c in range(NF):
        lw = p.dma("gpsimd", wb[c % 2].ap().rearrange("p (a o) -> p a o", o=128),
                   wo_d.ap().rearrange("(a p) o -> p a o", p=128)[:, :, c * 128:(c + 1) * 128],
                   sem=f"ldw{c % 2}", waits=(("pe", pe_end.get(c - 2, 0)),))
        for h in range(2):
            ps = psQ[2 * (c % 2) + h]
            mm = 0
            for i in range(NF):
                mm = p.op("tensor", "matmul", ps[:],
                          wb[c % 2][:, i * 128:(i + 1) * 128],
                          YT[:, i * NTOK + h * 512:i * NTOK + h * 512 + 512],
                          start=(i == 0), stop=(i == NF - 1), sem="pe",
                          waits=(((f"ldw{c % 2}", lw), ("ac", YT_AC),
                                  ("dv", ev.get((c - 2, h), 0))) if i == 0 else ()))
            n = 2 * c + h
            ldx2 = p.dma("sync", xt[c % 2][:, h * 512:(h + 1) * 512],
                         xin_d.ap()[c * 128:(c + 1) * 128, h * 512:(h + 1) * 512],
                         sem="ldx2", waits=(("sto", STO0 + 16 * n),))
            c1 = p.op("vector", "tensor_scalar", scrB[:, 0:512], ps[:],
                      bos[:, c:c + 1], None, ALU.add,
                      waits=(("pe", mm), ("ldc", LDC)))
            evd = scrC[:, (n % 2) * 512:(n % 2) * 512 + 512]
            c2 = p.op("vector", "tensor_add", evd, scrB[:, 0:512],
                      xt[c % 2][:, h * 512:(h + 1) * 512],
                      waits=(("ldx2", ldx2), ("sto", STO0 + 16 * n)))
            ev[(c, h)] = c2
            p.dma("gpsimd", xout_d.ap()[c * 128:(c + 1) * 128, h * 512:(h + 1) * 512],
                  evd, sem="sto", waits=(("dv", c2),))
        pe_end[c] = p.cnt["pe"]
    p.wait("gpsimd", ("sto", p.cnt["sto"]))
    return p


def host_l1_attn_c_inputs(x_fm_c, yrows, ins):
    return {
        "xin": x_fm_c, "yrows": yrows,
        "wo": np.ascontiguousarray(np.float32(ins["L1_attn_Wo"])),
        "bo": np.ascontiguousarray(np.float32(ins["L1_attn_bo"]).reshape(16, 128).T),
    }


def build_ydist(p, io):
    ocg_d = io["ocg"]
    y_d = io["yrows"]
    ytbl_d = p.dram_io(io, "ytbl", [1, 8], I32, kind="ExternalInput")
    ocx_d = p.dram("ocx", [128, NTOK * 16], BF16, kind="Internal")
    ytbl = p.sbuf("ytbl_t", [1, 8], I32)
    p.dma("scalar", ytbl[:], ytbl_d.ap(), sem="yld")
    YLD = p.cnt["yld"]

    vals = {}

    def sc_hoist(engine):
        engine.wait_ge(p.semh("yld"), YLD)
        for i in range(8):
            r = engine.alloc_register()
            engine.reg_load(r, ytbl[0:1, i:i + 1])
            vals[i] = engine.snap(r, True, min_val=0, max_val=4 * 128 - 16)
    p.raw("scalar", sc_hoist)
    h1 = 0
    for i in range(8):
        h1 = p.dma("scalar", ocx_d.ap()[i * 16:(i + 1) * 16, :],
                   (lambda _i=i: ocg_d.ap()[bass.ds(vals[_i], 16), :]),
                   sem="yx1")
    for half in range(2):
        for gqblk in range(32):
            g, qb = gqblk // 4, gqblk % 4
            cc, half2 = min(g, 7 - g), (0 if g < 4 else 1)
            lo = (half2 * 512 + qb * 128) * 16
            src = bass.AP(ocx_d, (half * 64 + cc * 16) * (NTOK * 16) + lo,
                          [[NTOK * 16, 16], [1, 2048]])
            dst = bass.AP(y_d, (half * 512 + gqblk) * D, [[32 * D, 16], [1, 2048]])
            p.dma("gpsimd", dst, src, sem="yx2",
                  waits=((("yx1", h1),) if (half, gqblk) == (0, 0) else ()))
    return p.cnt["yx2"]


def host_ydist_table(c):
    cp = c % 4
    t = np.zeros((1, 8), dtype=np.int32)
    for half in range(2):
        dblk = cp if half == 0 else (7 - cp)
        for cc in range(4):
            t[0, half * 4 + cc] = cc * 128 + dblk * 16
    return t


def host_assemble_Y(oc_shards):
    Ys = []
    for b in range(B):
        Y = np.empty((T, D), dtype=BF)
        for cc in range(4):
            c = b * 4 + cc
            oc = np.asarray(oc_shards[c])
            s1, s2 = core_token_slices(c)
            for half, sl in ((0, s1), (1, s2)):
                for qb in range(4):
                    gqblk = sl.start // 128 + qb
                    lo = (half * 512 + qb * 128) * 16
                    blk = oc[:, lo:lo + 128 * 16]
                    Y[gqblk::32, :][0:0] if False else None
                    Y[np.arange(128) * 32 + gqblk, :] = blk
        Ys.append(Y)
    return Ys


G4 = [[0, 1, 2, 3], [4, 5, 6, 7]]
G8 = [list(range(8))]


def build_fused():
    p = Prog("FUSED")
    p.begin_stage("pp")
    tg_s = p.dram("tg_s", [D, 1024], BF16)
    tu_s = p.dram("tu_s", [D, 1024], BF16)
    td_s = p.dram("td_s", [1024, D], BF16)
    fs_s = p.dram("fs_s", [D, 256], BF16)
    mv_d = p.dram("mv", [1, 4], F32)
    build_prep(p, {"tg_T": tg_s, "tu_T": tu_s, "td_T": td_s,
                   "fsum": fs_s, "mvals": mv_d})
    tg_g = p.dram("tg_g", [8 * D, 1024], BF16)
    tu_g = p.dram("tu_g", [8 * D, 1024], BF16)
    td_g = p.dram("td_g", [8 * 1024, D], BF16)
    fs_g = p.dram("fs_g", [8 * D, 256], BF16)
    p.allgather(tg_s, tg_g, G8, "ccw", waits=(("sto", p.cnt["sto"]),))
    p.allgather(tu_s, tu_g, G8, "ccw")
    p.allgather(td_s, td_g, G8, "ccw")
    p.allgather(fs_s, fs_g, G8, "ccw")
    wio = {"tgT": tg_g, "tuT": tu_g, "tdT": td_g, "mv": mv_d, "gather_w": True}
    p.begin_stage("f1")
    xh1 = p.dram("xh1", [D, NTOK], F32)
    build_l0_ffn(p, dict(wio, xout=xh1))
    p.begin_stage("a1")
    qs1 = p.dram("qs1", [D, NTOK], BF16)
    ko1 = p.dram("ko1", [D, NTOK], BF16)
    vt1 = p.dram("vt1", [NTOK, D], BF16)
    go1 = p.dram("go1", [D, NTOK], BF16)
    build_l0_attn_a(p, {"xin": xh1, "qs": qs1, "ko": ko1, "vt": vt1, "go": go1})
    kg1 = p.dram("kg1", [4 * D, NTOK], BF16)
    vg1 = p.dram("vg1", [4 * NTOK, D], BF16)
    p.allgather(ko1, kg1, G4, "cck", waits=(("sto", p.cnt["sto"]),))
    p.allgather(vt1, vg1, G4, "cck")
    p.begin_stage("ab")
    xh2 = p.dram("xh2", [D, NTOK], F32)
    build_l0_attn_bu(p, {"xin": xh1, "qs": qs1, "kf": kg1, "vf": vg1,
                         "go": go1, "xout": xh2, "kv_gathered": True})
    p.begin_stage("f2")
    xh3 = p.dram("xh3", [D, NTOK], F32)
    build_l0_ffn(p, dict(wio, xin=xh2, xout=xh3))
    p.begin_stage("f3")
    xh4 = p.dram("xh4", [D, NTOK], F32)
    build_l1_ffn(p, {"xin": xh3, "xout": xh4, "fsum": fs_g, "gather_fs": True})
    p.begin_stage("a2")
    qs2 = p.dram("qs2", [D, NTOK], BF16)
    ko2 = p.dram("ko2", [D, NTOK], BF16)
    vt2 = p.dram("vt2", [NTOK, D], BF16)
    build_l1_attn_a(p, {"xin": xh4, "qs": qs2, "ko": ko2, "vt": vt2})
    kg2 = p.dram("kg2", [4 * D, NTOK], BF16)
    vg2 = p.dram("vg2", [4 * NTOK, D], BF16)
    p.allgather(ko2, kg2, G4, "cck", waits=(("sto", p.cnt["sto"]),))
    p.allgather(vt2, vg2, G4, "cck")
    p.begin_stage("bb")
    oc_d = p.dram("oc", [128, NTOK * 16], BF16)
    build_l1_attn_b(p, {"qs": qs2, "kf": kg2, "vf": vg2, "oc": oc_d,
                        "kv_gathered": True})
    ocg = p.dram("ocg", [4 * 128, NTOK * 16], BF16)
    p.allgather(oc_d, ocg, G4, "cco", waits=(("sto", p.cnt["sto"]),))
    p.begin_stage("yy")
    y_d = p.dram("yrows", [NTOK, D], BF16)
    build_ydist(p, {"ocg": ocg, "yrows": y_d})
    p.begin_stage("ac")
    xh5 = p.dram("xh5", [D, NTOK], F32)
    build_l1_attn_c(p, {"xin": xh4, "yrows": y_d, "xout": xh5})
    p.begin_stage("f4")
    build_l1_ffn(p, {"xin": xh5, "fsum": fs_g, "gather_fs": True})
    return p


def _asf(a, dt=None):
    a = np.asarray(a)
    if a.dtype == np.uint16:
        a = a.view(BF)
    if dt is not None:
        a = np.asarray(a, dt)
    return a


def _pref(tag, m, skip=()):
    return {tag + k: v for k, v in m.items() if k not in skip}


def _run(build_fn, in_maps, core_ids=None):
    p = build_fn() if callable(build_fn) else build_fn
    nc = p.finalize()
    if core_ids is None:
        core_ids = list(range(len(in_maps)))
    res = run_bass_kernel_spmd(nc, in_maps, core_ids=core_ids)
    return res.results


def kernel(**ins):
    x0 = np.asarray(ins["L0_attn_hidden_states"], np.float32)
    lnw_dev = np.ascontiguousarray(
        np.float32(ins["L0_ffn_ln_weight"]).reshape(64, 128).T)

    p = build_fused()
    zq = np.zeros((1, 1), dtype=np.float32)
    im = []
    for c in range(NCORE):
        xfc = np.ascontiguousarray(fm(shard_tokens(x0, c)))
        base_ffn = {"lnw": lnw_dev,
                    "onesc": np.ones((1, 128), dtype=np.float32),
                    "onescolb": np.ones((128, 1), dtype=BF)}
        m = {}
        m.update(_pref("pp_", host_prep_inputs(ins, c)))
        m.update(_pref("f1_", dict(base_ffn, xin=xfc)))
        m.update(_pref("a1_", host_attn_a_inputs(xfc, ins, c), skip=("xin",)))
        ab = host_attn_b_inputs(zq, zq, zq, zq, zq, ins)
        m.update(_pref("ab_", ab, skip=("xin", "qs", "kf", "vf", "go")))
        m["ab_tbl"] = host_attn_b_table(c, gathered=True)
        m.update(_pref("f2_", base_ffn))
        l1f = host_l1_ffn_inputs(zq, zq, ins)
        m.update(_pref("f3_", l1f, skip=("xin", "fsum")))
        m.update(_pref("a2_", host_l1_attn_a_inputs(zq, ins), skip=("xin",)))
        m.update(_pref("bb_", host_l1_attn_b_inputs(zq, zq, zq),
                       skip=("qs", "kf", "vf")))
        m["yy_ytbl"] = host_ydist_table(c)
        m.update(_pref("ac_", host_l1_attn_c_inputs(zq, zq, ins),
                       skip=("xin", "yrows")))
        m.update(_pref("f4_", l1f, skip=("xin", "fsum")))
        im.append(m)
    res = _run(p, im)
    xs = [np.ascontiguousarray(_asf(res[c]["f4_xout"], np.float32).T)
          for c in range(NCORE)]
    out = unshard_tokens(xs)
    return np.ascontiguousarray(out.astype(np.float32))


def kernel_5launch(**ins):
    x0 = np.asarray(ins["L0_attn_hidden_states"], np.float32)

    prep_res = _run(build_prep, [host_prep_inputs(ins, c) for c in range(NCORE)])
    tgT = np.ascontiguousarray(np.concatenate(
        [_asf(prep_res[c]["tg_T"]) for c in range(NCORE)], axis=1).astype(BF))
    tuT = np.ascontiguousarray(np.concatenate(
        [_asf(prep_res[c]["tu_T"]) for c in range(NCORE)], axis=1).astype(BF))
    tdT = np.ascontiguousarray(np.concatenate(
        [_asf(prep_res[c]["td_T"]) for c in range(NCORE)], axis=0).astype(BF))
    fsum = np.ascontiguousarray(np.concatenate(
        [_asf(prep_res[c]["fsum"]) for c in range(NCORE)], axis=1).astype(BF))
    mvals = [np.ascontiguousarray(_asf(prep_res[c]["mvals"], np.float32))
             for c in range(NCORE)]
    lnw_dev = np.ascontiguousarray(
        np.float32(ins["L0_ffn_ln_weight"]).reshape(64, 128).T)

    xf = [np.ascontiguousarray(fm(shard_tokens(x0, c))) for c in range(NCORE)]

    def assemble_kv(res, kname, vname):
        kfull, vfull = [], []
        for b in range(B):
            kf = np.empty((D, T), dtype=BF)
            vf = np.empty((T, D), dtype=BF)
            for cc in range(4):
                c = b * 4 + cc
                s1, s2 = core_token_slices(c)
                ko = _asf(res[c][kname]).astype(BF)
                vt = _asf(res[c][vname]).astype(BF)
                kf[:, s1] = ko[:, :CH]; kf[:, s2] = ko[:, CH:]
                vf[s1, :] = vt[:CH]; vf[s2, :] = vt[CH:]
            kfull.append(np.ascontiguousarray(kf))
            vfull.append(np.ascontiguousarray(vf))
        return kfull, vfull

    p = Prog("L2")
    p.begin_stage("f1")
    xh1 = p.dram("xh", [D, NTOK], F32, kind="ExternalOutput")
    build_l0_ffn(p, {"xout": xh1})
    p.begin_stage("a1")
    build_l0_attn_a(p, {"xin": xh1})
    im = []
    for c in range(NCORE):
        m = _pref("f1_", host_ffn_inputs(xf[c], tgT, tuT, tdT, lnw_dev, mvals[c]))
        m.update(_pref("a1_", host_attn_a_inputs(xf[c], ins, c), skip=("xin",)))
        im.append(m)
    res_a = _run(p, im)
    for c in range(NCORE):
        xf[c] = np.ascontiguousarray(_asf(res_a[c]["f1_xh"], np.float32))

    kfull, vfull = assemble_kv(res_a, "a1_ko", "a1_vt")

    p = Prog("L3")
    p.begin_stage("ab")
    xh2 = p.dram("xh2", [D, NTOK], F32)
    build_l0_attn_bu(p, {"xout": xh2})
    p.begin_stage("f2")
    xh3 = p.dram("xh3", [D, NTOK], F32)
    build_l0_ffn(p, {"xin": xh2, "xout": xh3})
    p.begin_stage("f3")
    xh4 = p.dram("xh4", [D, NTOK], F32, kind="ExternalOutput")
    build_l1_ffn(p, {"xin": xh3, "xout": xh4})
    p.begin_stage("a2")
    build_l1_attn_a(p, {"xin": xh4})
    im = []
    for c in range(NCORE):
        b = c // 4
        m = host_attn_b_inputs(
            xf[c], np.ascontiguousarray(_asf(res_a[c]["a1_qs"]).astype(BF)),
            kfull[b], vfull[b],
            np.ascontiguousarray(_asf(res_a[c]["a1_go"]).astype(BF)), ins)
        m["tbl"] = host_attn_b_table(c)
        m = _pref("ab_", m)
        m.update(_pref("f2_", host_ffn_inputs(xf[c], tgT, tuT, tdT, lnw_dev,
                                              mvals[c]), skip=("xin",)))
        m.update(_pref("f3_", host_l1_ffn_inputs(xf[c], fsum, ins),
                       skip=("xin",)))
        m.update(_pref("a2_", host_l1_attn_a_inputs(xf[c], ins), skip=("xin",)))
        im.append(m)
    res_a = _run(p, im)
    for c in range(NCORE):
        xf[c] = np.ascontiguousarray(_asf(res_a[c]["f3_xh4"], np.float32))

    kfull, vfull = assemble_kv(res_a, "a2_ko", "a2_vt")
    im = []
    for c in range(NCORE):
        b = c // 4
        im.append(host_l1_attn_b_inputs(
            np.ascontiguousarray(_asf(res_a[c]["a2_qs"]).astype(BF)),
            kfull[b], vfull[b]))
    res_b = _run(build_l1_attn_b, im)
    Ys = host_assemble_Y([_asf(res_b[c]["oc"]).astype(BF) for c in range(NCORE)])
    Ybtd = np.stack(Ys).reshape(B, T, D)

    p = Prog("L5")
    p.begin_stage("ac")
    xh5 = p.dram("xh5", [D, NTOK], F32)
    build_l1_attn_c(p, {"xout": xh5})
    p.begin_stage("f4")
    build_l1_ffn(p, {"xin": xh5})
    im = []
    for c in range(NCORE):
        m = _pref("ac_", host_l1_attn_c_inputs(
            xf[c], np.ascontiguousarray(shard_tokens(Ybtd, c)), ins))
        m.update(_pref("f4_", host_l1_ffn_inputs(xf[c], fsum, ins),
                       skip=("xin",)))
        im.append(m)
    res = _run(p, im)
    for c in range(NCORE):
        xf[c] = np.ascontiguousarray(_asf(res[c]["f4_xout"], np.float32))

    out = unshard_tokens([np.ascontiguousarray(xf[c].T) for c in range(NCORE)])
    return np.ascontiguousarray(out.astype(np.float32))
